# version 01
# baseline (speedup 1.0000x reference)
import numpy as np
from contextlib import ExitStack

import concourse.bass as bass
import concourse.mybir as mybir
from concourse.bass_utils import run_bass_kernel_spmd

F32 = mybir.dt.float32
BF16 = mybir.dt.bfloat16
AF = mybir.ActivationFunctionType
ALU = mybir.AluOpType
AX = mybir.AxisListType

D = 1024
T = 4096
TILE = 512
NT = T // TILE
SL = 128
DEPTH = 6
FF = 4096
FFT = 256
E = 8
HIDD = 256
LEAK = 0.01
EPS_LN = 1e-5
EPS_RMS = 1e-6

V, A, TT, G, SY = "vector", "scalar", "tensor", "gpsimd", "sync"

RG = [[0, 1], [2, 3], [4, 5], [6, 7]]


class KB:
    ENGS = ("sync", "gpsimd", "vector", "scalar", "tensor")

    def __init__(self):
        self.nc = bass.Bass("TRN2", target_bir_lowering=False)
        self.prog = {e: [] for e in self.ENGS}
        self.es = ExitStack()
        self.ph = None
        self.semh = {}
        self.ctr = {}
        for s in ("s_x", "s_w", "s_c", "s_ac", "s_cc", "s_pe", "s_v", "s_a", "s_g", "s_f",
                  "s_w1a", "s_w1b", "s_w2a", "s_w2b", "s_xia", "s_xib", "s_st"):
            self.semh[s] = self.es.enter_context(self.nc.semaphore(s))
            self.ctr[s] = 0
        self.stopped = False

    _uid = 0

    def sb(self, name, shape, dt, local=False):
        KB._uid += 1
        st = self.ph if local else self.es
        return st.enter_context(self.nc.sbuf_tensor(f"{name}_{KB._uid}", shape, dt))

    def psum(self, name, shape, dt):
        return self.es.enter_context(self.nc.psum_tensor(name, shape, dt))

    def dram(self, name, shape, dt, kind="Internal"):
        return self.nc.dram_tensor(name, shape, dt, kind=kind)

    def phase(self):
        if self.ph is not None and not self.stopped:
            for e in ("vector", "scalar"):
                self.di(e, "s_f", 1)
            for e in ("gpsimd", "tensor", "sync"):
                self.op(e, "nop", inc=("s_f", 1))
            tgt = self.ctr["s_f"]
            for e in ("vector", "scalar", "gpsimd", "tensor", "sync"):
                self.w(e, "s_f", tgt)
        if self.ph is not None:
            self.ph.close()
        self.ph = ExitStack()

    def w(self, eng, sem, val=None):
        if self.stopped:
            return
        if val is None:
            val = self.ctr[sem]
        if val <= 0:
            return
        h = self.semh[sem]
        self.prog[eng].append(lambda e, hh=h, vv=val: e.wait_ge(hh, vv))

    def op(self, eng, meth, *a, inc=None, **kw):
        if self.stopped:
            return 0
        if inc:
            nm, k = inc
            self.ctr[nm] += k
            h = self.semh[nm]
            self.prog[eng].append(
                lambda e, m=meth, aa=a, kk=kw, hh=h, n=k: getattr(e, m)(*aa, **kk).then_inc(hh, n))
            return self.ctr[nm]
        self.prog[eng].append(lambda e, m=meth, aa=a, kk=kw: getattr(e, m)(*aa, **kk))
        return 0

    def dma(self, eng, out, in_, sem="s_x", **kw):
        return self.op(eng, "dma_start", inc=(sem, 16), out=out, in_=in_, **kw)

    def mm(self, out, lhsT, rhs, start, stop, inc=None):
        return self.op("tensor", "matmul", out, lhsT, rhs, start=start, stop=stop, inc=inc)

    def di(self, eng, sem, k=1):
        return self.op(eng, "drain", inc=(sem, k))

    def build(self):
        nc = self.nc
        with nc.Block() as block:
            @block.sync
            def _(eng):
                for f in self.prog["sync"]:
                    f(eng)

            @block.gpsimd
            def _(eng):
                for f in self.prog["gpsimd"]:
                    f(eng)

            @block.vector
            def _(eng):
                for f in self.prog["vector"]:
                    f(eng)

            @block.scalar
            def _(eng):
                for f in self.prog["scalar"]:
                    f(eng)

            @block.tensor
            def _(eng):
                for f in self.prog["tensor"]:
                    f(eng)
        return nc


class Mega:
    def __init__(self, dbg=None):
        self.K = KB()
        self.dbg = dbg
        self._declare()
        self._prep()

    def _declare(self):
        K = self.K
        self.xin = K.dram("x", [T, D], F32, kind="ExternalInput").ap()
        self.hmask = K.dram("hmask", [128, 1], F32, kind="ExternalInput").ap()
        self.win = {}
        for nm, shp in [
            ("ln1_w", [DEPTH, D]), ("ln1_b", [DEPTH, D]), ("wq", [DEPTH, D, D]),
            ("wk", [DEPTH, D, D]), ("wv", [DEPTH, D, D]), ("wo", [DEPTH, D, D]),
            ("bo", [DEPTH, D]), ("ln2_w", [DEPTH, D]), ("ln2_b", [DEPTH, D]),
            ("w1", [DEPTH, D, FF]), ("w2", [DEPTH, FF, D]),
            ("dw_kernel", [D, 1, 3]), ("dw_bias", [D]), ("pw_w", [D, D]), ("pw_b", [D]),
            ("router_w", [E, D]), ("router_b", [E]), ("expert_w", [E, D, D]),
            ("expert_b", [E, D]), ("out_w", [D, D]), ("out_b", [D]), ("norm_w", [D]),
            ("ff_down_w", [HIDD, D]), ("ff_down_b", [HIDD]), ("ff_up_w", [D, HIDD]),
            ("ff_up_b", [D]),
        ]:
            self.win[nm] = K.dram("w_" + nm, shp, F32, kind="ExternalInput").ap()
        self.out = K.dram("out", [T, D], F32, kind="ExternalOutput").ap()
        if self.dbg:
            self.dump = K.dram("dump", [8, 128, T], F32, kind="ExternalOutput").ap()
            self.dump2 = K.dram("dump2", [1032, 128], F32, kind="ExternalOutput").ap()

        self.x1sp = K.dram("x1sp", [8, 128, T], F32)
        self.rowsd = K.dram("rowsd", [NT, 1, 2 * TILE], F32)
        self.combd = K.dram("combd", [T, D], F32)
        self.xibd = K.dram("xibd", [8, 128, T], BF16)
        self.w1_s = K.dram("w1_s", [DEPTH, FF // 128, 128, 8 * 128], BF16)
        self.w2_s = K.dram("w2_s", [DEPTH, 8, 128, 32 * 128], BF16)
        self.pw_s = K.dram("pw_s", [D, D], BF16)
        self.ow_s = K.dram("ow_s", [D, D], BF16)
        self.ew_s = K.dram("ew_s", [E, D, D], BF16)
        self.fd_s = K.dram("fd_s", [HIDD, D], BF16)
        self.fu_s = K.dram("fu_s", [D, HIDD], BF16)
        self.eb_s = K.dram("eb_s", [E, D], BF16)
        self.cc_in = K.dram("cc_in", [1032, 128], F32)
        self.cc_out = K.dram("cc_out", [1032, 128], F32)
        self.hl_in = K.dram("hl_in", [D, 2], F32)
        self.hl_out = K.dram("hl_out", [2 * D, 2], F32)

        self.X = [K.sb(f"X{c}", [128, T], F32) for c in range(8)]
        self.WS = [K.sb("WS0", [128, 8 * D], BF16), K.sb("WS1", [128, 8 * D], BF16)]
        self.ABR = K.sb("ABR", [128, 2 * TILE], F32)
        self.SCR = K.sb("SCR", [128, TILE], F32)
        self.SCR2 = K.sb("SCR2", [128, TILE], F32)
        self.ROW = K.sb("ROW", [1, 2 * TILE], F32)
        self.ONC = K.sb("ONC", [128, 1], F32)
        self.QMASK = K.sb("QMASK", [128, 8 * 8], BF16)
        self.QMD = K.sb("QMD", [128, 4], BF16)
        self.IDF = K.sb("IDF", [128, 128], F32)
        self.IDB = K.sb("IDB", [128, 128], BF16)
        self.ONEF = K.sb("ONEF", [1, 128], F32)
        self.ONEB = K.sb("ONEB", [1, 128], BF16)
        self.HALO = K.sb("HALO", [128, 16], F32)
        self.CARRY = K.sb("CARRY", [128, 16], F32)
        self.MCOL = K.sb("MCOL", [128, 1], F32)
        self.RBROW = K.sb("RBROW", [1, E], F32)
        self.RB128 = K.sb("RB128", [128, E], F32)
        self.RW = K.sb("RW", [128, (T // SL) * E], F32)
        self.PK = {}
        for nm, ncol in [("ln1_w", 48), ("ln1_b", 48), ("ln2_w", 48), ("ln2_b", 48),
                         ("bo", 48), ("vec5", 40), ("dwk", 24), ("fdb", 2)]:
            self.PK[nm] = K.sb("PK_" + nm, [128, ncol], F32)
        self.CONSTP = K.sb("CONSTP", [128, 8], F32)
        self.PS = [K.psum(f"PS{i}", [128, 512], F32) for i in range(7)]
        self.PSB = K.psum("PSB", [128, 1024], BF16)
        self.wld = {nm: [0] * DEPTH for nm in ("wq", "wk", "wv", "wo", "w2")}

    def load_w(self, slot_ap, src_dd):
        return self.K.dma(G, slot_ap.rearrange("p (k m) -> p k m", k=8),
                          src_dd.rearrange("(k p) m -> p k m", p=128), "s_w")

    def cast_w12(self, l):
        K = self.K
        for mi in range(32):
            K.dma(G, self.w1_s.ap()[l, mi].rearrange("p (k m) -> p k m", k=8),
                  self.win["w1"][l][:, mi * 128:(mi + 1) * 128]
                  .rearrange("(k p) m -> p k m", p=128), "s_c")
        for j in range(8):
            self.w12_cast[l] = K.dma(
                G, self.w2_s.ap()[l, j].rearrange("p (mi jj) -> p mi jj", mi=32),
                self.win["w2"][l][:, j * 128:(j + 1) * 128]
                .rearrange("(mi p) jj -> p mi jj", p=128), "s_c")

    def _prep(self):
        K = self.K
        win = self.win
        dg = lambda: K.op(G, "drain")
        K.op(G, "memset", self.IDF[:], 0.0)
        dg()
        K.op(G, "affine_select", out=self.IDF[:], in_=self.IDF[:], compare_op=ALU.not_equal,
             fill=1.0, base=0, pattern=[[-1, 128]], channel_multiplier=1)
        K.op(G, "memset", self.IDB[:], 0.0)
        dg()
        K.op(G, "affine_select", out=self.IDB[:], in_=self.IDB[:], compare_op=ALU.not_equal,
             fill=1.0, base=0, pattern=[[-1, 128]], channel_multiplier=1)
        K.op(G, "memset", self.ONEF[:], 1.0)
        K.op(G, "memset", self.ONEB[:], 1.0)
        K.op(G, "memset", self.QMASK[:], 0.0)
        dg()
        for h in range(8):
            K.op(G, "memset", self.QMASK[:, h * 8 + h:h * 8 + h + 1], 1.0)
        K.op(G, "memset", self.ONC[:], 1.0)
        K.op(G, "memset", self.QMD[:], 0.0)
        dg()
        K.op(G, "memset", self.QMD[:, 0:1], 1.0)
        K.op(G, "memset", self.QMD[:, 3:4], 1.0)
        K.op(G, "memset", self.CARRY[:], 0.0)
        for i, cv in enumerate([EPS_LN, EPS_RMS, 1.0, -0.5, -1.0, LEAK, 1.0 / D, 2.0]):
            K.op(G, "memset", self.CONSTP[:, i:i + 1], cv)
            K.nc.const_aps.register(F32, cv, self.CONSTP[:, i:i + 1])
        K.op(G, "drain", inc=("s_g", 1))
        self.g_masks = K.ctr["s_g"]

        K.dma(G, self.pw_s.ap(), win["pw_w"], "s_c")
        K.dma(G, self.ow_s.ap(), win["out_w"], "s_c")
        for e in range(E):
            K.dma(G, self.ew_s.ap()[e], win["expert_w"][e], "s_c")
        K.dma(G, self.fd_s.ap(), win["ff_down_w"], "s_c")
        K.dma(G, self.fu_s.ap(), win["ff_up_w"], "s_c")
        K.dma(G, self.eb_s.ap(), win["expert_b"], "s_c")
        self.g_scr = K.ctr["s_c"]
        K.w(G, "s_c")
        self.load_w(self.WS[0][:], win["wk"][0])
        self.wld["wk"][0] = self.wld["wv"][0] = \
            self.load_w(self.WS[1][:], win["wv"][0])
        K.w(G, "s_w")
        self.w12_cast = [0] * DEPTH
        self.cast_w12(0)
        K.w(G, "s_c")

        K.phase()
        PKS = K.sb("PKS", [64, 128], F32, local=True)
        jobs = [
            ("ln1_w", win["ln1_w"].rearrange("l (c p) -> (l c) p", p=128), 48),
            ("ln1_b", win["ln1_b"].rearrange("l (c p) -> (l c) p", p=128), 48),
            ("ln2_w", win["ln2_w"].rearrange("l (c p) -> (l c) p", p=128), 48),
            ("ln2_b", win["ln2_b"].rearrange("l (c p) -> (l c) p", p=128), 48),
            ("bo", win["bo"].rearrange("l (c p) -> (l c) p", p=128), 48),
            ("fdb", win["ff_down_b"].rearrange("(c p) -> c p", p=128), 2),
        ]
        K.w(TT, "s_g", self.g_masks)
        for nm, src, ncol in jobs:
            K.w(SY, "s_v")
            ld = K.dma(SY, PKS[0:ncol, :], src)
            K.w(TT, "s_x", ld)
            K.w(TT, "s_v")
            K.op(TT, "transpose", self.PS[0][:, 0:ncol], PKS[0:ncol, :],
                 self.IDF[0:ncol, 0:ncol], inc=("s_pe", 1))
            K.w(V, "s_pe")
            K.op(V, "tensor_copy", self.PK[nm][:], self.PS[0][:, 0:ncol])
            K.di(V, "s_v")
        for j, nm in enumerate(["dw_bias", "pw_b", "out_b", "norm_w", "ff_up_b"]):
            K.w(SY, "s_v")
            ld = K.dma(SY, PKS[0:8, :], win[nm].rearrange("(c p) -> c p", p=128))
            K.w(TT, "s_x", ld)
            K.w(TT, "s_v")
            K.op(TT, "transpose", self.PS[0][:, 0:8], PKS[0:8, :], self.IDF[0:8, 0:8],
                 inc=("s_pe", 1))
            K.w(V, "s_pe")
            K.op(V, "tensor_copy", self.PK["vec5"][:, j * 8:(j + 1) * 8], self.PS[0][:, 0:8])
            K.di(V, "s_v")
        K.w(SY, "s_v")
        K.dma(SY, self.PK["dwk"][:].rearrange("p (c k) -> p c k", c=8),
              win["dw_kernel"].rearrange("(c p) o k -> p c (o k)", p=128))
        K.dma(SY, self.RBROW[:], win["router_b"].rearrange("(a e) -> a e", a=1))
        K.dma(SY, self.MCOL[:], self.hmask)
        K.w(TT, "s_x", K.ctr["s_x"])
        K.w(TT, "s_v")
        K.op(TT, "matmul", self.PS[0][:, 0:E], self.ONEF[:], self.RBROW[:],
             start=True, stop=True, inc=("s_pe", 1))
        K.w(V, "s_pe")
        K.op(V, "tensor_copy", self.RB128[:], self.PS[0][:, 0:E])
        K.di(V, "s_v")
        self.prep_v = K.ctr["s_v"]

    def ln_stats_rows(self, tcols, bank, rowbuf, eps, with_mean=True, rms=False,
                      srcs=None, scr=None, raw=False):
        K = self.K
        t0, tn = tcols
        srcs = srcs or [self.X[c][:, t0:t0 + tn] for c in range(8)]
        scr = scr if scr is not None else self.SCR
        PS = self.PS
        for c in range(8):
            K.w(V, "s_pe")
            K.op(V, "tensor_mul", scr[:, 0:tn], srcs[c], srcs[c])
            K.di(V, "s_v")
            K.w(TT, "s_v")
            if not rms:
                K.mm(PS[bank][0:1, 0:tn], self.ONC[:], srcs[c], start=(c == 0), stop=(c == 7))
            K.mm(PS[bank + 1][0:1, 0:tn], self.ONC[:], scr[:, 0:tn],
                 start=(c == 0), stop=(c == 7), inc=("s_pe", 1))
        K.w(A, "s_pe")
        K.w(A, "s_x")
        if not rms:
            K.op(A, "activation", rowbuf[0:1, 0:tn], PS[bank][0:1, 0:tn], AF.Identity,
                 scale=1.0 / D)
        K.op(A, "activation", rowbuf[0:1, TILE:TILE + tn], PS[bank + 1][0:1, 0:tn],
             AF.Identity, scale=1.0 / D)
        K.di(A, "s_a")
        if raw:
            return
        self.finish_rows(rowbuf, tn, eps, with_mean, scr)

    def finish_rows(self, rowbuf, tn, eps, with_mean, scr=None):
        K = self.K
        scr = scr if scr is not None else self.SCR
        K.w(V, "s_a")
        if with_mean:
            K.op(V, "tensor_mul", scr[0:1, 0:tn], rowbuf[0:1, 0:tn], rowbuf[0:1, 0:tn])
            K.op(V, "drain")
            K.op(V, "tensor_sub", rowbuf[0:1, TILE:TILE + tn],
                 rowbuf[0:1, TILE:TILE + tn], scr[0:1, 0:tn])
            K.di(V, "s_v")
            K.w(A, "s_v")
        K.op(A, "activation", scr[0:1, 0:tn], rowbuf[0:1, TILE:TILE + tn], AF.Ln, bias=eps)
        K.op(A, "drain")
        K.op(A, "activation", scr[0:1, 0:tn], scr[0:1, 0:tn], AF.Exp, scale=-0.5)
        K.di(A, "s_a")
        K.w(V, "s_a")
        if with_mean:
            K.op(V, "scalar_tensor_tensor", rowbuf[0:1, TILE:TILE + tn],
                 rowbuf[0:1, 0:tn], -1.0, scr[0:1, 0:tn], ALU.mult, ALU.mult)
            K.op(V, "drain")
        else:
            K.op(V, "memset", rowbuf[0:1, TILE:TILE + tn], 0.0)
        K.op(V, "tensor_copy", rowbuf[0:1, 0:tn], scr[0:1, 0:tn])
        K.di(V, "s_v")

    def ln_bcast(self, rowsrc, tn, banks, abr, beta=True):
        K = self.K
        K.w(TT, "s_v")
        K.w(TT, "s_a")
        K.mm(self.PS[banks[0]][:, 0:tn], self.ONEF[:], rowsrc[0:1, 0:tn],
             start=True, stop=True, inc=None if beta else ("s_pe", 1))
        if beta:
            K.mm(self.PS[banks[1]][:, 0:tn], self.ONEF[:], rowsrc[0:1, TILE:TILE + tn],
                 start=True, stop=True, inc=("s_pe", 1))
        K.w(A, "s_pe")
        K.op(A, "activation", abr[:, 0:tn], self.PS[banks[0]][:, 0:tn], AF.Identity)
        if beta:
            K.op(A, "activation", abr[:, TILE:TILE + tn], self.PS[banks[1]][:, 0:tn],
                 AF.Identity)
        K.di(A, "s_a")
        return K.ctr["s_a"]

    def ln_apply_chunk(self, xsrc, tn, abr, wcol, bcol, out_bf, scrf):
        K = self.K
        K.w(V, "s_a")
        K.op(V, "tensor_mul", scrf[:, 0:tn], xsrc, abr[:, 0:tn])
        K.op(V, "drain")
        K.op(V, "tensor_add", scrf[:, 0:tn], scrf[:, 0:tn], abr[:, TILE:TILE + tn])
        K.di(V, "s_v")
        K.w(A, "s_v")
        if wcol is not None:
            K.op(A, "activation", out_bf, scrf[:, 0:tn], AF.Identity, scale=wcol, bias=bcol)
        else:
            K.op(A, "activation", out_bf, scrf[:, 0:tn], AF.Identity)
        K.di(A, "s_a")
        return K.ctr["s_a"]

    def initial(self, nsl=T // SL, do_ln=True):
        K = self.K
        K.phase()
        TMJ = [K.sb(f"iTMJ{i}", [128, D], F32, local=True) for i in range(2)]
        TSQ = K.sb("iTSQ", [128, D], F32, local=True)
        AL = K.sb("iAL", [128, 4], F32, local=True)
        X, PS, IDF = self.X, self.PS, self.IDF
        K.w(TT, "s_g", self.g_masks)
        K.w(A, "s_g", self.g_masks)
        K.w(V, "s_g", self.g_masks)
        K.w(V, "s_v", self.prep_v)
        for s in range(nsl):
            par = s % 2
            K.w(SY, "s_pe")
            ld = K.dma(SY, TMJ[par][:], self.xin[s * SL:(s + 1) * SL, :])
            K.w(V, "s_x", ld)
            K.w(V, "s_pe")
            if not do_ln:
                K.op(V, "tensor_copy", TSQ[:], TMJ[par][:])
                K.di(V, "s_v")
                self._init_transpose(s, TSQ, par)
                continue
            K.op(V, "tensor_mul", TSQ[:], TMJ[par][:], TMJ[par][:])
            K.op(V, "reduce_sum", AL[:, 0:1], TMJ[par][:], axis=AX.X)
            K.op(V, "drain")
            K.op(V, "reduce_sum", AL[:, 1:2], TSQ[:], axis=AX.X)
            K.op(V, "drain")
            K.op(V, "tensor_scalar", AL[:, 0:1], AL[:, 0:1], 1.0 / D, None, ALU.mult)
            K.op(V, "tensor_scalar", AL[:, 1:2], AL[:, 1:2], 1.0 / D, None, ALU.mult)
            K.op(V, "drain")
            K.op(V, "tensor_mul", TSQ[:, 0:1], AL[:, 0:1], AL[:, 0:1])
            K.op(V, "drain")
            K.op(V, "tensor_sub", AL[:, 1:2], AL[:, 1:2], TSQ[:, 0:1])
            K.di(V, "s_v")
            K.w(A, "s_v")
            K.op(A, "activation", AL[:, 2:3], AL[:, 1:2], AF.Ln, bias=EPS_LN)
            K.op(A, "drain")
            K.op(A, "activation", AL[:, 2:3], AL[:, 2:3], AF.Exp, scale=-0.5)
            K.di(A, "s_a")
            K.w(V, "s_a")
            K.op(V, "scalar_tensor_tensor", AL[:, 3:4], AL[:, 0:1], -1.0, AL[:, 2:3],
                 ALU.mult, ALU.mult)
            K.op(V, "drain")
            K.op(V, "tensor_scalar", TSQ[:], TMJ[par][:], AL[:, 2:3], AL[:, 3:4],
                 ALU.mult, ALU.add)
            K.di(V, "s_v")
            self._init_transpose(s, TSQ, par)

    def _init_transpose(self, s, TSQ, par):
        K = self.K
        X, PS, IDF = self.X, self.PS, self.IDF
        K.w(TT, "s_v")
        for c in range(8):
            K.op(TT, "transpose", PS[c // 4][:, (c % 4) * 128:(c % 4 + 1) * 128],
                 TSQ[:, c * 128:(c + 1) * 128], IDF[:],
                 inc=("s_pe", 1) if c == 7 else None)
        K.w(V, "s_pe")
        K.w(A, "s_pe")
        for c in range(8):
            eng, meth = (V, "tensor_copy") if c < 4 else (A, "copy")
            K.op(eng, meth, X[c][:, s * SL:(s + 1) * SL],
                 PS[c // 4][:, (c % 4) * 128:(c % 4 + 1) * 128])
        K.di(V, "s_v")
        K.di(A, "s_a")
        K.w(TT, "s_v")
        K.w(TT, "s_a")

    def pass1(self, l):
        K = self.K
        K.phase()
        XN = K.sb("p1XN", [128, 2 * 8 * TILE], BF16, local=True)
        KK = [K.sb(f"p1K{i}", [128, 512], BF16, local=True) for i in range(2)]
        VV = [K.sb(f"p1V{i}", [128, 512], BF16, local=True) for i in range(2)]
        USB = K.sb("p1U", [128, 1024], F32, local=True)
        DRW = K.sb("p1DRW", [2, 512], F32, local=True)
        X, PS = self.X, self.PS
        WK = self.WS[l % 2]
        WV = self.WS[1 - l % 2]
        abrs = [self.ABR, self.ABR]

        K.w(TT, "s_w", self.wld["wk"][l])
        K.w(TT, "s_w", self.wld["wv"][l])
        K.op(V, "memset", USB[:], 0.0, inc=("s_v", 1))

        self.rows_x = [0] * NT
        kfree = [0, 0]
        vfree = [0, 0]
        kkfree = [0, 0]
        rows_x = [0] * NT

        def tile_stats(t):
            self.ln_stats_rows((t * TILE, TILE), 0, self.ROW, EPS_LN)
            K.w(SY, "s_v")
            rows_x[t] = K.dma(SY, self.rowsd.ap()[t], self.ROW[0:1, :])
            K.w(SY, "s_x")
            self.rows_x[t] = rows_x[t]

        def tile_apply(t):
            abr = abrs[t % 2]
            self.ln_bcast(self.ROW, TILE, (0, 1), abr)
            xnb = XN[:, (t % 2) * 8 * TILE:(t % 2) * 8 * TILE + 8 * TILE]
            for c in range(8):
                self.ln_apply_chunk(X[c][:, t * TILE:(t + 1) * TILE], TILE, abr,
                                    self.PK["ln1_w"][:, l * 8 + c:l * 8 + c + 1],
                                    self.PK["ln1_b"][:, l * 8 + c:l * 8 + c + 1],
                                    xnb[:, c * TILE:(c + 1) * TILE], self.SCR2)

        tile_stats(0)
        tile_apply(0)
        xn_done = [0] * NT
        xn_done[0] = K.ctr["s_a"]
        for t in range(NT):
            xnb = XN[:, (t % 2) * 8 * TILE:(t % 2) * 8 * TILE + 8 * TILE]
            for s in range(4):
                for half in range(2):
                    kb, vb = 2 + half, 4 + half
                    K.w(TT, "s_a", xn_done[t])
                    K.w(TT, "s_v", kfree[half])
                    for kc in range(8):
                        K.mm(PS[kb][:], xnb[:, kc * TILE + s * 128: kc * TILE + s * 128 + 128],
                             WK[:, kc * 1024 + half * 512: kc * 1024 + (half + 1) * 512],
                             start=(kc == 0), stop=(kc == 7),
                             inc=("s_pe", 1) if kc == 7 else None)
                    k_done = K.ctr["s_pe"]
                    K.w(TT, "s_a", vfree[half])
                    for kc in range(8):
                        K.mm(PS[vb][:], xnb[:, kc * TILE + s * 128: kc * TILE + s * 128 + 128],
                             WV[:, kc * 1024 + half * 512: kc * 1024 + (half + 1) * 512],
                             start=(kc == 0), stop=(kc == 7),
                             inc=("s_pe", 1) if kc == 7 else None)
                    v_done = K.ctr["s_pe"]
                    K.w(A, "s_pe", k_done)
                    K.w(A, "s_pe", kkfree[half])
                    K.op(A, "activation", KK[half][:], PS[kb][:], AF.Exp)
                    K.di(A, "s_a")
                    K.w(A, "s_pe", v_done)
                    K.op(A, "activation", VV[half][:], PS[vb][:], AF.Identity)
                    K.di(A, "s_a")
                    ev_done = K.ctr["s_a"]
                    vfree[half] = ev_done
                    K.w(TT, "s_a", ev_done)
                    for hh in range(4):
                        hcol = hh * 128
                        K.mm(PS[kb][:, hcol:hcol + 128], KK[half][:, hcol:hcol + 128],
                             VV[half][:, hcol:hcol + 128], start=True, stop=True)
                    islast = (t == NT - 1 and s == 3 and half == 1)
                    K.mm(PS[6][0:2, 0:512], self.QMD[:, half * 2:half * 2 + 2], KK[half][:],
                         start=(t == 0 and s == 0 and half == 0), stop=islast,
                         inc=("s_pe", 1))
                    kkfree[half] = K.ctr["s_pe"]
                    K.w(V, "s_pe", kkfree[half])
                    K.op(V, "tensor_add", USB[:, half * 512:(half + 1) * 512],
                         USB[:, half * 512:(half + 1) * 512], PS[kb][:], inc=("s_v", 1))
                    kfree[half] = K.ctr["s_v"]
                if s == 1 and t + 1 < NT:
                    tile_stats(t + 1)
            if t + 1 < NT:
                tile_apply(t + 1)
                xn_done[t + 1] = K.ctr["s_a"]
        K.w(A, "s_pe")
        K.op(A, "activation", DRW[0:2, :], PS[6][0:2, 0:512], AF.Identity)
        K.di(A, "s_a")
        K.w(SY, "s_v")
        K.w(SY, "s_a")
        K.w(SY, "s_x")
        K.dma(SY, self.cc_in.ap()[0:1024, :].rearrange("(h d) e -> d h e", d=128),
              USB[:].rearrange("p (h e) -> p h e", h=8))
        n = K.dma(SY, self.cc_in.ap()[1024:1032, :].rearrange("(a h) d -> a h d", a=2),
                  DRW[0:2, :].rearrange("a (h d) -> a h d", h=4))
        K.w(G, "s_x", n)
        K.op(G, "collective_compute", "AllReduce", ALU.add,
             replica_groups=RG,
             ins=[self.cc_in.ap()], outs=[self.cc_out.ap()], inc=("s_cc", 1))
        K.w(G, "s_pe")
        self.load_w(WK[:], self.win["wq"][l])
        self.wld["wq"][l] = self.wld["wo"][l] = self.load_w(WV[:], self.win["wo"][l])
        K.w(G, "s_w")
        if l + 1 < DEPTH:
            self.cast_w12(l + 1)
            K.w(G, "s_c")

    def pass2a(self, l):
        K = self.K
        K.phase()
        XN = K.sb("p2XN", [128, 8 * TILE], BF16, local=True)
        SEL8 = K.sb("p2SEL8", [8, 8 * 128], F32, local=True)
        K.op(G, "memset", SEL8[:], 1.0)
        K.op(G, "drain")
        K.op(G, "affine_select", out=SEL8[:], in_=SEL8[:], compare_op=ALU.is_gt,
             fill=0.0, base=1, pattern=[[1, 1024]], channel_multiplier=-128)
        K.op(G, "drain")
        K.op(G, "affine_select", out=SEL8[:], in_=SEL8[:], compare_op=ALU.is_gt,
             fill=0.0, base=128, pattern=[[-1, 1024]], channel_multiplier=128)
        K.op(G, "drain", inc=("s_g", 1))
        sel8_g = K.ctr["s_g"]
        QE = K.sb("p2QE", [128, 8 * TILE], BF16, local=True)
        ATT = XN
        CTX = K.sb("p2CTX", [128, 8 * 128], BF16, local=True)
        QROW = K.sb("p2QROW", [8, TILE], F32, local=True)
        USB2 = K.sb("p2USB", [128, 1024], F32, local=True)
        DTMP = K.sb("p2DT", [8, 128], F32, local=True)
        DCOL = K.sb("p2DC", [128, 8], F32, local=True)
        X, PS = self.X, self.PS
        WQ = self.WS[l % 2]
        WO = self.WS[1 - l % 2]

        K.w(G, "s_cc")
        n = K.dma(G, DTMP[:], self.cc_out.ap()[1024:1032, :], "s_w")
        n = K.dma(G, USB2[:].rearrange("p (h e) -> p h e", h=8),
                  self.cc_out.ap()[0:1024, :].rearrange("(h d) e -> d h e", d=128), "s_w")
        K.w(G, "s_w")
        K.w(TT, "s_w", n)
        K.op(TT, "transpose", PS[0][:, 0:8], DTMP[:], self.IDF[0:8, 0:8], inc=("s_pe", 1))
        K.w(A, "s_pe")
        K.op(A, "activation", DCOL[:], PS[0][:, 0:8], AF.Ln)
        K.op(A, "drain")
        K.op(A, "activation", DCOL[:], DCOL[:], AF.Exp, scale=-1.0)
        K.di(A, "s_a")
        K.w(V, "s_a")
        for h in range(8):
            K.op(V, "tensor_scalar", CTX[:, h * 128:(h + 1) * 128],
                 USB2[:, h * 128:(h + 1) * 128], DCOL[:, h:h + 1], None, ALU.mult)
        K.di(V, "s_v")
        ctx_v = K.ctr["s_v"]
        K.w(TT, "s_w", self.wld["wq"][l])
        K.w(TT, "s_w", self.wld["wo"][l])

        for t in range(NT):
            t0 = t * TILE
            K.w(SY, "s_pe")
            K.w(SY, "s_x", self.rows_x[t])
            nr = K.dma(SY, self.ROW[0:1, :], self.rowsd.ap()[t])
            K.w(V, "s_x", nr)
            K.w(TT, "s_x", nr)
            K.w(V, "s_a")
            self.ln_bcast(self.ROW, TILE, (0, 1), self.ABR)
            for c in range(8):
                self.ln_apply_chunk(X[c][:, t0:t0 + TILE], TILE, self.ABR,
                                    self.PK["ln1_w"][:, l * 8 + c:l * 8 + c + 1],
                                    self.PK["ln1_b"][:, l * 8 + c:l * 8 + c + 1],
                                    XN[:, c * TILE:(c + 1) * TILE], self.SCR2)
            xn_done = K.ctr["s_a"]
            K.w(TT, "s_a", xn_done)
            qev = [0, 0]
            for m in range(8):
                qb = 2 + m % 2
                K.w(TT, "s_a", qev[m % 2])
                for kc in range(8):
                    K.mm(PS[qb][:], WQ[:, kc * 1024 + m * 128: kc * 1024 + (m + 1) * 128],
                         XN[:, kc * TILE:(kc + 1) * TILE], start=(kc == 0), stop=(kc == 7),
                         inc=("s_pe", 1) if kc == 7 else None)
                qdone = K.ctr["s_pe"]
                K.w(A, "s_pe", qdone)
                K.op(A, "activation", QE[:, m * TILE:(m + 1) * TILE], PS[qb][:], AF.Exp)
                K.di(A, "s_a")
                qev[m % 2] = K.ctr["s_a"]
                K.w(TT, "s_a", qev[m % 2])
                K.mm(PS[4][0:8, 0:TILE], self.QMASK[:, m * 8:(m + 1) * 8],
                     QE[:, m * TILE:(m + 1) * TILE], start=(m == 0), stop=(m == 7),
                     inc=("s_pe", 1) if m == 7 else None)
            qd_done = K.ctr["s_pe"]
            K.w(A, "s_pe", qd_done)
            K.op(A, "activation", QROW[0:8, :], PS[4][0:8, 0:TILE], AF.Ln)
            K.op(A, "drain")
            K.op(A, "activation", QROW[0:8, :], QROW[0:8, :], AF.Exp, scale=-1.0)
            K.di(A, "s_a")
            qr_done = K.ctr["s_a"]
            if t == 0:
                K.w(TT, "s_v", ctx_v)
            aev_v, aev_a = 0, 0
            for h in range(8):
                ab = 5 if h % 2 == 0 else 2
                bb = 6 if h % 2 == 0 else 3
                K.w(TT, "s_a", qr_done)
                K.w(TT, "s_v", aev_v)
                K.w(TT, "s_a", aev_a)
                K.mm(PS[ab][:], CTX[:, h * 128:(h + 1) * 128], QE[:, h * TILE:(h + 1) * TILE],
                     start=True, stop=True)
                if t == 0 and h == 0:
                    K.w(TT, "s_g", sel8_g)
                K.mm(PS[bb][:], SEL8[:, h * 128:(h + 1) * 128], QROW[0:8, :],
                     start=True, stop=True, inc=("s_pe", 1))
                hdone = K.ctr["s_pe"]
                K.w(A, "s_pe", hdone)
                K.w(A, "s_v")
                K.op(A, "activation", self.SCR[:], PS[bb][:], AF.Identity)
                K.di(A, "s_a")
                aev_a = K.ctr["s_a"]
                K.w(V, "s_a", aev_a)
                K.op(V, "tensor_mul", ATT[:, h * TILE:(h + 1) * TILE], PS[ab][:], self.SCR[:])
                K.di(V, "s_v")
                aev_v = K.ctr["s_v"]
            wev = [aev_v, aev_v]
            for m in range(8):
                ob = 5 if m % 2 == 0 else 2
                K.w(TT, "s_v", wev[m % 2])
                for kc in range(8):
                    K.mm(PS[ob][:], WO[:, kc * 1024 + m * 128: kc * 1024 + (m + 1) * 128],
                         ATT[:, kc * TILE:(kc + 1) * TILE], start=(kc == 0), stop=(kc == 7),
                         inc=("s_pe", 1) if kc == 7 else None)
                odone = K.ctr["s_pe"]
                K.w(A, "s_pe", odone)
                K.w(A, "s_v")
                K.op(A, "activation", self.SCR2[:], PS[ob][:], AF.Identity,
                     bias=self.PK["bo"][:, l * 8 + m:l * 8 + m + 1])
                K.di(A, "s_a")
                K.w(V, "s_a")
                K.op(V, "tensor_add", X[m][:, t0:t0 + TILE], X[m][:, t0:t0 + TILE], self.SCR2[:])
                K.di(V, "s_v")
                wev[m % 2] = K.ctr["s_v"]
        K.w(SY, "s_pe")
        K.w(SY, "s_c", self.w12_cast[l])
        self.wld["w2"][l] = K.dma(SY, WQ[:, 0:4096], self.w2_s.ap()[l, 0], "s_w2a")

    def pass2b(self, l):
        K = self.K
        NFT = T // FFT
        K.phase()
        XN = K.sb("p3XN", [128, 2 * 8 * FFT], BF16, local=True)
        HID = K.sb("p3HID", [128, 32 * FFT], BF16, local=True)
        W1B = K.sb("p3W1", [128, 2 * 1024], BF16, local=True)
        X, PS = self.X, self.PS
        W2S = self.WS[l % 2]
        WKN = self.WS[1 - l % 2]
        abrs = [self.ABR, self.ABR]
        w1ld = [0, 0]
        w2ld = [self.wld["w2"][l], 0]

        def load_w1(mi, buf):
            K.w(SY, "s_pe")
            w1ld[buf] = K.dma(SY, W1B[:, buf * 1024:(buf + 1) * 1024],
                              self.w1_s.ap()[l, mi], "s_w1a" if buf == 0 else "s_w1b")

        def load_w2(j, buf):
            K.w(SY, "s_pe")
            w2ld[buf] = K.dma(SY, W2S[:, buf * 4096:(buf + 1) * 4096],
                              self.w2_s.ap()[l, j], "s_w2a" if buf == 0 else "s_w2b")

        K.w(SY, "s_c", self.w12_cast[l])
        load_w1(0, 0)
        load_w1(1, 1)
        load_w2(1, 1)

        def tile_stats(t):
            self.ln_stats_rows((t * FFT, FFT), 0, self.ROW, EPS_LN, scr=self.SCR2)

        def tile_apply(t):
            abr = abrs[t % 2]
            self.ln_bcast(self.ROW, FFT, (0, 1), abr)
            xnb = XN[:, (t % 2) * 8 * FFT:(t % 2) * 8 * FFT + 8 * FFT]
            for c in range(8):
                self.ln_apply_chunk(X[c][:, t * FFT:(t + 1) * FFT], FFT, abr,
                                    self.PK["ln2_w"][:, l * 8 + c:l * 8 + c + 1],
                                    self.PK["ln2_b"][:, l * 8 + c:l * 8 + c + 1],
                                    xnb[:, c * FFT:(c + 1) * FFT], self.SCR2)

        tile_stats(0)
        tile_apply(0)
        xn_done = K.ctr["s_a"]
        leak_v = 0
        for t in range(NFT):
            t0 = t * FFT
            xnb = XN[:, (t % 2) * 8 * FFT:(t % 2) * 8 * FFT + 8 * FFT]
            K.w(TT, "s_a", xn_done)
            lk = [leak_v, leak_v]
            for mi in range(32):
                hb = 2 + mi % 2
                K.w(TT, "s_w1a" if mi % 2 == 0 else "s_w1b", w1ld[mi % 2])
                K.w(TT, "s_v", lk[mi % 2])
                for kc in range(8):
                    K.mm(PS[hb][:, 0:FFT],
                         W1B[:, (mi % 2) * 1024 + kc * 128:(mi % 2) * 1024 + (kc + 1) * 128],
                         xnb[:, kc * FFT:(kc + 1) * FFT], start=(kc == 0), stop=(kc == 7),
                         inc=("s_pe", 1) if kc == 7 else None)
                hdone = K.ctr["s_pe"]
                nxt = mi + 2
                if nxt < 32 or t + 1 < NFT:
                    load_w1(nxt % 32, nxt % 2)
                K.w(A, "s_pe", hdone)
                K.w(A, "s_v")
                K.op(A, "activation", self.SCR[:, 0:FFT], PS[hb][:, 0:FFT], AF.Identity,
                     scale=LEAK)
                K.di(A, "s_a")
                K.w(V, "s_a")
                K.op(V, "tensor_tensor", HID[:, mi * FFT:(mi + 1) * FFT], PS[hb][:, 0:FFT],
                     self.SCR[:, 0:FFT], ALU.max)
                K.di(V, "s_v")
                lk[mi % 2] = K.ctr["s_v"]
            leak_v = K.ctr["s_v"]
            if t + 1 < NFT:
                tile_stats(t + 1)
            res_v = [0, 0]
            K.w(TT, "s_v", leak_v)
            for j in range(8):
                jb = 4 + j % 2
                K.w(TT, "s_w2a" if j % 2 == 0 else "s_w2b", w2ld[j % 2])
                K.w(TT, "s_v", res_v[j % 2])
                for mi in range(32):
                    K.mm(PS[jb][:, 0:FFT],
                         W2S[:, (j % 2) * 4096 + mi * 128:(j % 2) * 4096 + (mi + 1) * 128],
                         HID[:, mi * FFT:(mi + 1) * FFT], start=(mi == 0), stop=(mi == 31),
                         inc=("s_pe", 1) if mi == 31 else None)
                jdone = K.ctr["s_pe"]
                nj = j + 2
                if nj < 8 or t + 1 < NFT:
                    load_w2(nj % 8, nj % 2)
                K.w(V, "s_pe", jdone)
                K.op(V, "tensor_add", X[j][:, t0:t0 + FFT], X[j][:, t0:t0 + FFT],
                     PS[jb][:, 0:FFT])
                K.di(V, "s_v")
                res_v[j % 2] = K.ctr["s_v"]
            if t + 1 < NFT:
                tile_apply(t + 1)
                xn_done = K.ctr["s_a"]
        if l + 1 < DEPTH:
            K.w(G, "s_pe")
            self.load_w(WKN[:], self.win["wk"][l + 1])
            self.wld["wk"][l + 1] = self.wld["wv"][l + 1] =                 self.load_w(W2S[:], self.win["wv"][l + 1])

    def x1_form(self):
        K = self.K
        K.phase()
        TMJ = [K.sb(f"fTMJ{i}", [128, D], F32, local=True) for i in range(2)]
        X, PS, IDF = self.X, self.PS, self.IDF
        abrs = [self.ABR, self.ABR]
        self.ln_stats_rows((0, TILE), 0, self.ROW, EPS_LN)
        ab_done = self.ln_bcast(self.ROW, TILE, (0, 1), abrs[0])
        for t in range(NT):
            t0 = t * TILE
            abr = abrs[t % 2]
            for s in range(4):
                par = s % 2
                K.w(SY, "s_pe")
                ld = K.dma(SY, TMJ[par][:], self.xin[t0 + s * 128: t0 + (s + 1) * 128, :])
                K.w(TT, "s_x", ld)
                K.w(TT, "s_v")
                for c in range(8):
                    K.op(TT, "transpose", PS[2 + (c % 4)][:, (c // 4) * 128:(c // 4) * 128 + 128],
                         TMJ[par][:, c * 128:(c + 1) * 128], IDF[:],
                         inc=("s_pe", 1) if c == 7 else None)
                trd = K.ctr["s_pe"]
                K.w(V, "s_pe", trd)
                K.w(V, "s_a", ab_done)
                for c in range(8):
                    cs = slice(t0 + s * 128, t0 + (s + 1) * 128)
                    K.op(V, "tensor_mul", self.SCR[:, 0:128], X[c][:, cs],
                         abr[:, s * 128:(s + 1) * 128])
                    K.op(V, "drain")
                    K.op(V, "tensor_add", self.SCR[:, 0:128], self.SCR[:, 0:128],
                         abr[:, TILE + s * 128:TILE + (s + 1) * 128])
                    K.op(V, "drain")
                    K.op(V, "tensor_add", X[c][:, cs], self.SCR[:, 0:128],
                         PS[2 + (c % 4)][:, (c // 4) * 128:(c // 4) * 128 + 128])
                    K.op(V, "drain")
                K.di(V, "s_v")
            K.w(SY, "s_v")
            for c in range(8):
                self.spill_x = K.dma(SY, self.x1sp.ap()[c, :, t0:t0 + TILE],
                                     X[c][:, t0:t0 + TILE])
            if t + 1 < NT:
                self.ln_stats_rows(((t + 1) * TILE, TILE), 0, self.ROW, EPS_LN)
                ab_done = self.ln_bcast(self.ROW, TILE, (0, 1), abrs[(t + 1) % 2])

    def ffn_s1_s2(self):
        K = self.K
        K.phase()
        X, PS = self.X, self.PS
        HB = K.sb("s2HB", [128, 2], F32, local=True)
        for t in range(NT):
            self.ln_stats_rows((t * TILE, TILE), 0, self.ROW, EPS_LN)
            K.w(SY, "s_v")
            self.rows_x[t] = K.dma(SY, self.rowsd.ap()[t], self.ROW[0:1, :])
            K.w(SY, "s_x")
            if t == NT - 1:
                K.w(TT, "s_v")
                K.mm(PS[1][:, 0:2], self.ONEF[:], self.ROW[0:1, 510:512],
                     start=True, stop=True)
                K.mm(PS[2][:, 0:2], self.ONEF[:], self.ROW[0:1, TILE + 510:TILE + 512],
                     start=True, stop=True, inc=("s_pe", 1))
                K.w(V, "s_pe")
                nh = 0
                for c in range(8):
                    K.w(V, "s_x")
                    K.op(V, "tensor_mul", HB[:, 0:2], X[c][:, 4094:4096], PS[1][:, 0:2])
                    K.op(V, "drain")
                    K.op(V, "tensor_add", HB[:, 0:2], HB[:, 0:2], PS[2][:, 0:2])
                    K.di(V, "s_v")
                    K.w(SY, "s_v")
                    nh = K.dma(SY, self.hl_in.ap()[c * 128:(c + 1) * 128, :], HB[:, 0:2])
                K.w(G, "s_x", nh)
                K.op(G, "collective_compute", "AllGather", ALU.bypass,
                     replica_groups=RG,
                     ins=[self.hl_in.ap()], outs=[self.hl_out.ap()], inc=("s_cc", 1))
                K.w(G, "s_cc")
                nhl = K.dma(G, self.HALO[:].rearrange("p (c t) -> p c t", c=8),
                            self.hl_out.ap()[0:1024, :].rearrange("(c p) t -> p c t", p=128),
                            "s_w")
                K.w(G, "s_w")
                K.w(V, "s_w", nhl)
                K.op(V, "tensor_scalar", self.HALO[:], self.HALO[:], self.MCOL[:, 0:1],
                     None, ALU.mult)
                K.di(V, "s_v")

        XA = K.sb("s2XA", [128, 8 * (TILE + 2)], F32, local=True)
        XC = K.sb("s2XC", [128, 8 * TILE], BF16, local=True)
        PWT = self.WS[0]
        K.w(SY, "s_x")
        K.w(SY, "s_c", self.g_scr)
        K.w(SY, "s_pe")
        npw = 0
        for st in range(8):
            npw = K.dma(SY, PWT[:, st * 1024:(st + 1) * 1024],
                        self.pw_s.ap()[:, st * 128:(st + 1) * 128], transpose=True)
        K.w(SY, "s_x")
        MG = TILE + 2
        resv = 0
        for t in range(NT):
            t0 = t * TILE
            K.w(SY, "s_pe")
            K.w(SY, "s_x", self.rows_x[t])
            nr = K.dma(SY, self.ROW[0:1, :], self.rowsd.ap()[t])
            K.w(V, "s_x", nr)
            K.w(TT, "s_x", nr)
            K.w(V, "s_a")
            self.ln_bcast(self.ROW, TILE, (0, 1), self.ABR)
            ab = K.ctr["s_a"]
            K.w(V, "s_a", ab)
            for c in range(8):
                src = self.HALO[:, c * 2:c * 2 + 2] if t == 0 else self.CARRY[:, c * 2:c * 2 + 2]
                K.w(V, "s_a")
                K.op(V, "tensor_copy", XA[:, c * MG:c * MG + 2], src)
                K.op(V, "tensor_mul", self.SCR[:], X[c][:, t0:t0 + TILE], self.ABR[:, 0:TILE])
                K.op(V, "drain")
                K.op(V, "tensor_add", XA[:, c * MG + 2:c * MG + 2 + TILE], self.SCR[:],
                     self.ABR[:, TILE:2 * TILE])
                K.op(V, "drain")
                K.op(V, "tensor_copy", self.CARRY[:, c * 2:c * 2 + 2],
                     XA[:, c * MG + TILE:c * MG + 2 + TILE])
                K.op(V, "tensor_scalar", self.SCR[:], XA[:, c * MG:c * MG + TILE],
                     self.PK["dwk"][:, c * 3:c * 3 + 1], None, ALU.mult)
                K.op(V, "drain")
                K.op(V, "scalar_tensor_tensor", self.SCR[:], XA[:, c * MG + 1:c * MG + 1 + TILE],
                     self.PK["dwk"][:, c * 3 + 1:c * 3 + 2], self.SCR[:], ALU.mult, ALU.add)
                K.op(V, "drain")
                K.op(V, "scalar_tensor_tensor", self.SCR[:], XA[:, c * MG + 2:c * MG + 2 + TILE],
                     self.PK["dwk"][:, c * 3 + 2:c * 3 + 3], self.SCR[:], ALU.mult, ALU.add)
                K.di(V, "s_v")
                K.w(A, "s_v")
                K.op(A, "activation", XC[:, c * TILE:(c + 1) * TILE], self.SCR[:], AF.Identity,
                     bias=self.PK["vec5"][:, 0 * 8 + c:0 * 8 + c + 1])
                K.di(A, "s_a")
            xc_done = K.ctr["s_a"]
            if t == 0:
                K.w(TT, "s_x", npw)
            K.w(TT, "s_a", xc_done)
            pwv = [resv, resv]
            for m in range(8):
                pb = 2 + m % 2
                K.w(TT, "s_v", pwv[m % 2])
                for kc in range(8):
                    K.mm(PS[pb][:], PWT[:, kc * 1024 + m * 128: kc * 1024 + (m + 1) * 128],
                         XC[:, kc * TILE:(kc + 1) * TILE], start=(kc == 0), stop=(kc == 7),
                         inc=("s_pe", 1) if kc == 7 else None)
                pdone = K.ctr["s_pe"]
                K.w(A, "s_pe", pdone)
                K.w(A, "s_v")
                K.op(A, "activation", self.SCR2[:], PS[pb][:], AF.Silu,
                     bias=self.PK["vec5"][:, 1 * 8 + m:1 * 8 + m + 1])
                K.di(A, "s_a")
                K.w(V, "s_a")
                K.op(V, "tensor_add", X[m][:, t0:t0 + TILE],
                     XA[:, m * MG + 2:m * MG + 2 + TILE], self.SCR2[:])
                K.di(V, "s_v")
                pwv[m % 2] = K.ctr["s_v"]
            resv = K.ctr["s_v"]

    def ffn_s3(self):
        K = self.K
        K.phase()
        X, PS = self.X, self.PS
        LGR = K.sb("s3LGR", [128, 8], F32, local=True)
        EL = K.sb("s3EL", [128, 8], F32, local=True)
        TP = K.sb("s3TP", [128, 8], F32, local=True)
        M1 = K.sb("s3M1", [128, 4], F32, local=True)
        RWTC = K.sb("s3RWTC", [128, 8 * 8], F32, local=True)
        RWST = K.sb("s3RWST", [8, 1024], F32, local=True)
        LGD = K.sb("s3LGD", [128, 8], F32, local=True)
        self._LGD = LGD
        n = K.dma(SY, RWST[:], self.win["router_w"])
        K.w(SY, "s_x")
        K.w(TT, "s_x", n)
        for kc in range(8):
            K.op(TT, "transpose", PS[2][:, (kc % 2) * 8:(kc % 2) * 8 + 8],
                 RWST[:, kc * 128:(kc + 1) * 128], self.IDF[0:8, 0:8],
                 inc=("s_pe", 1))
            K.w(V, "s_pe")
            K.op(V, "tensor_copy", RWTC[:, kc * 8:(kc + 1) * 8],
                 PS[2][:, (kc % 2) * 8:(kc % 2) * 8 + 8])
            K.di(V, "s_v")
            K.w(TT, "s_v")
        rw_ready = K.ctr["s_v"]
        K.w(TT, "s_v", rw_ready)
        self.rw_done = [0] * NT
        for t in range(NT):
            t0 = t * TILE
            for s in range(4):
                K.w(TT, "s_v")
                for kc in range(8):
                    K.mm(PS[2][0:128, 0:8],
                         X[kc][:, t0 + s * 128:t0 + s * 128 + 128],
                         RWTC[:, kc * 8:(kc + 1) * 8], start=(kc == 0), stop=(kc == 7),
                         inc=("s_pe", 1) if kc == 7 else None)
                lgd = K.ctr["s_pe"]
                K.w(V, "s_pe", lgd)
                K.op(V, "tensor_add", LGR[:], PS[2][0:128, 0:8], self.RB128[:])
                K.op(V, "drain")
                if self.dbg == "rw" and t == 0 and s == 0:
                    K.op(V, "tensor_copy", LGD[:], LGR[:])
                    K.op(V, "drain")
                K.op(V, "reduce_max", M1[:, 0:1], LGR[:], axis=AX.X)
                K.op(V, "drain")
                K.op(V, "tensor_scalar", M1[:, 1:2], M1[:, 0:1], -1.0, None, ALU.mult)
                K.op(V, "tensor_scalar", TP[:], LGR[:], M1[:, 0:1], None, ALU.is_ge)
                K.op(V, "drain")
                K.op(V, "scalar_tensor_tensor", TP[:], TP[:], -1e30, LGR[:], ALU.mult, ALU.add)
                K.op(V, "drain")
                K.op(V, "reduce_max", M1[:, 2:3], TP[:], axis=AX.X)
                K.di(V, "s_v")
                K.w(A, "s_v")
                K.op(A, "activation", EL[:], LGR[:], AF.Exp, bias=M1[:, 1:2])
                K.op(A, "activation", M1[:, 3:4], M1[:, 2:3], AF.Exp, bias=M1[:, 1:2])
                K.op(A, "drain")
                K.op(A, "activation", M1[:, 3:4], M1[:, 3:4], AF.Ln, bias=1.0)
                K.op(A, "drain")
                K.op(A, "activation", M1[:, 3:4], M1[:, 3:4], AF.Exp, scale=-1.0)
                K.di(A, "s_a")
                K.w(V, "s_a")
                K.op(V, "tensor_scalar", TP[:], LGR[:], M1[:, 2:3], None, ALU.is_ge)
                K.op(V, "drain")
                K.op(V, "tensor_mul", TP[:], TP[:], EL[:])
                K.op(V, "drain")
                sg = t * 4 + s
                K.op(V, "tensor_scalar", self.RW[:, sg * 8:(sg + 1) * 8], TP[:],
                     M1[:, 3:4], None, ALU.mult)
                K.di(V, "s_v")
            self.rw_done[t] = K.ctr["s_v"]
        K.w(G, "s_v")
        self.xib_ld = 0
        for c in range(8):
            self.xib_ld = K.dma(G, self.xibd.ap()[c], X[c][:], "s_c")
        K.w(G, "s_c")

    def ffn_s4(self):
        K = self.K
        K.phase()
        PS = self.PS
        XIB = K.sb("s4XIB", [128, 2 * 8 * TILE], BF16, local=True)
        COMBP = K.sb("s4CP", [128, 2 * 1024], F32, local=True)
        EBR = K.sb("s4EB", [1, D], BF16, local=True)
        accd = [[0] * NT for _ in range(E)]
        last_comb_dma = [0]
        xld = [0, 0]

        def load_xib(t, buf):
            K.w(SY, "s_pe")
            sm = "s_xia" if buf == 0 else "s_xib"
            n = 0
            for c in range(8):
                n = K.dma(SY, XIB[:, buf * 8 * TILE + c * TILE: buf * 8 * TILE + (c + 1) * TILE],
                          self.xibd.ap()[c, :, t * TILE:(t + 1) * TILE], sm)
            xld[buf] = n

        K.w(SY, "s_c", self.xib_ld)
        load_xib(0, 0)
        load_xib(1, 1)
        ws_free_pe = 0
        EMAX = 1 if self.dbg == "comb1" else (2 if self.dbg == "comb2" else E)
        for e in range(EMAX):
            K.w(G, "s_ac")
            WT = self.WS[1 - e % 2]
            K.w(SY, "s_pe", ws_free_pe)
            nw = 0
            for st in range(8):
                nw = K.dma(SY, WT[:, st * 1024:(st + 1) * 1024],
                           self.ew_s.ap()[e][:, st * 128:(st + 1) * 128], transpose=True)
            K.w(SY, "s_x")
            ne = K.dma(G, EBR[:], self.eb_s.ap()[e:e + 1, :], "s_w")
            K.w(G, "s_w")
            K.w(TT, "s_x", nw)
            K.w(TT, "s_w", ne)
            for t in range(NT):
                buf = t % 2
                K.w(TT, "s_xia" if buf == 0 else "s_xib", xld[buf])
                for s in range(4):
                    if s % 2 == 0:
                        K.w(V, "s_ac", last_comb_dma[0])
                    for half in range(2):
                        yb = 2 + (s * 2 + half) % 2
                        K.w(TT, "s_a")
                        xb = XIB[:, buf * 8 * TILE:]
                        for kc in range(8):
                            K.mm(PS[yb][:],
                                 xb[:, kc * TILE + s * 128: kc * TILE + s * 128 + 128],
                                 WT[:, kc * 1024 + half * 512: kc * 1024 + (half + 1) * 512],
                                 start=(kc == 0), stop=False)
                        K.mm(PS[yb][:], self.ONEB[:],
                             EBR[0:1, half * 512:(half + 1) * 512],
                             start=False, stop=True, inc=("s_pe", 1))
                        ydone = K.ctr["s_pe"]
                        K.w(A, "s_pe", ydone)
                        K.w(A, "s_v")
                        scr = self.SCR if half == 0 else self.SCR2
                        K.op(A, "activation", scr[:], PS[yb][:], AF.Silu)
                        K.di(A, "s_a")
                        K.w(V, "s_a")
                        dst = COMBP[:, (s % 2) * 1024 + half * 512:
                                    (s % 2) * 1024 + (half + 1) * 512]
                        rwc = self.RW[:, (t * 4 + s) * 8 + e:(t * 4 + s) * 8 + e + 1]
                        K.op(V, "tensor_scalar", dst, scr[:], rwc, None, ALU.mult)
                        K.di(V, "s_v")
                    if s % 2 == 1:
                        hf0 = t * TILE + (s // 2) * 256
                        K.w(G, "s_v")
                        K.w(G, "s_ac")
                        accd[e][t] = K.dma(
                            G, self.combd.ap()[hf0:hf0 + 256, :]
                            .rearrange("(sl p) o -> p sl o", p=128),
                            COMBP[:].rearrange("p (sl o) -> p sl o", sl=2), "s_ac",
                            accum_op=(ALU.bypass if e == 0 else ALU.add))
                        last_comb_dma[0] = accd[e][t]
                if not (e == E - 1 and t >= NT - 2):
                    load_xib((t + 2) % NT, buf)
            ws_free_pe = K.ctr["s_pe"]
        self.acc_last = accd[E - 1]
        K.w(SY, "s_xia")
        K.w(SY, "s_xib")

    def ffn_s4b(self):
        K = self.K
        K.phase()
        X, PS, PSB = self.X, self.PS, self.PSB
        CMB = K.sb("s4bC", [128, 2 * 1024], F32, local=True)
        CB = K.sb("s4bCB", [128, 2 * 1024], BF16, local=True)
        CMT = K.sb("s4bT", [128, 8 * 256], BF16, local=True)
        OWT = self.WS[1]
        K.w(SY, "s_pe")
        nw = 0
        for st in range(8):
            nw = K.dma(SY, OWT[:, st * 1024:(st + 1) * 1024],
                       self.ow_s.ap()[:, st * 128:(st + 1) * 128], transpose=True)
        K.w(SY, "s_x")
        K.w(TT, "s_x", nw)
        for t in range(NT):
            t0 = t * TILE
            K.w(SY, "s_ac", self.acc_last[t])
            for hf in range(2):
                hf0 = t0 + hf * 256
                K.w(SY, "s_v")
                nc_ = K.dma(SY, CMB[:].rearrange("p (sl o) -> p sl o", sl=2),
                            self.combd.ap()[hf0:hf0 + 256, :]
                            .rearrange("(sl p) o -> p sl o", p=128))
                K.w(V, "s_x", nc_)
                K.op(V, "tensor_copy", CB[:], CMB[:])
                K.di(V, "s_v")
                cbd = K.ctr["s_v"]
                K.w(TT, "s_v", cbd)
                K.w(TT, "s_a")
                for sl in range(2):
                    for oc in range(8):
                        K.op(TT, "transpose", PSB[:, (oc % 4) * 256 + sl * 128:
                                                  (oc % 4) * 256 + sl * 128 + 128],
                             CB[:, sl * 1024 + oc * 128: sl * 1024 + (oc + 1) * 128],
                             self.IDB[:], inc=("s_pe", 1) if oc % 4 == 3 else None)
                        if oc % 4 == 3:
                            grp = K.ctr["s_pe"]
                            K.w(V, "s_pe", grp)
                            for o2 in range(oc - 3, oc + 1):
                                K.op(V, "tensor_copy",
                                     CMT[:, o2 * 256 + sl * 128: o2 * 256 + sl * 128 + 128],
                                     PSB[:, (o2 % 4) * 256 + sl * 128:
                                         (o2 % 4) * 256 + sl * 128 + 128])
                            K.di(V, "s_v")
                            K.w(TT, "s_v")
                cmt_v = K.ctr["s_v"]
                K.w(TT, "s_v", cmt_v)
                oev = [0, 0]
                for m in range(8):
                    ob = 2 + m % 2
                    K.w(TT, "s_v", oev[m % 2])
                    for oc in range(8):
                        K.mm(PS[ob][:, 0:256],
                             OWT[:, oc * 1024 + m * 128: oc * 1024 + (m + 1) * 128],
                             CMT[:, oc * 256:(oc + 1) * 256], start=(oc == 0), stop=(oc == 7),
                             inc=("s_pe", 1) if oc == 7 else None)
                    od = K.ctr["s_pe"]
                    K.w(A, "s_pe", od)
                    K.w(A, "s_v")
                    K.op(A, "activation", self.SCR2[:, 0:256], PS[ob][:, 0:256], AF.Identity,
                         bias=self.PK["vec5"][:, 2 * 8 + m:2 * 8 + m + 1])
                    K.di(A, "s_a")
                    K.w(V, "s_a")
                    K.op(V, "tensor_add", X[m][:, hf0:hf0 + 256], X[m][:, hf0:hf0 + 256],
                         self.SCR2[:, 0:256])
                    K.di(V, "s_v")
                    oev[m % 2] = K.ctr["s_v"]
            self.ln_stats_rows((t0, TILE), 0, self.ROW, EPS_RMS, with_mean=False, rms=True,
                               raw=True)
            K.w(SY, "s_a")
            self.rows_x[t] = K.dma(SY, self.rowsd.ap()[t], self.ROW[0:1, :])
            K.w(SY, "s_x")

    def ffn_s5(self):
        K = self.K
        K.phase()
        X, PS = self.X, self.PS
        HBUF = K.sb("s5HB", [128, 8 * TILE], BF16, local=True)
        FH = K.sb("s5FH", [128, 2 * TILE], BF16, local=True)
        FDT = K.sb("s5FDT", [128, 8 * HIDD], BF16, local=True)
        FUT = K.sb("s5FUT", [128, 2 * 1024], BF16, local=True)
        n = 0
        K.w(SY, "s_pe")
        for st in range(8):
            n = K.dma(SY, FDT[:, st * HIDD:(st + 1) * HIDD],
                      self.fd_s.ap()[:, st * 128:(st + 1) * 128], transpose=True)
        for st in range(2):
            n = K.dma(SY, FUT[:, st * 1024:(st + 1) * 1024],
                      self.fu_s.ap()[:, st * 128:(st + 1) * 128], transpose=True)
        K.w(SY, "s_x")
        K.w(TT, "s_x", n)
        for t in range(NT):
            t0 = t * TILE
            K.w(SY, "s_pe")
            K.w(SY, "s_x", self.rows_x[t])
            nr = K.dma(SY, self.ROW[0:1, :], self.rowsd.ap()[t])
            K.w(A, "s_x", nr)
            K.w(A, "s_v")
            K.op(A, "activation", self.ROW[0:1, 0:TILE], self.ROW[0:1, TILE:2 * TILE],
                 AF.Ln, bias=EPS_RMS)
            K.op(A, "drain")
            K.op(A, "activation", self.ROW[0:1, 0:TILE], self.ROW[0:1, 0:TILE],
                 AF.Exp, scale=-0.5)
            K.di(A, "s_a")
            self.ln_bcast(self.ROW, TILE, (0, 1), self.ABR, beta=False)
            abd = K.ctr["s_a"]
            K.w(V, "s_a", abd)
            for c in range(8):
                K.w(V, "s_a")
                K.op(V, "scalar_tensor_tensor", self.SCR[:], X[c][:, t0:t0 + TILE],
                     1.0, self.ABR[:, 0:TILE], ALU.mult, ALU.mult)
                K.di(V, "s_v")
                K.w(A, "s_v")
                K.op(A, "activation", X[c][:, t0:t0 + TILE], self.SCR[:], AF.Identity,
                     scale=self.PK["vec5"][:, 3 * 8 + c:3 * 8 + c + 1])
                K.op(A, "drain")
                K.op(A, "copy", HBUF[:, c * TILE:(c + 1) * TILE], X[c][:, t0:t0 + TILE])
                K.di(A, "s_a")
            hb_done = K.ctr["s_a"]
            K.w(TT, "s_a", hb_done)
            for m2 in range(2):
                fb = 2 + m2
                K.w(TT, "s_a")
                for kc in range(8):
                    K.mm(PS[fb][:], FDT[:, kc * HIDD + m2 * 128: kc * HIDD + (m2 + 1) * 128],
                         HBUF[:, kc * TILE:(kc + 1) * TILE], start=(kc == 0), stop=(kc == 7),
                         inc=("s_pe", 1) if kc == 7 else None)
                fd = K.ctr["s_pe"]
                K.w(A, "s_pe", fd)
                K.op(A, "activation", FH[:, m2 * TILE:(m2 + 1) * TILE], PS[fb][:], AF.Gelu,
                     bias=self.PK["fdb"][:, m2:m2 + 1])
                K.di(A, "s_a")
            fh_done = K.ctr["s_a"]
            K.w(TT, "s_a", fh_done)
            fev = [0, 0]
            for m in range(8):
                ub = 4 + m % 2
                K.w(TT, "s_v", fev[m % 2])
                for kc in range(2):
                    K.mm(PS[ub][:], FUT[:, kc * 1024 + m * 128: kc * 1024 + (m + 1) * 128],
                         FH[:, kc * TILE:(kc + 1) * TILE], start=(kc == 0), stop=(kc == 1),
                         inc=("s_pe", 1) if kc == 1 else None)
                ud = K.ctr["s_pe"]
                K.w(A, "s_pe", ud)
                K.w(A, "s_v")
                K.op(A, "activation", self.SCR2[:], PS[ub][:], AF.Identity,
                     bias=self.PK["vec5"][:, 4 * 8 + m:4 * 8 + m + 1])
                K.di(A, "s_a")
                K.w(V, "s_a")
                K.op(V, "tensor_add", X[m][:, t0:t0 + TILE], X[m][:, t0:t0 + TILE],
                     self.SCR2[:])
                K.di(V, "s_v")
                fev[m % 2] = K.ctr["s_v"]
            self.ln_stats_rows((t0, TILE), 0, self.ROW, EPS_LN, raw=True)
            K.w(SY, "s_a")
            self.rows_x[t] = K.dma(SY, self.rowsd.ap()[t], self.ROW[0:1, :])
            K.w(SY, "s_x")

    def ffn_s7(self):
        K = self.K
        K.phase()
        X, PS, IDF = self.X, self.PS, self.IDF
        X1B = K.sb("s7X1", [128, 8 * TILE], F32, local=True)
        OUTT = [K.sb(f"s7O{i}", [128, 1024], F32, local=True) for i in range(2)]
        st_done = 0
        K.w(SY, "s_x", self.spill_x)
        for t in range(NT):
            t0 = t * TILE
            K.w(SY, "s_pe")
            K.w(SY, "s_x", self.rows_x[t])
            nr = K.dma(SY, self.ROW[0:1, :], self.rowsd.ap()[t])
            K.w(SY, "s_x")
            K.w(SY, "s_v")
            nx = 0
            for c in range(8):
                nx = K.dma(SY, X1B[:, c * TILE:(c + 1) * TILE], self.x1sp.ap()[c, :, t0:t0 + TILE])
            K.w(A, "s_x", nr)
            K.w(V, "s_x", nr)
            self.finish_rows(self.ROW, TILE, EPS_LN, True)
            self.ln_bcast(self.ROW, TILE, (0, 1), self.ABR)
            abd = K.ctr["s_a"]
            K.w(V, "s_a", abd)
            K.w(V, "s_x", nx)
            for c in range(8):
                K.op(V, "tensor_mul", self.SCR[:], X[c][:, t0:t0 + TILE], self.ABR[:, 0:TILE])
                K.op(V, "drain")
                K.op(V, "tensor_add", self.SCR[:], self.SCR[:], self.ABR[:, TILE:2 * TILE])
                K.op(V, "drain")
                K.op(V, "tensor_add", X[c][:, t0:t0 + TILE], self.SCR[:],
                     X1B[:, c * TILE:(c + 1) * TILE])
                K.op(V, "drain")
            K.di(V, "s_v")
            od = K.ctr["s_v"]
            for s in range(4):
                par = s % 2
                K.w(TT, "s_v", od)
                K.w(TT, "s_st")
                K.w(TT, "s_a")
                for c in range(8):
                    K.op(TT, "transpose", PS[c // 4][:, (c % 4) * 128:(c % 4 + 1) * 128],
                         X[c][:, t0 + s * 128:t0 + (s + 1) * 128], IDF[:],
                         inc=("s_pe", 1) if c == 7 else None)
                trd = K.ctr["s_pe"]
                K.w(V, "s_pe", trd)
                K.w(A, "s_pe", trd)
                for c in range(8):
                    eng, meth = (V, "tensor_copy") if c < 4 else (A, "copy")
                    K.op(eng, meth, OUTT[par][:, c * 128:(c + 1) * 128],
                         PS[c // 4][:, (c % 4) * 128:(c % 4 + 1) * 128])
                K.di(V, "s_v")
                K.di(A, "s_a")
                K.w(SY, "s_v")
                K.w(SY, "s_a")
                st_done = K.dma(SY, self.out[t0 + s * 128:t0 + (s + 1) * 128, :], OUTT[par][:],
                                "s_st")
                K.w(TT, "s_v")
                K.w(TT, "s_a")
        K.w(G, "s_st", K.ctr["s_st"])
        K.w(G, "s_x", K.ctr["s_x"])

    def dump_X(self):
        K = self.K
        K.w(SY, "s_v")
        K.w(SY, "s_a")
        K.w(SY, "s_pe")
        n = 0
        for c in range(8):
            n = K.dma(SY, self.dump[c], self.X[c][:])
        K.w(G, "s_x", n)
        K.w(G, "s_w")
        K.w(G, "s_cc")
        K.stopped = True

    def dump_masks(self):
        K = self.K
        K.w(SY, "s_g", self.g_masks)
        K.w(SY, "s_v")
        K.w(SY, "s_a")
        n = K.dma(SY, self.dump2[0:64, :].rearrange("(a b) c -> a b c", a=8),
                  self.SEL8[:].rearrange("a (b c) -> a b c", b=8))
        n = K.dma(SY, self.dump2[128:256, :], self.IDF[:])
        n = K.dma(SY, self.dump2[256:384, 0:8], self.CONSTP[:, 0:8])
        K.w(G, "s_x", n)
        K.w(G, "s_w")
        K.stopped = True

    def dump_cc(self):
        K = self.K
        K.w(SY, "s_cc")
        n = K.dma(SY, self.dump2[:], self.cc_out.ap())
        K.w(G, "s_x", n)
        K.stopped = True

    def dump_rw(self):
        K = self.K
        K.w(SY, "s_v")
        n = K.dma(SY, self.dump2[0:128, :], self.RW[:, 0:128])
        n = K.dma(SY, self.dump2[128:256, 0:8], self._LGD[:])
        K.w(G, "s_x", n)
        K.stopped = True

    def generate(self):
        dbg = self.dbg
        if dbg == "prep":
            self.dump_masks()
            return self.finish()
        if dbg == "init2":
            self.initial(nsl=2)
            self.dump_X()
            return self.finish()
        if dbg == "initnoln":
            self.initial(do_ln=False)
            self.dump_X()
            return self.finish()
        self.initial()
        if dbg == "xint0":
            self.dump_X()
            return self.finish()
        for l in range(DEPTH):
            self.pass1(l)
            if dbg == f"l{l}_U":
                self.dump_cc()
                return self.finish()
            self.pass2a(l)
            if dbg == f"l{l}_xmid":
                self.dump_X()
                return self.finish()
            self.pass2b(l)
            if dbg == f"l{l}_x":
                self.dump_X()
                return self.finish()
        self.x1_form()
        if dbg == "x1":
            self.dump_X()
            return self.finish()
        self.ffn_s1_s2()
        if dbg == "x1i":
            self.dump_X()
            return self.finish()
        self.ffn_s3()
        if dbg == "rw":
            self.dump_rw()
            return self.finish()
        self.ffn_s4()
        if dbg in ("comb", "comb1", "comb2"):
            K = self.K
            K.w(SY, "s_ac")
            n = K.dma(SY, self.dump.rearrange("a b c -> (a b c)"),
                      self.combd.ap().rearrange("a b -> (a b)"))
            K.w(G, "s_x", n)
            K.stopped = True
            return self.finish()
        self.ffn_s4b()
        if dbg == "u":
            self.dump_X()
            return self.finish()
        self.ffn_s5()
        if dbg == "ffret":
            self.dump_X()
            return self.finish()
        self.ffn_s7()
        return self.finish()

    def finish(self):
        if self.K.ph is not None:
            self.K.ph.close()
            self.K.ph = None
        nc = self.K.build()
        self.K.es.close()
        return nc


_NC_CACHE = {}


def get_nc(dbg=None):
    if dbg not in _NC_CACHE:
        _NC_CACHE[dbg] = Mega(dbg).generate()
    return _NC_CACHE[dbg]


def kernel(**inputs):
    nc = get_nc(None)
    return _run(nc, inputs, None)


def _run(nc, inputs, dbg):
    xfull = np.ascontiguousarray(np.asarray(inputs["L0_attn_x"], np.float32))
    in_maps = []
    for c in range(8):
        b, half = c // 2, c % 2
        m = {"x": np.ascontiguousarray(xfull[b, half * 4096:(half + 1) * 4096, :]),
             "hmask": np.full((128, 1), 1.0 if half == 1 else 0.0, np.float32)}
        for nm in ["ln1_w", "ln1_b", "wq", "wk", "wv", "wo", "bo", "ln2_w", "ln2_b",
                   "w1", "w2"]:
            m["w_" + nm] = np.ascontiguousarray(np.asarray(inputs["L0_attn_" + nm], np.float32))
        for nm in ["dw_kernel", "dw_bias", "pw_w", "pw_b", "router_w", "router_b",
                   "expert_w", "expert_b", "out_w", "out_b", "norm_w", "ff_down_w",
                   "ff_down_b", "ff_up_w", "ff_up_b"]:
            m["w_" + nm] = np.ascontiguousarray(np.asarray(inputs["L0_ffn_" + nm], np.float32))
        in_maps.append(m)
    res = run_bass_kernel_spmd(nc, in_maps, core_ids=list(range(8)))
    if dbg:
        return res
    out = np.empty((4, 8192, 1024), np.float32)
    for c in range(8):
        b, half = c // 2, c % 2
        out[b, half * 4096:(half + 1) * 4096, :] = res.results[c]["out"]
    return out


def run_debug(dbg, inputs):
    nc = get_nc(dbg)
    return _run(nc, inputs, dbg)


# version 04
# speedup vs baseline: 1.1457x; 1.1457x over previous
import numpy as np
from contextlib import ExitStack

import concourse.bass as bass
import concourse.mybir as mybir
from concourse.bass_utils import run_bass_kernel_spmd

F32 = mybir.dt.float32
BF16 = mybir.dt.bfloat16
AF = mybir.ActivationFunctionType
ALU = mybir.AluOpType
AX = mybir.AxisListType

D = 1024
T = 4096
TILE = 512
NT = T // TILE
SL = 128
DEPTH = 6
FF = 4096
FFT = 256
E = 8
HIDD = 256
LEAK = 0.01
EPS_LN = 1e-5
EPS_RMS = 1e-6

V, A, TT, G, SY = "vector", "scalar", "tensor", "gpsimd", "sync"

RG = [[0, 1], [2, 3], [4, 5], [6, 7]]


class KB:
    ENGS = ("sync", "gpsimd", "vector", "scalar", "tensor")

    def __init__(self):
        self.nc = bass.Bass("TRN2", target_bir_lowering=False)
        self.prog = {e: [] for e in self.ENGS}
        self.es = ExitStack()
        self.ph = None
        self.semh = {}
        self.ctr = {}
        for s in ("s_x", "s_w", "s_c", "s_ac", "s_cc", "s_pe", "s_v", "s_a", "s_g", "s_f",
                  "s_w1a", "s_w1b", "s_w2a", "s_w2b", "s_xia", "s_xib", "s_st"):
            self.semh[s] = self.es.enter_context(self.nc.semaphore(s))
            self.ctr[s] = 0
        self.stopped = False

    _uid = 0

    def sb(self, name, shape, dt, local=False):
        KB._uid += 1
        st = self.ph if local else self.es
        return st.enter_context(self.nc.sbuf_tensor(f"{name}_{KB._uid}", shape, dt))

    def psum(self, name, shape, dt):
        return self.es.enter_context(self.nc.psum_tensor(name, shape, dt))

    def dram(self, name, shape, dt, kind="Internal"):
        return self.nc.dram_tensor(name, shape, dt, kind=kind)

    def phase(self):
        if self.ph is not None and not self.stopped:
            for e in ("vector", "scalar"):
                self.di(e, "s_f", 1)
            for e in ("gpsimd", "tensor", "sync"):
                self.op(e, "nop", inc=("s_f", 1))
            tgt = self.ctr["s_f"]
            for e in ("vector", "scalar", "gpsimd", "tensor", "sync"):
                self.w(e, "s_f", tgt)
        if self.ph is not None:
            self.ph.close()
        self.ph = ExitStack()

    def w(self, eng, sem, val=None):
        if self.stopped:
            return
        if val is None:
            val = self.ctr[sem]
        if val <= 0:
            return
        h = self.semh[sem]
        self.prog[eng].append(lambda e, hh=h, vv=val: e.wait_ge(hh, vv))

    def op(self, eng, meth, *a, inc=None, **kw):
        if self.stopped:
            return 0
        if inc:
            nm, k = inc
            self.ctr[nm] += k
            h = self.semh[nm]
            self.prog[eng].append(
                lambda e, m=meth, aa=a, kk=kw, hh=h, n=k: getattr(e, m)(*aa, **kk).then_inc(hh, n))
            return self.ctr[nm]
        self.prog[eng].append(lambda e, m=meth, aa=a, kk=kw: getattr(e, m)(*aa, **kk))
        return 0

    def dma(self, eng, out, in_, sem="s_x", **kw):
        return self.op(eng, "dma_start", inc=(sem, 16), out=out, in_=in_, **kw)

    def mm(self, out, lhsT, rhs, start, stop, inc=None):
        return self.op("tensor", "matmul", out, lhsT, rhs, start=start, stop=stop, inc=inc)

    def di(self, eng, sem, k=1):
        return self.op(eng, "drain", inc=(sem, k))

    def build(self):
        nc = self.nc
        with nc.Block() as block:
            @block.sync
            def _(eng):
                for f in self.prog["sync"]:
                    f(eng)

            @block.gpsimd
            def _(eng):
                for f in self.prog["gpsimd"]:
                    f(eng)

            @block.vector
            def _(eng):
                for f in self.prog["vector"]:
                    f(eng)

            @block.scalar
            def _(eng):
                for f in self.prog["scalar"]:
                    f(eng)

            @block.tensor
            def _(eng):
                for f in self.prog["tensor"]:
                    f(eng)
        return nc


class Mega:
    def __init__(self, dbg=None):
        self.K = KB()
        self.dbg = dbg
        self._declare()
        self._prep()

    def _declare(self):
        K = self.K
        self.xin = K.dram("x", [T, D], F32, kind="ExternalInput").ap()
        self.hmask = K.dram("hmask", [128, 1], F32, kind="ExternalInput").ap()
        self.win = {}
        for nm, shp in [
            ("ln1_w", [DEPTH, D]), ("ln1_b", [DEPTH, D]), ("wq", [DEPTH, D, D]),
            ("wk", [DEPTH, D, D]), ("wv", [DEPTH, D, D]), ("wo", [DEPTH, D, D]),
            ("bo", [DEPTH, D]), ("ln2_w", [DEPTH, D]), ("ln2_b", [DEPTH, D]),
            ("w1", [DEPTH, D, FF]), ("w2", [DEPTH, FF, D]),
            ("dw_kernel", [D, 1, 3]), ("dw_bias", [D]), ("pw_w", [D, D]), ("pw_b", [D]),
            ("router_w", [E, D]), ("router_b", [E]), ("expert_w", [E, D, D]),
            ("expert_b", [E, D]), ("out_w", [D, D]), ("out_b", [D]), ("norm_w", [D]),
            ("ff_down_w", [HIDD, D]), ("ff_down_b", [HIDD]), ("ff_up_w", [D, HIDD]),
            ("ff_up_b", [D]),
        ]:
            self.win[nm] = K.dram("w_" + nm, shp, F32, kind="ExternalInput").ap()
        self.out = K.dram("out", [T, D], F32, kind="ExternalOutput").ap()
        if self.dbg:
            self.dump = K.dram("dump", [8, 128, T], F32, kind="ExternalOutput").ap()
            self.dump2 = K.dram("dump2", [1032, 128], F32, kind="ExternalOutput").ap()

        self.x1sp = K.dram("x1sp", [8, 128, T], F32)
        self.rowsd = K.dram("rowsd", [NT, 1, 2 * TILE], F32)
        self.combd = K.dram("combd", [T, D], F32)
        self.xibd = K.dram("xibd", [8, 128, T], BF16)
        self.w1_s = K.dram("w1_s", [DEPTH, FF // 128, 128, 8 * 128], BF16)
        self.qkvo_s = {nm: K.dram(nm + "_s", [DEPTH, 128, 8 * D], BF16)
                       for nm in ("wq", "wk", "wv", "wo")}
        self.w2_s = K.dram("w2_s", [DEPTH, 8, 128, 32 * 128], BF16)
        self.pw_s = K.dram("pw_s", [D, D], BF16)
        self.ow_s = K.dram("ow_s", [D, D], BF16)
        self.ew_s = K.dram("ew_s", [E, D, D], BF16)
        self.fd_s = K.dram("fd_s", [HIDD, D], BF16)
        self.fu_s = K.dram("fu_s", [D, HIDD], BF16)
        self.eb_s = K.dram("eb_s", [E, D], BF16)
        self.cc_in = K.dram("cc_in", [1032, 128], F32)
        self.cc_out = K.dram("cc_out", [1032, 128], F32)
        self.hl_in = K.dram("hl_in", [D, 2], F32)
        self.hl_out = K.dram("hl_out", [2 * D, 2], F32)
        self.xb2d = K.dram("xb2d", [128, 8, T], BF16)
        self.mrid1 = K.dram("mrid1", [NT, 2 * TILE], BF16)
        self.mrid2 = K.dram("mrid2", [NT, 2 * TILE], BF16)
        self.rowsd2 = K.dram("rowsd2", [NT, TILE], F32)

        self.X = [K.sb(f"X{c}", [128, T], F32) for c in range(8)]
        self.WS = [K.sb("WS0", [128, 8 * D], BF16), K.sb("WS1", [128, 8 * D], BF16)]
        self.ABR = K.sb("ABR", [128, 2 * TILE], F32)
        self.SCR = K.sb("SCR", [128, TILE], F32)
        self.SCR2 = K.sb("SCR2", [128, TILE], F32)
        self.ROW = K.sb("ROW", [1, 2 * TILE], F32)
        self.ONC = K.sb("ONC", [128, 1], F32)
        self.QMASK = K.sb("QMASK", [128, 8 * 8], BF16)
        self.QMD = K.sb("QMD", [128, 4], BF16)
        self.IDF = K.sb("IDF", [128, 128], F32)
        self.IDB = K.sb("IDB", [128, 128], BF16)
        self.ONEF = K.sb("ONEF", [1, 128], F32)
        self.ONEB = K.sb("ONEB", [1, 128], BF16)
        self.HALO = K.sb("HALO", [128, 16], F32)
        self.CARRY = K.sb("CARRY", [128, 16], F32)
        self.MCOL = K.sb("MCOL", [128, 1], F32)
        self.RBROW = K.sb("RBROW", [1, E], F32)
        self.RB128 = K.sb("RB128", [128, E], F32)
        self.RW = K.sb("RW", [128, (T // SL) * E], F32)
        self.CTX = K.sb("CTX", [128, 8 * 128], BF16)
        self.PK = {}
        for nm, ncol in [("ln1_w", 48), ("ln1_b", 48), ("ln2_w", 48), ("ln2_b", 48),
                         ("bo", 48), ("vec5", 40), ("dwk", 24), ("fdb", 2)]:
            self.PK[nm] = K.sb("PK_" + nm, [128, ncol], F32)
        self.CONSTP = K.sb("CONSTP", [128, 12], F32)
        self.PS = [K.psum(f"PS{i}", [128, 512], F32) for i in range(7)]
        self.PSB = K.psum("PSB", [128, 1024], BF16)
        self.wld = {nm: [0] * DEPTH for nm in ("wq", "wk", "wv", "wo", "w2")}
        self.wk_sem = ["s_w"] * DEPTH
        self.pair_ld = [0, 0]
        self.mrx_dma = 0
        self.rows2_wr = [0] * NT

    def load_w(self, slot_ap, src_dd, sem="s_w"):
        return self.K.dma(G, slot_ap.rearrange("p (k m) -> p k m", k=8),
                          src_dd.rearrange("(k p) m -> p k m", p=128), sem)

    def cast_qkvo(self, l):
        K = self.K
        for nm in ("wq", "wk", "wv", "wo"):
            self.qkvo_cast[l] = K.dma(
                G, self.qkvo_s[nm].ap()[l].rearrange("p (k m) -> p k m", k=8),
                self.win[nm][l].rearrange("(k p) m -> p k m", p=128), "s_c")

    def load_wb(self, slot_ap, nm, l, sem="s_w"):
        K = self.K
        K.w(G, "s_c", self.qkvo_cast[l])
        return K.dma(G, slot_ap, self.qkvo_s[nm].ap()[l], sem)

    def cast_w12(self, l):
        K = self.K
        for mi in range(32):
            K.dma(G, self.w1_s.ap()[l, mi].rearrange("p (k m) -> p k m", k=8),
                  self.win["w1"][l][:, mi * 128:(mi + 1) * 128]
                  .rearrange("(k p) m -> p k m", p=128), "s_c")
        for j in range(8):
            self.w12_cast[l] = K.dma(
                G, self.w2_s.ap()[l, j].rearrange("p (mi jj) -> p mi jj", mi=32),
                self.win["w2"][l][:, j * 128:(j + 1) * 128]
                .rearrange("(mi p) jj -> p mi jj", p=128), "s_c")

    def _prep(self):
        K = self.K
        win = self.win
        dg = lambda: K.op(G, "drain")
        K.op(G, "memset", self.IDF[:], 0.0)
        dg()
        K.op(G, "affine_select", out=self.IDF[:], in_=self.IDF[:], compare_op=ALU.not_equal,
             fill=1.0, base=0, pattern=[[-1, 128]], channel_multiplier=1)
        K.op(G, "memset", self.IDB[:], 0.0)
        dg()
        K.op(G, "affine_select", out=self.IDB[:], in_=self.IDB[:], compare_op=ALU.not_equal,
             fill=1.0, base=0, pattern=[[-1, 128]], channel_multiplier=1)
        K.op(G, "memset", self.ONEF[:], 1.0)
        K.op(G, "memset", self.ONEB[:], 1.0)
        K.op(G, "memset", self.QMASK[:], 0.0)
        dg()
        for h in range(8):
            K.op(G, "memset", self.QMASK[:, h * 8 + h:h * 8 + h + 1], 1.0)
        K.op(G, "memset", self.ONC[:], 1.0)
        K.op(G, "memset", self.QMD[:], 0.0)
        dg()
        K.op(G, "memset", self.QMD[:, 0:1], 1.0)
        K.op(G, "memset", self.QMD[:, 3:4], 1.0)
        K.op(G, "memset", self.CARRY[:], 0.0)
        for i, cv in enumerate([EPS_LN, EPS_RMS, 1.0, -0.5, -1.0, LEAK, 1.0 / D, 2.0, 0.5]):
            K.op(G, "memset", self.CONSTP[:, i:i + 1], cv)
            K.nc.const_aps.register(F32, cv, self.CONSTP[:, i:i + 1])
        K.op(G, "drain", inc=("s_g", 1))
        self.g_masks = K.ctr["s_g"]

        K.dma(G, self.pw_s.ap(), win["pw_w"], "s_c")
        K.dma(G, self.ow_s.ap(), win["out_w"], "s_c")
        for e in range(E):
            K.dma(G, self.ew_s.ap()[e], win["expert_w"][e], "s_c")
        K.dma(G, self.fd_s.ap(), win["ff_down_w"], "s_c")
        K.dma(G, self.fu_s.ap(), win["ff_up_w"], "s_c")
        K.dma(G, self.eb_s.ap(), win["expert_b"], "s_c")
        self.g_scr = K.ctr["s_c"]
        K.w(G, "s_c")
        self.load_w(self.WS[0][:], win["wk"][0])
        self.wld["wk"][0] = self.wld["wv"][0] = \
            self.load_w(self.WS[1][:], win["wv"][0])
        K.w(G, "s_w")
        self.w12_cast = [0] * DEPTH
        self.qkvo_cast = [0] * DEPTH
        self.cast_w12(0)
        K.w(G, "s_c")
        self.cast_qkvo(0)
        K.w(G, "s_c")

        K.phase()
        PKS = K.sb("PKS", [64, 128], F32, local=True)
        jobs = [
            ("ln1_w", win["ln1_w"].rearrange("l (c p) -> (l c) p", p=128), 48),
            ("ln1_b", win["ln1_b"].rearrange("l (c p) -> (l c) p", p=128), 48),
            ("ln2_w", win["ln2_w"].rearrange("l (c p) -> (l c) p", p=128), 48),
            ("ln2_b", win["ln2_b"].rearrange("l (c p) -> (l c) p", p=128), 48),
            ("bo", win["bo"].rearrange("l (c p) -> (l c) p", p=128), 48),
            ("fdb", win["ff_down_b"].rearrange("(c p) -> c p", p=128), 2),
        ]
        K.w(TT, "s_g", self.g_masks)
        for nm, src, ncol in jobs:
            K.w(SY, "s_v")
            ld = K.dma(SY, PKS[0:ncol, :], src)
            K.w(TT, "s_x", ld)
            K.w(TT, "s_v")
            K.op(TT, "transpose", self.PS[0][:, 0:ncol], PKS[0:ncol, :],
                 self.IDF[0:ncol, 0:ncol], inc=("s_pe", 1))
            K.w(V, "s_pe")
            K.op(V, "tensor_copy", self.PK[nm][:], self.PS[0][:, 0:ncol])
            K.di(V, "s_v")
        for j, nm in enumerate(["dw_bias", "pw_b", "out_b", "norm_w", "ff_up_b"]):
            K.w(SY, "s_v")
            ld = K.dma(SY, PKS[0:8, :], win[nm].rearrange("(c p) -> c p", p=128))
            K.w(TT, "s_x", ld)
            K.w(TT, "s_v")
            K.op(TT, "transpose", self.PS[0][:, 0:8], PKS[0:8, :], self.IDF[0:8, 0:8],
                 inc=("s_pe", 1))
            K.w(V, "s_pe")
            K.op(V, "tensor_copy", self.PK["vec5"][:, j * 8:(j + 1) * 8], self.PS[0][:, 0:8])
            K.di(V, "s_v")
        K.w(SY, "s_v")
        K.dma(SY, self.PK["dwk"][:].rearrange("p (c k) -> p c k", c=8),
              win["dw_kernel"].rearrange("(c p) o k -> p c (o k)", p=128))
        K.dma(SY, self.RBROW[:], win["router_b"].rearrange("(a e) -> a e", a=1))
        K.dma(SY, self.MCOL[:], self.hmask)
        K.w(TT, "s_x", K.ctr["s_x"])
        K.w(TT, "s_v")
        K.op(TT, "matmul", self.PS[0][:, 0:E], self.ONEF[:], self.RBROW[:],
             start=True, stop=True, inc=("s_pe", 1))
        K.w(V, "s_pe")
        K.op(V, "tensor_copy", self.RB128[:], self.PS[0][:, 0:E])
        K.di(V, "s_v")
        self.prep_v = K.ctr["s_v"]

    def ln_stats_rows(self, tcols, bank, rowbuf, eps, with_mean=True, rms=False,
                      srcs=None, scr=None, raw=False):
        K = self.K
        t0, tn = tcols
        srcs = srcs or [self.X[c][:, t0:t0 + tn] for c in range(8)]
        scr = scr if scr is not None else self.SCR
        PS = self.PS
        for c in range(8):
            K.w(V, "s_pe")
            K.op(V, "tensor_mul", scr[:, 0:tn], srcs[c], srcs[c])
            K.di(V, "s_v")
            K.w(TT, "s_v")
            if not rms:
                K.mm(PS[bank][0:1, 0:tn], self.ONC[:], srcs[c], start=(c == 0), stop=(c == 7))
            K.mm(PS[bank + 1][0:1, 0:tn], self.ONC[:], scr[:, 0:tn],
                 start=(c == 0), stop=(c == 7), inc=("s_pe", 1))
        K.w(A, "s_pe")
        K.w(A, "s_x")
        if not rms:
            K.op(A, "activation", rowbuf[0:1, 0:tn], PS[bank][0:1, 0:tn], AF.Identity,
                 scale=1.0 / D)
        K.op(A, "activation", rowbuf[0:1, TILE:TILE + tn], PS[bank + 1][0:1, 0:tn],
             AF.Identity, scale=1.0 / D)
        K.di(A, "s_a")
        if raw:
            return
        self.finish_rows(rowbuf, tn, eps, with_mean, scr)

    def finish_rows(self, rowbuf, tn, eps, with_mean, scr=None):
        K = self.K
        scr = scr if scr is not None else self.SCR
        K.w(V, "s_a")
        if with_mean:
            K.op(V, "tensor_mul", scr[0:1, 0:tn], rowbuf[0:1, 0:tn], rowbuf[0:1, 0:tn])
            K.op(V, "drain")
            K.op(V, "tensor_sub", rowbuf[0:1, TILE:TILE + tn],
                 rowbuf[0:1, TILE:TILE + tn], scr[0:1, 0:tn])
            K.di(V, "s_v")
            K.w(A, "s_v")
        K.op(A, "activation", scr[0:1, 0:tn], rowbuf[0:1, TILE:TILE + tn], AF.Ln, bias=eps)
        K.op(A, "drain")
        K.op(A, "activation", scr[0:1, 0:tn], scr[0:1, 0:tn], AF.Exp, scale=-0.5)
        K.di(A, "s_a")
        K.w(V, "s_a")
        if with_mean:
            K.op(V, "scalar_tensor_tensor", rowbuf[0:1, TILE:TILE + tn],
                 rowbuf[0:1, 0:tn], -1.0, scr[0:1, 0:tn], ALU.mult, ALU.mult)
            K.op(V, "drain")
        else:
            K.op(V, "memset", rowbuf[0:1, TILE:TILE + tn], 0.0)
        K.op(V, "tensor_copy", rowbuf[0:1, 0:tn], scr[0:1, 0:tn])
        K.di(V, "s_v")

    def ln_bcast(self, rowsrc, tn, banks, abr, beta=True):
        K = self.K
        K.w(TT, "s_v")
        K.w(TT, "s_a")
        K.mm(self.PS[banks[0]][:, 0:tn], self.ONEF[:], rowsrc[0:1, 0:tn],
             start=True, stop=True, inc=None if beta else ("s_pe", 1))
        if beta:
            K.mm(self.PS[banks[1]][:, 0:tn], self.ONEF[:], rowsrc[0:1, TILE:TILE + tn],
                 start=True, stop=True, inc=("s_pe", 1))
        K.w(A, "s_pe")
        K.op(A, "activation", abr[:, 0:tn], self.PS[banks[0]][:, 0:tn], AF.Identity)
        if beta:
            K.op(A, "activation", abr[:, TILE:TILE + tn], self.PS[banks[1]][:, 0:tn],
                 AF.Identity)
        K.di(A, "s_a")
        return K.ctr["s_a"]

    def ln_apply_chunk(self, xsrc, tn, abr, wcol, bcol, out_bf, scrf):
        K = self.K
        K.w(V, "s_a")
        K.op(V, "tensor_mul", scrf[:, 0:tn], xsrc, abr[:, 0:tn])
        K.op(V, "drain")
        K.op(V, "tensor_add", scrf[:, 0:tn], scrf[:, 0:tn], abr[:, TILE:TILE + tn])
        K.di(V, "s_v")
        K.w(A, "s_v")
        if wcol is not None:
            K.op(A, "activation", out_bf, scrf[:, 0:tn], AF.Identity, scale=wcol, bias=bcol)
        else:
            K.op(A, "activation", out_bf, scrf[:, 0:tn], AF.Identity)
        K.di(A, "s_a")
        return K.ctr["s_a"]

    def initial(self, nsl=T // SL, do_ln=True):
        K = self.K
        K.phase()
        TMJ = [K.sb(f"iTMJ{i}", [128, D], F32, local=True) for i in range(2)]
        TSQ = K.sb("iTSQ", [128, D], F32, local=True)
        AL = K.sb("iAL", [128, 4], F32, local=True)
        X, PS, IDF = self.X, self.PS, self.IDF
        K.w(TT, "s_g", self.g_masks)
        K.w(A, "s_g", self.g_masks)
        K.w(V, "s_g", self.g_masks)
        K.w(V, "s_v", self.prep_v)
        for s in range(nsl):
            par = s % 2
            K.w(SY, "s_pe")
            ld = K.dma(SY, TMJ[par][:], self.xin[s * SL:(s + 1) * SL, :])
            K.w(V, "s_x", ld)
            K.w(V, "s_pe")
            if not do_ln:
                K.op(V, "tensor_copy", TSQ[:], TMJ[par][:])
                K.di(V, "s_v")
                self._init_transpose(s, TSQ, par)
                continue
            K.op(V, "tensor_mul", TSQ[:], TMJ[par][:], TMJ[par][:])
            K.op(V, "reduce_sum", AL[:, 0:1], TMJ[par][:], axis=AX.X)
            K.op(V, "drain")
            K.op(V, "reduce_sum", AL[:, 1:2], TSQ[:], axis=AX.X)
            K.op(V, "drain")
            K.op(V, "tensor_scalar", AL[:, 0:1], AL[:, 0:1], 1.0 / D, None, ALU.mult)
            K.op(V, "tensor_scalar", AL[:, 1:2], AL[:, 1:2], 1.0 / D, None, ALU.mult)
            K.op(V, "drain")
            K.op(V, "tensor_mul", TSQ[:, 0:1], AL[:, 0:1], AL[:, 0:1])
            K.op(V, "drain")
            K.op(V, "tensor_sub", AL[:, 1:2], AL[:, 1:2], TSQ[:, 0:1])
            K.di(V, "s_v")
            K.w(A, "s_v")
            K.op(A, "activation", AL[:, 2:3], AL[:, 1:2], AF.Ln, bias=EPS_LN)
            K.op(A, "drain")
            K.op(A, "activation", AL[:, 2:3], AL[:, 2:3], AF.Exp, scale=-0.5)
            K.di(A, "s_a")
            K.w(V, "s_a")
            K.op(V, "scalar_tensor_tensor", AL[:, 3:4], AL[:, 0:1], -1.0, AL[:, 2:3],
                 ALU.mult, ALU.mult)
            K.op(V, "drain")
            K.op(V, "tensor_scalar", TSQ[:], TMJ[par][:], AL[:, 2:3], AL[:, 3:4],
                 ALU.mult, ALU.add)
            K.di(V, "s_v")
            self._init_transpose(s, TSQ, par)

    def _init_transpose(self, s, TSQ, par):
        K = self.K
        X, PS, IDF = self.X, self.PS, self.IDF
        K.w(TT, "s_v")
        for c in range(8):
            K.op(TT, "transpose", PS[c // 4][:, (c % 4) * 128:(c % 4 + 1) * 128],
                 TSQ[:, c * 128:(c + 1) * 128], IDF[:],
                 inc=("s_pe", 1) if c == 7 else None)
        K.w(V, "s_pe")
        K.w(A, "s_pe")
        for c in range(8):
            eng, meth = (V, "tensor_copy") if c < 4 else (A, "copy")
            K.op(eng, meth, X[c][:, s * SL:(s + 1) * SL],
                 PS[c // 4][:, (c % 4) * 128:(c % 4 + 1) * 128])
        K.di(V, "s_v")
        K.di(A, "s_a")
        K.w(TT, "s_v")
        K.w(TT, "s_a")

    def pass1(self, l):
        K = self.K
        K.phase()
        XBS = K.sb("p1XBS", [128, 2 * 8 * 128], BF16, local=True)
        KK = [K.sb(f"p1K{i}", [128, 512], BF16, local=True) for i in range(2)]
        VV = [K.sb(f"p1V{i}", [128, 512], BF16, local=True) for i in range(2)]
        USB = K.sb("p1U", [128, 1024], F32, local=True)
        DRW = K.sb("p1DRW", [2, 512], F32, local=True)
        SKK = K.sb("p1SKK", [2, 1024], BF16, local=True)
        SKV = K.sb("p1SKV", [2, 1024], BF16, local=True)
        MRIT = K.sb("p1MRI", [2, TILE], BF16, local=True)
        MRX = K.sb("p1MRX", [1, 2 * TILE], BF16, local=True)
        RSTDC = K.sb("p1RC", [128, 8], F32, local=True)
        GB1 = K.sb("p1GB", [128, 16], BF16, local=True)
        FINROW = self.ABR
        X, PS = self.X, self.PS
        WK = self.WS[l % 2]
        WV = self.WS[1 - l % 2]

        K.w(TT, "s_w", self.wld["wv"][l])
        K.op(V, "memset", USB[:], 0.0, inc=("s_v", 1))
        for kc in range(8):
            K.op(A, "activation", GB1[:, kc * 2:kc * 2 + 1],
                 self.PK["ln1_w"][:, l * 8 + kc:l * 8 + kc + 1], AF.Identity)
            K.op(A, "activation", GB1[:, kc * 2 + 1:kc * 2 + 2],
                 self.PK["ln1_b"][:, l * 8 + kc:l * 8 + kc + 1], AF.Identity)
        K.di(A, "s_a")
        gb1_done = K.ctr["s_a"]

        def skck(W, DST):
            K.w(TT, "s_a", gb1_done)
            for half in range(2):
                K.w(TT, "s_a")
                for kc in range(8):
                    K.mm(PS[0][0:2, 0:512], GB1[:, kc * 2:kc * 2 + 2],
                         W[:, kc * 1024 + half * 512: kc * 1024 + (half + 1) * 512],
                         start=(kc == 0), stop=(kc == 7),
                         inc=("s_pe", 1) if kc == 7 else None)
                pe = K.ctr["s_pe"]
                K.w(A, "s_pe", pe)
                K.op(A, "activation", DST[0:2, half * 512:(half + 1) * 512],
                     PS[0][0:2, 0:512], AF.Identity, inc=("s_a", 1))
            return K.ctr["s_a"]

        skv_done = skck(WV, SKV)
        skk_done = [0]

        self.rows_x = [0] * NT
        self.mrid1_wr = [0] * NT
        fin_done = [0] * NT
        rc_tr = [0] * NT
        mrit_ld = 0
        fix_last = 0
        kfree = [0, 0]
        vfree = [0, 0]
        kkfree = [0, 0]
        xbs_use = [0, 0]

        def stats1(t):
            K.w(TT, "s_a")
            self.ln_stats_rows((t * TILE, TILE), 0, self.ROW, EPS_LN, scr=self.SCR2,
                               raw=True)
            fin_done[t] = self.finish2(t, t % 2, FINROW, MRX,
                                       rc_tr[t - 2] if t >= 2 else 0,
                                       self.mrid1.ap()[t:t + 1, :],
                                       self.rowsd.ap()[t][0:1, 0:TILE])
            self.mrid1_wr[t] = self.rows_x[t] = self.rows2_wr[t]

        stats1(0)
        stats1(1)

        for t in range(NT):
            t0 = t * TILE
            par = t % 2
            K.w(TT, "s_a", fin_done[t])
            K.w(TT, "s_a")
            for sl in range(4):
                K.op(TT, "transpose", PS[0][:, sl:sl + 1],
                     FINROW[0:1, par * TILE + sl * 128: par * TILE + sl * 128 + 128],
                     self.IDF[0:1, 0:1], inc=("s_pe", 1) if sl == 3 else None)
            rc_tr[t] = K.ctr["s_pe"]
            K.w(A, "s_pe", rc_tr[t])
            K.op(A, "activation", RSTDC[:, par * 4:par * 4 + 4], PS[0][:, 0:4],
                 AF.Identity, inc=("s_a", 1))
            K.w(SY, "s_pe", fix_last)
            K.w(SY, "s_st", self.mrid1_wr[t])
            mrit_ld = K.dma(SY, MRIT[0:2, :],
                            self.mrid1.ap()[t].rearrange("(a b) -> a b", a=2), "s_w2a")
            for s in range(4):
                sp = s % 2
                K.w(A, "s_pe", xbs_use[sp])
                for c in range(8):
                    K.op(A, "activation",
                         XBS[:, sp * 1024 + c * 128: sp * 1024 + (c + 1) * 128],
                         X[c][:, t0 + s * 128: t0 + (s + 1) * 128], AF.Identity,
                         scale=self.PK["ln1_w"][:, l * 8 + c:l * 8 + c + 1],
                         inc=("s_a", 1) if c == 7 else None)
                cast_a = K.ctr["s_a"]
                for half in range(2):
                    kb, vb = 2 + half, 4 + half
                    K.w(TT, "s_a", cast_a)
                    K.w(TT, "s_a", vfree[half])
                    K.w(TT, "s_w2a", mrit_ld)
                    K.w(TT, "s_a", skv_done)
                    for kc in range(8):
                        K.mm(PS[vb][:],
                             XBS[:, sp * 1024 + kc * 128: sp * 1024 + (kc + 1) * 128],
                             WV[:, kc * 1024 + half * 512: kc * 1024 + (half + 1) * 512],
                             start=(kc == 0), stop=False)
                    K.mm(PS[vb][:], MRIT[0:2, s * 128:(s + 1) * 128],
                         SKV[0:2, half * 512:(half + 1) * 512],
                         start=False, stop=True, inc=("s_pe", 1))
                    v_done = K.ctr["s_pe"]
                    if t == 0 and s == 0 and half == 0:
                        K.w(TT, self.wk_sem[l], self.wld["wk"][l])
                        skk_done[0] = skck(WK, SKK)
                    K.w(TT, "s_v", kfree[half])
                    K.w(TT, "s_a", skk_done[0])
                    for kc in range(8):
                        K.mm(PS[kb][:],
                             XBS[:, sp * 1024 + kc * 128: sp * 1024 + (kc + 1) * 128],
                             WK[:, kc * 1024 + half * 512: kc * 1024 + (half + 1) * 512],
                             start=(kc == 0), stop=False)
                    K.mm(PS[kb][:], MRIT[0:2, s * 128:(s + 1) * 128],
                         SKK[0:2, half * 512:(half + 1) * 512],
                         start=False, stop=True, inc=("s_pe", 1))
                    k_done = K.ctr["s_pe"]
                    K.w(A, "s_pe", v_done)
                    K.w(A, "s_pe", kkfree[half])
                    K.op(A, "activation", VV[half][:], PS[vb][:], AF.Identity,
                         scale=RSTDC[:, par * 4 + s:par * 4 + s + 1], inc=("s_a", 1))
                    vfree[half] = K.ctr["s_a"]
                    K.w(A, "s_pe", k_done)
                    K.op(A, "activation", KK[half][:], PS[kb][:], AF.Exp,
                         scale=RSTDC[:, par * 4 + s:par * 4 + s + 1], inc=("s_a", 1))
                    ev_done = K.ctr["s_a"]
                    K.w(TT, "s_a", ev_done)
                    for hh in range(4):
                        hcol = hh * 128
                        K.mm(PS[kb][:, hcol:hcol + 128], KK[half][:, hcol:hcol + 128],
                             VV[half][:, hcol:hcol + 128], start=True, stop=True)
                    islast = (t == NT - 1 and s == 3 and half == 1)
                    K.mm(PS[6][0:2, 0:512], self.QMD[:, half * 2:half * 2 + 2],
                         KK[half][:],
                         start=(t == 0 and s == 0 and half == 0), stop=islast,
                         inc=("s_pe", 1))
                    kkfree[half] = K.ctr["s_pe"]
                    K.w(V, "s_pe", kkfree[half])
                    K.op(V, "tensor_add", USB[:, half * 512:(half + 1) * 512],
                         USB[:, half * 512:(half + 1) * 512], PS[kb][:],
                         inc=("s_v", 1))
                    kfree[half] = K.ctr["s_v"]
                xbs_use[sp] = K.ctr["s_pe"]
            fix_last = K.ctr["s_pe"]
            if t + 2 < NT:
                stats1(t + 2)
        K.w(A, "s_pe")
        K.op(A, "activation", DRW[0:2, :], PS[6][0:2, 0:512], AF.Identity)
        K.di(A, "s_a")
        K.w(SY, "s_v")
        K.w(SY, "s_a")
        K.dma(SY, self.cc_in.ap()[0:1024, :].rearrange("(h d) e -> d h e", d=128),
              USB[:].rearrange("p (h e) -> p h e", h=8))
        n = K.dma(SY, self.cc_in.ap()[1024:1032, :].rearrange("(a h) d -> a h d", a=2),
                  DRW[0:2, :].rearrange("a (h d) -> a h d", h=4))
        K.w(SY, "s_x")
        K.w(G, "s_x", n)
        K.op(G, "collective_compute", "AllReduce", ALU.add,
             replica_groups=RG,
             ins=[self.cc_in.ap()], outs=[self.cc_out.ap()], inc=("s_cc", 1))
        K.w(G, "s_pe")
        self.load_wb(WK[:], "wq", l)
        self.wld["wq"][l] = self.wld["wo"][l] = self.load_wb(WV[:], "wo", l)
        K.w(G, "s_w")
        if l + 1 < DEPTH:
            self.cast_w12(l + 1)
            K.w(G, "s_c")
            self.cast_qkvo(l + 1)
            K.w(G, "s_c")

    def pass2a_prep(self, l):
        K = self.K
        K.phase()
        PS = self.PS
        USB2 = K.sb("p2U", [128, 1024], F32, local=True)
        DTMP = K.sb("p2DT", [8, 128], F32, local=True)
        DCOL = K.sb("p2DC", [128, 8], F32, local=True)
        K.w(G, "s_cc")
        K.dma(G, DTMP[:], self.cc_out.ap()[1024:1032, :], "s_w")
        n = K.dma(G, USB2[:].rearrange("p (h e) -> p h e", h=8),
                  self.cc_out.ap()[0:1024, :].rearrange("(h d) e -> d h e", d=128), "s_w")
        K.w(G, "s_w")
        K.w(TT, "s_w", n)
        K.w(TT, "s_v")
        K.op(TT, "transpose", PS[0][:, 0:8], DTMP[:], self.IDF[0:8, 0:8],
             inc=("s_pe", 1))
        K.w(A, "s_pe")
        K.op(A, "activation", DCOL[:], PS[0][:, 0:8], AF.Ln)
        K.op(A, "drain")
        K.op(A, "activation", DCOL[:], DCOL[:], AF.Exp, scale=-1.0)
        K.di(A, "s_a")
        K.w(V, "s_a")
        K.w(V, "s_pe")
        for h in range(8):
            K.op(V, "tensor_scalar", self.CTX[:, h * 128:(h + 1) * 128],
                 USB2[:, h * 128:(h + 1) * 128], DCOL[:, h:h + 1], None, ALU.mult)
        K.di(V, "s_v")
        self.ctx_v = K.ctr["s_v"]

    def pass2a(self, l):
        self.pass2a_prep(l)
        K = self.K
        K.phase()
        X, PS = self.X, self.PS
        XB = K.sb("p2XB", [128, 8 * TILE], BF16, local=True)
        QE = K.sb("p2QE", [128, 8 * TILE], BF16, local=True)
        ATT = QE
        SEL8B = K.sb("p2SEL", [8, 8 * 128], BF16, local=True)
        QROWB = K.sb("p2QR", [8, TILE], BF16, local=True)
        MRIT = K.sb("p2MRI", [2, TILE], BF16, local=True)
        SKQ = K.sb("p2SKQ", [2, 1024], BF16, local=True)
        GB1 = K.sb("p2GB", [128, 16], BF16, local=True)
        RSTDB = self.ABR
        WQ = self.WS[l % 2]
        WO = self.WS[1 - l % 2]
        SCRX = [self.SCR, self.SCR2]

        K.op(G, "memset", SEL8B[:], 1.0)
        K.op(G, "drain")
        K.op(G, "affine_select", out=SEL8B[:], in_=SEL8B[:], compare_op=ALU.is_gt,
             fill=0.0, base=1, pattern=[[1, 1024]], channel_multiplier=-128)
        K.op(G, "drain")
        K.op(G, "affine_select", out=SEL8B[:], in_=SEL8B[:], compare_op=ALU.is_gt,
             fill=0.0, base=128, pattern=[[-1, 1024]], channel_multiplier=128)
        K.op(G, "drain", inc=("s_g", 1))
        sel8_g = K.ctr["s_g"]

        for kc in range(8):
            K.op(A, "activation", GB1[:, kc * 2:kc * 2 + 1],
                 self.PK["ln1_w"][:, l * 8 + kc:l * 8 + kc + 1], AF.Identity)
            K.op(A, "activation", GB1[:, kc * 2 + 1:kc * 2 + 2],
                 self.PK["ln1_b"][:, l * 8 + kc:l * 8 + kc + 1], AF.Identity)
        K.di(A, "s_a")
        gb1_done = K.ctr["s_a"]
        K.w(TT, "s_w", self.wld["wq"][l])
        K.w(TT, "s_a", gb1_done)
        skq_done = 0
        for half in range(2):
            K.w(TT, "s_a")
            for kc in range(8):
                K.mm(PS[4][0:2, 0:512], GB1[:, kc * 2:kc * 2 + 2],
                     WQ[:, kc * 1024 + half * 512: kc * 1024 + (half + 1) * 512],
                     start=(kc == 0), stop=(kc == 7),
                     inc=("s_pe", 1) if kc == 7 else None)
            pe = K.ctr["s_pe"]
            K.w(A, "s_pe", pe)
            K.op(A, "activation", SKQ[0:2, half * 512:(half + 1) * 512],
                 PS[4][0:2, 0:512], AF.Identity, inc=("s_a", 1))
            skq_done = K.ctr["s_a"]

        rowld = 0
        mritld = 0
        bc_prev = 0
        rstdb_evac = 0
        rstdb_war = 0
        xb_use = 0
        fix_last = 0
        qe_war = 0
        scr_war = [skq_done, skq_done]
        qev = [self.ctx_v, self.ctx_v]
        aev_v, aev_a = 0, 0
        wev = [0, 0]
        qr_prev = 0
        for t in range(NT):
            t0 = t * TILE
            K.w(SY, "s_pe", bc_prev)
            K.w(SY, "s_st", self.rows_x[t])
            rowld = K.dma(SY, self.ROW[0:1, 0:TILE],
                          self.rowsd.ap()[t][0:1, 0:TILE], "s_w2b")
            K.w(TT, "s_w2b", rowld)
            K.w(TT, "s_a", rstdb_evac if t > 0 else skq_done)
            K.w(TT, "s_a", qr_prev)
            K.mm(PS[4][:, 0:TILE], self.ONEF[:], self.ROW[0:1, 0:TILE],
                 start=True, stop=True, inc=("s_pe", 1))
            bc_prev = K.ctr["s_pe"]
            K.w(A, "s_pe", bc_prev)
            K.w(A, "s_v", rstdb_war)
            K.op(A, "activation", RSTDB[:, 0:TILE], PS[4][:, 0:TILE], AF.Identity,
                 inc=("s_a", 1))
            rstdb_evac = K.ctr["s_a"]
            K.w(SY, "s_pe", fix_last)
            K.w(SY, "s_st", self.mrid1_wr[t])
            mritld = K.dma(SY, MRIT[0:2, :],
                           self.mrid1.ap()[t].rearrange("(a b) -> a b", a=2), "s_w2a")
            K.w(A, "s_pe", xb_use)
            for c in range(8):
                K.op(A, "activation", XB[:, c * TILE:(c + 1) * TILE],
                     X[c][:, t0:t0 + TILE], AF.Identity,
                     scale=self.PK["ln1_w"][:, l * 8 + c:l * 8 + c + 1],
                     inc=("s_a", 1) if c == 7 else None)
            cast_a = K.ctr["s_a"]
            for m in range(8):
                qb = 2 + m % 2
                K.w(TT, "s_a", cast_a)
                K.w(TT, "s_w2a", mritld)
                K.w(TT, "s_v", qev[m % 2])
                for kc in range(8):
                    K.mm(PS[qb][:], WQ[:, kc * 1024 + m * 128: kc * 1024 + (m + 1) * 128],
                         XB[:, kc * TILE:(kc + 1) * TILE], start=(kc == 0), stop=False)
                K.mm(PS[qb][:], SKQ[0:2, m * 128:(m + 1) * 128], MRIT[0:2, :],
                     start=False, stop=True, inc=("s_pe", 1))
                qdone = K.ctr["s_pe"]
                K.w(V, "s_pe", qdone)
                K.w(V, "s_a", rstdb_evac)
                K.w(V, "s_a", scr_war[m % 2])
                K.op(V, "tensor_mul", SCRX[m % 2][:], PS[qb][:], RSTDB[:, 0:TILE],
                     inc=("s_v", 1))
                qev[m % 2] = K.ctr["s_v"]
                K.w(A, "s_v", qev[m % 2])
                K.w(A, "s_pe", qe_war)
                K.op(A, "activation", QE[:, m * TILE:(m + 1) * TILE], SCRX[m % 2][:],
                     AF.Exp, inc=("s_a", 1))
                scr_war[m % 2] = K.ctr["s_a"]
                K.w(TT, "s_a", scr_war[m % 2])
                if m == 0:
                    K.w(TT, "s_a", rstdb_evac)
                K.mm(PS[4][0:8, 0:TILE], self.QMASK[:, m * 8:(m + 1) * 8],
                     QE[:, m * TILE:(m + 1) * TILE], start=(m == 0), stop=(m == 7),
                     inc=("s_pe", 1) if m == 7 else None)
            xb_use = K.ctr["s_pe"]
            fix_last = xb_use
            qd = K.ctr["s_pe"]
            K.w(A, "s_pe", qd)
            K.op(A, "activation", self.SCR[0:8, :], PS[4][0:8, 0:TILE], AF.Ln)
            K.op(A, "drain")
            K.op(A, "activation", QROWB[0:8, :], self.SCR[0:8, 0:TILE], AF.Exp,
                 scale=-1.0, inc=("s_a", 1))
            qr_done = K.ctr["s_a"]
            qr_prev = qr_done
            scr_war[0] = qr_done
            if t == 0:
                K.w(TT, "s_v", self.ctx_v)
                K.w(TT, "s_g", sel8_g)
            for h in range(8):
                ab = 5 if h % 2 == 0 else 6
                bb = 0 if h % 2 == 0 else 1
                K.w(TT, "s_a", qr_done)
                K.w(TT, "s_v", aev_v)
                K.w(TT, "s_a", aev_a)
                K.mm(PS[ab][:], self.CTX[:, h * 128:(h + 1) * 128],
                     QE[:, h * TILE:(h + 1) * TILE], start=True, stop=True)
                K.mm(PS[bb][:], SEL8B[:, h * 128:(h + 1) * 128], QROWB[0:8, :],
                     start=True, stop=True, inc=("s_pe", 1))
                hdone = K.ctr["s_pe"]
                K.w(A, "s_pe", hdone)
                K.w(A, "s_v")
                K.op(A, "activation", self.SCR2[:], PS[bb][:], AF.Identity,
                     inc=("s_a", 1))
                aev_a = K.ctr["s_a"]
                K.w(V, "s_a", aev_a)
                K.op(V, "tensor_mul", ATT[:, h * TILE:(h + 1) * TILE], PS[ab][:],
                     self.SCR2[:], inc=("s_v", 1))
                aev_v = K.ctr["s_v"]
            for m in range(8):
                ob = 5 if m % 2 == 0 else 6
                K.w(TT, "s_v", wev[m % 2])
                K.w(TT, "s_v", aev_v)
                for kc in range(8):
                    K.mm(PS[ob][:], WO[:, kc * 1024 + m * 128: kc * 1024 + (m + 1) * 128],
                         ATT[:, kc * TILE:(kc + 1) * TILE], start=(kc == 0),
                         stop=(kc == 7), inc=("s_pe", 1) if kc == 7 else None)
                odone = K.ctr["s_pe"]
                K.w(V, "s_pe", odone)
                K.op(V, "scalar_tensor_tensor", X[m][:, t0:t0 + TILE], PS[ob][:],
                     self.PK["bo"][:, l * 8 + m:l * 8 + m + 1], X[m][:, t0:t0 + TILE],
                     ALU.add, ALU.add, inc=("s_v", 1))
                wev[m % 2] = K.ctr["s_v"]
            qe_war = K.ctr["s_pe"]
        K.w(G, "s_pe")
        K.w(G, "s_c", self.w12_cast[l])
        self.load_eighth(l, 0, G)

    def load_eighth(self, l, e, eng):
        K = self.K
        b = (l + e) % 2
        sem = "s_w1a" if b == 0 else "s_w1b"
        WSb = self.WS[b]
        K.dma(eng, WSb[:, 0:4096].rearrange("p (m c) -> p m c", m=4),
              self.w1_s.ap()[l, 4 * e:4 * e + 4].rearrange("m p c -> p m c"), sem)
        self.pair_ld[b] = K.dma(
            eng, WSb[:, 4096:8192].rearrange("p (j c) -> p j c", j=8),
            self.w2_s.ap()[l, :, :, e * 512:(e + 1) * 512].rearrange("j p c -> p j c"), sem)
        return self.pair_ld[b]

    def finish2(self, t, par, FINROW, MRX, bc_war, mrd=None, rowd=None):
        K = self.K
        K.w(V, "s_a")
        K.op(V, "tensor_mul", self.SCR2[0:1, 0:TILE], self.ROW[0:1, 0:TILE],
             self.ROW[0:1, 0:TILE])
        K.op(V, "drain")
        K.op(V, "tensor_sub", self.ROW[0:1, TILE:2 * TILE],
             self.ROW[0:1, TILE:2 * TILE], self.SCR2[0:1, 0:TILE])
        K.di(V, "s_v")
        K.w(A, "s_v")
        K.w(A, "s_st", self.mrx_dma)
        K.w(A, "s_pe", bc_war)
        K.op(A, "activation", MRX[0:1, 0:TILE], self.ROW[0:1, 0:TILE], AF.Identity,
             scale=-1.0)
        K.op(A, "drain")
        K.op(A, "activation", self.ROW[0:1, TILE:2 * TILE],
             self.ROW[0:1, TILE:2 * TILE], AF.Ln, bias=EPS_LN)
        K.op(A, "drain")
        K.op(A, "activation", MRX[0:1, TILE:2 * TILE], self.ROW[0:1, TILE:2 * TILE],
             AF.Exp, scale=0.5)
        K.op(A, "activation", FINROW[0:1, par * TILE:(par + 1) * TILE],
             self.ROW[0:1, TILE:2 * TILE], AF.Exp, scale=-0.5)
        K.di(A, "s_a")
        fin_a = self.K.ctr["s_a"]
        K.w(SY, "s_a", fin_a)
        if mrd is None:
            mrd = self.mrid2.ap()[t:t + 1, :]
        if rowd is None:
            rowd = self.rowsd2.ap()[t:t + 1, :]
        K.dma(SY, mrd, MRX[0:1, :], "s_st")
        self.mrx_dma = K.dma(SY, rowd,
                             FINROW[0:1, par * TILE:(par + 1) * TILE], "s_st")
        K.w(SY, "s_st", self.mrx_dma)
        self.rows2_wr[t] = self.mrx_dma
        return fin_a

    def pass2b(self, l):
        K = self.K
        K.phase()
        X, PS = self.X, self.PS
        XB2P = K.sb("p3XB", [128, 2 * 8 * TILE], BF16, local=True)
        HID = K.sb("p3HID", [128, 4 * TILE], BF16, local=True)
        LK = K.sb("p3LK", [128, TILE], F32, local=True)
        RSTD2B = K.sb("p3RSB", [128, TILE], F32, local=True)
        MRIT2 = K.sb("p3MRI", [2, TILE], BF16, local=True)
        MRX = K.sb("p3MRX", [1, 2 * TILE], BF16, local=True)
        FINROW = self.ABR
        S1C1 = K.sb("p3SC", [2, 4 * 128], BF16, local=True)
        GB2 = K.sb("p3GB", [128, 16], BF16, local=True)

        for kc in range(8):
            K.op(A, "activation", GB2[:, kc * 2:kc * 2 + 1],
                 self.PK["ln2_w"][:, l * 8 + kc:l * 8 + kc + 1], AF.Identity)
            K.op(A, "activation", GB2[:, kc * 2 + 1:kc * 2 + 2],
                 self.PK["ln2_b"][:, l * 8 + kc:l * 8 + kc + 1], AF.Identity)
        K.di(A, "s_a")
        gb2_done = K.ctr["s_a"]

        hid_free = [0, 0]
        y_free = [0, 0]
        xb2_use = [0, 0]
        xb2_wr = [0] * NT
        cast_done = [0] * NT
        fin_done = [0] * NT
        mrid2_wr = [0] * NT
        fix_done = 0
        mrit2_ld = 0
        rstdb_evac = 0
        rstdb_war = 0
        bc_done = [0] * NT
        row_ld = 0
        prev_eighth_pe = 0
        eighth_pe = 0

        def stats2(t):
            K.w(TT, "s_a")
            self.ln_stats_rows((t * TILE, TILE), 0, self.ROW, EPS_LN, scr=self.SCR2,
                               raw=True)
            fin_done[t] = self.finish2(t, t % 2, FINROW, MRX,
                                       bc_done[t - 2] if t >= 2 else 0)
            mrid2_wr[t] = self.mrx_dma

        stats2(0)
        stats2(1)

        for e in range(8):
            be = (l + e) % 2
            sem_e = "s_w1a" if be == 0 else "s_w1b"
            pair = self.pair_ld[be]
            e_gate = eighth_pe
            if e + 1 < 8:
                K.w(G, "s_pe", e_gate)
                K.w(G, "s_c", self.w12_cast[l])
                self.load_eighth(l, e + 1, G)
            K.w(TT, sem_e, pair)
            K.w(TT, "s_a", gb2_done)
            K.w(TT, "s_a", rstdb_evac)
            K.w(TT, "s_pe", fix_done)
            for m in range(4):
                for kc in range(8):
                    K.mm(PS[6][0:2, m * 128:(m + 1) * 128], GB2[:, kc * 2:kc * 2 + 2],
                         self.WS[be][:, m * 1024 + kc * 128:m * 1024 + (kc + 1) * 128],
                         start=(kc == 0), stop=(kc == 7),
                         inc=("s_pe", 1) if (m == 3 and kc == 7) else None)
            s1c1_pe = K.ctr["s_pe"]
            K.w(A, "s_pe", s1c1_pe)
            K.op(A, "activation", S1C1[0:2, :], PS[6][0:2, 0:512], AF.Identity,
                 inc=("s_a", 1))
            s1c1_done = K.ctr["s_a"]

            for t in range(NT):
                t0 = t * TILE
                buf = t % 2
                if e > 0:
                    K.w(SY, "s_pe", bc_done[t - 1] if t >= 1 else bc_done[NT - 1])
                    row_ld = K.dma(SY, FINROW[0:1, buf * TILE:(buf + 1) * TILE],
                                   self.rowsd2.ap()[t:t + 1, :], "s_w2b")
                K.w(TT, "s_a", rstdb_evac)
                K.w(TT, "s_a", s1c1_done)
                if e == 0:
                    K.w(TT, "s_a", fin_done[t])
                else:
                    K.w(TT, "s_w2b", row_ld)
                K.mm(PS[6][:, 0:TILE], self.ONEF[:],
                     FINROW[0:1, buf * TILE:(buf + 1) * TILE],
                     start=True, stop=True, inc=("s_pe", 1))
                bc_done[t] = K.ctr["s_pe"]
                K.w(A, "s_pe", bc_done[t])
                K.w(A, "s_v", rstdb_war)
                K.op(A, "activation", RSTD2B[:], PS[6][:, 0:TILE], AF.Identity,
                     inc=("s_a", 1))
                rstdb_evac = K.ctr["s_a"]
                xb = XB2P[:, buf * 8 * TILE:(buf + 1) * 8 * TILE]
                if e == 0:
                    K.w(A, "s_pe", xb2_use[buf])
                    if t >= 2:
                        K.w(A, "s_xia" if buf == 0 else "s_xib", xb2_wr[t - 2])
                    for c in range(8):
                        K.op(A, "activation", xb[:, c * TILE:(c + 1) * TILE],
                             X[c][:, t0:t0 + TILE], AF.Identity,
                             scale=self.PK["ln2_w"][:, l * 8 + c:l * 8 + c + 1],
                             inc=("s_a", 1) if c == 7 else None)
                    cast_done[t] = K.ctr["s_a"]
                    K.w(SY, "s_a", cast_done[t])
                    xb2_wr[t] = K.dma(SY, self.xb2d.ap()[:, :, t0:t0 + TILE],
                                      xb.rearrange("p (c w) -> p c w", c=8),
                                      "s_xia" if buf == 0 else "s_xib")
                    xb_rdy = ("s_a", cast_done[t])
                else:
                    K.w(SY, "s_pe", xb2_use[buf])
                    if e == 1:
                        K.w(SY, "s_xia" if buf == 0 else "s_xib",
                            xb2_wr[NT - 2 + buf])
                    n = K.dma(SY, xb.rearrange("p (c w) -> p c w", c=8),
                              self.xb2d.ap()[:, :, t0:t0 + TILE],
                              "s_xia" if buf == 0 else "s_xib")
                    xb_rdy = ("s_xia" if buf == 0 else "s_xib", n)
                K.w(SY, "s_pe", fix_done)
                mrit2_ld = K.dma(SY, MRIT2[0:2, :],
                                 self.mrid2.ap()[t].rearrange("(a b) -> a b", a=2),
                                 "s_w2a")
                for m in range(4):
                    hb = 2 + m % 2
                    K.w(TT, sem_e, pair)
                    K.w(TT, xb_rdy[0], xb_rdy[1])
                    K.w(TT, "s_w2a", mrit2_ld)
                    K.w(TT, "s_a", s1c1_done)
                    K.w(TT, "s_v", hid_free[m % 2])
                    for kc in range(8):
                        K.mm(PS[hb][:, 0:TILE],
                             self.WS[be][:, m * 1024 + kc * 128:m * 1024 + (kc + 1) * 128],
                             xb[:, kc * TILE:(kc + 1) * TILE],
                             start=(kc == 0), stop=False)
                    K.mm(PS[hb][:, 0:TILE], S1C1[0:2, m * 128:(m + 1) * 128],
                         MRIT2[0:2, :], start=False, stop=True, inc=("s_pe", 1))
                    hdone = K.ctr["s_pe"]
                    K.w(A, "s_pe", hdone)
                    K.w(A, "s_v")
                    K.op(A, "activation", self.SCR[:], PS[hb][:, 0:TILE], AF.Identity,
                         scale=LEAK, inc=("s_a", 1))
                    K.w(V, "s_a")
                    K.w(V, "s_a", rstdb_evac)
                    K.op(V, "tensor_tensor", LK[:], PS[hb][:, 0:TILE], self.SCR[:],
                         ALU.max)
                    K.op(V, "drain")
                    K.op(V, "tensor_mul", HID[:, m * TILE:(m + 1) * TILE], LK[:],
                         RSTD2B[:], inc=("s_v", 1))
                    hid_free[m % 2] = K.ctr["s_v"]
                fix_done = K.ctr["s_pe"]
                xb2_use[buf] = fix_done
                hid_done = K.ctr["s_v"]
                rstdb_war = hid_done
                for j in range(8):
                    jb = 4 + j % 2
                    K.w(TT, "s_v", hid_done)
                    K.w(TT, "s_v", y_free[j % 2])
                    for mi in range(4):
                        K.mm(PS[jb][:, 0:TILE],
                             self.WS[be][:, 4096 + j * 512 + mi * 128:
                                         4096 + j * 512 + (mi + 1) * 128],
                             HID[:, mi * TILE:(mi + 1) * TILE],
                             start=(mi == 0), stop=(mi == 3),
                             inc=("s_pe", 1) if mi == 3 else None)
                    jdone = K.ctr["s_pe"]
                    K.w(V, "s_pe", jdone)
                    K.op(V, "tensor_add", X[j][:, t0:t0 + TILE], X[j][:, t0:t0 + TILE],
                         PS[jb][:, 0:TILE], inc=("s_v", 1))
                    y_free[j % 2] = K.ctr["s_v"]
                eighth_pe = K.ctr["s_pe"]
                if e == 0 and t + 2 < NT:
                    stats2(t + 2)
            if e == 7 and l + 1 < DEPTH:
                K.w(G, "s_pe", e_gate)
                self.wld["wv"][l + 1] = self.load_wb(self.WS[l % 2][:], "wv", l + 1)
        if l + 1 < DEPTH:
            K.w(G, "s_pe")
            self.wld["wk"][l + 1] = self.load_wb(self.WS[(l + 1) % 2][:], "wk",
                                                 l + 1, "s_ac")
            self.wk_sem[l + 1] = "s_ac"

    def x1_form(self):
        K = self.K
        K.phase()
        TMJ = [K.sb(f"fTMJ{i}", [128, D], F32, local=True) for i in range(2)]
        X, PS, IDF = self.X, self.PS, self.IDF
        abrs = [self.ABR, self.ABR]
        self.ln_stats_rows((0, TILE), 0, self.ROW, EPS_LN)
        ab_done = self.ln_bcast(self.ROW, TILE, (0, 1), abrs[0])
        for t in range(NT):
            t0 = t * TILE
            abr = abrs[t % 2]
            for s in range(4):
                par = s % 2
                K.w(SY, "s_pe")
                ld = K.dma(SY, TMJ[par][:], self.xin[t0 + s * 128: t0 + (s + 1) * 128, :])
                K.w(TT, "s_x", ld)
                K.w(TT, "s_v")
                for c in range(8):
                    K.op(TT, "transpose", PS[2 + (c % 4)][:, (c // 4) * 128:(c // 4) * 128 + 128],
                         TMJ[par][:, c * 128:(c + 1) * 128], IDF[:],
                         inc=("s_pe", 1) if c == 7 else None)
                trd = K.ctr["s_pe"]
                K.w(V, "s_pe", trd)
                K.w(V, "s_a", ab_done)
                for c in range(8):
                    cs = slice(t0 + s * 128, t0 + (s + 1) * 128)
                    K.op(V, "tensor_mul", self.SCR[:, 0:128], X[c][:, cs],
                         abr[:, s * 128:(s + 1) * 128])
                    K.op(V, "drain")
                    K.op(V, "tensor_add", self.SCR[:, 0:128], self.SCR[:, 0:128],
                         abr[:, TILE + s * 128:TILE + (s + 1) * 128])
                    K.op(V, "drain")
                    K.op(V, "tensor_add", X[c][:, cs], self.SCR[:, 0:128],
                         PS[2 + (c % 4)][:, (c // 4) * 128:(c // 4) * 128 + 128])
                    K.op(V, "drain")
                K.di(V, "s_v")
            K.w(SY, "s_v")
            for c in range(8):
                self.spill_x = K.dma(SY, self.x1sp.ap()[c, :, t0:t0 + TILE],
                                     X[c][:, t0:t0 + TILE])
            if t + 1 < NT:
                self.ln_stats_rows(((t + 1) * TILE, TILE), 0, self.ROW, EPS_LN)
                ab_done = self.ln_bcast(self.ROW, TILE, (0, 1), abrs[(t + 1) % 2])

    def ffn_s1_s2(self):
        K = self.K
        K.phase()
        X, PS = self.X, self.PS
        HB = K.sb("s2HB", [128, 2], F32, local=True)
        for t in range(NT):
            self.ln_stats_rows((t * TILE, TILE), 0, self.ROW, EPS_LN)
            K.w(SY, "s_v")
            self.rows_x[t] = K.dma(SY, self.rowsd.ap()[t], self.ROW[0:1, :])
            K.w(SY, "s_x")
            if t == NT - 1:
                K.w(TT, "s_v")
                K.mm(PS[1][:, 0:2], self.ONEF[:], self.ROW[0:1, 510:512],
                     start=True, stop=True)
                K.mm(PS[2][:, 0:2], self.ONEF[:], self.ROW[0:1, TILE + 510:TILE + 512],
                     start=True, stop=True, inc=("s_pe", 1))
                K.w(V, "s_pe")
                nh = 0
                for c in range(8):
                    K.w(V, "s_x")
                    K.op(V, "tensor_mul", HB[:, 0:2], X[c][:, 4094:4096], PS[1][:, 0:2])
                    K.op(V, "drain")
                    K.op(V, "tensor_add", HB[:, 0:2], HB[:, 0:2], PS[2][:, 0:2])
                    K.di(V, "s_v")
                    K.w(SY, "s_v")
                    nh = K.dma(SY, self.hl_in.ap()[c * 128:(c + 1) * 128, :], HB[:, 0:2])
                K.w(G, "s_x", nh)
                K.op(G, "collective_compute", "AllGather", ALU.bypass,
                     replica_groups=RG,
                     ins=[self.hl_in.ap()], outs=[self.hl_out.ap()], inc=("s_cc", 1))
                K.w(G, "s_cc")
                nhl = K.dma(G, self.HALO[:].rearrange("p (c t) -> p c t", c=8),
                            self.hl_out.ap()[0:1024, :].rearrange("(c p) t -> p c t", p=128),
                            "s_w")
                K.w(G, "s_w")
                K.w(V, "s_w", nhl)
                K.op(V, "tensor_scalar", self.HALO[:], self.HALO[:], self.MCOL[:, 0:1],
                     None, ALU.mult)
                K.di(V, "s_v")

        XA = K.sb("s2XA", [128, 8 * (TILE + 2)], F32, local=True)
        XC = K.sb("s2XC", [128, 8 * TILE], BF16, local=True)
        PWT = self.WS[0]
        K.w(SY, "s_x")
        K.w(SY, "s_c", self.g_scr)
        K.w(SY, "s_pe")
        npw = 0
        for st in range(8):
            npw = K.dma(SY, PWT[:, st * 1024:(st + 1) * 1024],
                        self.pw_s.ap()[:, st * 128:(st + 1) * 128], transpose=True)
        K.w(SY, "s_x")
        MG = TILE + 2
        resv = 0
        for t in range(NT):
            t0 = t * TILE
            K.w(SY, "s_pe")
            K.w(SY, "s_x", self.rows_x[t])
            nr = K.dma(SY, self.ROW[0:1, :], self.rowsd.ap()[t])
            K.w(V, "s_x", nr)
            K.w(TT, "s_x", nr)
            K.w(V, "s_a")
            self.ln_bcast(self.ROW, TILE, (0, 1), self.ABR)
            ab = K.ctr["s_a"]
            K.w(V, "s_a", ab)
            for c in range(8):
                src = self.HALO[:, c * 2:c * 2 + 2] if t == 0 else self.CARRY[:, c * 2:c * 2 + 2]
                K.w(V, "s_a")
                K.op(V, "tensor_copy", XA[:, c * MG:c * MG + 2], src)
                K.op(V, "tensor_mul", self.SCR[:], X[c][:, t0:t0 + TILE], self.ABR[:, 0:TILE])
                K.op(V, "drain")
                K.op(V, "tensor_add", XA[:, c * MG + 2:c * MG + 2 + TILE], self.SCR[:],
                     self.ABR[:, TILE:2 * TILE])
                K.op(V, "drain")
                K.op(V, "tensor_copy", self.CARRY[:, c * 2:c * 2 + 2],
                     XA[:, c * MG + TILE:c * MG + 2 + TILE])
                K.op(V, "tensor_scalar", self.SCR[:], XA[:, c * MG:c * MG + TILE],
                     self.PK["dwk"][:, c * 3:c * 3 + 1], None, ALU.mult)
                K.op(V, "drain")
                K.op(V, "scalar_tensor_tensor", self.SCR[:], XA[:, c * MG + 1:c * MG + 1 + TILE],
                     self.PK["dwk"][:, c * 3 + 1:c * 3 + 2], self.SCR[:], ALU.mult, ALU.add)
                K.op(V, "drain")
                K.op(V, "scalar_tensor_tensor", self.SCR[:], XA[:, c * MG + 2:c * MG + 2 + TILE],
                     self.PK["dwk"][:, c * 3 + 2:c * 3 + 3], self.SCR[:], ALU.mult, ALU.add)
                K.di(V, "s_v")
                K.w(A, "s_v")
                K.op(A, "activation", XC[:, c * TILE:(c + 1) * TILE], self.SCR[:], AF.Identity,
                     bias=self.PK["vec5"][:, 0 * 8 + c:0 * 8 + c + 1])
                K.di(A, "s_a")
            xc_done = K.ctr["s_a"]
            if t == 0:
                K.w(TT, "s_x", npw)
            K.w(TT, "s_a", xc_done)
            pwv = [resv, resv]
            for m in range(8):
                pb = 2 + m % 2
                K.w(TT, "s_v", pwv[m % 2])
                for kc in range(8):
                    K.mm(PS[pb][:], PWT[:, kc * 1024 + m * 128: kc * 1024 + (m + 1) * 128],
                         XC[:, kc * TILE:(kc + 1) * TILE], start=(kc == 0), stop=(kc == 7),
                         inc=("s_pe", 1) if kc == 7 else None)
                pdone = K.ctr["s_pe"]
                K.w(A, "s_pe", pdone)
                K.w(A, "s_v")
                K.op(A, "activation", self.SCR2[:], PS[pb][:], AF.Silu,
                     bias=self.PK["vec5"][:, 1 * 8 + m:1 * 8 + m + 1])
                K.di(A, "s_a")
                K.w(V, "s_a")
                K.op(V, "tensor_add", X[m][:, t0:t0 + TILE],
                     XA[:, m * MG + 2:m * MG + 2 + TILE], self.SCR2[:])
                K.di(V, "s_v")
                pwv[m % 2] = K.ctr["s_v"]
            resv = K.ctr["s_v"]

    def ffn_s3(self):
        K = self.K
        K.phase()
        X, PS = self.X, self.PS
        LGR = K.sb("s3LGR", [128, 8], F32, local=True)
        EL = K.sb("s3EL", [128, 8], F32, local=True)
        TP = K.sb("s3TP", [128, 8], F32, local=True)
        M1 = K.sb("s3M1", [128, 4], F32, local=True)
        RWTC = K.sb("s3RWTC", [128, 8 * 8], F32, local=True)
        RWST = K.sb("s3RWST", [8, 1024], F32, local=True)
        LGD = K.sb("s3LGD", [128, 8], F32, local=True)
        self._LGD = LGD
        n = K.dma(SY, RWST[:], self.win["router_w"])
        K.w(SY, "s_x")
        K.w(TT, "s_x", n)
        for kc in range(8):
            K.op(TT, "transpose", PS[2][:, (kc % 2) * 8:(kc % 2) * 8 + 8],
                 RWST[:, kc * 128:(kc + 1) * 128], self.IDF[0:8, 0:8],
                 inc=("s_pe", 1))
            K.w(V, "s_pe")
            K.op(V, "tensor_copy", RWTC[:, kc * 8:(kc + 1) * 8],
                 PS[2][:, (kc % 2) * 8:(kc % 2) * 8 + 8])
            K.di(V, "s_v")
            K.w(TT, "s_v")
        rw_ready = K.ctr["s_v"]
        K.w(TT, "s_v", rw_ready)
        self.rw_done = [0] * NT
        for t in range(NT):
            t0 = t * TILE
            for s in range(4):
                K.w(TT, "s_v")
                for kc in range(8):
                    K.mm(PS[2][0:128, 0:8],
                         X[kc][:, t0 + s * 128:t0 + s * 128 + 128],
                         RWTC[:, kc * 8:(kc + 1) * 8], start=(kc == 0), stop=(kc == 7),
                         inc=("s_pe", 1) if kc == 7 else None)
                lgd = K.ctr["s_pe"]
                K.w(V, "s_pe", lgd)
                K.op(V, "tensor_add", LGR[:], PS[2][0:128, 0:8], self.RB128[:])
                K.op(V, "drain")
                if self.dbg == "rw" and t == 0 and s == 0:
                    K.op(V, "tensor_copy", LGD[:], LGR[:])
                    K.op(V, "drain")
                K.op(V, "reduce_max", M1[:, 0:1], LGR[:], axis=AX.X)
                K.op(V, "drain")
                K.op(V, "tensor_scalar", M1[:, 1:2], M1[:, 0:1], -1.0, None, ALU.mult)
                K.op(V, "tensor_scalar", TP[:], LGR[:], M1[:, 0:1], None, ALU.is_ge)
                K.op(V, "drain")
                K.op(V, "scalar_tensor_tensor", TP[:], TP[:], -1e30, LGR[:], ALU.mult, ALU.add)
                K.op(V, "drain")
                K.op(V, "reduce_max", M1[:, 2:3], TP[:], axis=AX.X)
                K.di(V, "s_v")
                K.w(A, "s_v")
                K.op(A, "activation", EL[:], LGR[:], AF.Exp, bias=M1[:, 1:2])
                K.op(A, "activation", M1[:, 3:4], M1[:, 2:3], AF.Exp, bias=M1[:, 1:2])
                K.op(A, "drain")
                K.op(A, "activation", M1[:, 3:4], M1[:, 3:4], AF.Ln, bias=1.0)
                K.op(A, "drain")
                K.op(A, "activation", M1[:, 3:4], M1[:, 3:4], AF.Exp, scale=-1.0)
                K.di(A, "s_a")
                K.w(V, "s_a")
                K.op(V, "tensor_scalar", TP[:], LGR[:], M1[:, 2:3], None, ALU.is_ge)
                K.op(V, "drain")
                K.op(V, "tensor_mul", TP[:], TP[:], EL[:])
                K.op(V, "drain")
                sg = t * 4 + s
                K.op(V, "tensor_scalar", self.RW[:, sg * 8:(sg + 1) * 8], TP[:],
                     M1[:, 3:4], None, ALU.mult)
                K.di(V, "s_v")
            self.rw_done[t] = K.ctr["s_v"]
        K.w(G, "s_v")
        self.xib_ld = 0
        for c in range(8):
            self.xib_ld = K.dma(G, self.xibd.ap()[c], X[c][:], "s_c")
        K.w(G, "s_c")

    def ffn_s4(self):
        K = self.K
        K.phase()
        PS = self.PS
        XIB = K.sb("s4XIB", [128, 2 * 8 * TILE], BF16, local=True)
        COMBP = K.sb("s4CP", [128, 2 * 1024], F32, local=True)
        EBR = K.sb("s4EB", [1, D], BF16, local=True)
        accd = [[0] * NT for _ in range(E)]
        last_comb_dma = [0]
        xld = [0, 0]

        def load_xib(t, buf):
            K.w(SY, "s_pe")
            sm = "s_xia" if buf == 0 else "s_xib"
            n = 0
            for c in range(8):
                n = K.dma(SY, XIB[:, buf * 8 * TILE + c * TILE: buf * 8 * TILE + (c + 1) * TILE],
                          self.xibd.ap()[c, :, t * TILE:(t + 1) * TILE], sm)
            xld[buf] = n

        K.w(SY, "s_c", self.xib_ld)
        load_xib(0, 0)
        load_xib(1, 1)
        ws_free_pe = 0
        EMAX = 1 if self.dbg == "comb1" else (2 if self.dbg == "comb2" else E)
        for e in range(EMAX):
            K.w(G, "s_ac")
            WT = self.WS[1 - e % 2]
            K.w(SY, "s_pe", ws_free_pe)
            nw = 0
            for st in range(8):
                nw = K.dma(SY, WT[:, st * 1024:(st + 1) * 1024],
                           self.ew_s.ap()[e][:, st * 128:(st + 1) * 128], transpose=True)
            K.w(SY, "s_x")
            ne = K.dma(G, EBR[:], self.eb_s.ap()[e:e + 1, :], "s_w")
            K.w(G, "s_w")
            K.w(TT, "s_x", nw)
            K.w(TT, "s_w", ne)
            for t in range(NT):
                buf = t % 2
                K.w(TT, "s_xia" if buf == 0 else "s_xib", xld[buf])
                for s in range(4):
                    if s % 2 == 0:
                        K.w(V, "s_ac", last_comb_dma[0])
                    for half in range(2):
                        yb = 2 + (s * 2 + half) % 2
                        K.w(TT, "s_a")
                        xb = XIB[:, buf * 8 * TILE:]
                        for kc in range(8):
                            K.mm(PS[yb][:],
                                 xb[:, kc * TILE + s * 128: kc * TILE + s * 128 + 128],
                                 WT[:, kc * 1024 + half * 512: kc * 1024 + (half + 1) * 512],
                                 start=(kc == 0), stop=False)
                        K.mm(PS[yb][:], self.ONEB[:],
                             EBR[0:1, half * 512:(half + 1) * 512],
                             start=False, stop=True, inc=("s_pe", 1))
                        ydone = K.ctr["s_pe"]
                        K.w(A, "s_pe", ydone)
                        K.w(A, "s_v")
                        scr = self.SCR if half == 0 else self.SCR2
                        K.op(A, "activation", scr[:], PS[yb][:], AF.Silu,
                             inc=("s_a", 1))
                        K.w(V, "s_a")
                        dst = COMBP[:, (s % 2) * 1024 + half * 512:
                                    (s % 2) * 1024 + (half + 1) * 512]
                        rwc = self.RW[:, (t * 4 + s) * 8 + e:(t * 4 + s) * 8 + e + 1]
                        K.op(V, "tensor_scalar", dst, scr[:], rwc, None, ALU.mult,
                             inc=("s_v", 1))
                    if s % 2 == 1:
                        hf0 = t * TILE + (s // 2) * 256
                        K.w(G, "s_v")
                        K.w(G, "s_ac")
                        accd[e][t] = K.dma(
                            G, self.combd.ap()[hf0:hf0 + 256, :]
                            .rearrange("(sl p) o -> p sl o", p=128),
                            COMBP[:].rearrange("p (sl o) -> p sl o", sl=2), "s_ac",
                            accum_op=(ALU.bypass if e == 0 else ALU.add))
                        last_comb_dma[0] = accd[e][t]
                if not (e == E - 1 and t >= NT - 2):
                    load_xib((t + 2) % NT, buf)
            ws_free_pe = K.ctr["s_pe"]
        self.acc_last = accd[E - 1]
        K.w(SY, "s_xia")
        K.w(SY, "s_xib")

    def ffn_s4b(self):
        K = self.K
        K.phase()
        X, PS, PSB = self.X, self.PS, self.PSB
        CMB = K.sb("s4bC", [128, 2 * 1024], F32, local=True)
        CB = K.sb("s4bCB", [128, 2 * 1024], BF16, local=True)
        CMT = K.sb("s4bT", [128, 8 * 256], BF16, local=True)
        OWT = self.WS[1]
        K.w(SY, "s_pe")
        nw = 0
        for st in range(8):
            nw = K.dma(SY, OWT[:, st * 1024:(st + 1) * 1024],
                       self.ow_s.ap()[:, st * 128:(st + 1) * 128], transpose=True)
        K.w(SY, "s_x")
        K.w(TT, "s_x", nw)
        for t in range(NT):
            t0 = t * TILE
            K.w(SY, "s_ac", self.acc_last[t])
            for hf in range(2):
                hf0 = t0 + hf * 256
                K.w(SY, "s_v")
                nc_ = K.dma(SY, CMB[:].rearrange("p (sl o) -> p sl o", sl=2),
                            self.combd.ap()[hf0:hf0 + 256, :]
                            .rearrange("(sl p) o -> p sl o", p=128))
                K.w(V, "s_x", nc_)
                K.op(V, "tensor_copy", CB[:], CMB[:])
                K.di(V, "s_v")
                cbd = K.ctr["s_v"]
                K.w(TT, "s_v", cbd)
                K.w(TT, "s_a")
                for sl in range(2):
                    for oc in range(8):
                        K.op(TT, "transpose", PSB[:, (oc % 4) * 256 + sl * 128:
                                                  (oc % 4) * 256 + sl * 128 + 128],
                             CB[:, sl * 1024 + oc * 128: sl * 1024 + (oc + 1) * 128],
                             self.IDB[:], inc=("s_pe", 1) if oc % 4 == 3 else None)
                        if oc % 4 == 3:
                            grp = K.ctr["s_pe"]
                            K.w(V, "s_pe", grp)
                            for o2 in range(oc - 3, oc + 1):
                                K.op(V, "tensor_copy",
                                     CMT[:, o2 * 256 + sl * 128: o2 * 256 + sl * 128 + 128],
                                     PSB[:, (o2 % 4) * 256 + sl * 128:
                                         (o2 % 4) * 256 + sl * 128 + 128])
                            K.di(V, "s_v")
                            K.w(TT, "s_v")
                cmt_v = K.ctr["s_v"]
                K.w(TT, "s_v", cmt_v)
                oev = [0, 0]
                for m in range(8):
                    ob = 2 + m % 2
                    K.w(TT, "s_v", oev[m % 2])
                    for oc in range(8):
                        K.mm(PS[ob][:, 0:256],
                             OWT[:, oc * 1024 + m * 128: oc * 1024 + (m + 1) * 128],
                             CMT[:, oc * 256:(oc + 1) * 256], start=(oc == 0), stop=(oc == 7),
                             inc=("s_pe", 1) if oc == 7 else None)
                    od = K.ctr["s_pe"]
                    K.w(A, "s_pe", od)
                    K.w(A, "s_v")
                    K.op(A, "activation", self.SCR2[:, 0:256], PS[ob][:, 0:256], AF.Identity,
                         bias=self.PK["vec5"][:, 2 * 8 + m:2 * 8 + m + 1])
                    K.di(A, "s_a")
                    K.w(V, "s_a")
                    K.op(V, "tensor_add", X[m][:, hf0:hf0 + 256], X[m][:, hf0:hf0 + 256],
                         self.SCR2[:, 0:256])
                    K.di(V, "s_v")
                    oev[m % 2] = K.ctr["s_v"]
            self.ln_stats_rows((t0, TILE), 0, self.ROW, EPS_RMS, with_mean=False, rms=True,
                               raw=True)
            K.w(SY, "s_a")
            self.rows_x[t] = K.dma(SY, self.rowsd.ap()[t], self.ROW[0:1, :])
            K.w(SY, "s_x")

    def ffn_s5(self):
        K = self.K
        K.phase()
        X, PS = self.X, self.PS
        HBUF = K.sb("s5HB", [128, 8 * TILE], BF16, local=True)
        FH = K.sb("s5FH", [128, 2 * TILE], BF16, local=True)
        FDT = K.sb("s5FDT", [128, 8 * HIDD], BF16, local=True)
        FUT = K.sb("s5FUT", [128, 2 * 1024], BF16, local=True)
        n = 0
        K.w(SY, "s_pe")
        for st in range(8):
            n = K.dma(SY, FDT[:, st * HIDD:(st + 1) * HIDD],
                      self.fd_s.ap()[:, st * 128:(st + 1) * 128], transpose=True)
        for st in range(2):
            n = K.dma(SY, FUT[:, st * 1024:(st + 1) * 1024],
                      self.fu_s.ap()[:, st * 128:(st + 1) * 128], transpose=True)
        K.w(SY, "s_x")
        K.w(TT, "s_x", n)
        for t in range(NT):
            t0 = t * TILE
            K.w(SY, "s_pe")
            K.w(SY, "s_x", self.rows_x[t])
            nr = K.dma(SY, self.ROW[0:1, :], self.rowsd.ap()[t])
            K.w(A, "s_x", nr)
            K.w(A, "s_v")
            K.op(A, "activation", self.ROW[0:1, 0:TILE], self.ROW[0:1, TILE:2 * TILE],
                 AF.Ln, bias=EPS_RMS)
            K.op(A, "drain")
            K.op(A, "activation", self.ROW[0:1, 0:TILE], self.ROW[0:1, 0:TILE],
                 AF.Exp, scale=-0.5)
            K.di(A, "s_a")
            self.ln_bcast(self.ROW, TILE, (0, 1), self.ABR, beta=False)
            abd = K.ctr["s_a"]
            K.w(V, "s_a", abd)
            for c in range(8):
                K.w(V, "s_a")
                K.op(V, "scalar_tensor_tensor", self.SCR[:], X[c][:, t0:t0 + TILE],
                     1.0, self.ABR[:, 0:TILE], ALU.mult, ALU.mult)
                K.di(V, "s_v")
                K.w(A, "s_v")
                K.op(A, "activation", X[c][:, t0:t0 + TILE], self.SCR[:], AF.Identity,
                     scale=self.PK["vec5"][:, 3 * 8 + c:3 * 8 + c + 1])
                K.op(A, "drain")
                K.op(A, "copy", HBUF[:, c * TILE:(c + 1) * TILE], X[c][:, t0:t0 + TILE])
                K.di(A, "s_a")
            hb_done = K.ctr["s_a"]
            K.w(TT, "s_a", hb_done)
            for m2 in range(2):
                fb = 2 + m2
                K.w(TT, "s_a")
                for kc in range(8):
                    K.mm(PS[fb][:], FDT[:, kc * HIDD + m2 * 128: kc * HIDD + (m2 + 1) * 128],
                         HBUF[:, kc * TILE:(kc + 1) * TILE], start=(kc == 0), stop=(kc == 7),
                         inc=("s_pe", 1) if kc == 7 else None)
                fd = K.ctr["s_pe"]
                K.w(A, "s_pe", fd)
                K.op(A, "activation", FH[:, m2 * TILE:(m2 + 1) * TILE], PS[fb][:], AF.Gelu,
                     bias=self.PK["fdb"][:, m2:m2 + 1])
                K.di(A, "s_a")
            fh_done = K.ctr["s_a"]
            K.w(TT, "s_a", fh_done)
            fev = [0, 0]
            for m in range(8):
                ub = 4 + m % 2
                K.w(TT, "s_v", fev[m % 2])
                for kc in range(2):
                    K.mm(PS[ub][:], FUT[:, kc * 1024 + m * 128: kc * 1024 + (m + 1) * 128],
                         FH[:, kc * TILE:(kc + 1) * TILE], start=(kc == 0), stop=(kc == 1),
                         inc=("s_pe", 1) if kc == 1 else None)
                ud = K.ctr["s_pe"]
                K.w(A, "s_pe", ud)
                K.w(A, "s_v")
                K.op(A, "activation", self.SCR2[:], PS[ub][:], AF.Identity,
                     bias=self.PK["vec5"][:, 4 * 8 + m:4 * 8 + m + 1])
                K.di(A, "s_a")
                K.w(V, "s_a")
                K.op(V, "tensor_add", X[m][:, t0:t0 + TILE], X[m][:, t0:t0 + TILE],
                     self.SCR2[:])
                K.di(V, "s_v")
                fev[m % 2] = K.ctr["s_v"]
            self.ln_stats_rows((t0, TILE), 0, self.ROW, EPS_LN, raw=True)
            K.w(SY, "s_a")
            self.rows_x[t] = K.dma(SY, self.rowsd.ap()[t], self.ROW[0:1, :])
            K.w(SY, "s_x")

    def ffn_s7(self):
        K = self.K
        K.phase()
        X, PS, IDF = self.X, self.PS, self.IDF
        X1B = K.sb("s7X1", [128, 8 * TILE], F32, local=True)
        OUTT = [K.sb(f"s7O{i}", [128, 1024], F32, local=True) for i in range(2)]
        st_done = 0
        K.w(SY, "s_x", self.spill_x)
        for t in range(NT):
            t0 = t * TILE
            K.w(SY, "s_pe")
            K.w(SY, "s_x", self.rows_x[t])
            nr = K.dma(SY, self.ROW[0:1, :], self.rowsd.ap()[t])
            K.w(SY, "s_x")
            K.w(SY, "s_v")
            nx = 0
            for c in range(8):
                nx = K.dma(SY, X1B[:, c * TILE:(c + 1) * TILE], self.x1sp.ap()[c, :, t0:t0 + TILE])
            K.w(A, "s_x", nr)
            K.w(V, "s_x", nr)
            self.finish_rows(self.ROW, TILE, EPS_LN, True)
            self.ln_bcast(self.ROW, TILE, (0, 1), self.ABR)
            abd = K.ctr["s_a"]
            K.w(V, "s_a", abd)
            K.w(V, "s_x", nx)
            for c in range(8):
                K.op(V, "tensor_mul", self.SCR[:], X[c][:, t0:t0 + TILE], self.ABR[:, 0:TILE])
                K.op(V, "drain")
                K.op(V, "tensor_add", self.SCR[:], self.SCR[:], self.ABR[:, TILE:2 * TILE])
                K.op(V, "drain")
                K.op(V, "tensor_add", X[c][:, t0:t0 + TILE], self.SCR[:],
                     X1B[:, c * TILE:(c + 1) * TILE])
                K.op(V, "drain")
            K.di(V, "s_v")
            od = K.ctr["s_v"]
            for s in range(4):
                par = s % 2
                K.w(TT, "s_v", od)
                K.w(TT, "s_st")
                K.w(TT, "s_a")
                for c in range(8):
                    K.op(TT, "transpose", PS[c // 4][:, (c % 4) * 128:(c % 4 + 1) * 128],
                         X[c][:, t0 + s * 128:t0 + (s + 1) * 128], IDF[:],
                         inc=("s_pe", 1) if c == 7 else None)
                trd = K.ctr["s_pe"]
                K.w(V, "s_pe", trd)
                K.w(A, "s_pe", trd)
                for c in range(8):
                    eng, meth = (V, "tensor_copy") if c < 4 else (A, "copy")
                    K.op(eng, meth, OUTT[par][:, c * 128:(c + 1) * 128],
                         PS[c // 4][:, (c % 4) * 128:(c % 4 + 1) * 128])
                K.di(V, "s_v")
                K.di(A, "s_a")
                K.w(SY, "s_v")
                K.w(SY, "s_a")
                st_done = K.dma(SY, self.out[t0 + s * 128:t0 + (s + 1) * 128, :], OUTT[par][:],
                                "s_st")
                K.w(TT, "s_v")
                K.w(TT, "s_a")
        K.w(G, "s_st", K.ctr["s_st"])
        K.w(G, "s_x", K.ctr["s_x"])

    def dump_X(self):
        K = self.K
        K.w(SY, "s_v")
        K.w(SY, "s_a")
        K.w(SY, "s_pe")
        n = 0
        for c in range(8):
            n = K.dma(SY, self.dump[c], self.X[c][:])
        K.w(G, "s_x", n)
        K.w(G, "s_w")
        K.w(G, "s_cc")
        K.stopped = True

    def dump_masks(self):
        K = self.K
        K.w(SY, "s_g", self.g_masks)
        K.w(SY, "s_v")
        K.w(SY, "s_a")
        n = K.dma(SY, self.dump2[0:64, :].rearrange("(a b) c -> a b c", a=8),
                  self.SEL8[:].rearrange("a (b c) -> a b c", b=8))
        n = K.dma(SY, self.dump2[128:256, :], self.IDF[:])
        n = K.dma(SY, self.dump2[256:384, 0:8], self.CONSTP[:, 0:8])
        K.w(G, "s_x", n)
        K.w(G, "s_w")
        K.stopped = True

    def dump_cc(self):
        K = self.K
        K.w(SY, "s_cc")
        n = K.dma(SY, self.dump2[:], self.cc_out.ap())
        K.w(G, "s_x", n)
        K.stopped = True

    def dump_rw(self):
        K = self.K
        K.w(SY, "s_v")
        n = K.dma(SY, self.dump2[0:128, :], self.RW[:, 0:128])
        n = K.dma(SY, self.dump2[128:256, 0:8], self._LGD[:])
        K.w(G, "s_x", n)
        K.stopped = True

    def generate(self):
        dbg = self.dbg
        if dbg == "prep":
            self.dump_masks()
            return self.finish()
        if dbg == "init2":
            self.initial(nsl=2)
            self.dump_X()
            return self.finish()
        if dbg == "initnoln":
            self.initial(do_ln=False)
            self.dump_X()
            return self.finish()
        self.initial()
        if dbg == "xint0":
            self.dump_X()
            return self.finish()
        for l in range(DEPTH):
            self.pass1(l)
            if dbg == f"l{l}_U":
                self.dump_cc()
                return self.finish()
            self.pass2a(l)
            if dbg == f"l{l}_xmid":
                self.dump_X()
                return self.finish()
            self.pass2b(l)
            if dbg == f"l{l}_x":
                self.dump_X()
                return self.finish()
        self.x1_form()
        if dbg == "x1":
            self.dump_X()
            return self.finish()
        self.ffn_s1_s2()
        if dbg == "x1i":
            self.dump_X()
            return self.finish()
        self.ffn_s3()
        if dbg == "rw":
            self.dump_rw()
            return self.finish()
        self.ffn_s4()
        if dbg in ("comb", "comb1", "comb2"):
            K = self.K
            K.w(SY, "s_ac")
            n = K.dma(SY, self.dump.rearrange("a b c -> (a b c)"),
                      self.combd.ap().rearrange("a b -> (a b)"))
            K.w(G, "s_x", n)
            K.stopped = True
            return self.finish()
        self.ffn_s4b()
        if dbg == "u":
            self.dump_X()
            return self.finish()
        self.ffn_s5()
        if dbg == "ffret":
            self.dump_X()
            return self.finish()
        self.ffn_s7()
        return self.finish()

    def finish(self):
        if self.K.ph is not None:
            self.K.ph.close()
            self.K.ph = None
        nc = self.K.build()
        self.K.es.close()
        return nc


_NC_CACHE = {}


def get_nc(dbg=None):
    if dbg not in _NC_CACHE:
        _NC_CACHE[dbg] = Mega(dbg).generate()
    return _NC_CACHE[dbg]


def kernel(**inputs):
    nc = get_nc(None)
    return _run(nc, inputs, None)


def _run(nc, inputs, dbg):
    xfull = np.ascontiguousarray(np.asarray(inputs["L0_attn_x"], np.float32))
    in_maps = []
    for c in range(8):
        b, half = c // 2, c % 2
        m = {"x": np.ascontiguousarray(xfull[b, half * 4096:(half + 1) * 4096, :]),
             "hmask": np.full((128, 1), 1.0 if half == 1 else 0.0, np.float32)}
        for nm in ["ln1_w", "ln1_b", "wq", "wk", "wv", "wo", "bo", "ln2_w", "ln2_b",
                   "w1", "w2"]:
            m["w_" + nm] = np.ascontiguousarray(np.asarray(inputs["L0_attn_" + nm], np.float32))
        for nm in ["dw_kernel", "dw_bias", "pw_w", "pw_b", "router_w", "router_b",
                   "expert_w", "expert_b", "out_w", "out_b", "norm_w", "ff_down_w",
                   "ff_down_b", "ff_up_w", "ff_up_b"]:
            m["w_" + nm] = np.ascontiguousarray(np.asarray(inputs["L0_ffn_" + nm], np.float32))
        in_maps.append(m)
    res = run_bass_kernel_spmd(nc, in_maps, core_ids=list(range(8)))
    if dbg:
        return res
    out = np.empty((4, 8192, 1024), np.float32)
    for c in range(8):
        b, half = c // 2, c % 2
        out[b, half * 4096:(half + 1) * 4096, :] = res.results[c]["out"]
    return out


def run_debug(dbg, inputs):
    nc = get_nc(dbg)
    return _run(nc, inputs, dbg)


# version 05
# speedup vs baseline: 1.1847x; 1.0341x over previous
import numpy as np
from contextlib import ExitStack

import concourse.bass as bass
import concourse.mybir as mybir
from concourse.bass_utils import run_bass_kernel_spmd

F32 = mybir.dt.float32
BF16 = mybir.dt.bfloat16
AF = mybir.ActivationFunctionType
ALU = mybir.AluOpType
AX = mybir.AxisListType

D = 1024
T = 4096
TILE = 512
NT = T // TILE
SL = 128
DEPTH = 6
FF = 4096
FFT = 256
E = 8
HIDD = 256
LEAK = 0.01
EPS_LN = 1e-5
EPS_RMS = 1e-6

V, A, TT, G, SY = "vector", "scalar", "tensor", "gpsimd", "sync"

RG = [[0, 1], [2, 3], [4, 5], [6, 7]]


class KB:
    ENGS = ("sync", "gpsimd", "vector", "scalar", "tensor")

    def __init__(self):
        self.nc = bass.Bass("TRN2", target_bir_lowering=False)
        self.prog = {e: [] for e in self.ENGS}
        self.es = ExitStack()
        self.ph = None
        self.semh = {}
        self.ctr = {}
        for s in ("s_x", "s_w", "s_c", "s_ac", "s_cc", "s_pe", "s_v", "s_a", "s_g", "s_f",
                  "s_w1a", "s_w1b", "s_w2a", "s_w2b", "s_xia", "s_xib", "s_st"):
            self.semh[s] = self.es.enter_context(self.nc.semaphore(s))
            self.ctr[s] = 0
        self.stopped = False

    _uid = 0

    def sb(self, name, shape, dt, local=False):
        KB._uid += 1
        st = self.ph if local else self.es
        return st.enter_context(self.nc.sbuf_tensor(f"{name}_{KB._uid}", shape, dt))

    def psum(self, name, shape, dt):
        return self.es.enter_context(self.nc.psum_tensor(name, shape, dt))

    def dram(self, name, shape, dt, kind="Internal"):
        return self.nc.dram_tensor(name, shape, dt, kind=kind)

    def phase(self):
        if self.ph is not None and not self.stopped:
            for e in ("vector", "scalar"):
                self.di(e, "s_f", 1)
            for e in ("gpsimd", "tensor", "sync"):
                self.op(e, "nop", inc=("s_f", 1))
            tgt = self.ctr["s_f"]
            for e in ("vector", "scalar", "gpsimd", "tensor", "sync"):
                self.w(e, "s_f", tgt)
        if self.ph is not None:
            self.ph.close()
        self.ph = ExitStack()

    def w(self, eng, sem, val=None):
        if self.stopped:
            return
        if val is None:
            val = self.ctr[sem]
        if val <= 0:
            return
        h = self.semh[sem]
        self.prog[eng].append(lambda e, hh=h, vv=val: e.wait_ge(hh, vv))

    def op(self, eng, meth, *a, inc=None, **kw):
        if self.stopped:
            return 0
        if inc:
            nm, k = inc
            self.ctr[nm] += k
            h = self.semh[nm]
            self.prog[eng].append(
                lambda e, m=meth, aa=a, kk=kw, hh=h, n=k: getattr(e, m)(*aa, **kk).then_inc(hh, n))
            return self.ctr[nm]
        self.prog[eng].append(lambda e, m=meth, aa=a, kk=kw: getattr(e, m)(*aa, **kk))
        return 0

    def dma(self, eng, out, in_, sem="s_x", **kw):
        return self.op(eng, "dma_start", inc=(sem, 16), out=out, in_=in_, **kw)

    def mm(self, out, lhsT, rhs, start, stop, inc=None):
        return self.op("tensor", "matmul", out, lhsT, rhs, start=start, stop=stop, inc=inc)

    def di(self, eng, sem, k=1):
        return self.op(eng, "drain", inc=(sem, k))

    def build(self):
        nc = self.nc
        with nc.Block() as block:
            @block.sync
            def _(eng):
                for f in self.prog["sync"]:
                    f(eng)

            @block.gpsimd
            def _(eng):
                for f in self.prog["gpsimd"]:
                    f(eng)

            @block.vector
            def _(eng):
                for f in self.prog["vector"]:
                    f(eng)

            @block.scalar
            def _(eng):
                for f in self.prog["scalar"]:
                    f(eng)

            @block.tensor
            def _(eng):
                for f in self.prog["tensor"]:
                    f(eng)
        return nc


class Mega:
    def __init__(self, dbg=None):
        self.K = KB()
        self.dbg = dbg
        self._declare()
        self._prep()

    def _declare(self):
        K = self.K
        self.xin = K.dram("x", [T, D], F32, kind="ExternalInput").ap()
        self.hmask = K.dram("hmask", [128, 1], F32, kind="ExternalInput").ap()
        self.win = {}
        for nm, shp in [
            ("ln1_w", [DEPTH, D]), ("ln1_b", [DEPTH, D]), ("wq", [DEPTH, D, D]),
            ("wk", [DEPTH, D, D]), ("wv", [DEPTH, D, D]), ("wo", [DEPTH, D, D]),
            ("bo", [DEPTH, D]), ("ln2_w", [DEPTH, D]), ("ln2_b", [DEPTH, D]),
            ("w1", [DEPTH, D, FF]), ("w2", [DEPTH, FF, D]),
            ("dw_kernel", [D, 1, 3]), ("dw_bias", [D]), ("pw_w", [D, D]), ("pw_b", [D]),
            ("router_w", [E, D]), ("router_b", [E]), ("expert_w", [E, D, D]),
            ("expert_b", [E, D]), ("out_w", [D, D]), ("out_b", [D]), ("norm_w", [D]),
            ("ff_down_w", [HIDD, D]), ("ff_down_b", [HIDD]), ("ff_up_w", [D, HIDD]),
            ("ff_up_b", [D]),
        ]:
            self.win[nm] = K.dram("w_" + nm, shp, F32, kind="ExternalInput").ap()
        self.out = K.dram("out", [T, D], F32, kind="ExternalOutput").ap()
        if self.dbg:
            self.dump = K.dram("dump", [8, 128, T], F32, kind="ExternalOutput").ap()
            self.dump2 = K.dram("dump2", [1032, 128], F32, kind="ExternalOutput").ap()

        self.x1sp = K.dram("x1sp", [8, 128, T], F32)
        self.rowsd = K.dram("rowsd", [NT, 1, 2 * TILE], F32)
        self.combd = K.dram("combd", [T, D], F32)
        self.xibd = K.dram("xibd", [8, 128, T], BF16)
        self.w1_s = K.dram("w1_s", [DEPTH, FF // 128, 128, 8 * 128], BF16)
        self.qkvo_s = {nm: K.dram(nm + "_s", [DEPTH, 128, 8 * D], BF16)
                       for nm in ("wq", "wk", "wv", "wo")}
        self.w2_s = K.dram("w2_s", [DEPTH, 8, 128, 32 * 128], BF16)
        self.pw_s = K.dram("pw_s", [D, D], BF16)
        self.ow_s = K.dram("ow_s", [D, D], BF16)
        self.ew_s = K.dram("ew_s", [E, D, D], BF16)
        self.fd_s = K.dram("fd_s", [HIDD, D], BF16)
        self.fu_s = K.dram("fu_s", [D, HIDD], BF16)
        self.eb_s = K.dram("eb_s", [E, D], BF16)
        self.cc_in = K.dram("cc_in", [1032, 128], F32)
        self.cc_out = K.dram("cc_out", [1032, 128], F32)
        self.hl_in = K.dram("hl_in", [D, 2], F32)
        self.hl_out = K.dram("hl_out", [2 * D, 2], F32)
        self.xb2d = K.dram("xb2d", [128, 8, T], BF16)
        self.mrid1 = K.dram("mrid1", [NT, 2 * TILE], BF16)
        self.mrid2 = K.dram("mrid2", [NT, 2 * TILE], BF16)
        self.rowsd2 = K.dram("rowsd2", [NT, TILE], F32)

        self.X = [K.sb(f"X{c}", [128, T], F32) for c in range(8)]
        self.WS = [K.sb("WS0", [128, 8 * D], BF16), K.sb("WS1", [128, 8 * D], BF16)]
        self.ABR = K.sb("ABR", [128, 2 * TILE], F32)
        self.SCR = K.sb("SCR", [128, TILE], F32)
        self.SCR2 = K.sb("SCR2", [128, TILE], F32)
        self.ROW = K.sb("ROW", [1, 2 * TILE], F32)
        self.ONC = K.sb("ONC", [128, 1], F32)
        self.QMASK = K.sb("QMASK", [128, 8 * 8], BF16)
        self.QMD = K.sb("QMD", [128, 4], BF16)
        self.IDF = K.sb("IDF", [128, 128], F32)
        self.IDB = K.sb("IDB", [128, 128], BF16)
        self.ONEF = K.sb("ONEF", [1, 128], F32)
        self.ONEB = K.sb("ONEB", [1, 128], BF16)
        self.HALO = K.sb("HALO", [128, 16], F32)
        self.CARRY = K.sb("CARRY", [128, 16], F32)
        self.MCOL = K.sb("MCOL", [128, 1], F32)
        self.RBROW = K.sb("RBROW", [1, E], F32)
        self.RB128 = K.sb("RB128", [128, E], F32)
        self.RW = K.sb("RW", [128, (T // SL) * E], F32)
        self.CTX = K.sb("CTX", [128, 8 * 128], BF16)
        self.PK = {}
        for nm, ncol in [("ln1_w", 48), ("ln1_b", 48), ("ln2_w", 48), ("ln2_b", 48),
                         ("bo", 48), ("vec5", 40), ("dwk", 24), ("fdb", 2)]:
            self.PK[nm] = K.sb("PK_" + nm, [128, ncol], F32)
        self.CONSTP = K.sb("CONSTP", [128, 12], F32)
        self.PS = [K.psum(f"PS{i}", [128, 512], F32) for i in range(7)]
        self.PSB = K.psum("PSB", [128, 1024], BF16)
        self.wld = {nm: [0] * DEPTH for nm in ("wq", "wk", "wv", "wo", "w2")}
        self.wk_sem = ["s_w"] * DEPTH
        self.pair_ld = [0, 0]
        self.mrx_dma = 0
        self.rows2_wr = [0] * NT

    def load_w(self, slot_ap, src_dd, sem="s_w"):
        return self.K.dma(G, slot_ap.rearrange("p (k m) -> p k m", k=8),
                          src_dd.rearrange("(k p) m -> p k m", p=128), sem)

    def cast_qkvo(self, l):
        K = self.K
        for nm in ("wq", "wk", "wv", "wo"):
            self.qkvo_cast[l] = K.dma(
                G, self.qkvo_s[nm].ap()[l].rearrange("p (k m) -> p k m", k=8),
                self.win[nm][l].rearrange("(k p) m -> p k m", p=128), "s_c")

    def load_wb(self, slot_ap, nm, l, sem="s_w"):
        K = self.K
        K.w(G, "s_c", self.qkvo_cast[l])
        return K.dma(G, slot_ap, self.qkvo_s[nm].ap()[l], sem)

    def cast_w12(self, l):
        K = self.K
        for mi in range(32):
            K.dma(G, self.w1_s.ap()[l, mi].rearrange("p (k m) -> p k m", k=8),
                  self.win["w1"][l][:, mi * 128:(mi + 1) * 128]
                  .rearrange("(k p) m -> p k m", p=128), "s_c")
        for j in range(8):
            self.w12_cast[l] = K.dma(
                G, self.w2_s.ap()[l, j].rearrange("p (mi jj) -> p mi jj", mi=32),
                self.win["w2"][l][:, j * 128:(j + 1) * 128]
                .rearrange("(mi p) jj -> p mi jj", p=128), "s_c")

    def _prep(self):
        K = self.K
        win = self.win
        dg = lambda: K.op(G, "drain")
        K.op(G, "memset", self.IDF[:], 0.0)
        dg()
        K.op(G, "affine_select", out=self.IDF[:], in_=self.IDF[:], compare_op=ALU.not_equal,
             fill=1.0, base=0, pattern=[[-1, 128]], channel_multiplier=1)
        K.op(G, "memset", self.IDB[:], 0.0)
        dg()
        K.op(G, "affine_select", out=self.IDB[:], in_=self.IDB[:], compare_op=ALU.not_equal,
             fill=1.0, base=0, pattern=[[-1, 128]], channel_multiplier=1)
        K.op(G, "memset", self.ONEF[:], 1.0)
        K.op(G, "memset", self.ONEB[:], 1.0)
        K.op(G, "memset", self.QMASK[:], 0.0)
        dg()
        for h in range(8):
            K.op(G, "memset", self.QMASK[:, h * 8 + h:h * 8 + h + 1], 1.0)
        K.op(G, "memset", self.ONC[:], 1.0)
        K.op(G, "memset", self.QMD[:], 0.0)
        dg()
        K.op(G, "memset", self.QMD[:, 0:1], 1.0)
        K.op(G, "memset", self.QMD[:, 3:4], 1.0)
        K.op(G, "memset", self.CARRY[:], 0.0)
        for i, cv in enumerate([EPS_LN, EPS_RMS, 1.0, -0.5, -1.0, LEAK, 1.0 / D, 2.0, 0.5]):
            K.op(G, "memset", self.CONSTP[:, i:i + 1], cv)
            K.nc.const_aps.register(F32, cv, self.CONSTP[:, i:i + 1])
        K.op(G, "drain", inc=("s_g", 1))
        self.g_masks = K.ctr["s_g"]

        K.dma(G, self.pw_s.ap(), win["pw_w"], "s_c")
        K.dma(G, self.ow_s.ap(), win["out_w"], "s_c")
        for e in range(E):
            K.dma(G, self.ew_s.ap()[e], win["expert_w"][e], "s_c")
        K.dma(G, self.fd_s.ap(), win["ff_down_w"], "s_c")
        K.dma(G, self.fu_s.ap(), win["ff_up_w"], "s_c")
        K.dma(G, self.eb_s.ap(), win["expert_b"], "s_c")
        self.g_scr = K.ctr["s_c"]
        K.w(G, "s_c")
        self.load_w(self.WS[0][:], win["wk"][0])
        self.wld["wk"][0] = self.wld["wv"][0] = \
            self.load_w(self.WS[1][:], win["wv"][0])
        K.w(G, "s_w")
        self.w12_cast = [0] * DEPTH
        self.qkvo_cast = [0] * DEPTH
        self.cast_w12(0)
        K.w(G, "s_c")
        self.cast_qkvo(0)
        K.w(G, "s_c")

        K.phase()
        PKS = K.sb("PKS", [64, 128], F32, local=True)
        jobs = [
            ("ln1_w", win["ln1_w"].rearrange("l (c p) -> (l c) p", p=128), 48),
            ("ln1_b", win["ln1_b"].rearrange("l (c p) -> (l c) p", p=128), 48),
            ("ln2_w", win["ln2_w"].rearrange("l (c p) -> (l c) p", p=128), 48),
            ("ln2_b", win["ln2_b"].rearrange("l (c p) -> (l c) p", p=128), 48),
            ("bo", win["bo"].rearrange("l (c p) -> (l c) p", p=128), 48),
            ("fdb", win["ff_down_b"].rearrange("(c p) -> c p", p=128), 2),
        ]
        K.w(TT, "s_g", self.g_masks)
        for nm, src, ncol in jobs:
            K.w(SY, "s_v")
            ld = K.dma(SY, PKS[0:ncol, :], src)
            K.w(TT, "s_x", ld)
            K.w(TT, "s_v")
            K.op(TT, "transpose", self.PS[0][:, 0:ncol], PKS[0:ncol, :],
                 self.IDF[0:ncol, 0:ncol], inc=("s_pe", 1))
            K.w(V, "s_pe")
            K.op(V, "tensor_copy", self.PK[nm][:], self.PS[0][:, 0:ncol])
            K.di(V, "s_v")
        for j, nm in enumerate(["dw_bias", "pw_b", "out_b", "norm_w", "ff_up_b"]):
            K.w(SY, "s_v")
            ld = K.dma(SY, PKS[0:8, :], win[nm].rearrange("(c p) -> c p", p=128))
            K.w(TT, "s_x", ld)
            K.w(TT, "s_v")
            K.op(TT, "transpose", self.PS[0][:, 0:8], PKS[0:8, :], self.IDF[0:8, 0:8],
                 inc=("s_pe", 1))
            K.w(V, "s_pe")
            K.op(V, "tensor_copy", self.PK["vec5"][:, j * 8:(j + 1) * 8], self.PS[0][:, 0:8])
            K.di(V, "s_v")
        K.w(SY, "s_v")
        K.dma(SY, self.PK["dwk"][:].rearrange("p (c k) -> p c k", c=8),
              win["dw_kernel"].rearrange("(c p) o k -> p c (o k)", p=128))
        K.dma(SY, self.RBROW[:], win["router_b"].rearrange("(a e) -> a e", a=1))
        K.dma(SY, self.MCOL[:], self.hmask)
        K.w(TT, "s_x", K.ctr["s_x"])
        K.w(TT, "s_v")
        K.op(TT, "matmul", self.PS[0][:, 0:E], self.ONEF[:], self.RBROW[:],
             start=True, stop=True, inc=("s_pe", 1))
        K.w(V, "s_pe")
        K.op(V, "tensor_copy", self.RB128[:], self.PS[0][:, 0:E])
        K.di(V, "s_v")
        self.prep_v = K.ctr["s_v"]

    def ln_stats_rows(self, tcols, bank, rowbuf, eps, with_mean=True, rms=False,
                      srcs=None, scr=None, raw=False):
        K = self.K
        t0, tn = tcols
        srcs = srcs or [self.X[c][:, t0:t0 + tn] for c in range(8)]
        scr = scr if scr is not None else self.SCR
        PS = self.PS
        for c in range(8):
            K.w(V, "s_pe")
            K.op(V, "tensor_mul", scr[:, 0:tn], srcs[c], srcs[c])
            K.di(V, "s_v")
            K.w(TT, "s_v")
            if not rms:
                K.mm(PS[bank][0:1, 0:tn], self.ONC[:], srcs[c], start=(c == 0), stop=(c == 7))
            K.mm(PS[bank + 1][0:1, 0:tn], self.ONC[:], scr[:, 0:tn],
                 start=(c == 0), stop=(c == 7), inc=("s_pe", 1))
        K.w(A, "s_pe")
        K.w(A, "s_x")
        if not rms:
            K.op(A, "activation", rowbuf[0:1, 0:tn], PS[bank][0:1, 0:tn], AF.Identity,
                 scale=1.0 / D)
        K.op(A, "activation", rowbuf[0:1, TILE:TILE + tn], PS[bank + 1][0:1, 0:tn],
             AF.Identity, scale=1.0 / D)
        K.di(A, "s_a")
        if raw:
            return
        self.finish_rows(rowbuf, tn, eps, with_mean, scr)

    def finish_rows(self, rowbuf, tn, eps, with_mean, scr=None):
        K = self.K
        scr = scr if scr is not None else self.SCR
        K.w(V, "s_a")
        if with_mean:
            K.op(V, "tensor_mul", scr[0:1, 0:tn], rowbuf[0:1, 0:tn], rowbuf[0:1, 0:tn])
            K.op(V, "drain")
            K.op(V, "tensor_sub", rowbuf[0:1, TILE:TILE + tn],
                 rowbuf[0:1, TILE:TILE + tn], scr[0:1, 0:tn])
            K.di(V, "s_v")
            K.w(A, "s_v")
        K.op(A, "activation", scr[0:1, 0:tn], rowbuf[0:1, TILE:TILE + tn], AF.Ln, bias=eps)
        K.op(A, "drain")
        K.op(A, "activation", scr[0:1, 0:tn], scr[0:1, 0:tn], AF.Exp, scale=-0.5)
        K.di(A, "s_a")
        K.w(V, "s_a")
        if with_mean:
            K.op(V, "scalar_tensor_tensor", rowbuf[0:1, TILE:TILE + tn],
                 rowbuf[0:1, 0:tn], -1.0, scr[0:1, 0:tn], ALU.mult, ALU.mult)
            K.op(V, "drain")
        else:
            K.op(V, "memset", rowbuf[0:1, TILE:TILE + tn], 0.0)
        K.op(V, "tensor_copy", rowbuf[0:1, 0:tn], scr[0:1, 0:tn])
        K.di(V, "s_v")

    def ln_bcast(self, rowsrc, tn, banks, abr, beta=True):
        K = self.K
        K.w(TT, "s_v")
        K.w(TT, "s_a")
        K.mm(self.PS[banks[0]][:, 0:tn], self.ONEF[:], rowsrc[0:1, 0:tn],
             start=True, stop=True, inc=None if beta else ("s_pe", 1))
        if beta:
            K.mm(self.PS[banks[1]][:, 0:tn], self.ONEF[:], rowsrc[0:1, TILE:TILE + tn],
                 start=True, stop=True, inc=("s_pe", 1))
        K.w(A, "s_pe")
        K.op(A, "activation", abr[:, 0:tn], self.PS[banks[0]][:, 0:tn], AF.Identity)
        if beta:
            K.op(A, "activation", abr[:, TILE:TILE + tn], self.PS[banks[1]][:, 0:tn],
                 AF.Identity)
        K.di(A, "s_a")
        return K.ctr["s_a"]

    def ln_apply_chunk(self, xsrc, tn, abr, wcol, bcol, out_bf, scrf):
        K = self.K
        K.w(V, "s_a")
        K.op(V, "tensor_mul", scrf[:, 0:tn], xsrc, abr[:, 0:tn])
        K.op(V, "drain")
        K.op(V, "tensor_add", scrf[:, 0:tn], scrf[:, 0:tn], abr[:, TILE:TILE + tn])
        K.di(V, "s_v")
        K.w(A, "s_v")
        if wcol is not None:
            K.op(A, "activation", out_bf, scrf[:, 0:tn], AF.Identity, scale=wcol, bias=bcol)
        else:
            K.op(A, "activation", out_bf, scrf[:, 0:tn], AF.Identity)
        K.di(A, "s_a")
        return K.ctr["s_a"]

    def initial(self, nsl=T // SL, do_ln=True):
        K = self.K
        K.phase()
        TMJ = [K.sb(f"iTMJ{i}", [128, D], F32, local=True) for i in range(2)]
        TSQ = K.sb("iTSQ", [128, D], F32, local=True)
        AL = K.sb("iAL", [128, 4], F32, local=True)
        X, PS, IDF = self.X, self.PS, self.IDF
        K.w(TT, "s_g", self.g_masks)
        K.w(A, "s_g", self.g_masks)
        K.w(V, "s_g", self.g_masks)
        K.w(V, "s_v", self.prep_v)
        for s in range(nsl):
            par = s % 2
            K.w(SY, "s_pe")
            ld = K.dma(SY, TMJ[par][:], self.xin[s * SL:(s + 1) * SL, :])
            K.w(V, "s_x", ld)
            K.w(V, "s_pe")
            if not do_ln:
                K.op(V, "tensor_copy", TSQ[:], TMJ[par][:])
                K.di(V, "s_v")
                self._init_transpose(s, TSQ, par)
                continue
            K.op(V, "tensor_mul", TSQ[:], TMJ[par][:], TMJ[par][:])
            K.op(V, "reduce_sum", AL[:, 0:1], TMJ[par][:], axis=AX.X)
            K.op(V, "drain")
            K.op(V, "reduce_sum", AL[:, 1:2], TSQ[:], axis=AX.X)
            K.op(V, "drain")
            K.op(V, "tensor_scalar", AL[:, 0:1], AL[:, 0:1], 1.0 / D, None, ALU.mult)
            K.op(V, "tensor_scalar", AL[:, 1:2], AL[:, 1:2], 1.0 / D, None, ALU.mult)
            K.op(V, "drain")
            K.op(V, "tensor_mul", TSQ[:, 0:1], AL[:, 0:1], AL[:, 0:1])
            K.op(V, "drain")
            K.op(V, "tensor_sub", AL[:, 1:2], AL[:, 1:2], TSQ[:, 0:1])
            K.di(V, "s_v")
            K.w(A, "s_v")
            K.op(A, "activation", AL[:, 2:3], AL[:, 1:2], AF.Ln, bias=EPS_LN)
            K.op(A, "drain")
            K.op(A, "activation", AL[:, 2:3], AL[:, 2:3], AF.Exp, scale=-0.5)
            K.di(A, "s_a")
            K.w(V, "s_a")
            K.op(V, "scalar_tensor_tensor", AL[:, 3:4], AL[:, 0:1], -1.0, AL[:, 2:3],
                 ALU.mult, ALU.mult)
            K.op(V, "drain")
            K.op(V, "tensor_scalar", TSQ[:], TMJ[par][:], AL[:, 2:3], AL[:, 3:4],
                 ALU.mult, ALU.add)
            K.di(V, "s_v")
            self._init_transpose(s, TSQ, par)

    def _init_transpose(self, s, TSQ, par):
        K = self.K
        X, PS, IDF = self.X, self.PS, self.IDF
        K.w(TT, "s_v")
        for c in range(8):
            K.op(TT, "transpose", PS[c // 4][:, (c % 4) * 128:(c % 4 + 1) * 128],
                 TSQ[:, c * 128:(c + 1) * 128], IDF[:],
                 inc=("s_pe", 1) if c == 7 else None)
        K.w(V, "s_pe")
        K.w(A, "s_pe")
        for c in range(8):
            eng, meth = (V, "tensor_copy") if c < 4 else (A, "copy")
            K.op(eng, meth, X[c][:, s * SL:(s + 1) * SL],
                 PS[c // 4][:, (c % 4) * 128:(c % 4 + 1) * 128])
        K.di(V, "s_v")
        K.di(A, "s_a")
        K.w(TT, "s_v")
        K.w(TT, "s_a")

    def pass1(self, l):
        K = self.K
        K.phase()
        XBS = K.sb("p1XBS", [128, 2 * 8 * 128], BF16, local=True)
        KK = [K.sb(f"p1K{i}", [128, 512], BF16, local=True) for i in range(2)]
        VV = [K.sb(f"p1V{i}", [128, 512], BF16, local=True) for i in range(2)]
        USB = K.sb("p1U", [128, 1024], F32, local=True)
        DRW = K.sb("p1DRW", [2, 512], F32, local=True)
        SKK = K.sb("p1SKK", [2, 1024], BF16, local=True)
        SKV = K.sb("p1SKV", [2, 1024], BF16, local=True)
        MRIT = K.sb("p1MRI", [2, TILE], BF16, local=True)
        MRX = K.sb("p1MRX", [1, TILE], BF16, local=True)
        RSTDC = K.sb("p1RC", [128, 8], F32, local=True)
        GB1 = K.sb("p1GB", [128, 16], BF16, local=True)
        FINROW = self.ABR
        X, PS = self.X, self.PS
        WK = self.WS[l % 2]
        WV = self.WS[1 - l % 2]

        K.w(TT, "s_w", self.wld["wv"][l])
        K.op(V, "memset", USB[:], 0.0, inc=("s_v", 1))
        for kc in range(8):
            K.op(A, "activation", GB1[:, kc * 2:kc * 2 + 1],
                 self.PK["ln1_w"][:, l * 8 + kc:l * 8 + kc + 1], AF.Identity)
            K.op(A, "activation", GB1[:, kc * 2 + 1:kc * 2 + 2],
                 self.PK["ln1_b"][:, l * 8 + kc:l * 8 + kc + 1], AF.Identity)
        K.di(A, "s_a")
        gb1_done = K.ctr["s_a"]

        def skck(W, DST):
            K.w(TT, "s_a", gb1_done)
            for half in range(2):
                K.w(TT, "s_a")
                for kc in range(8):
                    K.mm(PS[0][0:2, 0:512], GB1[:, kc * 2:kc * 2 + 2],
                         W[:, kc * 1024 + half * 512: kc * 1024 + (half + 1) * 512],
                         start=(kc == 0), stop=(kc == 7),
                         inc=("s_pe", 1) if kc == 7 else None)
                pe = K.ctr["s_pe"]
                K.w(A, "s_pe", pe)
                K.op(A, "activation", DST[0:2, half * 512:(half + 1) * 512],
                     PS[0][0:2, 0:512], AF.Identity, inc=("s_a", 1))
            return K.ctr["s_a"]

        skv_done = skck(WV, SKV)
        skk_done = [0]

        self.rows_x = [0] * NT
        self.mrid1_wr = [0] * NT
        fin_done = [0] * NT
        rc_tr = [0] * NT
        mrit_ld = 0
        fix_last = 0
        kfree = [0, 0]
        vfree = [0, 0]
        kkfree = [0, 0]
        xbs_use = [0, 0]

        def stats1(t):
            K.w(TT, "s_a")
            self.ln_stats_rows((t * TILE, TILE), 0, self.ROW, EPS_LN, scr=self.SCR2,
                               raw=True)
            fin_done[t] = self.finish2(t, t % 2, FINROW, MRX,
                                       rc_tr[t - 2] if t >= 2 else 0,
                                       self.mrid1.ap()[t:t + 1, :],
                                       self.rowsd.ap()[t][0:1, 0:TILE])
            self.mrid1_wr[t] = self.rows_x[t] = self.rows2_wr[t]

        stats1(0)
        stats1(1)

        pend = None

        def emit_u(kb, half, ev_done, isfirst, islast):
            K.w(TT, "s_a", ev_done)
            for hh in range(4):
                hcol = hh * 128
                K.mm(PS[kb][:, hcol:hcol + 128], KK[half][:, hcol:hcol + 128],
                     VV[half][:, hcol:hcol + 128], start=True, stop=True)
            K.mm(PS[6][0:2, 0:512], self.QMD[:, half * 2:half * 2 + 2],
                 KK[half][:], start=isfirst, stop=islast, inc=("s_pe", 1))
            kkfree[half] = K.ctr["s_pe"]
            K.w(V, "s_pe", kkfree[half])
            K.op(V, "tensor_add", USB[:, half * 512:(half + 1) * 512],
                 USB[:, half * 512:(half + 1) * 512], PS[kb][:], inc=("s_v", 1))
            kfree[half] = K.ctr["s_v"]

        for t in range(NT):
            t0 = t * TILE
            par = t % 2
            K.w(TT, "s_a", fin_done[t])
            K.w(TT, "s_a")
            for sl in range(4):
                K.op(TT, "transpose", PS[0][:, sl:sl + 1],
                     FINROW[0:1, par * TILE + sl * 128: par * TILE + sl * 128 + 128],
                     self.IDF[0:1, 0:1], inc=("s_pe", 1) if sl == 3 else None)
            rc_tr[t] = K.ctr["s_pe"]
            K.w(A, "s_pe", rc_tr[t])
            K.op(A, "activation", RSTDC[:, par * 4:par * 4 + 4], PS[0][:, 0:4],
                 AF.Identity, inc=("s_a", 1))
            K.w(SY, "s_pe", fix_last)
            K.w(SY, "s_st", self.mrid1_wr[t])
            mrit_ld = K.dma(SY, MRIT[0:2, :],
                            self.mrid1.ap()[t].rearrange("(a b) -> a b", a=2), "s_w2a")
            for s in range(4):
                sp = s % 2
                K.w(A, "s_pe", xbs_use[sp])
                for c in range(8):
                    K.op(A, "activation",
                         XBS[:, sp * 1024 + c * 128: sp * 1024 + (c + 1) * 128],
                         X[c][:, t0 + s * 128: t0 + (s + 1) * 128], AF.Identity,
                         scale=self.PK["ln1_w"][:, l * 8 + c:l * 8 + c + 1],
                         inc=("s_a", 1) if c == 7 else None)
                cast_a = K.ctr["s_a"]
                for half in range(2):
                    kb, vb = 2 + half, 4 + half
                    K.w(TT, "s_a", cast_a)
                    K.w(TT, "s_a", vfree[half])
                    K.w(TT, "s_w2a", mrit_ld)
                    K.w(TT, "s_a", skv_done)
                    for kc in range(8):
                        K.mm(PS[vb][:],
                             XBS[:, sp * 1024 + kc * 128: sp * 1024 + (kc + 1) * 128],
                             WV[:, kc * 1024 + half * 512: kc * 1024 + (half + 1) * 512],
                             start=(kc == 0), stop=False)
                    K.mm(PS[vb][:], MRIT[0:2, s * 128:(s + 1) * 128],
                         SKV[0:2, half * 512:(half + 1) * 512],
                         start=False, stop=True, inc=("s_pe", 1))
                    v_done = K.ctr["s_pe"]
                    if t == 0 and s == 0 and half == 0:
                        K.w(TT, self.wk_sem[l], self.wld["wk"][l])
                        skk_done[0] = skck(WK, SKK)
                    K.w(TT, "s_v", kfree[half])
                    K.w(TT, "s_a", skk_done[0])
                    for kc in range(8):
                        K.mm(PS[kb][:],
                             XBS[:, sp * 1024 + kc * 128: sp * 1024 + (kc + 1) * 128],
                             WK[:, kc * 1024 + half * 512: kc * 1024 + (half + 1) * 512],
                             start=(kc == 0), stop=False)
                    K.mm(PS[kb][:], MRIT[0:2, s * 128:(s + 1) * 128],
                         SKK[0:2, half * 512:(half + 1) * 512],
                         start=False, stop=True, inc=("s_pe", 1))
                    k_done = K.ctr["s_pe"]
                    K.w(A, "s_pe", v_done)
                    K.w(A, "s_pe", kkfree[half])
                    K.op(A, "activation", VV[half][:], PS[vb][:], AF.Identity,
                         scale=RSTDC[:, par * 4 + s:par * 4 + s + 1], inc=("s_a", 1))
                    vfree[half] = K.ctr["s_a"]
                    K.w(A, "s_pe", k_done)
                    K.op(A, "activation", KK[half][:], PS[kb][:], AF.Exp,
                         scale=RSTDC[:, par * 4 + s:par * 4 + s + 1], inc=("s_a", 1))
                    ev_done = K.ctr["s_a"]
                    if pend is not None:
                        emit_u(*pend)
                    pend = (kb, half, ev_done, t == 0 and s == 0 and half == 0,
                            t == NT - 1 and s == 3 and half == 1)
                xbs_use[sp] = K.ctr["s_pe"]
            fix_last = K.ctr["s_pe"]
            if t + 2 < NT:
                stats1(t + 2)
        emit_u(*pend)
        K.w(A, "s_pe")
        K.op(A, "activation", DRW[0:2, :], PS[6][0:2, 0:512], AF.Identity)
        K.di(A, "s_a")
        K.w(SY, "s_v")
        K.w(SY, "s_a")
        K.dma(SY, self.cc_in.ap()[0:1024, :].rearrange("(h d) e -> d h e", d=128),
              USB[:].rearrange("p (h e) -> p h e", h=8))
        n = K.dma(SY, self.cc_in.ap()[1024:1032, :].rearrange("(a h) d -> a h d", a=2),
                  DRW[0:2, :].rearrange("a (h d) -> a h d", h=4))
        K.w(SY, "s_x")
        K.w(G, "s_x", n)
        K.op(G, "collective_compute", "AllReduce", ALU.add,
             replica_groups=RG,
             ins=[self.cc_in.ap()], outs=[self.cc_out.ap()], inc=("s_cc", 1))
        K.w(G, "s_pe")
        self.load_wb(WK[:], "wq", l)
        self.wld["wq"][l] = self.wld["wo"][l] = self.load_wb(WV[:], "wo", l)
        K.w(G, "s_w")
        if l + 1 < DEPTH:
            self.cast_w12(l + 1)
            K.w(G, "s_c")
            self.cast_qkvo(l + 1)
            K.w(G, "s_c")

    def pass2a_prep(self, l):
        K = self.K
        PS = self.PS
        USB2 = K.sb("p2U", [128, 1024], F32, local=True)
        DTMP = K.sb("p2DT", [8, 128], F32, local=True)
        DCOL = K.sb("p2DC", [128, 8], F32, local=True)
        K.w(G, "s_cc")
        K.dma(G, DTMP[:], self.cc_out.ap()[1024:1032, :], "s_w")
        n = K.dma(G, USB2[:].rearrange("p (h e) -> p h e", h=8),
                  self.cc_out.ap()[0:1024, :].rearrange("(h d) e -> d h e", d=128), "s_w")
        K.w(G, "s_w")
        K.w(TT, "s_w", n)
        K.w(TT, "s_v")
        K.op(TT, "transpose", PS[0][:, 0:8], DTMP[:], self.IDF[0:8, 0:8],
             inc=("s_pe", 1))
        K.w(A, "s_pe")
        K.op(A, "activation", DCOL[:], PS[0][:, 0:8], AF.Ln)
        K.op(A, "drain")
        K.op(A, "activation", DCOL[:], DCOL[:], AF.Exp, scale=-1.0)
        K.di(A, "s_a")
        self._prep_a = K.ctr["s_a"]
        K.w(V, "s_a")
        K.w(V, "s_pe")
        for h in range(8):
            K.op(V, "tensor_scalar", self.CTX[:, h * 128:(h + 1) * 128],
                 USB2[:, h * 128:(h + 1) * 128], DCOL[:, h:h + 1], None, ALU.mult)
        K.di(V, "s_v")
        self.ctx_v = K.ctr["s_v"]

    def pass2a(self, l):
        K = self.K
        K.phase()
        self.pass2a_prep(l)
        X, PS = self.X, self.PS
        XB = K.sb("p2XB", [128, 8 * TILE], BF16, local=True)
        QE = K.sb("p2QE", [128, 8 * TILE], BF16, local=True)
        ATT = QE
        SEL8B = K.sb("p2SEL", [8, 8 * 128], BF16, local=True)
        QROWB = K.sb("p2QR", [8, TILE], BF16, local=True)
        MRIT = K.sb("p2MRI", [2, TILE], BF16, local=True)
        SKQ = K.sb("p2SKQ", [2, 1024], BF16, local=True)
        GB1 = K.sb("p2GB", [128, 16], BF16, local=True)
        RSTDB = self.ABR
        WQ = self.WS[l % 2]
        WO = self.WS[1 - l % 2]
        SCRX = [self.SCR, self.SCR2]

        K.op(G, "memset", SEL8B[:], 1.0)
        K.op(G, "drain")
        K.op(G, "affine_select", out=SEL8B[:], in_=SEL8B[:], compare_op=ALU.is_gt,
             fill=0.0, base=1, pattern=[[1, 1024]], channel_multiplier=-128)
        K.op(G, "drain")
        K.op(G, "affine_select", out=SEL8B[:], in_=SEL8B[:], compare_op=ALU.is_gt,
             fill=0.0, base=128, pattern=[[-1, 1024]], channel_multiplier=128)
        K.op(G, "drain", inc=("s_g", 1))
        sel8_g = K.ctr["s_g"]

        for kc in range(8):
            K.op(A, "activation", GB1[:, kc * 2:kc * 2 + 1],
                 self.PK["ln1_w"][:, l * 8 + kc:l * 8 + kc + 1], AF.Identity)
            K.op(A, "activation", GB1[:, kc * 2 + 1:kc * 2 + 2],
                 self.PK["ln1_b"][:, l * 8 + kc:l * 8 + kc + 1], AF.Identity)
        K.di(A, "s_a")
        gb1_done = K.ctr["s_a"]
        K.w(TT, "s_w", self.wld["wq"][l])
        K.w(TT, "s_a", gb1_done)
        skq_done = 0
        for half in range(2):
            K.w(TT, "s_a")
            for kc in range(8):
                K.mm(PS[4][0:2, 0:512], GB1[:, kc * 2:kc * 2 + 2],
                     WQ[:, kc * 1024 + half * 512: kc * 1024 + (half + 1) * 512],
                     start=(kc == 0), stop=(kc == 7),
                     inc=("s_pe", 1) if kc == 7 else None)
            pe = K.ctr["s_pe"]
            K.w(A, "s_pe", pe)
            K.op(A, "activation", SKQ[0:2, half * 512:(half + 1) * 512],
                 PS[4][0:2, 0:512], AF.Identity, inc=("s_a", 1))
            skq_done = K.ctr["s_a"]

        rowld = 0
        mritld = 0
        bc_prev = 0
        rstdb_evac = 0
        rstdb_war = 0
        xb_use = 0
        fix_last = 0
        qe_war = 0
        scr_war = [skq_done, skq_done]
        qev = [self.ctx_v, self.ctx_v]
        aev_v, aev_a = 0, self._prep_a
        wev = [0, 0]
        qr_prev = 0
        for t in range(NT):
            t0 = t * TILE
            K.w(SY, "s_pe", bc_prev)
            K.w(SY, "s_st", self.rows_x[t])
            rowld = K.dma(SY, self.ROW[0:1, 0:TILE],
                          self.rowsd.ap()[t][0:1, 0:TILE], "s_w2b")
            K.w(TT, "s_w2b", rowld)
            K.w(TT, "s_a", rstdb_evac if t > 0 else skq_done)
            K.w(TT, "s_a", qr_prev)
            K.mm(PS[4][:, 0:TILE], self.ONEF[:], self.ROW[0:1, 0:TILE],
                 start=True, stop=True, inc=("s_pe", 1))
            bc_prev = K.ctr["s_pe"]
            K.w(A, "s_pe", bc_prev)
            K.w(A, "s_v", rstdb_war)
            K.op(A, "activation", RSTDB[:, 0:TILE], PS[4][:, 0:TILE], AF.Identity,
                 inc=("s_a", 1))
            rstdb_evac = K.ctr["s_a"]
            K.w(SY, "s_pe", fix_last)
            K.w(SY, "s_st", self.mrid1_wr[t])
            mritld = K.dma(SY, MRIT[0:2, :],
                           self.mrid1.ap()[t].rearrange("(a b) -> a b", a=2), "s_w2a")
            K.w(A, "s_pe", xb_use)
            for c in range(8):
                K.op(A, "activation", XB[:, c * TILE:(c + 1) * TILE],
                     X[c][:, t0:t0 + TILE], AF.Identity,
                     scale=self.PK["ln1_w"][:, l * 8 + c:l * 8 + c + 1],
                     inc=("s_a", 1) if c == 7 else None)
            cast_a = K.ctr["s_a"]
            for m in range(8):
                qb = 2 + m % 2
                K.w(TT, "s_a", cast_a)
                K.w(TT, "s_w2a", mritld)
                K.w(TT, "s_v", qev[m % 2])
                for kc in range(8):
                    K.mm(PS[qb][:], WQ[:, kc * 1024 + m * 128: kc * 1024 + (m + 1) * 128],
                         XB[:, kc * TILE:(kc + 1) * TILE], start=(kc == 0), stop=False)
                K.mm(PS[qb][:], SKQ[0:2, m * 128:(m + 1) * 128], MRIT[0:2, :],
                     start=False, stop=True, inc=("s_pe", 1))
                qdone = K.ctr["s_pe"]
                K.w(V, "s_pe", qdone)
                K.w(V, "s_a", rstdb_evac)
                K.w(V, "s_a", scr_war[m % 2])
                K.op(V, "tensor_mul", SCRX[m % 2][:], PS[qb][:], RSTDB[:, 0:TILE],
                     inc=("s_v", 1))
                qev[m % 2] = K.ctr["s_v"]
                K.w(A, "s_v", qev[m % 2])
                K.w(A, "s_pe", qe_war)
                K.op(A, "activation", QE[:, m * TILE:(m + 1) * TILE], SCRX[m % 2][:],
                     AF.Exp, inc=("s_a", 1))
                scr_war[m % 2] = K.ctr["s_a"]
                K.w(TT, "s_a", scr_war[m % 2])
                if m == 0:
                    K.w(TT, "s_a", rstdb_evac)
                K.mm(PS[4][0:8, 0:TILE], self.QMASK[:, m * 8:(m + 1) * 8],
                     QE[:, m * TILE:(m + 1) * TILE], start=(m == 0), stop=(m == 7),
                     inc=("s_pe", 1) if m == 7 else None)
            xb_use = K.ctr["s_pe"]
            fix_last = xb_use
            qd = K.ctr["s_pe"]
            K.w(A, "s_pe", qd)
            K.op(A, "activation", self.SCR[0:8, :], PS[4][0:8, 0:TILE], AF.Ln)
            K.op(A, "drain")
            K.op(A, "activation", QROWB[0:8, :], self.SCR[0:8, 0:TILE], AF.Exp,
                 scale=-1.0, inc=("s_a", 1))
            qr_done = K.ctr["s_a"]
            qr_prev = qr_done
            scr_war[0] = qr_done
            if t == 0:
                K.w(TT, "s_v", self.ctx_v)
                K.w(TT, "s_g", sel8_g)
            for h in range(8):
                ab = 5 if h % 2 == 0 else 6
                bb = 0 if h % 2 == 0 else 1
                K.w(TT, "s_a", qr_done)
                K.w(TT, "s_v", aev_v)
                K.w(TT, "s_a", aev_a)
                K.mm(PS[ab][:], self.CTX[:, h * 128:(h + 1) * 128],
                     QE[:, h * TILE:(h + 1) * TILE], start=True, stop=True)
                K.mm(PS[bb][:], SEL8B[:, h * 128:(h + 1) * 128], QROWB[0:8, :],
                     start=True, stop=True, inc=("s_pe", 1))
                hdone = K.ctr["s_pe"]
                K.w(A, "s_pe", hdone)
                K.w(A, "s_v")
                K.op(A, "activation", self.SCR2[:], PS[bb][:], AF.Identity,
                     inc=("s_a", 1))
                aev_a = K.ctr["s_a"]
                K.w(V, "s_a", aev_a)
                K.op(V, "tensor_mul", ATT[:, h * TILE:(h + 1) * TILE], PS[ab][:],
                     self.SCR2[:], inc=("s_v", 1))
                aev_v = K.ctr["s_v"]
            for m in range(8):
                ob = 5 if m % 2 == 0 else 6
                K.w(TT, "s_v", wev[m % 2])
                K.w(TT, "s_v", aev_v)
                for kc in range(8):
                    K.mm(PS[ob][:], WO[:, kc * 1024 + m * 128: kc * 1024 + (m + 1) * 128],
                         ATT[:, kc * TILE:(kc + 1) * TILE], start=(kc == 0),
                         stop=(kc == 7), inc=("s_pe", 1) if kc == 7 else None)
                odone = K.ctr["s_pe"]
                K.w(V, "s_pe", odone)
                K.op(V, "scalar_tensor_tensor", X[m][:, t0:t0 + TILE], PS[ob][:],
                     self.PK["bo"][:, l * 8 + m:l * 8 + m + 1], X[m][:, t0:t0 + TILE],
                     ALU.add, ALU.add, inc=("s_v", 1))
                wev[m % 2] = K.ctr["s_v"]
            qe_war = K.ctr["s_pe"]
        K.w(G, "s_pe")
        K.w(G, "s_c", self.w12_cast[l])
        self.load_eighth(l, 0, G)

    def load_eighth(self, l, e, eng):
        K = self.K
        b = (l + e) % 2
        sem = "s_w1a" if b == 0 else "s_w1b"
        WSb = self.WS[b]
        K.dma(eng, WSb[:, 0:4096].rearrange("p (m c) -> p m c", m=4),
              self.w1_s.ap()[l, 4 * e:4 * e + 4].rearrange("m p c -> p m c"), sem)
        self.pair_ld[b] = K.dma(
            eng, WSb[:, 4096:8192].rearrange("p (j c) -> p j c", j=8),
            self.w2_s.ap()[l, :, :, e * 512:(e + 1) * 512].rearrange("j p c -> p j c"), sem)
        return self.pair_ld[b]

    def finish2(self, t, par, FINROW, MRX, bc_war, mrd=None, rowd=None):
        K = self.K
        K.w(V, "s_a")
        K.op(V, "tensor_mul", self.SCR2[0:1, 0:TILE], self.ROW[0:1, 0:TILE],
             self.ROW[0:1, 0:TILE])
        K.op(V, "drain")
        K.op(V, "tensor_sub", self.ROW[0:1, TILE:2 * TILE],
             self.ROW[0:1, TILE:2 * TILE], self.SCR2[0:1, 0:TILE])
        K.di(V, "s_v")
        if mrd is None:
            mrd = self.mrid2.ap()[t:t + 1, :]
        if rowd is None:
            rowd = self.rowsd2.ap()[t:t + 1, :]
        K.w(A, "s_v")
        K.w(A, "s_st", self.mrx_dma)
        K.w(A, "s_pe", bc_war)
        K.op(A, "activation", MRX[0:1, 0:TILE], self.ROW[0:1, 0:TILE], AF.Identity,
             scale=-1.0, inc=("s_a", 1))
        fa1 = self.K.ctr["s_a"]
        K.w(SY, "s_a", fa1)
        n1 = K.dma(SY, mrd[0:1, 0:TILE], MRX[0:1, 0:TILE], "s_st")
        K.w(SY, "s_st", n1)
        K.w(A, "s_st", n1)
        K.op(A, "activation", self.ROW[0:1, TILE:2 * TILE],
             self.ROW[0:1, TILE:2 * TILE], AF.Ln, bias=EPS_LN)
        K.op(A, "drain")
        K.op(A, "activation", MRX[0:1, 0:TILE], self.ROW[0:1, TILE:2 * TILE],
             AF.Exp, scale=0.5)
        K.op(A, "activation", FINROW[0:1, par * TILE:(par + 1) * TILE],
             self.ROW[0:1, TILE:2 * TILE], AF.Exp, scale=-0.5)
        K.di(A, "s_a")
        fin_a = self.K.ctr["s_a"]
        K.w(SY, "s_a", fin_a)
        K.dma(SY, mrd[0:1, TILE:2 * TILE], MRX[0:1, 0:TILE], "s_st")
        self.mrx_dma = K.dma(SY, rowd,
                             FINROW[0:1, par * TILE:(par + 1) * TILE], "s_st")
        K.w(SY, "s_st", self.mrx_dma)
        self.rows2_wr[t] = self.mrx_dma
        return fin_a

    def pass2b(self, l):
        K = self.K
        K.phase()
        X, PS = self.X, self.PS
        XB2P = K.sb("p3XB", [128, 2 * 8 * TILE], BF16, local=True)
        HID = K.sb("p3HID", [128, 2 * 4 * TILE], BF16, local=True)
        RSTD2B = K.sb("p3RSB", [128, TILE], F32, local=True)
        MRIT2 = K.sb("p3MRI", [2, TILE], BF16, local=True)
        MRX = K.sb("p3MRX", [1, TILE], BF16, local=True)
        FINROW = self.ABR
        S1C1 = K.sb("p3SC", [2, 4 * 128], BF16, local=True)
        GB2 = K.sb("p3GB", [128, 16], BF16, local=True)

        for kc in range(8):
            K.op(A, "activation", GB2[:, kc * 2:kc * 2 + 1],
                 self.PK["ln2_w"][:, l * 8 + kc:l * 8 + kc + 1], AF.Identity)
            K.op(A, "activation", GB2[:, kc * 2 + 1:kc * 2 + 2],
                 self.PK["ln2_b"][:, l * 8 + kc:l * 8 + kc + 1], AF.Identity)
        K.di(A, "s_a")
        gb2_done = K.ctr["s_a"]

        hid_free = [0, 0]
        hid_done = [0] * NT
        hid_rd = [0, 0]
        pend_t = None
        y_free = [0, 0]
        xb2_use = [0, 0]
        xb2_wr = [0] * NT
        cast_done = [0] * NT
        fin_done = [0] * NT
        mrid2_wr = [0] * NT
        fix_done = 0
        mrit2_ld = 0
        rstdb_evac = 0
        rstdb_war = 0
        bc_done = [0] * NT
        row_ld = 0
        prev_eighth_pe = 0
        eighth_pe = 0

        def stats2(t):
            K.w(TT, "s_a")
            self.ln_stats_rows((t * TILE, TILE), 0, self.ROW, EPS_LN, scr=self.SCR2,
                               raw=True)
            fin_done[t] = self.finish2(t, t % 2, FINROW, MRX,
                                       bc_done[t - 2] if t >= 2 else 0)
            mrid2_wr[t] = self.mrx_dma

        stats2(0)
        stats2(1)

        for e in range(8):
            be = (l + e) % 2
            sem_e = "s_w1a" if be == 0 else "s_w1b"
            pair = self.pair_ld[be]
            e_gate = eighth_pe
            if e + 1 < 8:
                K.w(G, "s_pe", e_gate)
                K.w(G, "s_c", self.w12_cast[l])
                self.load_eighth(l, e + 1, G)
            K.w(TT, sem_e, pair)
            K.w(TT, "s_a", gb2_done)
            K.w(TT, "s_a", rstdb_evac)
            K.w(TT, "s_pe", fix_done)
            for m in range(4):
                for kc in range(8):
                    K.mm(PS[6][0:2, m * 128:(m + 1) * 128], GB2[:, kc * 2:kc * 2 + 2],
                         self.WS[be][:, m * 1024 + kc * 128:m * 1024 + (kc + 1) * 128],
                         start=(kc == 0), stop=(kc == 7),
                         inc=("s_pe", 1) if (m == 3 and kc == 7) else None)
            s1c1_pe = K.ctr["s_pe"]
            K.w(A, "s_pe", s1c1_pe)
            K.op(A, "activation", S1C1[0:2, :], PS[6][0:2, 0:512], AF.Identity,
                 inc=("s_a", 1))
            s1c1_done = K.ctr["s_a"]

            for t in range(NT):
                t0 = t * TILE
                buf = t % 2
                if e > 0:
                    K.w(SY, "s_pe", bc_done[t - 1] if t >= 1 else bc_done[NT - 1])
                    row_ld = K.dma(SY, FINROW[0:1, buf * TILE:(buf + 1) * TILE],
                                   self.rowsd2.ap()[t:t + 1, :], "s_w2b")
                K.w(TT, "s_a", rstdb_evac)
                K.w(TT, "s_a", s1c1_done)
                if e == 0:
                    K.w(TT, "s_a", fin_done[t])
                else:
                    K.w(TT, "s_w2b", row_ld)
                K.mm(PS[6][:, 0:TILE], self.ONEF[:],
                     FINROW[0:1, buf * TILE:(buf + 1) * TILE],
                     start=True, stop=True, inc=("s_pe", 1))
                bc_done[t] = K.ctr["s_pe"]
                K.w(A, "s_pe", bc_done[t])
                K.w(A, "s_v", rstdb_war)
                K.op(A, "activation", RSTD2B[:], PS[6][:, 0:TILE], AF.Identity,
                     inc=("s_a", 1))
                rstdb_evac = K.ctr["s_a"]
                xb = XB2P[:, buf * 8 * TILE:(buf + 1) * 8 * TILE]
                if e == 0:
                    K.w(A, "s_pe", xb2_use[buf])
                    if t >= 2:
                        K.w(A, "s_xia" if buf == 0 else "s_xib", xb2_wr[t - 2])
                    for c in range(8):
                        K.op(A, "activation", xb[:, c * TILE:(c + 1) * TILE],
                             X[c][:, t0:t0 + TILE], AF.Identity,
                             scale=self.PK["ln2_w"][:, l * 8 + c:l * 8 + c + 1],
                             inc=("s_a", 1) if c == 7 else None)
                    cast_done[t] = K.ctr["s_a"]
                    K.w(SY, "s_a", cast_done[t])
                    xb2_wr[t] = K.dma(SY, self.xb2d.ap()[:, :, t0:t0 + TILE],
                                      xb.rearrange("p (c w) -> p c w", c=8),
                                      "s_xia" if buf == 0 else "s_xib")
                    xb_rdy = ("s_a", cast_done[t])
                else:
                    K.w(SY, "s_pe", xb2_use[buf])
                    if e == 1:
                        K.w(SY, "s_xia" if buf == 0 else "s_xib",
                            xb2_wr[NT - 2 + buf])
                    n = K.dma(SY, xb.rearrange("p (c w) -> p c w", c=8),
                              self.xb2d.ap()[:, :, t0:t0 + TILE],
                              "s_xia" if buf == 0 else "s_xib")
                    xb_rdy = ("s_xia" if buf == 0 else "s_xib", n)
                K.w(SY, "s_pe", fix_done)
                mrit2_ld = K.dma(SY, MRIT2[0:2, :],
                                 self.mrid2.ap()[t].rearrange("(a b) -> a b", a=2),
                                 "s_w2a")
                def w2j(j, pt):
                    jb = 4 + j % 2
                    pb = pt % 2
                    K.w(TT, "s_v", hid_done[pt])
                    K.w(TT, "s_v", y_free[j % 2])
                    for mi in range(4):
                        K.mm(PS[jb][:, 0:TILE],
                             self.WS[be][:, 4096 + j * 512 + mi * 128:
                                         4096 + j * 512 + (mi + 1) * 128],
                             HID[:, pb * 4 * TILE + mi * TILE:
                                 pb * 4 * TILE + (mi + 1) * TILE],
                             start=(mi == 0), stop=(mi == 3),
                             inc=("s_pe", 1) if mi == 3 else None)
                    jdone = K.ctr["s_pe"]
                    K.w(V, "s_pe", jdone)
                    K.op(V, "tensor_add", X[j][:, pt * TILE:(pt + 1) * TILE],
                         X[j][:, pt * TILE:(pt + 1) * TILE],
                         PS[jb][:, 0:TILE], inc=("s_v", 1))
                    y_free[j % 2] = K.ctr["s_v"]

                for m in range(4):
                    hb = 2 + m % 2
                    K.w(TT, sem_e, pair)
                    K.w(TT, xb_rdy[0], xb_rdy[1])
                    K.w(TT, "s_w2a", mrit2_ld)
                    K.w(TT, "s_a", s1c1_done)
                    K.w(TT, "s_v", hid_free[m % 2])
                    for kc in range(8):
                        K.mm(PS[hb][:, 0:TILE],
                             self.WS[be][:, m * 1024 + kc * 128:m * 1024 + (kc + 1) * 128],
                             xb[:, kc * TILE:(kc + 1) * TILE],
                             start=(kc == 0), stop=False)
                    K.mm(PS[hb][:, 0:TILE], S1C1[0:2, m * 128:(m + 1) * 128],
                         MRIT2[0:2, :], start=False, stop=True, inc=("s_pe", 1))
                    hdone = K.ctr["s_pe"]
                    K.w(A, "s_pe", hdone)
                    K.w(A, "s_v")
                    K.op(A, "activation", self.SCR[:], PS[hb][:, 0:TILE], AF.Identity,
                         scale=LEAK, inc=("s_a", 1))
                    K.w(V, "s_a")
                    K.w(V, "s_a", rstdb_evac)
                    K.w(V, "s_pe", hid_rd[buf])
                    K.op(V, "tensor_tensor", self.SCR[:], PS[hb][:, 0:TILE],
                         self.SCR[:], ALU.max)
                    K.op(V, "drain")
                    K.op(V, "tensor_mul",
                         HID[:, buf * 4 * TILE + m * TILE:
                             buf * 4 * TILE + (m + 1) * TILE],
                         self.SCR[:], RSTD2B[:], inc=("s_v", 1))
                    hid_free[m % 2] = K.ctr["s_v"]
                    if pend_t is not None:
                        w2j(2 * m, pend_t)
                        w2j(2 * m + 1, pend_t)
                fix_done = K.ctr["s_pe"]
                xb2_use[buf] = fix_done
                if pend_t is not None:
                    hid_rd[pend_t % 2] = K.ctr["s_pe"]
                hid_done[t] = K.ctr["s_v"]
                rstdb_war = hid_done[t]
                pend_t = t
                eighth_pe = K.ctr["s_pe"]
                if e == 0 and t + 2 < NT:
                    stats2(t + 2)
            for j in range(8):
                w2j(j, pend_t)
            hid_rd[pend_t % 2] = K.ctr["s_pe"]
            pend_t = None
            eighth_pe = K.ctr["s_pe"]
            if e == 7 and l + 1 < DEPTH:
                K.w(G, "s_pe", e_gate)
                self.wld["wv"][l + 1] = self.load_wb(self.WS[l % 2][:], "wv", l + 1)
        if l + 1 < DEPTH:
            K.w(G, "s_pe")
            self.wld["wk"][l + 1] = self.load_wb(self.WS[(l + 1) % 2][:], "wk",
                                                 l + 1, "s_ac")
            self.wk_sem[l + 1] = "s_ac"

    def x1_form(self):
        K = self.K
        K.phase()
        TMJ = [K.sb(f"fTMJ{i}", [128, D], F32, local=True) for i in range(2)]
        X, PS, IDF = self.X, self.PS, self.IDF
        abrs = [self.ABR, self.ABR]
        self.ln_stats_rows((0, TILE), 0, self.ROW, EPS_LN)
        ab_done = self.ln_bcast(self.ROW, TILE, (0, 1), abrs[0])
        for t in range(NT):
            t0 = t * TILE
            abr = abrs[t % 2]
            for s in range(4):
                par = s % 2
                K.w(SY, "s_pe")
                ld = K.dma(SY, TMJ[par][:], self.xin[t0 + s * 128: t0 + (s + 1) * 128, :])
                K.w(TT, "s_x", ld)
                K.w(TT, "s_v")
                for c in range(8):
                    K.op(TT, "transpose", PS[2 + (c % 4)][:, (c // 4) * 128:(c // 4) * 128 + 128],
                         TMJ[par][:, c * 128:(c + 1) * 128], IDF[:],
                         inc=("s_pe", 1) if c == 7 else None)
                trd = K.ctr["s_pe"]
                K.w(V, "s_pe", trd)
                K.w(V, "s_a", ab_done)
                for c in range(8):
                    cs = slice(t0 + s * 128, t0 + (s + 1) * 128)
                    K.op(V, "tensor_mul", self.SCR[:, 0:128], X[c][:, cs],
                         abr[:, s * 128:(s + 1) * 128])
                    K.op(V, "drain")
                    K.op(V, "tensor_add", self.SCR[:, 0:128], self.SCR[:, 0:128],
                         abr[:, TILE + s * 128:TILE + (s + 1) * 128])
                    K.op(V, "drain")
                    K.op(V, "tensor_add", X[c][:, cs], self.SCR[:, 0:128],
                         PS[2 + (c % 4)][:, (c // 4) * 128:(c // 4) * 128 + 128])
                    K.op(V, "drain")
                K.di(V, "s_v")
            K.w(SY, "s_v")
            for c in range(8):
                self.spill_x = K.dma(SY, self.x1sp.ap()[c, :, t0:t0 + TILE],
                                     X[c][:, t0:t0 + TILE])
            if t + 1 < NT:
                self.ln_stats_rows(((t + 1) * TILE, TILE), 0, self.ROW, EPS_LN)
                ab_done = self.ln_bcast(self.ROW, TILE, (0, 1), abrs[(t + 1) % 2])

    def ffn_s1_s2(self):
        K = self.K
        K.phase()
        X, PS = self.X, self.PS
        HB = K.sb("s2HB", [128, 2], F32, local=True)
        for t in range(NT):
            self.ln_stats_rows((t * TILE, TILE), 0, self.ROW, EPS_LN)
            K.w(SY, "s_v")
            self.rows_x[t] = K.dma(SY, self.rowsd.ap()[t], self.ROW[0:1, :])
            K.w(SY, "s_x")
            if t == NT - 1:
                K.w(TT, "s_v")
                K.mm(PS[1][:, 0:2], self.ONEF[:], self.ROW[0:1, 510:512],
                     start=True, stop=True)
                K.mm(PS[2][:, 0:2], self.ONEF[:], self.ROW[0:1, TILE + 510:TILE + 512],
                     start=True, stop=True, inc=("s_pe", 1))
                K.w(V, "s_pe")
                nh = 0
                for c in range(8):
                    K.w(V, "s_x")
                    K.op(V, "tensor_mul", HB[:, 0:2], X[c][:, 4094:4096], PS[1][:, 0:2])
                    K.op(V, "drain")
                    K.op(V, "tensor_add", HB[:, 0:2], HB[:, 0:2], PS[2][:, 0:2])
                    K.di(V, "s_v")
                    K.w(SY, "s_v")
                    nh = K.dma(SY, self.hl_in.ap()[c * 128:(c + 1) * 128, :], HB[:, 0:2])
                K.w(G, "s_x", nh)
                K.op(G, "collective_compute", "AllGather", ALU.bypass,
                     replica_groups=RG,
                     ins=[self.hl_in.ap()], outs=[self.hl_out.ap()], inc=("s_cc", 1))
                K.w(G, "s_cc")
                nhl = K.dma(G, self.HALO[:].rearrange("p (c t) -> p c t", c=8),
                            self.hl_out.ap()[0:1024, :].rearrange("(c p) t -> p c t", p=128),
                            "s_w")
                K.w(G, "s_w")
                K.w(V, "s_w", nhl)
                K.op(V, "tensor_scalar", self.HALO[:], self.HALO[:], self.MCOL[:, 0:1],
                     None, ALU.mult)
                K.di(V, "s_v")

        XA = K.sb("s2XA", [128, 8 * (TILE + 2)], F32, local=True)
        XC = K.sb("s2XC", [128, 8 * TILE], BF16, local=True)
        PWT = self.WS[0]
        K.w(SY, "s_x")
        K.w(SY, "s_c", self.g_scr)
        K.w(SY, "s_pe")
        npw = 0
        for st in range(8):
            npw = K.dma(SY, PWT[:, st * 1024:(st + 1) * 1024],
                        self.pw_s.ap()[:, st * 128:(st + 1) * 128], transpose=True)
        K.w(SY, "s_x")
        MG = TILE + 2
        resv = 0
        for t in range(NT):
            t0 = t * TILE
            K.w(SY, "s_pe")
            K.w(SY, "s_x", self.rows_x[t])
            nr = K.dma(SY, self.ROW[0:1, :], self.rowsd.ap()[t])
            K.w(V, "s_x", nr)
            K.w(TT, "s_x", nr)
            K.w(V, "s_a")
            self.ln_bcast(self.ROW, TILE, (0, 1), self.ABR)
            ab = K.ctr["s_a"]
            K.w(V, "s_a", ab)
            for c in range(8):
                src = self.HALO[:, c * 2:c * 2 + 2] if t == 0 else self.CARRY[:, c * 2:c * 2 + 2]
                K.w(V, "s_a")
                K.op(V, "tensor_copy", XA[:, c * MG:c * MG + 2], src)
                K.op(V, "tensor_mul", self.SCR[:], X[c][:, t0:t0 + TILE], self.ABR[:, 0:TILE])
                K.op(V, "drain")
                K.op(V, "tensor_add", XA[:, c * MG + 2:c * MG + 2 + TILE], self.SCR[:],
                     self.ABR[:, TILE:2 * TILE])
                K.op(V, "drain")
                K.op(V, "tensor_copy", self.CARRY[:, c * 2:c * 2 + 2],
                     XA[:, c * MG + TILE:c * MG + 2 + TILE])
                K.op(V, "tensor_scalar", self.SCR[:], XA[:, c * MG:c * MG + TILE],
                     self.PK["dwk"][:, c * 3:c * 3 + 1], None, ALU.mult)
                K.op(V, "drain")
                K.op(V, "scalar_tensor_tensor", self.SCR[:], XA[:, c * MG + 1:c * MG + 1 + TILE],
                     self.PK["dwk"][:, c * 3 + 1:c * 3 + 2], self.SCR[:], ALU.mult, ALU.add)
                K.op(V, "drain")
                K.op(V, "scalar_tensor_tensor", self.SCR[:], XA[:, c * MG + 2:c * MG + 2 + TILE],
                     self.PK["dwk"][:, c * 3 + 2:c * 3 + 3], self.SCR[:], ALU.mult, ALU.add)
                K.di(V, "s_v")
                K.w(A, "s_v")
                K.op(A, "activation", XC[:, c * TILE:(c + 1) * TILE], self.SCR[:], AF.Identity,
                     bias=self.PK["vec5"][:, 0 * 8 + c:0 * 8 + c + 1])
                K.di(A, "s_a")
            xc_done = K.ctr["s_a"]
            if t == 0:
                K.w(TT, "s_x", npw)
            K.w(TT, "s_a", xc_done)
            pwv = [resv, resv]
            for m in range(8):
                pb = 2 + m % 2
                K.w(TT, "s_v", pwv[m % 2])
                for kc in range(8):
                    K.mm(PS[pb][:], PWT[:, kc * 1024 + m * 128: kc * 1024 + (m + 1) * 128],
                         XC[:, kc * TILE:(kc + 1) * TILE], start=(kc == 0), stop=(kc == 7),
                         inc=("s_pe", 1) if kc == 7 else None)
                pdone = K.ctr["s_pe"]
                K.w(A, "s_pe", pdone)
                K.w(A, "s_v")
                K.op(A, "activation", self.SCR2[:], PS[pb][:], AF.Silu,
                     bias=self.PK["vec5"][:, 1 * 8 + m:1 * 8 + m + 1])
                K.di(A, "s_a")
                K.w(V, "s_a")
                K.op(V, "tensor_add", X[m][:, t0:t0 + TILE],
                     XA[:, m * MG + 2:m * MG + 2 + TILE], self.SCR2[:])
                K.di(V, "s_v")
                pwv[m % 2] = K.ctr["s_v"]
            resv = K.ctr["s_v"]

    def ffn_s3(self):
        K = self.K
        K.phase()
        X, PS = self.X, self.PS
        LGR = K.sb("s3LGR", [128, 8], F32, local=True)
        EL = K.sb("s3EL", [128, 8], F32, local=True)
        TP = K.sb("s3TP", [128, 8], F32, local=True)
        M1 = K.sb("s3M1", [128, 4], F32, local=True)
        RWTC = K.sb("s3RWTC", [128, 8 * 8], F32, local=True)
        RWST = K.sb("s3RWST", [8, 1024], F32, local=True)
        LGD = K.sb("s3LGD", [128, 8], F32, local=True)
        self._LGD = LGD
        n = K.dma(SY, RWST[:], self.win["router_w"])
        K.w(SY, "s_x")
        K.w(TT, "s_x", n)
        for kc in range(8):
            K.op(TT, "transpose", PS[2][:, (kc % 2) * 8:(kc % 2) * 8 + 8],
                 RWST[:, kc * 128:(kc + 1) * 128], self.IDF[0:8, 0:8],
                 inc=("s_pe", 1))
            K.w(V, "s_pe")
            K.op(V, "tensor_copy", RWTC[:, kc * 8:(kc + 1) * 8],
                 PS[2][:, (kc % 2) * 8:(kc % 2) * 8 + 8])
            K.di(V, "s_v")
            K.w(TT, "s_v")
        rw_ready = K.ctr["s_v"]
        K.w(TT, "s_v", rw_ready)
        self.rw_done = [0] * NT
        for t in range(NT):
            t0 = t * TILE
            for s in range(4):
                K.w(TT, "s_v")
                for kc in range(8):
                    K.mm(PS[2][0:128, 0:8],
                         X[kc][:, t0 + s * 128:t0 + s * 128 + 128],
                         RWTC[:, kc * 8:(kc + 1) * 8], start=(kc == 0), stop=(kc == 7),
                         inc=("s_pe", 1) if kc == 7 else None)
                lgd = K.ctr["s_pe"]
                K.w(V, "s_pe", lgd)
                K.op(V, "tensor_add", LGR[:], PS[2][0:128, 0:8], self.RB128[:])
                K.op(V, "drain")
                if self.dbg == "rw" and t == 0 and s == 0:
                    K.op(V, "tensor_copy", LGD[:], LGR[:])
                    K.op(V, "drain")
                K.op(V, "reduce_max", M1[:, 0:1], LGR[:], axis=AX.X)
                K.op(V, "drain")
                K.op(V, "tensor_scalar", M1[:, 1:2], M1[:, 0:1], -1.0, None, ALU.mult)
                K.op(V, "tensor_scalar", TP[:], LGR[:], M1[:, 0:1], None, ALU.is_ge)
                K.op(V, "drain")
                K.op(V, "scalar_tensor_tensor", TP[:], TP[:], -1e30, LGR[:], ALU.mult, ALU.add)
                K.op(V, "drain")
                K.op(V, "reduce_max", M1[:, 2:3], TP[:], axis=AX.X)
                K.di(V, "s_v")
                K.w(A, "s_v")
                K.op(A, "activation", EL[:], LGR[:], AF.Exp, bias=M1[:, 1:2])
                K.op(A, "activation", M1[:, 3:4], M1[:, 2:3], AF.Exp, bias=M1[:, 1:2])
                K.op(A, "drain")
                K.op(A, "activation", M1[:, 3:4], M1[:, 3:4], AF.Ln, bias=1.0)
                K.op(A, "drain")
                K.op(A, "activation", M1[:, 3:4], M1[:, 3:4], AF.Exp, scale=-1.0)
                K.di(A, "s_a")
                K.w(V, "s_a")
                K.op(V, "tensor_scalar", TP[:], LGR[:], M1[:, 2:3], None, ALU.is_ge)
                K.op(V, "drain")
                K.op(V, "tensor_mul", TP[:], TP[:], EL[:])
                K.op(V, "drain")
                sg = t * 4 + s
                K.op(V, "tensor_scalar", self.RW[:, sg * 8:(sg + 1) * 8], TP[:],
                     M1[:, 3:4], None, ALU.mult)
                K.di(V, "s_v")
            self.rw_done[t] = K.ctr["s_v"]
        K.w(G, "s_v")
        self.xib_ld = 0
        for c in range(8):
            self.xib_ld = K.dma(G, self.xibd.ap()[c], X[c][:], "s_c")
        K.w(G, "s_c")

    def ffn_s4(self):
        K = self.K
        K.phase()
        PS = self.PS
        XIB = K.sb("s4XIB", [128, 2 * 8 * TILE], BF16, local=True)
        COMBP = K.sb("s4CP", [128, 2 * 1024], F32, local=True)
        EBR = K.sb("s4EB", [1, D], BF16, local=True)
        accd = [[0] * NT for _ in range(E)]
        last_comb_dma = [0]
        xld = [0, 0]

        def load_xib(t, buf):
            K.w(SY, "s_pe")
            sm = "s_xia" if buf == 0 else "s_xib"
            n = 0
            for c in range(8):
                n = K.dma(SY, XIB[:, buf * 8 * TILE + c * TILE: buf * 8 * TILE + (c + 1) * TILE],
                          self.xibd.ap()[c, :, t * TILE:(t + 1) * TILE], sm)
            xld[buf] = n

        K.w(SY, "s_c", self.xib_ld)
        load_xib(0, 0)
        load_xib(1, 1)
        ws_free_pe = 0
        EMAX = 1 if self.dbg == "comb1" else (2 if self.dbg == "comb2" else E)
        for e in range(EMAX):
            K.w(G, "s_ac")
            WT = self.WS[1 - e % 2]
            K.w(SY, "s_pe", ws_free_pe)
            nw = 0
            for st in range(8):
                nw = K.dma(SY, WT[:, st * 1024:(st + 1) * 1024],
                           self.ew_s.ap()[e][:, st * 128:(st + 1) * 128], transpose=True)
            K.w(SY, "s_x")
            ne = K.dma(G, EBR[:], self.eb_s.ap()[e:e + 1, :], "s_w")
            K.w(G, "s_w")
            K.w(TT, "s_x", nw)
            K.w(TT, "s_w", ne)
            for t in range(NT):
                buf = t % 2
                K.w(TT, "s_xia" if buf == 0 else "s_xib", xld[buf])
                for s in range(4):
                    if s % 2 == 0:
                        K.w(V, "s_ac", last_comb_dma[0])
                    for half in range(2):
                        yb = 2 + (s * 2 + half) % 2
                        K.w(TT, "s_a")
                        xb = XIB[:, buf * 8 * TILE:]
                        for kc in range(8):
                            K.mm(PS[yb][:],
                                 xb[:, kc * TILE + s * 128: kc * TILE + s * 128 + 128],
                                 WT[:, kc * 1024 + half * 512: kc * 1024 + (half + 1) * 512],
                                 start=(kc == 0), stop=False)
                        K.mm(PS[yb][:], self.ONEB[:],
                             EBR[0:1, half * 512:(half + 1) * 512],
                             start=False, stop=True, inc=("s_pe", 1))
                        ydone = K.ctr["s_pe"]
                        K.w(A, "s_pe", ydone)
                        K.w(A, "s_v")
                        scr = self.SCR if half == 0 else self.SCR2
                        K.op(A, "activation", scr[:], PS[yb][:], AF.Silu,
                             inc=("s_a", 1))
                        K.w(V, "s_a")
                        dst = COMBP[:, (s % 2) * 1024 + half * 512:
                                    (s % 2) * 1024 + (half + 1) * 512]
                        rwc = self.RW[:, (t * 4 + s) * 8 + e:(t * 4 + s) * 8 + e + 1]
                        K.op(V, "tensor_scalar", dst, scr[:], rwc, None, ALU.mult,
                             inc=("s_v", 1))
                    if s % 2 == 1:
                        hf0 = t * TILE + (s // 2) * 256
                        K.w(G, "s_v")
                        K.w(G, "s_ac")
                        accd[e][t] = K.dma(
                            G, self.combd.ap()[hf0:hf0 + 256, :]
                            .rearrange("(sl p) o -> p sl o", p=128),
                            COMBP[:].rearrange("p (sl o) -> p sl o", sl=2), "s_ac",
                            accum_op=(ALU.bypass if e == 0 else ALU.add))
                        last_comb_dma[0] = accd[e][t]
                if not (e == E - 1 and t >= NT - 2):
                    load_xib((t + 2) % NT, buf)
            ws_free_pe = K.ctr["s_pe"]
        self.acc_last = accd[E - 1]
        K.w(SY, "s_xia")
        K.w(SY, "s_xib")

    def ffn_s4b(self):
        K = self.K
        K.phase()
        X, PS, PSB = self.X, self.PS, self.PSB
        CMB = K.sb("s4bC", [128, 2 * 1024], F32, local=True)
        CB = K.sb("s4bCB", [128, 2 * 1024], BF16, local=True)
        CMT = K.sb("s4bT", [128, 8 * 256], BF16, local=True)
        OWT = self.WS[1]
        K.w(SY, "s_pe")
        nw = 0
        for st in range(8):
            nw = K.dma(SY, OWT[:, st * 1024:(st + 1) * 1024],
                       self.ow_s.ap()[:, st * 128:(st + 1) * 128], transpose=True)
        K.w(SY, "s_x")
        K.w(TT, "s_x", nw)
        for t in range(NT):
            t0 = t * TILE
            K.w(SY, "s_ac", self.acc_last[t])
            for hf in range(2):
                hf0 = t0 + hf * 256
                K.w(SY, "s_v")
                nc_ = K.dma(SY, CMB[:].rearrange("p (sl o) -> p sl o", sl=2),
                            self.combd.ap()[hf0:hf0 + 256, :]
                            .rearrange("(sl p) o -> p sl o", p=128))
                K.w(V, "s_x", nc_)
                K.op(V, "tensor_copy", CB[:], CMB[:])
                K.di(V, "s_v")
                cbd = K.ctr["s_v"]
                K.w(TT, "s_v", cbd)
                K.w(TT, "s_a")
                for sl in range(2):
                    for oc in range(8):
                        K.op(TT, "transpose", PSB[:, (oc % 4) * 256 + sl * 128:
                                                  (oc % 4) * 256 + sl * 128 + 128],
                             CB[:, sl * 1024 + oc * 128: sl * 1024 + (oc + 1) * 128],
                             self.IDB[:], inc=("s_pe", 1) if oc % 4 == 3 else None)
                        if oc % 4 == 3:
                            grp = K.ctr["s_pe"]
                            K.w(V, "s_pe", grp)
                            for o2 in range(oc - 3, oc + 1):
                                K.op(V, "tensor_copy",
                                     CMT[:, o2 * 256 + sl * 128: o2 * 256 + sl * 128 + 128],
                                     PSB[:, (o2 % 4) * 256 + sl * 128:
                                         (o2 % 4) * 256 + sl * 128 + 128])
                            K.di(V, "s_v")
                            K.w(TT, "s_v")
                cmt_v = K.ctr["s_v"]
                K.w(TT, "s_v", cmt_v)
                oev = [0, 0]
                for m in range(8):
                    ob = 2 + m % 2
                    K.w(TT, "s_v", oev[m % 2])
                    for oc in range(8):
                        K.mm(PS[ob][:, 0:256],
                             OWT[:, oc * 1024 + m * 128: oc * 1024 + (m + 1) * 128],
                             CMT[:, oc * 256:(oc + 1) * 256], start=(oc == 0), stop=(oc == 7),
                             inc=("s_pe", 1) if oc == 7 else None)
                    od = K.ctr["s_pe"]
                    K.w(A, "s_pe", od)
                    K.w(A, "s_v")
                    K.op(A, "activation", self.SCR2[:, 0:256], PS[ob][:, 0:256], AF.Identity,
                         bias=self.PK["vec5"][:, 2 * 8 + m:2 * 8 + m + 1])
                    K.di(A, "s_a")
                    K.w(V, "s_a")
                    K.op(V, "tensor_add", X[m][:, hf0:hf0 + 256], X[m][:, hf0:hf0 + 256],
                         self.SCR2[:, 0:256])
                    K.di(V, "s_v")
                    oev[m % 2] = K.ctr["s_v"]
            self.ln_stats_rows((t0, TILE), 0, self.ROW, EPS_RMS, with_mean=False, rms=True,
                               raw=True)
            K.w(SY, "s_a")
            self.rows_x[t] = K.dma(SY, self.rowsd.ap()[t], self.ROW[0:1, :])
            K.w(SY, "s_x")

    def ffn_s5(self):
        K = self.K
        K.phase()
        X, PS = self.X, self.PS
        HBUF = K.sb("s5HB", [128, 8 * TILE], BF16, local=True)
        FH = K.sb("s5FH", [128, 2 * TILE], BF16, local=True)
        FDT = K.sb("s5FDT", [128, 8 * HIDD], BF16, local=True)
        FUT = K.sb("s5FUT", [128, 2 * 1024], BF16, local=True)
        n = 0
        K.w(SY, "s_pe")
        for st in range(8):
            n = K.dma(SY, FDT[:, st * HIDD:(st + 1) * HIDD],
                      self.fd_s.ap()[:, st * 128:(st + 1) * 128], transpose=True)
        for st in range(2):
            n = K.dma(SY, FUT[:, st * 1024:(st + 1) * 1024],
                      self.fu_s.ap()[:, st * 128:(st + 1) * 128], transpose=True)
        K.w(SY, "s_x")
        K.w(TT, "s_x", n)
        for t in range(NT):
            t0 = t * TILE
            K.w(SY, "s_pe")
            K.w(SY, "s_x", self.rows_x[t])
            nr = K.dma(SY, self.ROW[0:1, :], self.rowsd.ap()[t])
            K.w(A, "s_x", nr)
            K.w(A, "s_v")
            K.op(A, "activation", self.ROW[0:1, 0:TILE], self.ROW[0:1, TILE:2 * TILE],
                 AF.Ln, bias=EPS_RMS)
            K.op(A, "drain")
            K.op(A, "activation", self.ROW[0:1, 0:TILE], self.ROW[0:1, 0:TILE],
                 AF.Exp, scale=-0.5)
            K.di(A, "s_a")
            self.ln_bcast(self.ROW, TILE, (0, 1), self.ABR, beta=False)
            abd = K.ctr["s_a"]
            K.w(V, "s_a", abd)
            for c in range(8):
                K.w(V, "s_a")
                K.op(V, "scalar_tensor_tensor", self.SCR[:], X[c][:, t0:t0 + TILE],
                     1.0, self.ABR[:, 0:TILE], ALU.mult, ALU.mult)
                K.di(V, "s_v")
                K.w(A, "s_v")
                K.op(A, "activation", X[c][:, t0:t0 + TILE], self.SCR[:], AF.Identity,
                     scale=self.PK["vec5"][:, 3 * 8 + c:3 * 8 + c + 1])
                K.op(A, "drain")
                K.op(A, "copy", HBUF[:, c * TILE:(c + 1) * TILE], X[c][:, t0:t0 + TILE])
                K.di(A, "s_a")
            hb_done = K.ctr["s_a"]
            K.w(TT, "s_a", hb_done)
            for m2 in range(2):
                fb = 2 + m2
                K.w(TT, "s_a")
                for kc in range(8):
                    K.mm(PS[fb][:], FDT[:, kc * HIDD + m2 * 128: kc * HIDD + (m2 + 1) * 128],
                         HBUF[:, kc * TILE:(kc + 1) * TILE], start=(kc == 0), stop=(kc == 7),
                         inc=("s_pe", 1) if kc == 7 else None)
                fd = K.ctr["s_pe"]
                K.w(A, "s_pe", fd)
                K.op(A, "activation", FH[:, m2 * TILE:(m2 + 1) * TILE], PS[fb][:], AF.Gelu,
                     bias=self.PK["fdb"][:, m2:m2 + 1])
                K.di(A, "s_a")
            fh_done = K.ctr["s_a"]
            K.w(TT, "s_a", fh_done)
            fev = [0, 0]
            for m in range(8):
                ub = 4 + m % 2
                K.w(TT, "s_v", fev[m % 2])
                for kc in range(2):
                    K.mm(PS[ub][:], FUT[:, kc * 1024 + m * 128: kc * 1024 + (m + 1) * 128],
                         FH[:, kc * TILE:(kc + 1) * TILE], start=(kc == 0), stop=(kc == 1),
                         inc=("s_pe", 1) if kc == 1 else None)
                ud = K.ctr["s_pe"]
                K.w(A, "s_pe", ud)
                K.w(A, "s_v")
                K.op(A, "activation", self.SCR2[:], PS[ub][:], AF.Identity,
                     bias=self.PK["vec5"][:, 4 * 8 + m:4 * 8 + m + 1])
                K.di(A, "s_a")
                K.w(V, "s_a")
                K.op(V, "tensor_add", X[m][:, t0:t0 + TILE], X[m][:, t0:t0 + TILE],
                     self.SCR2[:])
                K.di(V, "s_v")
                fev[m % 2] = K.ctr["s_v"]
            self.ln_stats_rows((t0, TILE), 0, self.ROW, EPS_LN, raw=True)
            K.w(SY, "s_a")
            self.rows_x[t] = K.dma(SY, self.rowsd.ap()[t], self.ROW[0:1, :])
            K.w(SY, "s_x")

    def ffn_s7(self):
        K = self.K
        K.phase()
        X, PS, IDF = self.X, self.PS, self.IDF
        X1B = K.sb("s7X1", [128, 8 * TILE], F32, local=True)
        OUTT = [K.sb(f"s7O{i}", [128, 1024], F32, local=True) for i in range(2)]
        st_done = 0
        K.w(SY, "s_x", self.spill_x)
        for t in range(NT):
            t0 = t * TILE
            K.w(SY, "s_pe")
            K.w(SY, "s_x", self.rows_x[t])
            nr = K.dma(SY, self.ROW[0:1, :], self.rowsd.ap()[t])
            K.w(SY, "s_x")
            K.w(SY, "s_v")
            nx = 0
            for c in range(8):
                nx = K.dma(SY, X1B[:, c * TILE:(c + 1) * TILE], self.x1sp.ap()[c, :, t0:t0 + TILE])
            K.w(A, "s_x", nr)
            K.w(V, "s_x", nr)
            self.finish_rows(self.ROW, TILE, EPS_LN, True)
            self.ln_bcast(self.ROW, TILE, (0, 1), self.ABR)
            abd = K.ctr["s_a"]
            K.w(V, "s_a", abd)
            K.w(V, "s_x", nx)
            for c in range(8):
                K.op(V, "tensor_mul", self.SCR[:], X[c][:, t0:t0 + TILE], self.ABR[:, 0:TILE])
                K.op(V, "drain")
                K.op(V, "tensor_add", self.SCR[:], self.SCR[:], self.ABR[:, TILE:2 * TILE])
                K.op(V, "drain")
                K.op(V, "tensor_add", X[c][:, t0:t0 + TILE], self.SCR[:],
                     X1B[:, c * TILE:(c + 1) * TILE])
                K.op(V, "drain")
            K.di(V, "s_v")
            od = K.ctr["s_v"]
            for s in range(4):
                par = s % 2
                K.w(TT, "s_v", od)
                K.w(TT, "s_st")
                K.w(TT, "s_a")
                for c in range(8):
                    K.op(TT, "transpose", PS[c // 4][:, (c % 4) * 128:(c % 4 + 1) * 128],
                         X[c][:, t0 + s * 128:t0 + (s + 1) * 128], IDF[:],
                         inc=("s_pe", 1) if c == 7 else None)
                trd = K.ctr["s_pe"]
                K.w(V, "s_pe", trd)
                K.w(A, "s_pe", trd)
                for c in range(8):
                    eng, meth = (V, "tensor_copy") if c < 4 else (A, "copy")
                    K.op(eng, meth, OUTT[par][:, c * 128:(c + 1) * 128],
                         PS[c // 4][:, (c % 4) * 128:(c % 4 + 1) * 128])
                K.di(V, "s_v")
                K.di(A, "s_a")
                K.w(SY, "s_v")
                K.w(SY, "s_a")
                st_done = K.dma(SY, self.out[t0 + s * 128:t0 + (s + 1) * 128, :], OUTT[par][:],
                                "s_st")
                K.w(TT, "s_v")
                K.w(TT, "s_a")
        K.w(G, "s_st", K.ctr["s_st"])
        K.w(G, "s_x", K.ctr["s_x"])

    def dump_X(self):
        K = self.K
        K.w(SY, "s_v")
        K.w(SY, "s_a")
        K.w(SY, "s_pe")
        n = 0
        for c in range(8):
            n = K.dma(SY, self.dump[c], self.X[c][:])
        K.w(G, "s_x", n)
        K.w(G, "s_w")
        K.w(G, "s_cc")
        K.stopped = True

    def dump_masks(self):
        K = self.K
        K.w(SY, "s_g", self.g_masks)
        K.w(SY, "s_v")
        K.w(SY, "s_a")
        n = K.dma(SY, self.dump2[0:64, :].rearrange("(a b) c -> a b c", a=8),
                  self.SEL8[:].rearrange("a (b c) -> a b c", b=8))
        n = K.dma(SY, self.dump2[128:256, :], self.IDF[:])
        n = K.dma(SY, self.dump2[256:384, 0:8], self.CONSTP[:, 0:8])
        K.w(G, "s_x", n)
        K.w(G, "s_w")
        K.stopped = True

    def dump_cc(self):
        K = self.K
        K.w(SY, "s_cc")
        n = K.dma(SY, self.dump2[:], self.cc_out.ap())
        K.w(G, "s_x", n)
        K.stopped = True

    def dump_rw(self):
        K = self.K
        K.w(SY, "s_v")
        n = K.dma(SY, self.dump2[0:128, :], self.RW[:, 0:128])
        n = K.dma(SY, self.dump2[128:256, 0:8], self._LGD[:])
        K.w(G, "s_x", n)
        K.stopped = True

    def generate(self):
        dbg = self.dbg
        if dbg == "prep":
            self.dump_masks()
            return self.finish()
        if dbg == "init2":
            self.initial(nsl=2)
            self.dump_X()
            return self.finish()
        if dbg == "initnoln":
            self.initial(do_ln=False)
            self.dump_X()
            return self.finish()
        self.initial()
        if dbg == "xint0":
            self.dump_X()
            return self.finish()
        for l in range(DEPTH):
            self.pass1(l)
            if dbg == f"l{l}_U":
                self.dump_cc()
                return self.finish()
            self.pass2a(l)
            if dbg == f"l{l}_xmid":
                self.dump_X()
                return self.finish()
            self.pass2b(l)
            if dbg == f"l{l}_x":
                self.dump_X()
                return self.finish()
        self.x1_form()
        if dbg == "x1":
            self.dump_X()
            return self.finish()
        self.ffn_s1_s2()
        if dbg == "x1i":
            self.dump_X()
            return self.finish()
        self.ffn_s3()
        if dbg == "rw":
            self.dump_rw()
            return self.finish()
        self.ffn_s4()
        if dbg in ("comb", "comb1", "comb2"):
            K = self.K
            K.w(SY, "s_ac")
            n = K.dma(SY, self.dump.rearrange("a b c -> (a b c)"),
                      self.combd.ap().rearrange("a b -> (a b)"))
            K.w(G, "s_x", n)
            K.stopped = True
            return self.finish()
        self.ffn_s4b()
        if dbg == "u":
            self.dump_X()
            return self.finish()
        self.ffn_s5()
        if dbg == "ffret":
            self.dump_X()
            return self.finish()
        self.ffn_s7()
        return self.finish()

    def finish(self):
        if self.K.ph is not None:
            self.K.ph.close()
            self.K.ph = None
        nc = self.K.build()
        self.K.es.close()
        return nc


_NC_CACHE = {}


def get_nc(dbg=None):
    if dbg not in _NC_CACHE:
        _NC_CACHE[dbg] = Mega(dbg).generate()
    return _NC_CACHE[dbg]


def kernel(**inputs):
    nc = get_nc(None)
    return _run(nc, inputs, None)


def _run(nc, inputs, dbg):
    xfull = np.ascontiguousarray(np.asarray(inputs["L0_attn_x"], np.float32))
    in_maps = []
    for c in range(8):
        b, half = c // 2, c % 2
        m = {"x": np.ascontiguousarray(xfull[b, half * 4096:(half + 1) * 4096, :]),
             "hmask": np.full((128, 1), 1.0 if half == 1 else 0.0, np.float32)}
        for nm in ["ln1_w", "ln1_b", "wq", "wk", "wv", "wo", "bo", "ln2_w", "ln2_b",
                   "w1", "w2"]:
            m["w_" + nm] = np.ascontiguousarray(np.asarray(inputs["L0_attn_" + nm], np.float32))
        for nm in ["dw_kernel", "dw_bias", "pw_w", "pw_b", "router_w", "router_b",
                   "expert_w", "expert_b", "out_w", "out_b", "norm_w", "ff_down_w",
                   "ff_down_b", "ff_up_w", "ff_up_b"]:
            m["w_" + nm] = np.ascontiguousarray(np.asarray(inputs["L0_ffn_" + nm], np.float32))
        in_maps.append(m)
    res = run_bass_kernel_spmd(nc, in_maps, core_ids=list(range(8)))
    if dbg:
        return res
    out = np.empty((4, 8192, 1024), np.float32)
    for c in range(8):
        b, half = c // 2, c % 2
        out[b, half * 4096:(half + 1) * 4096, :] = res.results[c]["out"]
    return out


def run_debug(dbg, inputs):
    nc = get_nc(dbg)
    return _run(nc, inputs, dbg)


# version 06
# speedup vs baseline: 1.1894x; 1.0040x over previous
import numpy as np
from contextlib import ExitStack

import concourse.bass as bass
import concourse.mybir as mybir
from concourse.bass_utils import run_bass_kernel_spmd

F32 = mybir.dt.float32
BF16 = mybir.dt.bfloat16
AF = mybir.ActivationFunctionType
ALU = mybir.AluOpType
AX = mybir.AxisListType

D = 1024
T = 4096
TILE = 512
NT = T // TILE
SL = 128
DEPTH = 6
FF = 4096
FFT = 256
E = 8
HIDD = 256
LEAK = 0.01
EPS_LN = 1e-5
EPS_RMS = 1e-6

V, A, TT, G, SY = "vector", "scalar", "tensor", "gpsimd", "sync"

RG = [[0, 1], [2, 3], [4, 5], [6, 7]]


class KB:
    ENGS = ("sync", "gpsimd", "vector", "scalar", "tensor")

    def __init__(self):
        self.nc = bass.Bass("TRN2", target_bir_lowering=False)
        self.prog = {e: [] for e in self.ENGS}
        self.es = ExitStack()
        self.ph = None
        self.semh = {}
        self.ctr = {}
        for s in ("s_x", "s_w", "s_c", "s_ac", "s_cc", "s_pe", "s_v", "s_a", "s_g", "s_f",
                  "s_w1a", "s_w1b", "s_w2a", "s_w2b", "s_xia", "s_xib", "s_st"):
            self.semh[s] = self.es.enter_context(self.nc.semaphore(s))
            self.ctr[s] = 0
        self.stopped = False

    _uid = 0

    def sb(self, name, shape, dt, local=False):
        KB._uid += 1
        st = self.ph if local else self.es
        return st.enter_context(self.nc.sbuf_tensor(f"{name}_{KB._uid}", shape, dt))

    def psum(self, name, shape, dt):
        return self.es.enter_context(self.nc.psum_tensor(name, shape, dt))

    def dram(self, name, shape, dt, kind="Internal"):
        return self.nc.dram_tensor(name, shape, dt, kind=kind)

    def phase(self):
        if self.ph is not None and not self.stopped:
            for e in ("vector", "scalar"):
                self.di(e, "s_f", 1)
            for e in ("gpsimd", "tensor", "sync"):
                self.op(e, "nop", inc=("s_f", 1))
            tgt = self.ctr["s_f"]
            for e in ("vector", "scalar", "gpsimd", "tensor", "sync"):
                self.w(e, "s_f", tgt)
        if self.ph is not None:
            self.ph.close()
        self.ph = ExitStack()

    def w(self, eng, sem, val=None):
        if self.stopped:
            return
        if val is None:
            val = self.ctr[sem]
        if val <= 0:
            return
        h = self.semh[sem]
        self.prog[eng].append(lambda e, hh=h, vv=val: e.wait_ge(hh, vv))

    def op(self, eng, meth, *a, inc=None, **kw):
        if self.stopped:
            return 0
        if inc:
            nm, k = inc
            self.ctr[nm] += k
            h = self.semh[nm]
            self.prog[eng].append(
                lambda e, m=meth, aa=a, kk=kw, hh=h, n=k: getattr(e, m)(*aa, **kk).then_inc(hh, n))
            return self.ctr[nm]
        self.prog[eng].append(lambda e, m=meth, aa=a, kk=kw: getattr(e, m)(*aa, **kk))
        return 0

    def dma(self, eng, out, in_, sem="s_x", **kw):
        return self.op(eng, "dma_start", inc=(sem, 16), out=out, in_=in_, **kw)

    def mm(self, out, lhsT, rhs, start, stop, inc=None):
        return self.op("tensor", "matmul", out, lhsT, rhs, start=start, stop=stop, inc=inc)

    def di(self, eng, sem, k=1):
        return self.op(eng, "drain", inc=(sem, k))

    def build(self):
        nc = self.nc
        with nc.Block() as block:
            @block.sync
            def _(eng):
                for f in self.prog["sync"]:
                    f(eng)

            @block.gpsimd
            def _(eng):
                for f in self.prog["gpsimd"]:
                    f(eng)

            @block.vector
            def _(eng):
                for f in self.prog["vector"]:
                    f(eng)

            @block.scalar
            def _(eng):
                for f in self.prog["scalar"]:
                    f(eng)

            @block.tensor
            def _(eng):
                for f in self.prog["tensor"]:
                    f(eng)
        return nc


class Mega:
    def __init__(self, dbg=None):
        self.K = KB()
        self.dbg = dbg
        self._declare()
        self._prep()

    def _declare(self):
        K = self.K
        self.xin = K.dram("x", [T, D], F32, kind="ExternalInput").ap()
        self.hmask = K.dram("hmask", [128, 1], F32, kind="ExternalInput").ap()
        self.win = {}
        for nm, shp in [
            ("ln1_w", [DEPTH, D]), ("ln1_b", [DEPTH, D]), ("wq", [DEPTH, D, D]),
            ("wk", [DEPTH, D, D]), ("wv", [DEPTH, D, D]), ("wo", [DEPTH, D, D]),
            ("bo", [DEPTH, D]), ("ln2_w", [DEPTH, D]), ("ln2_b", [DEPTH, D]),
            ("w1", [DEPTH, D, FF]), ("w2", [DEPTH, FF, D]),
            ("dw_kernel", [D, 1, 3]), ("dw_bias", [D]), ("pw_w", [D, D]), ("pw_b", [D]),
            ("router_w", [E, D]), ("router_b", [E]), ("expert_w", [E, D, D]),
            ("expert_b", [E, D]), ("out_w", [D, D]), ("out_b", [D]), ("norm_w", [D]),
            ("ff_down_w", [HIDD, D]), ("ff_down_b", [HIDD]), ("ff_up_w", [D, HIDD]),
            ("ff_up_b", [D]),
        ]:
            self.win[nm] = K.dram("w_" + nm, shp, F32, kind="ExternalInput").ap()
        self.out = K.dram("out", [T, D], F32, kind="ExternalOutput").ap()
        if self.dbg:
            self.dump = K.dram("dump", [8, 128, T], F32, kind="ExternalOutput").ap()
            self.dump2 = K.dram("dump2", [1032, 128], F32, kind="ExternalOutput").ap()

        self.x1sp = K.dram("x1sp", [8, 128, T], F32)
        self.rowsd = K.dram("rowsd", [NT, 1, 2 * TILE], F32)
        self.combd = K.dram("combd", [T, D], F32)
        self.xibd = K.dram("xibd", [8, 128, T], BF16)
        self.w1_s = K.dram("w1_s", [DEPTH, FF // 128, 128, 8 * 128], BF16)
        self.qkvo_s = {nm: K.dram(nm + "_s", [DEPTH, 128, 8 * D], BF16)
                       for nm in ("wq", "wk", "wv", "wo")}
        self.w2_s = K.dram("w2_s", [DEPTH, 8, 128, 32 * 128], BF16)
        self.pw_s = K.dram("pw_s", [D, D], BF16)
        self.ow_s = K.dram("ow_s", [D, D], BF16)
        self.ew_s = K.dram("ew_s", [E, D, D], BF16)
        self.fd_s = K.dram("fd_s", [HIDD, D], BF16)
        self.fu_s = K.dram("fu_s", [D, HIDD], BF16)
        self.eb_s = K.dram("eb_s", [E, D], BF16)
        self.cc_in = K.dram("cc_in", [1032, 128], F32)
        self.cc_out = K.dram("cc_out", [1032, 128], F32)
        self.hl_in = K.dram("hl_in", [D, 2], F32)
        self.hl_out = K.dram("hl_out", [2 * D, 2], F32)
        self.xb2d = K.dram("xb2d", [128, 8, T], BF16)
        self.mrid1 = K.dram("mrid1", [NT, 2 * TILE], BF16)
        self.mrid2 = K.dram("mrid2", [NT, 2 * TILE], BF16)
        self.rowsd2 = K.dram("rowsd2", [NT, TILE], F32)

        self.X = [K.sb(f"X{c}", [128, T], F32) for c in range(8)]
        self.WS = [K.sb("WS0", [128, 8 * D], BF16), K.sb("WS1", [128, 8 * D], BF16)]
        self.ABR = K.sb("ABR", [128, 2 * TILE], F32)
        self.SCR = K.sb("SCR", [128, TILE], F32)
        self.SCR2 = K.sb("SCR2", [128, TILE], F32)
        self.ROW = K.sb("ROW", [1, 2 * TILE], F32)
        self.ONC = K.sb("ONC", [128, 1], F32)
        self.QMASK = K.sb("QMASK", [128, 8 * 8], BF16)
        self.QMD = K.sb("QMD", [128, 4], BF16)
        self.IDF = K.sb("IDF", [128, 128], F32)
        self.IDB = K.sb("IDB", [128, 128], BF16)
        self.ONEF = K.sb("ONEF", [1, 128], F32)
        self.ONEB = K.sb("ONEB", [1, 128], BF16)
        self.HALO = K.sb("HALO", [128, 16], F32)
        self.CARRY = K.sb("CARRY", [128, 16], F32)
        self.MCOL = K.sb("MCOL", [128, 1], F32)
        self.RBROW = K.sb("RBROW", [1, E], F32)
        self.RB128 = K.sb("RB128", [128, E], F32)
        self.RW = K.sb("RW", [128, (T // SL) * E], F32)
        self.CTX = K.sb("CTX", [128, 8 * 128], BF16)
        self.PK = {}
        for nm, ncol in [("ln1_w", 48), ("ln1_b", 48), ("ln2_w", 48), ("ln2_b", 48),
                         ("bo", 48), ("vec5", 40), ("dwk", 24), ("fdb", 2)]:
            self.PK[nm] = K.sb("PK_" + nm, [128, ncol], F32)
        self.CONSTP = K.sb("CONSTP", [128, 12], F32)
        self.PS = [K.psum(f"PS{i}", [128, 512], F32) for i in range(7)]
        self.PSB = K.psum("PSB", [128, 1024], BF16)
        self.wld = {nm: [0] * DEPTH for nm in ("wq", "wk", "wv", "wo", "w2")}
        self.wk_sem = ["s_w"] * DEPTH
        self.pair_ld = [0, 0]
        self.mrx_dma = 0
        self.rows2_wr = [0] * NT
        self._lnst_pe = 0

    def load_w(self, slot_ap, src_dd, sem="s_w"):
        return self.K.dma(G, slot_ap.rearrange("p (k m) -> p k m", k=8),
                          src_dd.rearrange("(k p) m -> p k m", p=128), sem)

    def cast_qkvo(self, l):
        K = self.K
        for nm in ("wq", "wk", "wv", "wo"):
            self.qkvo_cast[l] = K.dma(
                G, self.qkvo_s[nm].ap()[l].rearrange("p (k m) -> p k m", k=8),
                self.win[nm][l].rearrange("(k p) m -> p k m", p=128), "s_c")

    def load_wb(self, slot_ap, nm, l, sem="s_w"):
        K = self.K
        K.w(G, "s_c", self.qkvo_cast[l])
        return K.dma(G, slot_ap, self.qkvo_s[nm].ap()[l], sem)

    def cast_w12(self, l):
        K = self.K
        for mi in range(32):
            K.dma(G, self.w1_s.ap()[l, mi].rearrange("p (k m) -> p k m", k=8),
                  self.win["w1"][l][:, mi * 128:(mi + 1) * 128]
                  .rearrange("(k p) m -> p k m", p=128), "s_c")
        for j in range(8):
            self.w12_cast[l] = K.dma(
                G, self.w2_s.ap()[l, j].rearrange("p (mi jj) -> p mi jj", mi=32),
                self.win["w2"][l][:, j * 128:(j + 1) * 128]
                .rearrange("(mi p) jj -> p mi jj", p=128), "s_c")

    def _prep(self):
        K = self.K
        win = self.win
        dg = lambda: K.op(G, "drain")
        K.op(G, "memset", self.IDF[:], 0.0)
        dg()
        K.op(G, "affine_select", out=self.IDF[:], in_=self.IDF[:], compare_op=ALU.not_equal,
             fill=1.0, base=0, pattern=[[-1, 128]], channel_multiplier=1)
        K.op(G, "memset", self.IDB[:], 0.0)
        dg()
        K.op(G, "affine_select", out=self.IDB[:], in_=self.IDB[:], compare_op=ALU.not_equal,
             fill=1.0, base=0, pattern=[[-1, 128]], channel_multiplier=1)
        K.op(G, "memset", self.ONEF[:], 1.0)
        K.op(G, "memset", self.ONEB[:], 1.0)
        K.op(G, "memset", self.QMASK[:], 0.0)
        dg()
        for h in range(8):
            K.op(G, "memset", self.QMASK[:, h * 8 + h:h * 8 + h + 1], 1.0)
        K.op(G, "memset", self.ONC[:], 1.0)
        K.op(G, "memset", self.QMD[:], 0.0)
        dg()
        K.op(G, "memset", self.QMD[:, 0:1], 1.0)
        K.op(G, "memset", self.QMD[:, 3:4], 1.0)
        K.op(G, "memset", self.CARRY[:], 0.0)
        for i, cv in enumerate([EPS_LN, EPS_RMS, 1.0, -0.5, -1.0, LEAK, 1.0 / D, 2.0, 0.5]):
            K.op(G, "memset", self.CONSTP[:, i:i + 1], cv)
            K.nc.const_aps.register(F32, cv, self.CONSTP[:, i:i + 1])
        K.op(G, "drain", inc=("s_g", 1))
        self.g_masks = K.ctr["s_g"]

        K.dma(G, self.pw_s.ap(), win["pw_w"], "s_c")
        K.dma(G, self.ow_s.ap(), win["out_w"], "s_c")
        for e in range(E):
            K.dma(G, self.ew_s.ap()[e], win["expert_w"][e], "s_c")
        K.dma(G, self.fd_s.ap(), win["ff_down_w"], "s_c")
        K.dma(G, self.fu_s.ap(), win["ff_up_w"], "s_c")
        K.dma(G, self.eb_s.ap(), win["expert_b"], "s_c")
        self.g_scr = K.ctr["s_c"]
        K.w(G, "s_c")
        self.load_w(self.WS[0][:], win["wk"][0])
        self.wld["wk"][0] = self.wld["wv"][0] = \
            self.load_w(self.WS[1][:], win["wv"][0])
        K.w(G, "s_w")
        self.w12_cast = [0] * DEPTH
        self.qkvo_cast = [0] * DEPTH
        self.cast_w12(0)
        K.w(G, "s_c")
        self.cast_qkvo(0)
        K.w(G, "s_c")

        K.phase()
        PKS = K.sb("PKS", [64, 128], F32, local=True)
        jobs = [
            ("ln1_w", win["ln1_w"].rearrange("l (c p) -> (l c) p", p=128), 48),
            ("ln1_b", win["ln1_b"].rearrange("l (c p) -> (l c) p", p=128), 48),
            ("ln2_w", win["ln2_w"].rearrange("l (c p) -> (l c) p", p=128), 48),
            ("ln2_b", win["ln2_b"].rearrange("l (c p) -> (l c) p", p=128), 48),
            ("bo", win["bo"].rearrange("l (c p) -> (l c) p", p=128), 48),
            ("fdb", win["ff_down_b"].rearrange("(c p) -> c p", p=128), 2),
        ]
        K.w(TT, "s_g", self.g_masks)
        for nm, src, ncol in jobs:
            K.w(SY, "s_v")
            ld = K.dma(SY, PKS[0:ncol, :], src)
            K.w(TT, "s_x", ld)
            K.w(TT, "s_v")
            K.op(TT, "transpose", self.PS[0][:, 0:ncol], PKS[0:ncol, :],
                 self.IDF[0:ncol, 0:ncol], inc=("s_pe", 1))
            K.w(V, "s_pe")
            K.op(V, "tensor_copy", self.PK[nm][:], self.PS[0][:, 0:ncol])
            K.di(V, "s_v")
        for j, nm in enumerate(["dw_bias", "pw_b", "out_b", "norm_w", "ff_up_b"]):
            K.w(SY, "s_v")
            ld = K.dma(SY, PKS[0:8, :], win[nm].rearrange("(c p) -> c p", p=128))
            K.w(TT, "s_x", ld)
            K.w(TT, "s_v")
            K.op(TT, "transpose", self.PS[0][:, 0:8], PKS[0:8, :], self.IDF[0:8, 0:8],
                 inc=("s_pe", 1))
            K.w(V, "s_pe")
            K.op(V, "tensor_copy", self.PK["vec5"][:, j * 8:(j + 1) * 8], self.PS[0][:, 0:8])
            K.di(V, "s_v")
        K.w(SY, "s_v")
        K.dma(SY, self.PK["dwk"][:].rearrange("p (c k) -> p c k", c=8),
              win["dw_kernel"].rearrange("(c p) o k -> p c (o k)", p=128))
        K.dma(SY, self.RBROW[:], win["router_b"].rearrange("(a e) -> a e", a=1))
        K.dma(SY, self.MCOL[:], self.hmask)
        K.w(TT, "s_x", K.ctr["s_x"])
        K.w(TT, "s_v")
        K.op(TT, "matmul", self.PS[0][:, 0:E], self.ONEF[:], self.RBROW[:],
             start=True, stop=True, inc=("s_pe", 1))
        K.w(V, "s_pe")
        K.op(V, "tensor_copy", self.RB128[:], self.PS[0][:, 0:E])
        K.di(V, "s_v")
        self.prep_v = K.ctr["s_v"]

    def ln_stats_rows(self, tcols, bank, rowbuf, eps, with_mean=True, rms=False,
                      srcs=None, scr=None, raw=False, precise=False):
        K = self.K
        t0, tn = tcols
        srcs = srcs or [self.X[c][:, t0:t0 + tn] for c in range(8)]
        scr = scr if scr is not None else self.SCR
        PS = self.PS
        for c in range(8):
            if precise:
                K.w(V, "s_pe", self._lnst_pe)
            else:
                K.w(V, "s_pe")
            K.op(V, "tensor_mul", scr[:, 0:tn], srcs[c], srcs[c], inc=("s_v", 1))
            sqv = K.ctr["s_v"]
            K.w(TT, "s_v", sqv if precise else None)
            if not rms:
                K.mm(PS[bank][0:1, 0:tn], self.ONC[:], srcs[c], start=(c == 0), stop=(c == 7))
            K.mm(PS[bank + 1][0:1, 0:tn], self.ONC[:], scr[:, 0:tn],
                 start=(c == 0), stop=(c == 7), inc=("s_pe", 1))
            self._lnst_pe = K.ctr["s_pe"]
        K.w(A, "s_pe", self._lnst_pe if precise else None)
        if not precise:
            K.w(A, "s_x")
        if not rms:
            K.op(A, "activation", rowbuf[0:1, 0:tn], PS[bank][0:1, 0:tn], AF.Identity,
                 scale=1.0 / D)
        K.op(A, "activation", rowbuf[0:1, TILE:TILE + tn], PS[bank + 1][0:1, 0:tn],
             AF.Identity, scale=1.0 / D)
        K.di(A, "s_a")
        if raw:
            return
        self.finish_rows(rowbuf, tn, eps, with_mean, scr)

    def finish_rows(self, rowbuf, tn, eps, with_mean, scr=None):
        K = self.K
        scr = scr if scr is not None else self.SCR
        K.w(V, "s_a")
        if with_mean:
            K.op(V, "tensor_mul", scr[0:1, 0:tn], rowbuf[0:1, 0:tn], rowbuf[0:1, 0:tn])
            K.op(V, "drain")
            K.op(V, "tensor_sub", rowbuf[0:1, TILE:TILE + tn],
                 rowbuf[0:1, TILE:TILE + tn], scr[0:1, 0:tn])
            K.di(V, "s_v")
            K.w(A, "s_v")
        K.op(A, "activation", scr[0:1, 0:tn], rowbuf[0:1, TILE:TILE + tn], AF.Ln, bias=eps)
        K.op(A, "drain")
        K.op(A, "activation", scr[0:1, 0:tn], scr[0:1, 0:tn], AF.Exp, scale=-0.5)
        K.di(A, "s_a")
        K.w(V, "s_a")
        if with_mean:
            K.op(V, "scalar_tensor_tensor", rowbuf[0:1, TILE:TILE + tn],
                 rowbuf[0:1, 0:tn], -1.0, scr[0:1, 0:tn], ALU.mult, ALU.mult)
            K.op(V, "drain")
        else:
            K.op(V, "memset", rowbuf[0:1, TILE:TILE + tn], 0.0)
        K.op(V, "tensor_copy", rowbuf[0:1, 0:tn], scr[0:1, 0:tn])
        K.di(V, "s_v")

    def ln_bcast(self, rowsrc, tn, banks, abr, beta=True):
        K = self.K
        K.w(TT, "s_v")
        K.w(TT, "s_a")
        K.mm(self.PS[banks[0]][:, 0:tn], self.ONEF[:], rowsrc[0:1, 0:tn],
             start=True, stop=True, inc=None if beta else ("s_pe", 1))
        if beta:
            K.mm(self.PS[banks[1]][:, 0:tn], self.ONEF[:], rowsrc[0:1, TILE:TILE + tn],
                 start=True, stop=True, inc=("s_pe", 1))
        K.w(A, "s_pe")
        K.op(A, "activation", abr[:, 0:tn], self.PS[banks[0]][:, 0:tn], AF.Identity)
        if beta:
            K.op(A, "activation", abr[:, TILE:TILE + tn], self.PS[banks[1]][:, 0:tn],
                 AF.Identity)
        K.di(A, "s_a")
        return K.ctr["s_a"]

    def ln_apply_chunk(self, xsrc, tn, abr, wcol, bcol, out_bf, scrf):
        K = self.K
        K.w(V, "s_a")
        K.op(V, "tensor_mul", scrf[:, 0:tn], xsrc, abr[:, 0:tn])
        K.op(V, "drain")
        K.op(V, "tensor_add", scrf[:, 0:tn], scrf[:, 0:tn], abr[:, TILE:TILE + tn])
        K.di(V, "s_v")
        K.w(A, "s_v")
        if wcol is not None:
            K.op(A, "activation", out_bf, scrf[:, 0:tn], AF.Identity, scale=wcol, bias=bcol)
        else:
            K.op(A, "activation", out_bf, scrf[:, 0:tn], AF.Identity)
        K.di(A, "s_a")
        return K.ctr["s_a"]

    def initial(self, nsl=T // SL, do_ln=True):
        K = self.K
        K.phase()
        TMJ = [K.sb(f"iTMJ{i}", [128, D], F32, local=True) for i in range(2)]
        TSQ = K.sb("iTSQ", [128, D], F32, local=True)
        AL = K.sb("iAL", [128, 4], F32, local=True)
        X, PS, IDF = self.X, self.PS, self.IDF
        K.w(TT, "s_g", self.g_masks)
        K.w(A, "s_g", self.g_masks)
        K.w(V, "s_g", self.g_masks)
        K.w(V, "s_v", self.prep_v)
        for s in range(nsl):
            par = s % 2
            K.w(SY, "s_pe")
            ld = K.dma(SY, TMJ[par][:], self.xin[s * SL:(s + 1) * SL, :])
            K.w(V, "s_x", ld)
            K.w(V, "s_pe")
            if not do_ln:
                K.op(V, "tensor_copy", TSQ[:], TMJ[par][:])
                K.di(V, "s_v")
                self._init_transpose(s, TSQ, par)
                continue
            K.op(V, "tensor_mul", TSQ[:], TMJ[par][:], TMJ[par][:])
            K.op(V, "reduce_sum", AL[:, 0:1], TMJ[par][:], axis=AX.X)
            K.op(V, "drain")
            K.op(V, "reduce_sum", AL[:, 1:2], TSQ[:], axis=AX.X)
            K.op(V, "drain")
            K.op(V, "tensor_scalar", AL[:, 0:1], AL[:, 0:1], 1.0 / D, None, ALU.mult)
            K.op(V, "tensor_scalar", AL[:, 1:2], AL[:, 1:2], 1.0 / D, None, ALU.mult)
            K.op(V, "drain")
            K.op(V, "tensor_mul", TSQ[:, 0:1], AL[:, 0:1], AL[:, 0:1])
            K.op(V, "drain")
            K.op(V, "tensor_sub", AL[:, 1:2], AL[:, 1:2], TSQ[:, 0:1])
            K.di(V, "s_v")
            K.w(A, "s_v")
            K.op(A, "activation", AL[:, 2:3], AL[:, 1:2], AF.Ln, bias=EPS_LN)
            K.op(A, "drain")
            K.op(A, "activation", AL[:, 2:3], AL[:, 2:3], AF.Exp, scale=-0.5)
            K.di(A, "s_a")
            K.w(V, "s_a")
            K.op(V, "scalar_tensor_tensor", AL[:, 3:4], AL[:, 0:1], -1.0, AL[:, 2:3],
                 ALU.mult, ALU.mult)
            K.op(V, "drain")
            K.op(V, "tensor_scalar", TSQ[:], TMJ[par][:], AL[:, 2:3], AL[:, 3:4],
                 ALU.mult, ALU.add)
            K.di(V, "s_v")
            self._init_transpose(s, TSQ, par)

    def _init_transpose(self, s, TSQ, par):
        K = self.K
        X, PS, IDF = self.X, self.PS, self.IDF
        K.w(TT, "s_v")
        for c in range(8):
            K.op(TT, "transpose", PS[c // 4][:, (c % 4) * 128:(c % 4 + 1) * 128],
                 TSQ[:, c * 128:(c + 1) * 128], IDF[:],
                 inc=("s_pe", 1) if c == 7 else None)
        K.w(V, "s_pe")
        K.w(A, "s_pe")
        for c in range(8):
            eng, meth = (V, "tensor_copy") if c < 4 else (A, "copy")
            K.op(eng, meth, X[c][:, s * SL:(s + 1) * SL],
                 PS[c // 4][:, (c % 4) * 128:(c % 4 + 1) * 128])
        K.di(V, "s_v")
        K.di(A, "s_a")
        K.w(TT, "s_v")
        K.w(TT, "s_a")

    def pass1(self, l):
        K = self.K
        K.phase()
        XBS = K.sb("p1XBS", [128, 2 * 8 * 128], BF16, local=True)
        KK = [K.sb(f"p1K{i}", [128, 512], BF16, local=True) for i in range(2)]
        VV = [K.sb(f"p1V{i}", [128, 512], BF16, local=True) for i in range(2)]
        USB = K.sb("p1U", [128, 1024], F32, local=True)
        DRW = K.sb("p1DRW", [2, 512], F32, local=True)
        SKK = K.sb("p1SKK", [2, 1024], BF16, local=True)
        SKV = K.sb("p1SKV", [2, 1024], BF16, local=True)
        MRIT = K.sb("p1MRI", [2, TILE], BF16, local=True)
        MRX = K.sb("p1MRX", [1, TILE], BF16, local=True)
        RSTDC = K.sb("p1RC", [128, 8], F32, local=True)
        GB1 = K.sb("p1GB", [128, 16], BF16, local=True)
        FINROW = self.ABR
        X, PS = self.X, self.PS
        WK = self.WS[l % 2]
        WV = self.WS[1 - l % 2]

        K.w(TT, "s_w", self.wld["wv"][l])
        K.op(V, "memset", USB[:], 0.0, inc=("s_v", 1))
        for kc in range(8):
            K.op(A, "activation", GB1[:, kc * 2:kc * 2 + 1],
                 self.PK["ln1_w"][:, l * 8 + kc:l * 8 + kc + 1], AF.Identity)
            K.op(A, "activation", GB1[:, kc * 2 + 1:kc * 2 + 2],
                 self.PK["ln1_b"][:, l * 8 + kc:l * 8 + kc + 1], AF.Identity)
        K.di(A, "s_a")
        gb1_done = K.ctr["s_a"]

        def skck(W, DST):
            K.w(TT, "s_a", gb1_done)
            for half in range(2):
                K.w(TT, "s_a")
                for kc in range(8):
                    K.mm(PS[0][0:2, 0:512], GB1[:, kc * 2:kc * 2 + 2],
                         W[:, kc * 1024 + half * 512: kc * 1024 + (half + 1) * 512],
                         start=(kc == 0), stop=(kc == 7),
                         inc=("s_pe", 1) if kc == 7 else None)
                pe = K.ctr["s_pe"]
                K.w(A, "s_pe", pe)
                K.op(A, "activation", DST[0:2, half * 512:(half + 1) * 512],
                     PS[0][0:2, 0:512], AF.Identity, inc=("s_a", 1))
            return K.ctr["s_a"]

        skv_done = skck(WV, SKV)
        skk_done = [0]

        self.rows_x = [0] * NT
        self.mrid1_wr = [0] * NT
        fin_done = [0] * NT
        rc_tr = [0] * NT
        mrit_ld = 0
        fix_last = 0
        kfree = [0, 0]
        vfree = [0, 0]
        kkfree = [0, 0]
        xbs_use = [0, 0]

        def stats1(t):
            K.w(TT, "s_a")
            self.ln_stats_rows((t * TILE, TILE), 0, self.ROW, EPS_LN, scr=self.SCR2,
                               raw=True, precise=True)
            fin_done[t] = self.finish2(t, t % 2, FINROW, MRX,
                                       rc_tr[t - 2] if t >= 2 else 0,
                                       self.mrid1.ap()[t:t + 1, :],
                                       self.rowsd.ap()[t][0:1, 0:TILE])
            self.mrid1_wr[t] = self.rows_x[t] = self.rows2_wr[t]

        stats1(0)
        stats1(1)

        pend = None

        def emit_u(kb, half, ev_done, isfirst, islast):
            K.w(TT, "s_a", ev_done)
            for hh in range(4):
                hcol = hh * 128
                K.mm(PS[kb][:, hcol:hcol + 128], KK[half][:, hcol:hcol + 128],
                     VV[half][:, hcol:hcol + 128], start=True, stop=True)
            K.mm(PS[6][0:2, 0:512], self.QMD[:, half * 2:half * 2 + 2],
                 KK[half][:], start=isfirst, stop=islast, inc=("s_pe", 1))
            kkfree[half] = K.ctr["s_pe"]
            K.w(V, "s_pe", kkfree[half])
            K.op(V, "tensor_add", USB[:, half * 512:(half + 1) * 512],
                 USB[:, half * 512:(half + 1) * 512], PS[kb][:], inc=("s_v", 1))
            kfree[half] = K.ctr["s_v"]

        for t in range(NT):
            t0 = t * TILE
            par = t % 2
            K.w(TT, "s_a", fin_done[t])
            K.w(TT, "s_a")
            for sl in range(4):
                K.op(TT, "transpose", PS[0][:, sl:sl + 1],
                     FINROW[0:1, par * TILE + sl * 128: par * TILE + sl * 128 + 128],
                     self.IDF[0:1, 0:1], inc=("s_pe", 1) if sl == 3 else None)
            rc_tr[t] = K.ctr["s_pe"]
            K.w(A, "s_pe", rc_tr[t])
            K.op(A, "activation", RSTDC[:, par * 4:par * 4 + 4], PS[0][:, 0:4],
                 AF.Identity, inc=("s_a", 1))
            K.w(SY, "s_pe", fix_last)
            K.w(SY, "s_st", self.mrid1_wr[t])
            mrit_ld = K.dma(SY, MRIT[0:2, :],
                            self.mrid1.ap()[t].rearrange("(a b) -> a b", a=2), "s_w2a")
            for s in range(4):
                sp = s % 2
                K.w(A, "s_pe", xbs_use[sp])
                for c in range(8):
                    K.op(A, "activation",
                         XBS[:, sp * 1024 + c * 128: sp * 1024 + (c + 1) * 128],
                         X[c][:, t0 + s * 128: t0 + (s + 1) * 128], AF.Identity,
                         scale=self.PK["ln1_w"][:, l * 8 + c:l * 8 + c + 1],
                         inc=("s_a", 1) if c == 7 else None)
                cast_a = K.ctr["s_a"]
                for half in range(2):
                    kb, vb = 2 + half, 4 + half
                    K.w(TT, "s_a", cast_a)
                    K.w(TT, "s_a", vfree[half])
                    K.w(TT, "s_w2a", mrit_ld)
                    K.w(TT, "s_a", skv_done)
                    for kc in range(8):
                        K.mm(PS[vb][:],
                             XBS[:, sp * 1024 + kc * 128: sp * 1024 + (kc + 1) * 128],
                             WV[:, kc * 1024 + half * 512: kc * 1024 + (half + 1) * 512],
                             start=(kc == 0), stop=False)
                    K.mm(PS[vb][:], MRIT[0:2, s * 128:(s + 1) * 128],
                         SKV[0:2, half * 512:(half + 1) * 512],
                         start=False, stop=True, inc=("s_pe", 1))
                    v_done = K.ctr["s_pe"]
                    if t == 0 and s == 0 and half == 0:
                        K.w(TT, self.wk_sem[l], self.wld["wk"][l])
                        skk_done[0] = skck(WK, SKK)
                    K.w(TT, "s_v", kfree[half])
                    K.w(TT, "s_a", skk_done[0])
                    for kc in range(8):
                        K.mm(PS[kb][:],
                             XBS[:, sp * 1024 + kc * 128: sp * 1024 + (kc + 1) * 128],
                             WK[:, kc * 1024 + half * 512: kc * 1024 + (half + 1) * 512],
                             start=(kc == 0), stop=False)
                    K.mm(PS[kb][:], MRIT[0:2, s * 128:(s + 1) * 128],
                         SKK[0:2, half * 512:(half + 1) * 512],
                         start=False, stop=True, inc=("s_pe", 1))
                    k_done = K.ctr["s_pe"]
                    K.w(A, "s_pe", v_done)
                    K.w(A, "s_pe", kkfree[half])
                    K.op(A, "activation", VV[half][:], PS[vb][:], AF.Identity,
                         scale=RSTDC[:, par * 4 + s:par * 4 + s + 1], inc=("s_a", 1))
                    vfree[half] = K.ctr["s_a"]
                    K.w(A, "s_pe", k_done)
                    K.op(A, "activation", KK[half][:], PS[kb][:], AF.Exp,
                         scale=RSTDC[:, par * 4 + s:par * 4 + s + 1], inc=("s_a", 1))
                    ev_done = K.ctr["s_a"]
                    if pend is not None:
                        emit_u(*pend)
                    pend = (kb, half, ev_done, t == 0 and s == 0 and half == 0,
                            t == NT - 1 and s == 3 and half == 1)
                xbs_use[sp] = K.ctr["s_pe"]
            fix_last = K.ctr["s_pe"]
            if t + 2 < NT:
                stats1(t + 2)
        emit_u(*pend)
        K.w(A, "s_pe")
        K.op(A, "activation", DRW[0:2, :], PS[6][0:2, 0:512], AF.Identity)
        K.di(A, "s_a")
        K.w(SY, "s_v")
        K.w(SY, "s_a")
        K.dma(SY, self.cc_in.ap()[0:1024, :].rearrange("(h d) e -> d h e", d=128),
              USB[:].rearrange("p (h e) -> p h e", h=8))
        n = K.dma(SY, self.cc_in.ap()[1024:1032, :].rearrange("(a h) d -> a h d", a=2),
                  DRW[0:2, :].rearrange("a (h d) -> a h d", h=4))
        K.w(SY, "s_x")
        K.w(G, "s_x", n)
        K.op(G, "collective_compute", "AllReduce", ALU.add,
             replica_groups=RG,
             ins=[self.cc_in.ap()], outs=[self.cc_out.ap()], inc=("s_cc", 1))
        K.w(G, "s_pe")
        self.load_wb(WK[:], "wq", l)
        self.wld["wq"][l] = self.wld["wo"][l] = self.load_wb(WV[:], "wo", l)
        K.w(G, "s_w")
        if l + 1 < DEPTH:
            self.cast_w12(l + 1)
            K.w(G, "s_c")
            self.cast_qkvo(l + 1)
            K.w(G, "s_c")

    def pass2a_prep(self, l):
        K = self.K
        PS = self.PS
        USB2 = K.sb("p2U", [128, 1024], F32, local=True)
        DTMP = K.sb("p2DT", [8, 128], F32, local=True)
        DCOL = K.sb("p2DC", [128, 8], F32, local=True)
        K.w(G, "s_cc")
        K.dma(G, DTMP[:], self.cc_out.ap()[1024:1032, :], "s_w")
        n = K.dma(G, USB2[:].rearrange("p (h e) -> p h e", h=8),
                  self.cc_out.ap()[0:1024, :].rearrange("(h d) e -> d h e", d=128), "s_w")
        K.w(G, "s_w")
        K.w(TT, "s_w", n)
        K.w(TT, "s_v")
        K.op(TT, "transpose", PS[0][:, 0:8], DTMP[:], self.IDF[0:8, 0:8],
             inc=("s_pe", 1))
        K.w(A, "s_pe")
        K.op(A, "activation", DCOL[:], PS[0][:, 0:8], AF.Ln)
        K.op(A, "drain")
        K.op(A, "activation", DCOL[:], DCOL[:], AF.Exp, scale=-1.0)
        K.di(A, "s_a")
        self._prep_a = K.ctr["s_a"]
        K.w(V, "s_a")
        K.w(V, "s_pe")
        for h in range(8):
            K.op(V, "tensor_scalar", self.CTX[:, h * 128:(h + 1) * 128],
                 USB2[:, h * 128:(h + 1) * 128], DCOL[:, h:h + 1], None, ALU.mult)
        K.di(V, "s_v")
        self.ctx_v = K.ctr["s_v"]

    def pass2a(self, l):
        K = self.K
        K.phase()
        self.pass2a_prep(l)
        X, PS = self.X, self.PS
        XB = K.sb("p2XB", [128, 8 * TILE], BF16, local=True)
        QE = K.sb("p2QE", [128, 8 * TILE], BF16, local=True)
        ATT = QE
        SEL8B = K.sb("p2SEL", [8, 8 * 128], BF16, local=True)
        QROWB = K.sb("p2QR", [8, TILE], BF16, local=True)
        MRIT = K.sb("p2MRI", [2, TILE], BF16, local=True)
        SKQ = K.sb("p2SKQ", [2, 1024], BF16, local=True)
        GB1 = K.sb("p2GB", [128, 16], BF16, local=True)
        RSTDB = self.ABR
        WQ = self.WS[l % 2]
        WO = self.WS[1 - l % 2]
        SCRX = [self.SCR, self.SCR2]

        K.op(G, "memset", SEL8B[:], 1.0)
        K.op(G, "drain")
        K.op(G, "affine_select", out=SEL8B[:], in_=SEL8B[:], compare_op=ALU.is_gt,
             fill=0.0, base=1, pattern=[[1, 1024]], channel_multiplier=-128)
        K.op(G, "drain")
        K.op(G, "affine_select", out=SEL8B[:], in_=SEL8B[:], compare_op=ALU.is_gt,
             fill=0.0, base=128, pattern=[[-1, 1024]], channel_multiplier=128)
        K.op(G, "drain", inc=("s_g", 1))
        sel8_g = K.ctr["s_g"]

        for kc in range(8):
            K.op(A, "activation", GB1[:, kc * 2:kc * 2 + 1],
                 self.PK["ln1_w"][:, l * 8 + kc:l * 8 + kc + 1], AF.Identity)
            K.op(A, "activation", GB1[:, kc * 2 + 1:kc * 2 + 2],
                 self.PK["ln1_b"][:, l * 8 + kc:l * 8 + kc + 1], AF.Identity)
        K.di(A, "s_a")
        gb1_done = K.ctr["s_a"]
        K.w(TT, "s_w", self.wld["wq"][l])
        K.w(TT, "s_a", gb1_done)
        skq_done = 0
        for half in range(2):
            K.w(TT, "s_a")
            for kc in range(8):
                K.mm(PS[4][0:2, 0:512], GB1[:, kc * 2:kc * 2 + 2],
                     WQ[:, kc * 1024 + half * 512: kc * 1024 + (half + 1) * 512],
                     start=(kc == 0), stop=(kc == 7),
                     inc=("s_pe", 1) if kc == 7 else None)
            pe = K.ctr["s_pe"]
            K.w(A, "s_pe", pe)
            K.op(A, "activation", SKQ[0:2, half * 512:(half + 1) * 512],
                 PS[4][0:2, 0:512], AF.Identity, inc=("s_a", 1))
            skq_done = K.ctr["s_a"]

        rowld = 0
        mritld = 0
        bc_prev = 0
        rstdb_evac = 0
        rstdb_war = 0
        xb_use = 0
        fix_last = 0
        qe_war = 0
        scr_war = [skq_done, skq_done]
        qev = [self.ctx_v, self.ctx_v]
        aev_v, aev_a = 0, self._prep_a
        wev = [0, 0]
        qr_prev = 0
        for t in range(NT):
            t0 = t * TILE
            K.w(SY, "s_pe", bc_prev)
            K.w(SY, "s_st", self.rows_x[t])
            rowld = K.dma(SY, self.ROW[0:1, 0:TILE],
                          self.rowsd.ap()[t][0:1, 0:TILE], "s_w2b")
            K.w(TT, "s_w2b", rowld)
            K.w(TT, "s_a", rstdb_evac if t > 0 else skq_done)
            K.w(TT, "s_a", qr_prev)
            K.mm(PS[4][:, 0:TILE], self.ONEF[:], self.ROW[0:1, 0:TILE],
                 start=True, stop=True, inc=("s_pe", 1))
            bc_prev = K.ctr["s_pe"]
            K.w(A, "s_pe", bc_prev)
            K.w(A, "s_v", rstdb_war)
            K.op(A, "activation", RSTDB[:, 0:TILE], PS[4][:, 0:TILE], AF.Identity,
                 inc=("s_a", 1))
            rstdb_evac = K.ctr["s_a"]
            K.w(SY, "s_pe", fix_last)
            K.w(SY, "s_st", self.mrid1_wr[t])
            mritld = K.dma(SY, MRIT[0:2, :],
                           self.mrid1.ap()[t].rearrange("(a b) -> a b", a=2), "s_w2a")
            K.w(A, "s_pe", xb_use)
            for c in range(8):
                K.op(A, "activation", XB[:, c * TILE:(c + 1) * TILE],
                     X[c][:, t0:t0 + TILE], AF.Identity,
                     scale=self.PK["ln1_w"][:, l * 8 + c:l * 8 + c + 1],
                     inc=("s_a", 1) if c == 7 else None)
            cast_a = K.ctr["s_a"]
            for m in range(8):
                qb = 2 + m % 2
                K.w(TT, "s_a", cast_a)
                K.w(TT, "s_w2a", mritld)
                K.w(TT, "s_v", qev[m % 2])
                for kc in range(8):
                    K.mm(PS[qb][:], WQ[:, kc * 1024 + m * 128: kc * 1024 + (m + 1) * 128],
                         XB[:, kc * TILE:(kc + 1) * TILE], start=(kc == 0), stop=False)
                K.mm(PS[qb][:], SKQ[0:2, m * 128:(m + 1) * 128], MRIT[0:2, :],
                     start=False, stop=True, inc=("s_pe", 1))
                qdone = K.ctr["s_pe"]
                K.w(V, "s_pe", qdone)
                K.w(V, "s_a", rstdb_evac)
                K.w(V, "s_a", scr_war[m % 2])
                K.op(V, "tensor_mul", SCRX[m % 2][:], PS[qb][:], RSTDB[:, 0:TILE],
                     inc=("s_v", 1))
                qev[m % 2] = K.ctr["s_v"]
                K.w(A, "s_v", qev[m % 2])
                K.w(A, "s_pe", qe_war)
                K.op(A, "activation", QE[:, m * TILE:(m + 1) * TILE], SCRX[m % 2][:],
                     AF.Exp, inc=("s_a", 1))
                scr_war[m % 2] = K.ctr["s_a"]
                K.w(TT, "s_a", scr_war[m % 2])
                if m == 0:
                    K.w(TT, "s_a", rstdb_evac)
                K.mm(PS[4][0:8, 0:TILE], self.QMASK[:, m * 8:(m + 1) * 8],
                     QE[:, m * TILE:(m + 1) * TILE], start=(m == 0), stop=(m == 7),
                     inc=("s_pe", 1) if m == 7 else None)
            xb_use = K.ctr["s_pe"]
            fix_last = xb_use
            qd = K.ctr["s_pe"]
            K.w(A, "s_pe", qd)
            K.op(A, "activation", self.SCR[0:8, :], PS[4][0:8, 0:TILE], AF.Ln)
            K.op(A, "drain")
            K.op(A, "activation", QROWB[0:8, :], self.SCR[0:8, 0:TILE], AF.Exp,
                 scale=-1.0, inc=("s_a", 1))
            qr_done = K.ctr["s_a"]
            qr_prev = qr_done
            scr_war[0] = qr_done
            if t == 0:
                K.w(TT, "s_v", self.ctx_v)
                K.w(TT, "s_g", sel8_g)
            for h in range(8):
                ab = 5 if h % 2 == 0 else 6
                bb = 0 if h % 2 == 0 else 1
                K.w(TT, "s_a", qr_done)
                K.w(TT, "s_v", aev_v)
                K.w(TT, "s_a", aev_a)
                K.mm(PS[ab][:], self.CTX[:, h * 128:(h + 1) * 128],
                     QE[:, h * TILE:(h + 1) * TILE], start=True, stop=True)
                K.mm(PS[bb][:], SEL8B[:, h * 128:(h + 1) * 128], QROWB[0:8, :],
                     start=True, stop=True, inc=("s_pe", 1))
                hdone = K.ctr["s_pe"]
                K.w(A, "s_pe", hdone)
                K.w(A, "s_v")
                K.op(A, "activation", self.SCR2[:], PS[bb][:], AF.Identity,
                     inc=("s_a", 1))
                aev_a = K.ctr["s_a"]
                K.w(V, "s_a", aev_a)
                K.op(V, "tensor_mul", ATT[:, h * TILE:(h + 1) * TILE], PS[ab][:],
                     self.SCR2[:], inc=("s_v", 1))
                aev_v = K.ctr["s_v"]
            for m in range(8):
                ob = 5 if m % 2 == 0 else 6
                K.w(TT, "s_v", wev[m % 2])
                K.w(TT, "s_v", aev_v)
                for kc in range(8):
                    K.mm(PS[ob][:], WO[:, kc * 1024 + m * 128: kc * 1024 + (m + 1) * 128],
                         ATT[:, kc * TILE:(kc + 1) * TILE], start=(kc == 0),
                         stop=(kc == 7), inc=("s_pe", 1) if kc == 7 else None)
                odone = K.ctr["s_pe"]
                K.w(V, "s_pe", odone)
                K.op(V, "scalar_tensor_tensor", X[m][:, t0:t0 + TILE], PS[ob][:],
                     self.PK["bo"][:, l * 8 + m:l * 8 + m + 1], X[m][:, t0:t0 + TILE],
                     ALU.add, ALU.add, inc=("s_v", 1))
                wev[m % 2] = K.ctr["s_v"]
            qe_war = K.ctr["s_pe"]
        K.w(G, "s_pe")
        K.w(G, "s_c", self.w12_cast[l])
        self.load_eighth(l, 0, G)

    def load_eighth(self, l, e, eng):
        K = self.K
        b = (l + e) % 2
        sem = "s_w1a" if b == 0 else "s_w1b"
        WSb = self.WS[b]
        K.dma(eng, WSb[:, 0:4096].rearrange("p (m c) -> p m c", m=4),
              self.w1_s.ap()[l, 4 * e:4 * e + 4].rearrange("m p c -> p m c"), sem)
        self.pair_ld[b] = K.dma(
            eng, WSb[:, 4096:8192].rearrange("p (j c) -> p j c", j=8),
            self.w2_s.ap()[l, :, :, e * 512:(e + 1) * 512].rearrange("j p c -> p j c"), sem)
        return self.pair_ld[b]

    def finish2(self, t, par, FINROW, MRX, bc_war, mrd=None, rowd=None):
        K = self.K
        K.w(V, "s_a")
        K.op(V, "tensor_mul", self.SCR2[0:1, 0:TILE], self.ROW[0:1, 0:TILE],
             self.ROW[0:1, 0:TILE])
        K.op(V, "drain")
        K.op(V, "tensor_sub", self.ROW[0:1, TILE:2 * TILE],
             self.ROW[0:1, TILE:2 * TILE], self.SCR2[0:1, 0:TILE])
        K.di(V, "s_v")
        if mrd is None:
            mrd = self.mrid2.ap()[t:t + 1, :]
        if rowd is None:
            rowd = self.rowsd2.ap()[t:t + 1, :]
        K.w(A, "s_v")
        K.w(A, "s_st", self.mrx_dma)
        K.w(A, "s_pe", bc_war)
        K.op(A, "activation", MRX[0:1, 0:TILE], self.ROW[0:1, 0:TILE], AF.Identity,
             scale=-1.0, inc=("s_a", 1))
        fa1 = self.K.ctr["s_a"]
        K.w(SY, "s_a", fa1)
        n1 = K.dma(SY, mrd[0:1, 0:TILE], MRX[0:1, 0:TILE], "s_st")
        K.w(SY, "s_st", n1)
        K.w(A, "s_st", n1)
        K.op(A, "activation", self.ROW[0:1, TILE:2 * TILE],
             self.ROW[0:1, TILE:2 * TILE], AF.Ln, bias=EPS_LN)
        K.op(A, "drain")
        K.op(A, "activation", MRX[0:1, 0:TILE], self.ROW[0:1, TILE:2 * TILE],
             AF.Exp, scale=0.5)
        K.op(A, "activation", FINROW[0:1, par * TILE:(par + 1) * TILE],
             self.ROW[0:1, TILE:2 * TILE], AF.Exp, scale=-0.5)
        K.di(A, "s_a")
        fin_a = self.K.ctr["s_a"]
        K.w(SY, "s_a", fin_a)
        K.dma(SY, mrd[0:1, TILE:2 * TILE], MRX[0:1, 0:TILE], "s_st")
        self.mrx_dma = K.dma(SY, rowd,
                             FINROW[0:1, par * TILE:(par + 1) * TILE], "s_st")
        K.w(SY, "s_st", self.mrx_dma)
        self.rows2_wr[t] = self.mrx_dma
        return fin_a

    def pass2b(self, l):
        K = self.K
        K.phase()
        X, PS = self.X, self.PS
        XB2P = K.sb("p3XB", [128, 2 * 8 * TILE], BF16, local=True)
        HID = K.sb("p3HID", [128, 2 * 4 * TILE], BF16, local=True)
        RSTD2B = K.sb("p3RSB", [128, TILE], F32, local=True)
        MRIT2 = K.sb("p3MRI", [2, TILE], BF16, local=True)
        MRX = K.sb("p3MRX", [1, TILE], BF16, local=True)
        FINROW = self.ABR
        S1C1 = K.sb("p3SC", [2, 4 * 128], BF16, local=True)
        GB2 = K.sb("p3GB", [128, 16], BF16, local=True)

        for kc in range(8):
            K.op(A, "activation", GB2[:, kc * 2:kc * 2 + 1],
                 self.PK["ln2_w"][:, l * 8 + kc:l * 8 + kc + 1], AF.Identity)
            K.op(A, "activation", GB2[:, kc * 2 + 1:kc * 2 + 2],
                 self.PK["ln2_b"][:, l * 8 + kc:l * 8 + kc + 1], AF.Identity)
        K.di(A, "s_a")
        gb2_done = K.ctr["s_a"]

        hid_free = [0, 0]
        hid_done = [0] * NT
        hid_rd = [0, 0]
        pend_t = None
        y_free = [0, 0]
        xb2_use = [0, 0]
        xb2_wr = [0] * NT
        cast_done = [0] * NT
        fin_done = [0] * NT
        mrid2_wr = [0] * NT
        fix_done = 0
        mrit2_ld = 0
        rstdb_evac = 0
        rstdb_war = 0
        bc_done = [0] * NT
        row_ld = 0
        prev_eighth_pe = 0
        eighth_pe = 0

        def stats2(t):
            K.w(TT, "s_a")
            self.ln_stats_rows((t * TILE, TILE), 0, self.ROW, EPS_LN, scr=self.SCR2,
                               raw=True, precise=True)
            fin_done[t] = self.finish2(t, t % 2, FINROW, MRX,
                                       bc_done[t - 2] if t >= 2 else 0)
            mrid2_wr[t] = self.mrx_dma

        stats2(0)
        stats2(1)

        for e in range(8):
            be = (l + e) % 2
            sem_e = "s_w1a" if be == 0 else "s_w1b"
            pair = self.pair_ld[be]
            e_gate = eighth_pe
            if e + 1 < 8:
                K.w(G, "s_pe", e_gate)
                K.w(G, "s_c", self.w12_cast[l])
                self.load_eighth(l, e + 1, G)
            K.w(TT, sem_e, pair)
            K.w(TT, "s_a", gb2_done)
            K.w(TT, "s_a", rstdb_evac)
            K.w(TT, "s_pe", fix_done)
            for m in range(4):
                for kc in range(8):
                    K.mm(PS[6][0:2, m * 128:(m + 1) * 128], GB2[:, kc * 2:kc * 2 + 2],
                         self.WS[be][:, m * 1024 + kc * 128:m * 1024 + (kc + 1) * 128],
                         start=(kc == 0), stop=(kc == 7),
                         inc=("s_pe", 1) if (m == 3 and kc == 7) else None)
            s1c1_pe = K.ctr["s_pe"]
            K.w(A, "s_pe", s1c1_pe)
            K.op(A, "activation", S1C1[0:2, :], PS[6][0:2, 0:512], AF.Identity,
                 inc=("s_a", 1))
            s1c1_done = K.ctr["s_a"]

            for t in range(NT):
                t0 = t * TILE
                buf = t % 2
                if e > 0:
                    K.w(SY, "s_pe", bc_done[t - 1] if t >= 1 else bc_done[NT - 1])
                    row_ld = K.dma(SY, FINROW[0:1, buf * TILE:(buf + 1) * TILE],
                                   self.rowsd2.ap()[t:t + 1, :], "s_w2b")
                K.w(TT, "s_a", rstdb_evac)
                K.w(TT, "s_a", s1c1_done)
                if e == 0:
                    K.w(TT, "s_a", fin_done[t])
                else:
                    K.w(TT, "s_w2b", row_ld)
                K.mm(PS[6][:, 0:TILE], self.ONEF[:],
                     FINROW[0:1, buf * TILE:(buf + 1) * TILE],
                     start=True, stop=True, inc=("s_pe", 1))
                bc_done[t] = K.ctr["s_pe"]
                K.w(A, "s_pe", bc_done[t])
                K.w(A, "s_v", rstdb_war)
                K.op(A, "activation", RSTD2B[:], PS[6][:, 0:TILE], AF.Identity,
                     inc=("s_a", 1))
                rstdb_evac = K.ctr["s_a"]
                xb = XB2P[:, buf * 8 * TILE:(buf + 1) * 8 * TILE]
                if e == 0:
                    K.w(A, "s_pe", xb2_use[buf])
                    if t >= 2:
                        K.w(A, "s_xia" if buf == 0 else "s_xib", xb2_wr[t - 2])
                    for c in range(8):
                        K.op(A, "activation", xb[:, c * TILE:(c + 1) * TILE],
                             X[c][:, t0:t0 + TILE], AF.Identity,
                             scale=self.PK["ln2_w"][:, l * 8 + c:l * 8 + c + 1],
                             inc=("s_a", 1) if c == 7 else None)
                    cast_done[t] = K.ctr["s_a"]
                    K.w(SY, "s_a", cast_done[t])
                    xb2_wr[t] = K.dma(SY, self.xb2d.ap()[:, :, t0:t0 + TILE],
                                      xb.rearrange("p (c w) -> p c w", c=8),
                                      "s_xia" if buf == 0 else "s_xib")
                    xb_rdy = ("s_a", cast_done[t])
                else:
                    K.w(SY, "s_pe", xb2_use[buf])
                    if e == 1:
                        K.w(SY, "s_xia" if buf == 0 else "s_xib",
                            xb2_wr[NT - 2 + buf])
                    n = K.dma(SY, xb.rearrange("p (c w) -> p c w", c=8),
                              self.xb2d.ap()[:, :, t0:t0 + TILE],
                              "s_xia" if buf == 0 else "s_xib")
                    xb_rdy = ("s_xia" if buf == 0 else "s_xib", n)
                K.w(SY, "s_pe", fix_done)
                mrit2_ld = K.dma(SY, MRIT2[0:2, :],
                                 self.mrid2.ap()[t].rearrange("(a b) -> a b", a=2),
                                 "s_w2a")
                def w2j(j, pt):
                    jb = 4 + j % 2
                    pb = pt % 2
                    K.w(TT, "s_v", hid_done[pt])
                    K.w(TT, "s_v", y_free[j % 2])
                    for mi in range(4):
                        K.mm(PS[jb][:, 0:TILE],
                             self.WS[be][:, 4096 + j * 512 + mi * 128:
                                         4096 + j * 512 + (mi + 1) * 128],
                             HID[:, pb * 4 * TILE + mi * TILE:
                                 pb * 4 * TILE + (mi + 1) * TILE],
                             start=(mi == 0), stop=(mi == 3),
                             inc=("s_pe", 1) if mi == 3 else None)
                    jdone = K.ctr["s_pe"]
                    K.w(V, "s_pe", jdone)
                    K.op(V, "tensor_add", X[j][:, pt * TILE:(pt + 1) * TILE],
                         X[j][:, pt * TILE:(pt + 1) * TILE],
                         PS[jb][:, 0:TILE], inc=("s_v", 1))
                    y_free[j % 2] = K.ctr["s_v"]

                for m in range(4):
                    hb = 2 + m % 2
                    K.w(TT, sem_e, pair)
                    K.w(TT, xb_rdy[0], xb_rdy[1])
                    K.w(TT, "s_w2a", mrit2_ld)
                    K.w(TT, "s_a", s1c1_done)
                    K.w(TT, "s_v", hid_free[m % 2])
                    for kc in range(8):
                        K.mm(PS[hb][:, 0:TILE],
                             self.WS[be][:, m * 1024 + kc * 128:m * 1024 + (kc + 1) * 128],
                             xb[:, kc * TILE:(kc + 1) * TILE],
                             start=(kc == 0), stop=False)
                    K.mm(PS[hb][:, 0:TILE], S1C1[0:2, m * 128:(m + 1) * 128],
                         MRIT2[0:2, :], start=False, stop=True, inc=("s_pe", 1))
                    hdone = K.ctr["s_pe"]
                    K.w(A, "s_pe", hdone)
                    K.w(A, "s_v")
                    K.op(A, "activation", self.SCR[:], PS[hb][:, 0:TILE], AF.Identity,
                         scale=LEAK, inc=("s_a", 1))
                    K.w(V, "s_a")
                    K.w(V, "s_a", rstdb_evac)
                    K.w(V, "s_pe", hid_rd[buf])
                    K.op(V, "tensor_tensor", self.SCR[:], PS[hb][:, 0:TILE],
                         self.SCR[:], ALU.max)
                    K.op(V, "drain")
                    K.op(V, "tensor_mul",
                         HID[:, buf * 4 * TILE + m * TILE:
                             buf * 4 * TILE + (m + 1) * TILE],
                         self.SCR[:], RSTD2B[:], inc=("s_v", 1))
                    hid_free[m % 2] = K.ctr["s_v"]
                    if pend_t is not None:
                        w2j(2 * m, pend_t)
                        w2j(2 * m + 1, pend_t)
                fix_done = K.ctr["s_pe"]
                xb2_use[buf] = fix_done
                if pend_t is not None:
                    hid_rd[pend_t % 2] = K.ctr["s_pe"]
                hid_done[t] = K.ctr["s_v"]
                rstdb_war = hid_done[t]
                pend_t = t
                eighth_pe = K.ctr["s_pe"]
                if e == 0 and t + 2 < NT:
                    stats2(t + 2)
            for j in range(8):
                w2j(j, pend_t)
            hid_rd[pend_t % 2] = K.ctr["s_pe"]
            pend_t = None
            eighth_pe = K.ctr["s_pe"]
            if e == 7 and l + 1 < DEPTH:
                K.w(G, "s_pe", e_gate)
                self.wld["wv"][l + 1] = self.load_wb(self.WS[l % 2][:], "wv", l + 1)
        if l + 1 < DEPTH:
            K.w(G, "s_pe")
            self.wld["wk"][l + 1] = self.load_wb(self.WS[(l + 1) % 2][:], "wk",
                                                 l + 1, "s_ac")
            self.wk_sem[l + 1] = "s_ac"

    def x1_form(self):
        K = self.K
        K.phase()
        TMJ = [K.sb(f"fTMJ{i}", [128, D], F32, local=True) for i in range(2)]
        X, PS, IDF = self.X, self.PS, self.IDF
        abrs = [self.ABR, self.ABR]
        self.ln_stats_rows((0, TILE), 0, self.ROW, EPS_LN)
        ab_done = self.ln_bcast(self.ROW, TILE, (0, 1), abrs[0])
        for t in range(NT):
            t0 = t * TILE
            abr = abrs[t % 2]
            for s in range(4):
                par = s % 2
                K.w(SY, "s_pe")
                ld = K.dma(SY, TMJ[par][:], self.xin[t0 + s * 128: t0 + (s + 1) * 128, :])
                K.w(TT, "s_x", ld)
                K.w(TT, "s_v")
                for c in range(8):
                    K.op(TT, "transpose", PS[2 + (c % 4)][:, (c // 4) * 128:(c // 4) * 128 + 128],
                         TMJ[par][:, c * 128:(c + 1) * 128], IDF[:],
                         inc=("s_pe", 1) if c == 7 else None)
                trd = K.ctr["s_pe"]
                K.w(V, "s_pe", trd)
                K.w(V, "s_a", ab_done)
                for c in range(8):
                    cs = slice(t0 + s * 128, t0 + (s + 1) * 128)
                    K.op(V, "tensor_mul", self.SCR[:, 0:128], X[c][:, cs],
                         abr[:, s * 128:(s + 1) * 128])
                    K.op(V, "drain")
                    K.op(V, "tensor_add", self.SCR[:, 0:128], self.SCR[:, 0:128],
                         abr[:, TILE + s * 128:TILE + (s + 1) * 128])
                    K.op(V, "drain")
                    K.op(V, "tensor_add", X[c][:, cs], self.SCR[:, 0:128],
                         PS[2 + (c % 4)][:, (c // 4) * 128:(c // 4) * 128 + 128])
                    K.op(V, "drain")
                K.di(V, "s_v")
            K.w(SY, "s_v")
            for c in range(8):
                self.spill_x = K.dma(SY, self.x1sp.ap()[c, :, t0:t0 + TILE],
                                     X[c][:, t0:t0 + TILE])
            if t + 1 < NT:
                self.ln_stats_rows(((t + 1) * TILE, TILE), 0, self.ROW, EPS_LN)
                ab_done = self.ln_bcast(self.ROW, TILE, (0, 1), abrs[(t + 1) % 2])

    def ffn_s1_s2(self):
        K = self.K
        K.phase()
        X, PS = self.X, self.PS
        HB = K.sb("s2HB", [128, 2], F32, local=True)
        for t in range(NT):
            self.ln_stats_rows((t * TILE, TILE), 0, self.ROW, EPS_LN)
            K.w(SY, "s_v")
            self.rows_x[t] = K.dma(SY, self.rowsd.ap()[t], self.ROW[0:1, :])
            K.w(SY, "s_x")
            if t == NT - 1:
                K.w(TT, "s_v")
                K.mm(PS[1][:, 0:2], self.ONEF[:], self.ROW[0:1, 510:512],
                     start=True, stop=True)
                K.mm(PS[2][:, 0:2], self.ONEF[:], self.ROW[0:1, TILE + 510:TILE + 512],
                     start=True, stop=True, inc=("s_pe", 1))
                K.w(V, "s_pe")
                nh = 0
                for c in range(8):
                    K.w(V, "s_x")
                    K.op(V, "tensor_mul", HB[:, 0:2], X[c][:, 4094:4096], PS[1][:, 0:2])
                    K.op(V, "drain")
                    K.op(V, "tensor_add", HB[:, 0:2], HB[:, 0:2], PS[2][:, 0:2])
                    K.di(V, "s_v")
                    K.w(SY, "s_v")
                    nh = K.dma(SY, self.hl_in.ap()[c * 128:(c + 1) * 128, :], HB[:, 0:2])
                K.w(G, "s_x", nh)
                K.op(G, "collective_compute", "AllGather", ALU.bypass,
                     replica_groups=RG,
                     ins=[self.hl_in.ap()], outs=[self.hl_out.ap()], inc=("s_cc", 1))
                K.w(G, "s_cc")
                nhl = K.dma(G, self.HALO[:].rearrange("p (c t) -> p c t", c=8),
                            self.hl_out.ap()[0:1024, :].rearrange("(c p) t -> p c t", p=128),
                            "s_w")
                K.w(G, "s_w")
                K.w(V, "s_w", nhl)
                K.op(V, "tensor_scalar", self.HALO[:], self.HALO[:], self.MCOL[:, 0:1],
                     None, ALU.mult)
                K.di(V, "s_v")

        XA = K.sb("s2XA", [128, 8 * (TILE + 2)], F32, local=True)
        XC = K.sb("s2XC", [128, 8 * TILE], BF16, local=True)
        PWT = self.WS[0]
        K.w(SY, "s_x")
        K.w(SY, "s_c", self.g_scr)
        K.w(SY, "s_pe")
        npw = 0
        for st in range(8):
            npw = K.dma(SY, PWT[:, st * 1024:(st + 1) * 1024],
                        self.pw_s.ap()[:, st * 128:(st + 1) * 128], transpose=True)
        K.w(SY, "s_x")
        MG = TILE + 2
        resv = 0
        for t in range(NT):
            t0 = t * TILE
            K.w(SY, "s_pe")
            K.w(SY, "s_x", self.rows_x[t])
            nr = K.dma(SY, self.ROW[0:1, :], self.rowsd.ap()[t])
            K.w(V, "s_x", nr)
            K.w(TT, "s_x", nr)
            K.w(V, "s_a")
            self.ln_bcast(self.ROW, TILE, (0, 1), self.ABR)
            ab = K.ctr["s_a"]
            K.w(V, "s_a", ab)
            for c in range(8):
                src = self.HALO[:, c * 2:c * 2 + 2] if t == 0 else self.CARRY[:, c * 2:c * 2 + 2]
                K.w(V, "s_a")
                K.op(V, "tensor_copy", XA[:, c * MG:c * MG + 2], src)
                K.op(V, "tensor_mul", self.SCR[:], X[c][:, t0:t0 + TILE], self.ABR[:, 0:TILE])
                K.op(V, "drain")
                K.op(V, "tensor_add", XA[:, c * MG + 2:c * MG + 2 + TILE], self.SCR[:],
                     self.ABR[:, TILE:2 * TILE])
                K.op(V, "drain")
                K.op(V, "tensor_copy", self.CARRY[:, c * 2:c * 2 + 2],
                     XA[:, c * MG + TILE:c * MG + 2 + TILE])
                K.op(V, "tensor_scalar", self.SCR[:], XA[:, c * MG:c * MG + TILE],
                     self.PK["dwk"][:, c * 3:c * 3 + 1], None, ALU.mult)
                K.op(V, "drain")
                K.op(V, "scalar_tensor_tensor", self.SCR[:], XA[:, c * MG + 1:c * MG + 1 + TILE],
                     self.PK["dwk"][:, c * 3 + 1:c * 3 + 2], self.SCR[:], ALU.mult, ALU.add)
                K.op(V, "drain")
                K.op(V, "scalar_tensor_tensor", self.SCR[:], XA[:, c * MG + 2:c * MG + 2 + TILE],
                     self.PK["dwk"][:, c * 3 + 2:c * 3 + 3], self.SCR[:], ALU.mult, ALU.add)
                K.di(V, "s_v")
                K.w(A, "s_v")
                K.op(A, "activation", XC[:, c * TILE:(c + 1) * TILE], self.SCR[:], AF.Identity,
                     bias=self.PK["vec5"][:, 0 * 8 + c:0 * 8 + c + 1])
                K.di(A, "s_a")
            xc_done = K.ctr["s_a"]
            if t == 0:
                K.w(TT, "s_x", npw)
            K.w(TT, "s_a", xc_done)
            pwv = [resv, resv]
            for m in range(8):
                pb = 2 + m % 2
                K.w(TT, "s_v", pwv[m % 2])
                for kc in range(8):
                    K.mm(PS[pb][:], PWT[:, kc * 1024 + m * 128: kc * 1024 + (m + 1) * 128],
                         XC[:, kc * TILE:(kc + 1) * TILE], start=(kc == 0), stop=(kc == 7),
                         inc=("s_pe", 1) if kc == 7 else None)
                pdone = K.ctr["s_pe"]
                K.w(A, "s_pe", pdone)
                K.w(A, "s_v")
                K.op(A, "activation", self.SCR2[:], PS[pb][:], AF.Silu,
                     bias=self.PK["vec5"][:, 1 * 8 + m:1 * 8 + m + 1])
                K.di(A, "s_a")
                K.w(V, "s_a")
                K.op(V, "tensor_add", X[m][:, t0:t0 + TILE],
                     XA[:, m * MG + 2:m * MG + 2 + TILE], self.SCR2[:])
                K.di(V, "s_v")
                pwv[m % 2] = K.ctr["s_v"]
            resv = K.ctr["s_v"]

    def ffn_s3(self):
        K = self.K
        K.phase()
        X, PS = self.X, self.PS
        LGR = K.sb("s3LGR", [128, 8], F32, local=True)
        EL = K.sb("s3EL", [128, 8], F32, local=True)
        TP = K.sb("s3TP", [128, 8], F32, local=True)
        M1 = K.sb("s3M1", [128, 4], F32, local=True)
        RWTC = K.sb("s3RWTC", [128, 8 * 8], F32, local=True)
        RWST = K.sb("s3RWST", [8, 1024], F32, local=True)
        LGD = K.sb("s3LGD", [128, 8], F32, local=True)
        self._LGD = LGD
        n = K.dma(SY, RWST[:], self.win["router_w"])
        K.w(SY, "s_x")
        K.w(TT, "s_x", n)
        for kc in range(8):
            K.op(TT, "transpose", PS[2][:, (kc % 2) * 8:(kc % 2) * 8 + 8],
                 RWST[:, kc * 128:(kc + 1) * 128], self.IDF[0:8, 0:8],
                 inc=("s_pe", 1))
            K.w(V, "s_pe")
            K.op(V, "tensor_copy", RWTC[:, kc * 8:(kc + 1) * 8],
                 PS[2][:, (kc % 2) * 8:(kc % 2) * 8 + 8])
            K.di(V, "s_v")
            K.w(TT, "s_v")
        rw_ready = K.ctr["s_v"]
        K.w(TT, "s_v", rw_ready)
        self.rw_done = [0] * NT
        for t in range(NT):
            t0 = t * TILE
            for s in range(4):
                K.w(TT, "s_v")
                for kc in range(8):
                    K.mm(PS[2][0:128, 0:8],
                         X[kc][:, t0 + s * 128:t0 + s * 128 + 128],
                         RWTC[:, kc * 8:(kc + 1) * 8], start=(kc == 0), stop=(kc == 7),
                         inc=("s_pe", 1) if kc == 7 else None)
                lgd = K.ctr["s_pe"]
                K.w(V, "s_pe", lgd)
                K.op(V, "tensor_add", LGR[:], PS[2][0:128, 0:8], self.RB128[:])
                K.op(V, "drain")
                if self.dbg == "rw" and t == 0 and s == 0:
                    K.op(V, "tensor_copy", LGD[:], LGR[:])
                    K.op(V, "drain")
                K.op(V, "reduce_max", M1[:, 0:1], LGR[:], axis=AX.X)
                K.op(V, "drain")
                K.op(V, "tensor_scalar", M1[:, 1:2], M1[:, 0:1], -1.0, None, ALU.mult)
                K.op(V, "tensor_scalar", TP[:], LGR[:], M1[:, 0:1], None, ALU.is_ge)
                K.op(V, "drain")
                K.op(V, "scalar_tensor_tensor", TP[:], TP[:], -1e30, LGR[:], ALU.mult, ALU.add)
                K.op(V, "drain")
                K.op(V, "reduce_max", M1[:, 2:3], TP[:], axis=AX.X)
                K.di(V, "s_v")
                K.w(A, "s_v")
                K.op(A, "activation", EL[:], LGR[:], AF.Exp, bias=M1[:, 1:2])
                K.op(A, "activation", M1[:, 3:4], M1[:, 2:3], AF.Exp, bias=M1[:, 1:2])
                K.op(A, "drain")
                K.op(A, "activation", M1[:, 3:4], M1[:, 3:4], AF.Ln, bias=1.0)
                K.op(A, "drain")
                K.op(A, "activation", M1[:, 3:4], M1[:, 3:4], AF.Exp, scale=-1.0)
                K.di(A, "s_a")
                K.w(V, "s_a")
                K.op(V, "tensor_scalar", TP[:], LGR[:], M1[:, 2:3], None, ALU.is_ge)
                K.op(V, "drain")
                K.op(V, "tensor_mul", TP[:], TP[:], EL[:])
                K.op(V, "drain")
                sg = t * 4 + s
                K.op(V, "tensor_scalar", self.RW[:, sg * 8:(sg + 1) * 8], TP[:],
                     M1[:, 3:4], None, ALU.mult)
                K.di(V, "s_v")
            self.rw_done[t] = K.ctr["s_v"]
        K.w(G, "s_v")
        self.xib_ld = 0
        for c in range(8):
            self.xib_ld = K.dma(G, self.xibd.ap()[c], X[c][:], "s_c")
        K.w(G, "s_c")

    def ffn_s4(self):
        K = self.K
        K.phase()
        PS = self.PS
        XIB = K.sb("s4XIB", [128, 2 * 8 * TILE], BF16, local=True)
        COMBP = K.sb("s4CP", [128, 2 * 1024], F32, local=True)
        EBR = K.sb("s4EB", [1, D], BF16, local=True)
        accd = [[0] * NT for _ in range(E)]
        last_comb_dma = [0]
        xld = [0, 0]

        def load_xib(t, buf):
            K.w(SY, "s_pe")
            sm = "s_xia" if buf == 0 else "s_xib"
            n = 0
            for c in range(8):
                n = K.dma(SY, XIB[:, buf * 8 * TILE + c * TILE: buf * 8 * TILE + (c + 1) * TILE],
                          self.xibd.ap()[c, :, t * TILE:(t + 1) * TILE], sm)
            xld[buf] = n

        K.w(SY, "s_c", self.xib_ld)
        load_xib(0, 0)
        load_xib(1, 1)
        ws_free_pe = 0
        EMAX = 1 if self.dbg == "comb1" else (2 if self.dbg == "comb2" else E)
        for e in range(EMAX):
            K.w(G, "s_ac")
            WT = self.WS[1 - e % 2]
            K.w(SY, "s_pe", ws_free_pe)
            nw = 0
            for st in range(8):
                nw = K.dma(SY, WT[:, st * 1024:(st + 1) * 1024],
                           self.ew_s.ap()[e][:, st * 128:(st + 1) * 128], transpose=True)
            K.w(SY, "s_x")
            ne = K.dma(G, EBR[:], self.eb_s.ap()[e:e + 1, :], "s_w")
            K.w(G, "s_w")
            K.w(TT, "s_x", nw)
            K.w(TT, "s_w", ne)
            for t in range(NT):
                buf = t % 2
                K.w(TT, "s_xia" if buf == 0 else "s_xib", xld[buf])
                for s in range(4):
                    if s % 2 == 0:
                        K.w(V, "s_ac", last_comb_dma[0])
                    for half in range(2):
                        yb = 2 + (s * 2 + half) % 2
                        K.w(TT, "s_a")
                        xb = XIB[:, buf * 8 * TILE:]
                        for kc in range(8):
                            K.mm(PS[yb][:],
                                 xb[:, kc * TILE + s * 128: kc * TILE + s * 128 + 128],
                                 WT[:, kc * 1024 + half * 512: kc * 1024 + (half + 1) * 512],
                                 start=(kc == 0), stop=False)
                        K.mm(PS[yb][:], self.ONEB[:],
                             EBR[0:1, half * 512:(half + 1) * 512],
                             start=False, stop=True, inc=("s_pe", 1))
                        ydone = K.ctr["s_pe"]
                        K.w(A, "s_pe", ydone)
                        K.w(A, "s_v")
                        scr = self.SCR if half == 0 else self.SCR2
                        K.op(A, "activation", scr[:], PS[yb][:], AF.Silu,
                             inc=("s_a", 1))
                        K.w(V, "s_a")
                        dst = COMBP[:, (s % 2) * 1024 + half * 512:
                                    (s % 2) * 1024 + (half + 1) * 512]
                        rwc = self.RW[:, (t * 4 + s) * 8 + e:(t * 4 + s) * 8 + e + 1]
                        K.op(V, "tensor_scalar", dst, scr[:], rwc, None, ALU.mult,
                             inc=("s_v", 1))
                    if s % 2 == 1:
                        hf0 = t * TILE + (s // 2) * 256
                        K.w(G, "s_v")
                        K.w(G, "s_ac")
                        accd[e][t] = K.dma(
                            G, self.combd.ap()[hf0:hf0 + 256, :]
                            .rearrange("(sl p) o -> p sl o", p=128),
                            COMBP[:].rearrange("p (sl o) -> p sl o", sl=2), "s_ac",
                            accum_op=(ALU.bypass if e == 0 else ALU.add))
                        last_comb_dma[0] = accd[e][t]
                if not (e == E - 1 and t >= NT - 2):
                    load_xib((t + 2) % NT, buf)
            ws_free_pe = K.ctr["s_pe"]
        self.acc_last = accd[E - 1]
        K.w(SY, "s_xia")
        K.w(SY, "s_xib")

    def ffn_s4b(self):
        K = self.K
        K.phase()
        X, PS, PSB = self.X, self.PS, self.PSB
        CMB = K.sb("s4bC", [128, 2 * 1024], F32, local=True)
        CB = K.sb("s4bCB", [128, 2 * 1024], BF16, local=True)
        CMT = K.sb("s4bT", [128, 8 * 256], BF16, local=True)
        OWT = self.WS[1]
        K.w(SY, "s_pe")
        nw = 0
        for st in range(8):
            nw = K.dma(SY, OWT[:, st * 1024:(st + 1) * 1024],
                       self.ow_s.ap()[:, st * 128:(st + 1) * 128], transpose=True)
        K.w(SY, "s_x")
        K.w(TT, "s_x", nw)
        for t in range(NT):
            t0 = t * TILE
            K.w(SY, "s_ac", self.acc_last[t])
            for hf in range(2):
                hf0 = t0 + hf * 256
                K.w(SY, "s_v")
                nc_ = K.dma(SY, CMB[:].rearrange("p (sl o) -> p sl o", sl=2),
                            self.combd.ap()[hf0:hf0 + 256, :]
                            .rearrange("(sl p) o -> p sl o", p=128))
                K.w(V, "s_x", nc_)
                K.op(V, "tensor_copy", CB[:], CMB[:])
                K.di(V, "s_v")
                cbd = K.ctr["s_v"]
                K.w(TT, "s_v", cbd)
                K.w(TT, "s_a")
                for sl in range(2):
                    for oc in range(8):
                        K.op(TT, "transpose", PSB[:, (oc % 4) * 256 + sl * 128:
                                                  (oc % 4) * 256 + sl * 128 + 128],
                             CB[:, sl * 1024 + oc * 128: sl * 1024 + (oc + 1) * 128],
                             self.IDB[:], inc=("s_pe", 1) if oc % 4 == 3 else None)
                        if oc % 4 == 3:
                            grp = K.ctr["s_pe"]
                            K.w(V, "s_pe", grp)
                            for o2 in range(oc - 3, oc + 1):
                                K.op(V, "tensor_copy",
                                     CMT[:, o2 * 256 + sl * 128: o2 * 256 + sl * 128 + 128],
                                     PSB[:, (o2 % 4) * 256 + sl * 128:
                                         (o2 % 4) * 256 + sl * 128 + 128])
                            K.di(V, "s_v")
                            K.w(TT, "s_v")
                cmt_v = K.ctr["s_v"]
                K.w(TT, "s_v", cmt_v)
                oev = [0, 0]
                for m in range(8):
                    ob = 2 + m % 2
                    K.w(TT, "s_v", oev[m % 2])
                    for oc in range(8):
                        K.mm(PS[ob][:, 0:256],
                             OWT[:, oc * 1024 + m * 128: oc * 1024 + (m + 1) * 128],
                             CMT[:, oc * 256:(oc + 1) * 256], start=(oc == 0), stop=(oc == 7),
                             inc=("s_pe", 1) if oc == 7 else None)
                    od = K.ctr["s_pe"]
                    K.w(A, "s_pe", od)
                    K.w(A, "s_v")
                    K.op(A, "activation", self.SCR2[:, 0:256], PS[ob][:, 0:256], AF.Identity,
                         bias=self.PK["vec5"][:, 2 * 8 + m:2 * 8 + m + 1])
                    K.di(A, "s_a")
                    K.w(V, "s_a")
                    K.op(V, "tensor_add", X[m][:, hf0:hf0 + 256], X[m][:, hf0:hf0 + 256],
                         self.SCR2[:, 0:256])
                    K.di(V, "s_v")
                    oev[m % 2] = K.ctr["s_v"]
            self.ln_stats_rows((t0, TILE), 0, self.ROW, EPS_RMS, with_mean=False, rms=True,
                               raw=True)
            K.w(SY, "s_a")
            self.rows_x[t] = K.dma(SY, self.rowsd.ap()[t], self.ROW[0:1, :])
            K.w(SY, "s_x")

    def ffn_s5(self):
        K = self.K
        K.phase()
        X, PS = self.X, self.PS
        HBUF = K.sb("s5HB", [128, 8 * TILE], BF16, local=True)
        FH = K.sb("s5FH", [128, 2 * TILE], BF16, local=True)
        FDT = K.sb("s5FDT", [128, 8 * HIDD], BF16, local=True)
        FUT = K.sb("s5FUT", [128, 2 * 1024], BF16, local=True)
        n = 0
        K.w(SY, "s_pe")
        for st in range(8):
            n = K.dma(SY, FDT[:, st * HIDD:(st + 1) * HIDD],
                      self.fd_s.ap()[:, st * 128:(st + 1) * 128], transpose=True)
        for st in range(2):
            n = K.dma(SY, FUT[:, st * 1024:(st + 1) * 1024],
                      self.fu_s.ap()[:, st * 128:(st + 1) * 128], transpose=True)
        K.w(SY, "s_x")
        K.w(TT, "s_x", n)
        for t in range(NT):
            t0 = t * TILE
            K.w(SY, "s_pe")
            K.w(SY, "s_x", self.rows_x[t])
            nr = K.dma(SY, self.ROW[0:1, :], self.rowsd.ap()[t])
            K.w(A, "s_x", nr)
            K.w(A, "s_v")
            K.op(A, "activation", self.ROW[0:1, 0:TILE], self.ROW[0:1, TILE:2 * TILE],
                 AF.Ln, bias=EPS_RMS)
            K.op(A, "drain")
            K.op(A, "activation", self.ROW[0:1, 0:TILE], self.ROW[0:1, 0:TILE],
                 AF.Exp, scale=-0.5)
            K.di(A, "s_a")
            self.ln_bcast(self.ROW, TILE, (0, 1), self.ABR, beta=False)
            abd = K.ctr["s_a"]
            K.w(V, "s_a", abd)
            for c in range(8):
                K.w(V, "s_a")
                K.op(V, "scalar_tensor_tensor", self.SCR[:], X[c][:, t0:t0 + TILE],
                     1.0, self.ABR[:, 0:TILE], ALU.mult, ALU.mult)
                K.di(V, "s_v")
                K.w(A, "s_v")
                K.op(A, "activation", X[c][:, t0:t0 + TILE], self.SCR[:], AF.Identity,
                     scale=self.PK["vec5"][:, 3 * 8 + c:3 * 8 + c + 1])
                K.op(A, "drain")
                K.op(A, "copy", HBUF[:, c * TILE:(c + 1) * TILE], X[c][:, t0:t0 + TILE])
                K.di(A, "s_a")
            hb_done = K.ctr["s_a"]
            K.w(TT, "s_a", hb_done)
            for m2 in range(2):
                fb = 2 + m2
                K.w(TT, "s_a")
                for kc in range(8):
                    K.mm(PS[fb][:], FDT[:, kc * HIDD + m2 * 128: kc * HIDD + (m2 + 1) * 128],
                         HBUF[:, kc * TILE:(kc + 1) * TILE], start=(kc == 0), stop=(kc == 7),
                         inc=("s_pe", 1) if kc == 7 else None)
                fd = K.ctr["s_pe"]
                K.w(A, "s_pe", fd)
                K.op(A, "activation", FH[:, m2 * TILE:(m2 + 1) * TILE], PS[fb][:], AF.Gelu,
                     bias=self.PK["fdb"][:, m2:m2 + 1])
                K.di(A, "s_a")
            fh_done = K.ctr["s_a"]
            K.w(TT, "s_a", fh_done)
            fev = [0, 0]
            for m in range(8):
                ub = 4 + m % 2
                K.w(TT, "s_v", fev[m % 2])
                for kc in range(2):
                    K.mm(PS[ub][:], FUT[:, kc * 1024 + m * 128: kc * 1024 + (m + 1) * 128],
                         FH[:, kc * TILE:(kc + 1) * TILE], start=(kc == 0), stop=(kc == 1),
                         inc=("s_pe", 1) if kc == 1 else None)
                ud = K.ctr["s_pe"]
                K.w(A, "s_pe", ud)
                K.w(A, "s_v")
                K.op(A, "activation", self.SCR2[:], PS[ub][:], AF.Identity,
                     bias=self.PK["vec5"][:, 4 * 8 + m:4 * 8 + m + 1])
                K.di(A, "s_a")
                K.w(V, "s_a")
                K.op(V, "tensor_add", X[m][:, t0:t0 + TILE], X[m][:, t0:t0 + TILE],
                     self.SCR2[:])
                K.di(V, "s_v")
                fev[m % 2] = K.ctr["s_v"]
            self.ln_stats_rows((t0, TILE), 0, self.ROW, EPS_LN, raw=True)
            K.w(SY, "s_a")
            self.rows_x[t] = K.dma(SY, self.rowsd.ap()[t], self.ROW[0:1, :])
            K.w(SY, "s_x")

    def ffn_s7(self):
        K = self.K
        K.phase()
        X, PS, IDF = self.X, self.PS, self.IDF
        X1B = K.sb("s7X1", [128, 8 * TILE], F32, local=True)
        OUTT = [K.sb(f"s7O{i}", [128, 1024], F32, local=True) for i in range(2)]
        st_done = 0
        K.w(SY, "s_x", self.spill_x)
        for t in range(NT):
            t0 = t * TILE
            K.w(SY, "s_pe")
            K.w(SY, "s_x", self.rows_x[t])
            nr = K.dma(SY, self.ROW[0:1, :], self.rowsd.ap()[t])
            K.w(SY, "s_x")
            K.w(SY, "s_v")
            nx = 0
            for c in range(8):
                nx = K.dma(SY, X1B[:, c * TILE:(c + 1) * TILE], self.x1sp.ap()[c, :, t0:t0 + TILE])
            K.w(A, "s_x", nr)
            K.w(V, "s_x", nr)
            self.finish_rows(self.ROW, TILE, EPS_LN, True)
            self.ln_bcast(self.ROW, TILE, (0, 1), self.ABR)
            abd = K.ctr["s_a"]
            K.w(V, "s_a", abd)
            K.w(V, "s_x", nx)
            for c in range(8):
                K.op(V, "tensor_mul", self.SCR[:], X[c][:, t0:t0 + TILE], self.ABR[:, 0:TILE])
                K.op(V, "drain")
                K.op(V, "tensor_add", self.SCR[:], self.SCR[:], self.ABR[:, TILE:2 * TILE])
                K.op(V, "drain")
                K.op(V, "tensor_add", X[c][:, t0:t0 + TILE], self.SCR[:],
                     X1B[:, c * TILE:(c + 1) * TILE])
                K.op(V, "drain")
            K.di(V, "s_v")
            od = K.ctr["s_v"]
            for s in range(4):
                par = s % 2
                K.w(TT, "s_v", od)
                K.w(TT, "s_st")
                K.w(TT, "s_a")
                for c in range(8):
                    K.op(TT, "transpose", PS[c // 4][:, (c % 4) * 128:(c % 4 + 1) * 128],
                         X[c][:, t0 + s * 128:t0 + (s + 1) * 128], IDF[:],
                         inc=("s_pe", 1) if c == 7 else None)
                trd = K.ctr["s_pe"]
                K.w(V, "s_pe", trd)
                K.w(A, "s_pe", trd)
                for c in range(8):
                    eng, meth = (V, "tensor_copy") if c < 4 else (A, "copy")
                    K.op(eng, meth, OUTT[par][:, c * 128:(c + 1) * 128],
                         PS[c // 4][:, (c % 4) * 128:(c % 4 + 1) * 128])
                K.di(V, "s_v")
                K.di(A, "s_a")
                K.w(SY, "s_v")
                K.w(SY, "s_a")
                st_done = K.dma(SY, self.out[t0 + s * 128:t0 + (s + 1) * 128, :], OUTT[par][:],
                                "s_st")
                K.w(TT, "s_v")
                K.w(TT, "s_a")
        K.w(G, "s_st", K.ctr["s_st"])
        K.w(G, "s_x", K.ctr["s_x"])

    def dump_X(self):
        K = self.K
        K.w(SY, "s_v")
        K.w(SY, "s_a")
        K.w(SY, "s_pe")
        n = 0
        for c in range(8):
            n = K.dma(SY, self.dump[c], self.X[c][:])
        K.w(G, "s_x", n)
        K.w(G, "s_w")
        K.w(G, "s_cc")
        K.stopped = True

    def dump_masks(self):
        K = self.K
        K.w(SY, "s_g", self.g_masks)
        K.w(SY, "s_v")
        K.w(SY, "s_a")
        n = K.dma(SY, self.dump2[0:64, :].rearrange("(a b) c -> a b c", a=8),
                  self.SEL8[:].rearrange("a (b c) -> a b c", b=8))
        n = K.dma(SY, self.dump2[128:256, :], self.IDF[:])
        n = K.dma(SY, self.dump2[256:384, 0:8], self.CONSTP[:, 0:8])
        K.w(G, "s_x", n)
        K.w(G, "s_w")
        K.stopped = True

    def dump_cc(self):
        K = self.K
        K.w(SY, "s_cc")
        n = K.dma(SY, self.dump2[:], self.cc_out.ap())
        K.w(G, "s_x", n)
        K.stopped = True

    def dump_rw(self):
        K = self.K
        K.w(SY, "s_v")
        n = K.dma(SY, self.dump2[0:128, :], self.RW[:, 0:128])
        n = K.dma(SY, self.dump2[128:256, 0:8], self._LGD[:])
        K.w(G, "s_x", n)
        K.stopped = True

    def generate(self):
        dbg = self.dbg
        if dbg == "prep":
            self.dump_masks()
            return self.finish()
        if dbg == "init2":
            self.initial(nsl=2)
            self.dump_X()
            return self.finish()
        if dbg == "initnoln":
            self.initial(do_ln=False)
            self.dump_X()
            return self.finish()
        self.initial()
        if dbg == "xint0":
            self.dump_X()
            return self.finish()
        for l in range(DEPTH):
            self.pass1(l)
            if dbg == f"l{l}_U":
                self.dump_cc()
                return self.finish()
            self.pass2a(l)
            if dbg == f"l{l}_xmid":
                self.dump_X()
                return self.finish()
            self.pass2b(l)
            if dbg == f"l{l}_x":
                self.dump_X()
                return self.finish()
        self.x1_form()
        if dbg == "x1":
            self.dump_X()
            return self.finish()
        self.ffn_s1_s2()
        if dbg == "x1i":
            self.dump_X()
            return self.finish()
        self.ffn_s3()
        if dbg == "rw":
            self.dump_rw()
            return self.finish()
        self.ffn_s4()
        if dbg in ("comb", "comb1", "comb2"):
            K = self.K
            K.w(SY, "s_ac")
            n = K.dma(SY, self.dump.rearrange("a b c -> (a b c)"),
                      self.combd.ap().rearrange("a b -> (a b)"))
            K.w(G, "s_x", n)
            K.stopped = True
            return self.finish()
        self.ffn_s4b()
        if dbg == "u":
            self.dump_X()
            return self.finish()
        self.ffn_s5()
        if dbg == "ffret":
            self.dump_X()
            return self.finish()
        self.ffn_s7()
        return self.finish()

    def finish(self):
        if self.K.ph is not None:
            self.K.ph.close()
            self.K.ph = None
        nc = self.K.build()
        self.K.es.close()
        return nc


_NC_CACHE = {}


def get_nc(dbg=None):
    if dbg not in _NC_CACHE:
        _NC_CACHE[dbg] = Mega(dbg).generate()
    return _NC_CACHE[dbg]


def kernel(**inputs):
    nc = get_nc(None)
    return _run(nc, inputs, None)


def _run(nc, inputs, dbg):
    xfull = np.ascontiguousarray(np.asarray(inputs["L0_attn_x"], np.float32))
    in_maps = []
    for c in range(8):
        b, half = c // 2, c % 2
        m = {"x": np.ascontiguousarray(xfull[b, half * 4096:(half + 1) * 4096, :]),
             "hmask": np.full((128, 1), 1.0 if half == 1 else 0.0, np.float32)}
        for nm in ["ln1_w", "ln1_b", "wq", "wk", "wv", "wo", "bo", "ln2_w", "ln2_b",
                   "w1", "w2"]:
            m["w_" + nm] = np.ascontiguousarray(np.asarray(inputs["L0_attn_" + nm], np.float32))
        for nm in ["dw_kernel", "dw_bias", "pw_w", "pw_b", "router_w", "router_b",
                   "expert_w", "expert_b", "out_w", "out_b", "norm_w", "ff_down_w",
                   "ff_down_b", "ff_up_w", "ff_up_b"]:
            m["w_" + nm] = np.ascontiguousarray(np.asarray(inputs["L0_ffn_" + nm], np.float32))
        in_maps.append(m)
    res = run_bass_kernel_spmd(nc, in_maps, core_ids=list(range(8)))
    if dbg:
        return res
    out = np.empty((4, 8192, 1024), np.float32)
    for c in range(8):
        b, half = c // 2, c % 2
        out[b, half * 4096:(half + 1) * 4096, :] = res.results[c]["out"]
    return out


def run_debug(dbg, inputs):
    nc = get_nc(dbg)
    return _run(nc, inputs, dbg)


# version 07
# speedup vs baseline: 1.2365x; 1.0396x over previous
import numpy as np
from contextlib import ExitStack

import concourse.bass as bass
import concourse.mybir as mybir
from concourse.bass_utils import run_bass_kernel_spmd

F32 = mybir.dt.float32
BF16 = mybir.dt.bfloat16
AF = mybir.ActivationFunctionType
ALU = mybir.AluOpType
AX = mybir.AxisListType

D = 1024
T = 4096
TILE = 512
NT = T // TILE
SL = 128
DEPTH = 6
FF = 4096
FFT = 256
E = 8
HIDD = 256
LEAK = 0.01
EPS_LN = 1e-5
EPS_RMS = 1e-6

V, A, TT, G, SY = "vector", "scalar", "tensor", "gpsimd", "sync"

RG = [[0, 1], [2, 3], [4, 5], [6, 7]]


class KB:
    ENGS = ("sync", "gpsimd", "vector", "scalar", "tensor")

    def __init__(self):
        self.nc = bass.Bass("TRN2", target_bir_lowering=False)
        self.prog = {e: [] for e in self.ENGS}
        self.es = ExitStack()
        self.ph = None
        self.semh = {}
        self.ctr = {}
        for s in ("s_x", "s_w", "s_c", "s_ac", "s_cc", "s_pe", "s_v", "s_a", "s_g", "s_f",
                  "s_w1a", "s_w1b", "s_w2a", "s_w2b", "s_xia", "s_xib", "s_st"):
            self.semh[s] = self.es.enter_context(self.nc.semaphore(s))
            self.ctr[s] = 0
        self.stopped = False

    _uid = 0

    def sb(self, name, shape, dt, local=False):
        KB._uid += 1
        st = self.ph if local else self.es
        return st.enter_context(self.nc.sbuf_tensor(f"{name}_{KB._uid}", shape, dt))

    def psum(self, name, shape, dt):
        return self.es.enter_context(self.nc.psum_tensor(name, shape, dt))

    def dram(self, name, shape, dt, kind="Internal"):
        return self.nc.dram_tensor(name, shape, dt, kind=kind)

    def phase(self):
        if self.ph is not None and not self.stopped:
            for e in ("vector", "scalar"):
                self.di(e, "s_f", 1)
            for e in ("gpsimd", "tensor", "sync"):
                self.op(e, "nop", inc=("s_f", 1))
            tgt = self.ctr["s_f"]
            for e in ("vector", "scalar", "gpsimd", "tensor", "sync"):
                self.w(e, "s_f", tgt)
        if self.ph is not None:
            self.ph.close()
        self.ph = ExitStack()

    def w(self, eng, sem, val=None):
        if self.stopped:
            return
        if val is None:
            val = self.ctr[sem]
        if val <= 0:
            return
        h = self.semh[sem]
        self.prog[eng].append(lambda e, hh=h, vv=val: e.wait_ge(hh, vv))

    def op(self, eng, meth, *a, inc=None, **kw):
        if self.stopped:
            return 0
        if inc:
            nm, k = inc
            self.ctr[nm] += k
            h = self.semh[nm]
            self.prog[eng].append(
                lambda e, m=meth, aa=a, kk=kw, hh=h, n=k: getattr(e, m)(*aa, **kk).then_inc(hh, n))
            return self.ctr[nm]
        self.prog[eng].append(lambda e, m=meth, aa=a, kk=kw: getattr(e, m)(*aa, **kk))
        return 0

    def dma(self, eng, out, in_, sem="s_x", **kw):
        return self.op(eng, "dma_start", inc=(sem, 16), out=out, in_=in_, **kw)

    def mm(self, out, lhsT, rhs, start, stop, inc=None):
        return self.op("tensor", "matmul", out, lhsT, rhs, start=start, stop=stop, inc=inc)

    def di(self, eng, sem, k=1):
        return self.op(eng, "drain", inc=(sem, k))

    def build(self):
        nc = self.nc
        with nc.Block() as block:
            @block.sync
            def _(eng):
                for f in self.prog["sync"]:
                    f(eng)

            @block.gpsimd
            def _(eng):
                for f in self.prog["gpsimd"]:
                    f(eng)

            @block.vector
            def _(eng):
                for f in self.prog["vector"]:
                    f(eng)

            @block.scalar
            def _(eng):
                for f in self.prog["scalar"]:
                    f(eng)

            @block.tensor
            def _(eng):
                for f in self.prog["tensor"]:
                    f(eng)
        return nc


class Mega:
    def __init__(self, dbg=None):
        self.K = KB()
        self.dbg = dbg
        self._declare()
        self._prep()

    def _declare(self):
        K = self.K
        self.xin = K.dram("x", [T, D], F32, kind="ExternalInput").ap()
        self.hmask = K.dram("hmask", [128, 1], F32, kind="ExternalInput").ap()
        self.win = {}
        for nm, shp in [
            ("ln1_w", [DEPTH, D]), ("ln1_b", [DEPTH, D]), ("wq", [DEPTH, D, D]),
            ("wk", [DEPTH, D, D]), ("wv", [DEPTH, D, D]), ("wo", [DEPTH, D, D]),
            ("bo", [DEPTH, D]), ("ln2_w", [DEPTH, D]), ("ln2_b", [DEPTH, D]),
            ("w1", [DEPTH, D, FF]), ("w2", [DEPTH, FF, D]),
            ("dw_kernel", [D, 1, 3]), ("dw_bias", [D]), ("pw_w", [D, D]), ("pw_b", [D]),
            ("router_w", [E, D]), ("router_b", [E]), ("expert_w", [E, D, D]),
            ("expert_b", [E, D]), ("out_w", [D, D]), ("out_b", [D]), ("norm_w", [D]),
            ("ff_down_w", [HIDD, D]), ("ff_down_b", [HIDD]), ("ff_up_w", [D, HIDD]),
            ("ff_up_b", [D]),
        ]:
            self.win[nm] = K.dram("w_" + nm, shp, F32, kind="ExternalInput").ap()
        self.out = K.dram("out", [T, D], F32, kind="ExternalOutput").ap()
        if self.dbg:
            self.dump = K.dram("dump", [8, 128, T], F32, kind="ExternalOutput").ap()
            self.dump2 = K.dram("dump2", [1032, 128], F32, kind="ExternalOutput").ap()

        self.x1sp = K.dram("x1sp", [8, 128, T], F32)
        self.rowsd = K.dram("rowsd", [NT, 1, 2 * TILE], F32)
        self.combd = K.dram("combd", [T, D], F32)
        self.xibd = K.dram("xibd", [8, 128, T], BF16)
        self.w1_s = K.dram("w1_s", [DEPTH, FF // 128, 128, 8 * 128], BF16)
        self.qkvo_s = {nm: K.dram(nm + "_s", [DEPTH, 128, 8 * D], BF16)
                       for nm in ("wq", "wk", "wv", "wo")}
        self.w2_s = K.dram("w2_s", [DEPTH, 8, 128, 32 * 128], BF16)
        self.pw_s = K.dram("pw_s", [D, D], BF16)
        self.ow_s = K.dram("ow_s", [D, D], BF16)
        self.ew_s = K.dram("ew_s", [E, D, D], BF16)
        self.fd_s = K.dram("fd_s", [HIDD, D], BF16)
        self.fu_s = K.dram("fu_s", [D, HIDD], BF16)
        self.eb_s = K.dram("eb_s", [E, D], BF16)
        self.cc_in = K.dram("cc_in", [1032, 128], F32)
        self.cc_out = K.dram("cc_out", [1032, 128], F32)
        self.hl_in = K.dram("hl_in", [D, 2], F32)
        self.hl_out = K.dram("hl_out", [2 * D, 2], F32)
        self.xb2d = K.dram("xb2d", [128, 8, T], BF16)
        self.mrid1 = K.dram("mrid1", [NT, 2 * TILE], BF16)
        self.mrid2 = K.dram("mrid2", [NT, 2 * TILE], BF16)
        self.rowsd2 = K.dram("rowsd2", [NT, TILE], F32)

        self.X = [K.sb(f"X{c}", [128, T], F32) for c in range(8)]
        self.WS = [K.sb("WS0", [128, 8 * D], BF16), K.sb("WS1", [128, 8 * D], BF16)]
        self.ABR = K.sb("ABR", [128, 2 * TILE], F32)
        self.SCR = K.sb("SCR", [128, TILE], F32)
        self.SCR2 = K.sb("SCR2", [128, TILE], F32)
        self.ROW = K.sb("ROW", [1, 2 * TILE], F32)
        self.ONC = K.sb("ONC", [128, 1], F32)
        self.QMASK = K.sb("QMASK", [128, 8 * 8], BF16)
        self.QMD = K.sb("QMD", [128, 4], BF16)
        self.IDF = K.sb("IDF", [128, 128], F32)
        self.IDB = K.sb("IDB", [128, 128], BF16)
        self.ONEF = K.sb("ONEF", [1, 128], F32)
        self.ONEB = K.sb("ONEB", [1, 128], BF16)
        self.HALO = K.sb("HALO", [128, 16], F32)
        self.CARRY = K.sb("CARRY", [128, 16], F32)
        self.MCOL = K.sb("MCOL", [128, 1], F32)
        self.RBROW = K.sb("RBROW", [1, E], F32)
        self.RB128 = K.sb("RB128", [128, E], F32)
        self.RW = K.sb("RW", [128, (T // SL) * E], F32)
        self.CTX = K.sb("CTX", [128, 8 * 128], BF16)
        self.PK = {}
        for nm, ncol in [("ln1_w", 48), ("ln1_b", 48), ("ln2_w", 48), ("ln2_b", 48),
                         ("bo", 48), ("vec5", 40), ("dwk", 24), ("fdb", 2)]:
            self.PK[nm] = K.sb("PK_" + nm, [128, ncol], F32)
        self.CONSTP = K.sb("CONSTP", [128, 12], F32)
        self.PS = [K.psum(f"PS{i}", [128, 512], F32) for i in range(7)]
        self.PSB = K.psum("PSB", [128, 1024], BF16)
        self.wld = {nm: [0] * DEPTH for nm in ("wq", "wk", "wv", "wo", "w2")}
        self.wk_sem = ["s_w"] * DEPTH
        self.pair_ld = [0, 0]
        self.mrx_dma = 0
        self.rows2_wr = [0] * NT
        self._lnst_pe = 0

    def load_w(self, slot_ap, src_dd, sem="s_w"):
        return self.K.dma(G, slot_ap.rearrange("p (k m) -> p k m", k=8),
                          src_dd.rearrange("(k p) m -> p k m", p=128), sem)

    def cast_qkvo(self, l):
        K = self.K
        for nm in ("wq", "wk", "wv", "wo"):
            self.qkvo_cast[l] = K.dma(
                G, self.qkvo_s[nm].ap()[l].rearrange("p (k m) -> p k m", k=8),
                self.win[nm][l].rearrange("(k p) m -> p k m", p=128), "s_c")

    def load_wb(self, slot_ap, nm, l, sem="s_w"):
        K = self.K
        K.w(G, "s_c", self.qkvo_cast[l])
        return K.dma(G, slot_ap, self.qkvo_s[nm].ap()[l], sem)

    def cast_w12(self, l):
        K = self.K
        for mi in range(32):
            K.dma(G, self.w1_s.ap()[l, mi].rearrange("p (k m) -> p k m", k=8),
                  self.win["w1"][l][:, mi * 128:(mi + 1) * 128]
                  .rearrange("(k p) m -> p k m", p=128), "s_c")
        for j in range(8):
            self.w12_cast[l] = K.dma(
                G, self.w2_s.ap()[l, j].rearrange("p (mi jj) -> p mi jj", mi=32),
                self.win["w2"][l][:, j * 128:(j + 1) * 128]
                .rearrange("(mi p) jj -> p mi jj", p=128), "s_c")

    def _prep(self):
        K = self.K
        win = self.win
        dg = lambda: K.op(G, "drain")
        K.op(G, "memset", self.IDF[:], 0.0)
        dg()
        K.op(G, "affine_select", out=self.IDF[:], in_=self.IDF[:], compare_op=ALU.not_equal,
             fill=1.0, base=0, pattern=[[-1, 128]], channel_multiplier=1)
        K.op(G, "memset", self.IDB[:], 0.0)
        dg()
        K.op(G, "affine_select", out=self.IDB[:], in_=self.IDB[:], compare_op=ALU.not_equal,
             fill=1.0, base=0, pattern=[[-1, 128]], channel_multiplier=1)
        K.op(G, "memset", self.ONEF[:], 1.0)
        K.op(G, "memset", self.ONEB[:], 1.0)
        K.op(G, "memset", self.QMASK[:], 0.0)
        dg()
        for h in range(8):
            K.op(G, "memset", self.QMASK[:, h * 8 + h:h * 8 + h + 1], 1.0)
        K.op(G, "memset", self.ONC[:], 1.0)
        K.op(G, "memset", self.QMD[:], 0.0)
        dg()
        K.op(G, "memset", self.QMD[:, 0:1], 1.0)
        K.op(G, "memset", self.QMD[:, 3:4], 1.0)
        K.op(G, "memset", self.CARRY[:], 0.0)
        for i, cv in enumerate([EPS_LN, EPS_RMS, 1.0, -0.5, -1.0, LEAK, 1.0 / D, 2.0, 0.5]):
            K.op(G, "memset", self.CONSTP[:, i:i + 1], cv)
            K.nc.const_aps.register(F32, cv, self.CONSTP[:, i:i + 1])
        K.op(G, "drain", inc=("s_g", 1))
        self.g_masks = K.ctr["s_g"]

        K.dma(G, self.pw_s.ap(), win["pw_w"], "s_c")
        K.dma(G, self.ow_s.ap(), win["out_w"], "s_c")
        for e in range(E):
            K.dma(G, self.ew_s.ap()[e], win["expert_w"][e], "s_c")
        K.dma(G, self.fd_s.ap(), win["ff_down_w"], "s_c")
        K.dma(G, self.fu_s.ap(), win["ff_up_w"], "s_c")
        K.dma(G, self.eb_s.ap(), win["expert_b"], "s_c")
        self.g_scr = K.ctr["s_c"]
        K.w(G, "s_c")
        self.load_w(self.WS[0][:], win["wk"][0])
        self.wld["wk"][0] = self.wld["wv"][0] = \
            self.load_w(self.WS[1][:], win["wv"][0])
        K.w(G, "s_w")
        self.w12_cast = [0] * DEPTH
        self.qkvo_cast = [0] * DEPTH
        self.cast_w12(0)
        K.w(G, "s_c")
        self.cast_qkvo(0)
        K.w(G, "s_c")

        K.phase()
        PKS = K.sb("PKS", [64, 128], F32, local=True)
        jobs = [
            ("ln1_w", win["ln1_w"].rearrange("l (c p) -> (l c) p", p=128), 48),
            ("ln1_b", win["ln1_b"].rearrange("l (c p) -> (l c) p", p=128), 48),
            ("ln2_w", win["ln2_w"].rearrange("l (c p) -> (l c) p", p=128), 48),
            ("ln2_b", win["ln2_b"].rearrange("l (c p) -> (l c) p", p=128), 48),
            ("bo", win["bo"].rearrange("l (c p) -> (l c) p", p=128), 48),
            ("fdb", win["ff_down_b"].rearrange("(c p) -> c p", p=128), 2),
        ]
        K.w(TT, "s_g", self.g_masks)
        for nm, src, ncol in jobs:
            K.w(SY, "s_v")
            ld = K.dma(SY, PKS[0:ncol, :], src)
            K.w(TT, "s_x", ld)
            K.w(TT, "s_v")
            K.op(TT, "transpose", self.PS[0][:, 0:ncol], PKS[0:ncol, :],
                 self.IDF[0:ncol, 0:ncol], inc=("s_pe", 1))
            K.w(V, "s_pe")
            K.op(V, "tensor_copy", self.PK[nm][:], self.PS[0][:, 0:ncol])
            K.di(V, "s_v")
        for j, nm in enumerate(["dw_bias", "pw_b", "out_b", "norm_w", "ff_up_b"]):
            K.w(SY, "s_v")
            ld = K.dma(SY, PKS[0:8, :], win[nm].rearrange("(c p) -> c p", p=128))
            K.w(TT, "s_x", ld)
            K.w(TT, "s_v")
            K.op(TT, "transpose", self.PS[0][:, 0:8], PKS[0:8, :], self.IDF[0:8, 0:8],
                 inc=("s_pe", 1))
            K.w(V, "s_pe")
            K.op(V, "tensor_copy", self.PK["vec5"][:, j * 8:(j + 1) * 8], self.PS[0][:, 0:8])
            K.di(V, "s_v")
        K.w(SY, "s_v")
        K.dma(SY, self.PK["dwk"][:].rearrange("p (c k) -> p c k", c=8),
              win["dw_kernel"].rearrange("(c p) o k -> p c (o k)", p=128))
        K.dma(SY, self.RBROW[:], win["router_b"].rearrange("(a e) -> a e", a=1))
        K.dma(SY, self.MCOL[:], self.hmask)
        K.w(TT, "s_x", K.ctr["s_x"])
        K.w(TT, "s_v")
        K.op(TT, "matmul", self.PS[0][:, 0:E], self.ONEF[:], self.RBROW[:],
             start=True, stop=True, inc=("s_pe", 1))
        K.w(V, "s_pe")
        K.op(V, "tensor_copy", self.RB128[:], self.PS[0][:, 0:E])
        K.di(V, "s_v")
        self.prep_v = K.ctr["s_v"]

    def ln_stats_rows(self, tcols, bank, rowbuf, eps, with_mean=True, rms=False,
                      srcs=None, scr=None, raw=False, precise=False):
        K = self.K
        t0, tn = tcols
        srcs = srcs or [self.X[c][:, t0:t0 + tn] for c in range(8)]
        scr = scr if scr is not None else self.SCR
        PS = self.PS
        for c in range(8):
            if precise:
                K.w(V, "s_pe", self._lnst_pe)
            else:
                K.w(V, "s_pe")
            K.op(V, "tensor_mul", scr[:, 0:tn], srcs[c], srcs[c], inc=("s_v", 1))
            sqv = K.ctr["s_v"]
            K.w(TT, "s_v", sqv if precise else None)
            if not rms:
                K.mm(PS[bank][0:1, 0:tn], self.ONC[:], srcs[c], start=(c == 0), stop=(c == 7))
            K.mm(PS[bank + 1][0:1, 0:tn], self.ONC[:], scr[:, 0:tn],
                 start=(c == 0), stop=(c == 7), inc=("s_pe", 1))
            self._lnst_pe = K.ctr["s_pe"]
        K.w(A, "s_pe", self._lnst_pe if precise else None)
        if not precise:
            K.w(A, "s_x")
        if not rms:
            K.op(A, "activation", rowbuf[0:1, 0:tn], PS[bank][0:1, 0:tn], AF.Identity,
                 scale=1.0 / D)
        K.op(A, "activation", rowbuf[0:1, TILE:TILE + tn], PS[bank + 1][0:1, 0:tn],
             AF.Identity, scale=1.0 / D)
        K.di(A, "s_a")
        if raw:
            return
        self.finish_rows(rowbuf, tn, eps, with_mean, scr)

    def finish_rows(self, rowbuf, tn, eps, with_mean, scr=None):
        K = self.K
        scr = scr if scr is not None else self.SCR
        K.w(V, "s_a")
        if with_mean:
            K.op(V, "tensor_mul", scr[0:1, 0:tn], rowbuf[0:1, 0:tn], rowbuf[0:1, 0:tn])
            K.op(V, "drain")
            K.op(V, "tensor_sub", rowbuf[0:1, TILE:TILE + tn],
                 rowbuf[0:1, TILE:TILE + tn], scr[0:1, 0:tn])
            K.di(V, "s_v")
            K.w(A, "s_v")
        K.op(A, "activation", scr[0:1, 0:tn], rowbuf[0:1, TILE:TILE + tn], AF.Ln, bias=eps)
        K.op(A, "drain")
        K.op(A, "activation", scr[0:1, 0:tn], scr[0:1, 0:tn], AF.Exp, scale=-0.5)
        K.di(A, "s_a")
        K.w(V, "s_a")
        if with_mean:
            K.op(V, "scalar_tensor_tensor", rowbuf[0:1, TILE:TILE + tn],
                 rowbuf[0:1, 0:tn], -1.0, scr[0:1, 0:tn], ALU.mult, ALU.mult)
            K.op(V, "drain")
        else:
            K.op(V, "memset", rowbuf[0:1, TILE:TILE + tn], 0.0)
        K.op(V, "tensor_copy", rowbuf[0:1, 0:tn], scr[0:1, 0:tn])
        K.di(V, "s_v")

    def ln_bcast(self, rowsrc, tn, banks, abr, beta=True):
        K = self.K
        K.w(TT, "s_v")
        K.w(TT, "s_a")
        K.mm(self.PS[banks[0]][:, 0:tn], self.ONEF[:], rowsrc[0:1, 0:tn],
             start=True, stop=True, inc=None if beta else ("s_pe", 1))
        if beta:
            K.mm(self.PS[banks[1]][:, 0:tn], self.ONEF[:], rowsrc[0:1, TILE:TILE + tn],
                 start=True, stop=True, inc=("s_pe", 1))
        K.w(A, "s_pe")
        K.op(A, "activation", abr[:, 0:tn], self.PS[banks[0]][:, 0:tn], AF.Identity)
        if beta:
            K.op(A, "activation", abr[:, TILE:TILE + tn], self.PS[banks[1]][:, 0:tn],
                 AF.Identity)
        K.di(A, "s_a")
        return K.ctr["s_a"]

    def ln_apply_chunk(self, xsrc, tn, abr, wcol, bcol, out_bf, scrf):
        K = self.K
        K.w(V, "s_a")
        K.op(V, "tensor_mul", scrf[:, 0:tn], xsrc, abr[:, 0:tn])
        K.op(V, "drain")
        K.op(V, "tensor_add", scrf[:, 0:tn], scrf[:, 0:tn], abr[:, TILE:TILE + tn])
        K.di(V, "s_v")
        K.w(A, "s_v")
        if wcol is not None:
            K.op(A, "activation", out_bf, scrf[:, 0:tn], AF.Identity, scale=wcol, bias=bcol)
        else:
            K.op(A, "activation", out_bf, scrf[:, 0:tn], AF.Identity)
        K.di(A, "s_a")
        return K.ctr["s_a"]

    def initial(self, nsl=T // SL, do_ln=True):
        K = self.K
        K.phase()
        TMJ = [K.sb(f"iTMJ{i}", [128, D], F32, local=True) for i in range(2)]
        TSQ = K.sb("iTSQ", [128, D], F32, local=True)
        AL = K.sb("iAL", [128, 4], F32, local=True)
        X, PS, IDF = self.X, self.PS, self.IDF
        K.w(TT, "s_g", self.g_masks)
        K.w(A, "s_g", self.g_masks)
        K.w(V, "s_g", self.g_masks)
        K.w(V, "s_v", self.prep_v)
        for s in range(nsl):
            par = s % 2
            K.w(SY, "s_pe")
            ld = K.dma(SY, TMJ[par][:], self.xin[s * SL:(s + 1) * SL, :])
            K.w(V, "s_x", ld)
            K.w(V, "s_pe")
            if not do_ln:
                K.op(V, "tensor_copy", TSQ[:], TMJ[par][:])
                K.di(V, "s_v")
                self._init_transpose(s, TSQ, par)
                continue
            K.op(V, "tensor_mul", TSQ[:], TMJ[par][:], TMJ[par][:])
            K.op(V, "reduce_sum", AL[:, 0:1], TMJ[par][:], axis=AX.X)
            K.op(V, "drain")
            K.op(V, "reduce_sum", AL[:, 1:2], TSQ[:], axis=AX.X)
            K.op(V, "drain")
            K.op(V, "tensor_scalar", AL[:, 0:1], AL[:, 0:1], 1.0 / D, None, ALU.mult)
            K.op(V, "tensor_scalar", AL[:, 1:2], AL[:, 1:2], 1.0 / D, None, ALU.mult)
            K.op(V, "drain")
            K.op(V, "tensor_mul", TSQ[:, 0:1], AL[:, 0:1], AL[:, 0:1])
            K.op(V, "drain")
            K.op(V, "tensor_sub", AL[:, 1:2], AL[:, 1:2], TSQ[:, 0:1])
            K.di(V, "s_v")
            K.w(A, "s_v")
            K.op(A, "activation", AL[:, 2:3], AL[:, 1:2], AF.Ln, bias=EPS_LN)
            K.op(A, "drain")
            K.op(A, "activation", AL[:, 2:3], AL[:, 2:3], AF.Exp, scale=-0.5)
            K.di(A, "s_a")
            K.w(V, "s_a")
            K.op(V, "scalar_tensor_tensor", AL[:, 3:4], AL[:, 0:1], -1.0, AL[:, 2:3],
                 ALU.mult, ALU.mult)
            K.op(V, "drain")
            K.op(V, "tensor_scalar", TSQ[:], TMJ[par][:], AL[:, 2:3], AL[:, 3:4],
                 ALU.mult, ALU.add)
            K.di(V, "s_v")
            self._init_transpose(s, TSQ, par)

    def _init_transpose(self, s, TSQ, par):
        K = self.K
        X, PS, IDF = self.X, self.PS, self.IDF
        K.w(TT, "s_v")
        for c in range(8):
            K.op(TT, "transpose", PS[c // 4][:, (c % 4) * 128:(c % 4 + 1) * 128],
                 TSQ[:, c * 128:(c + 1) * 128], IDF[:],
                 inc=("s_pe", 1) if c == 7 else None)
        K.w(V, "s_pe")
        K.w(A, "s_pe")
        for c in range(8):
            eng, meth = (V, "tensor_copy") if c < 4 else (A, "copy")
            K.op(eng, meth, X[c][:, s * SL:(s + 1) * SL],
                 PS[c // 4][:, (c % 4) * 128:(c % 4 + 1) * 128])
        K.di(V, "s_v")
        K.di(A, "s_a")
        K.w(TT, "s_v")
        K.w(TT, "s_a")

    def pass1(self, l):
        K = self.K
        K.phase()
        XBS = K.sb("p1XBS", [128, 2 * 8 * 128], BF16, local=True)
        KK = [K.sb(f"p1K{i}", [128, 512], BF16, local=True) for i in range(2)]
        VV = [K.sb(f"p1V{i}", [128, 512], BF16, local=True) for i in range(2)]
        USB = K.sb("p1U", [128, 1024], F32, local=True)
        DRW = K.sb("p1DRW", [2, 512], F32, local=True)
        SKK = K.sb("p1SKK", [2, 1024], BF16, local=True)
        SKV = K.sb("p1SKV", [2, 1024], BF16, local=True)
        MRIT = K.sb("p1MRI", [2, TILE], BF16, local=True)
        MRX = K.sb("p1MRX", [1, TILE], BF16, local=True)
        RSTDC = K.sb("p1RC", [128, 8], F32, local=True)
        GB1 = K.sb("p1GB", [128, 16], BF16, local=True)
        FINROW = self.ABR
        X, PS = self.X, self.PS
        WK = self.WS[l % 2]
        WV = self.WS[1 - l % 2]

        K.w(TT, "s_w", self.wld["wv"][l])
        K.op(V, "memset", USB[:], 0.0, inc=("s_v", 1))
        for kc in range(8):
            K.op(A, "activation", GB1[:, kc * 2:kc * 2 + 1],
                 self.PK["ln1_w"][:, l * 8 + kc:l * 8 + kc + 1], AF.Identity)
            K.op(A, "activation", GB1[:, kc * 2 + 1:kc * 2 + 2],
                 self.PK["ln1_b"][:, l * 8 + kc:l * 8 + kc + 1], AF.Identity)
        K.di(A, "s_a")
        gb1_done = K.ctr["s_a"]

        def skck(W, DST):
            K.w(TT, "s_a", gb1_done)
            for half in range(2):
                K.w(TT, "s_a")
                for kc in range(8):
                    K.mm(PS[0][0:2, 0:512], GB1[:, kc * 2:kc * 2 + 2],
                         W[:, kc * 1024 + half * 512: kc * 1024 + (half + 1) * 512],
                         start=(kc == 0), stop=(kc == 7),
                         inc=("s_pe", 1) if kc == 7 else None)
                pe = K.ctr["s_pe"]
                K.w(A, "s_pe", pe)
                K.op(A, "activation", DST[0:2, half * 512:(half + 1) * 512],
                     PS[0][0:2, 0:512], AF.Identity, inc=("s_a", 1))
            return K.ctr["s_a"]

        skv_done = skck(WV, SKV)
        skk_done = [0]

        self.rows_x = [0] * NT
        self.mrid1_wr = [0] * NT
        fin_done = [0] * NT
        rc_tr = [0] * NT
        mrit_ld = 0
        fix_last = 0
        kfree = [0, 0]
        vfree = [0, 0]
        kkfree = [0, 0]
        xbs_use = [0, 0]

        def stats1(t):
            K.w(TT, "s_a")
            self.ln_stats_rows((t * TILE, TILE), 0, self.ROW, EPS_LN, scr=self.SCR2,
                               raw=True)
            fin_done[t] = self.finish2(t, t % 2, FINROW, MRX,
                                       rc_tr[t - 2] if t >= 2 else 0,
                                       self.mrid1.ap()[t:t + 1, :],
                                       self.rowsd.ap()[t][0:1, 0:TILE])
            self.mrid1_wr[t] = self.rows_x[t] = self.rows2_wr[t]

        stats1(0)
        stats1(1)

        pend = None
        cast_done_sl = [0] * (NT * 4)

        def emit_cast(u):
            ut, us = u // 4, u % 4
            usp = u % 2
            K.w(A, "s_pe", xbs_use[usp])
            for c in range(8):
                K.op(A, "activation",
                     XBS[:, usp * 1024 + c * 128: usp * 1024 + (c + 1) * 128],
                     X[c][:, ut * TILE + us * 128: ut * TILE + (us + 1) * 128],
                     AF.Identity,
                     scale=self.PK["ln1_w"][:, l * 8 + c:l * 8 + c + 1],
                     inc=("s_a", 1) if c == 7 else None)
            cast_done_sl[u] = K.ctr["s_a"]

        def emit_u(kb, half, ev_done, isfirst, islast):
            K.w(TT, "s_a", ev_done)
            for hh in range(4):
                hcol = hh * 128
                K.mm(PS[kb][:, hcol:hcol + 128], KK[half][:, hcol:hcol + 128],
                     VV[half][:, hcol:hcol + 128], start=True, stop=True)
            K.mm(PS[6][0:2, 0:512], self.QMD[:, half * 2:half * 2 + 2],
                 KK[half][:], start=isfirst, stop=islast, inc=("s_pe", 1))
            kkfree[half] = K.ctr["s_pe"]
            K.w(V, "s_pe", kkfree[half])
            K.op(V, "tensor_add", USB[:, half * 512:(half + 1) * 512],
                 USB[:, half * 512:(half + 1) * 512], PS[kb][:], inc=("s_v", 1))
            kfree[half] = K.ctr["s_v"]

        for t in range(NT):
            t0 = t * TILE
            par = t % 2
            K.w(TT, "s_a", fin_done[t])
            K.w(TT, "s_a")
            for sl in range(4):
                K.op(TT, "transpose", PS[0][:, sl:sl + 1],
                     FINROW[0:1, par * TILE + sl * 128: par * TILE + sl * 128 + 128],
                     self.IDF[0:1, 0:1], inc=("s_pe", 1) if sl == 3 else None)
            rc_tr[t] = K.ctr["s_pe"]
            K.w(A, "s_pe", rc_tr[t])
            K.op(A, "activation", RSTDC[:, par * 4:par * 4 + 4], PS[0][:, 0:4],
                 AF.Identity)
            K.di(A, "s_a")
            K.w(SY, "s_pe", fix_last)
            K.w(SY, "s_st", self.mrid1_wr[t])
            mrit_ld = K.dma(SY, MRIT[0:2, :],
                            self.mrid1.ap()[t].rearrange("(a b) -> a b", a=2), "s_w2a")
            for s in range(4):
                sp = s % 2
                u = t * 4 + s
                if u == 0:
                    emit_cast(0)
                if u + 1 < NT * 4:
                    emit_cast(u + 1)
                cast_a = cast_done_sl[u]
                for half in range(2):
                    kb, vb = 2 + half, 4 + half
                    K.w(TT, "s_a", cast_a)
                    K.w(TT, "s_a", vfree[half])
                    K.w(TT, "s_w2a", mrit_ld)
                    K.w(TT, "s_a", skv_done)
                    for kc in range(8):
                        K.mm(PS[vb][:],
                             XBS[:, sp * 1024 + kc * 128: sp * 1024 + (kc + 1) * 128],
                             WV[:, kc * 1024 + half * 512: kc * 1024 + (half + 1) * 512],
                             start=(kc == 0), stop=False)
                    K.mm(PS[vb][:], MRIT[0:2, s * 128:(s + 1) * 128],
                         SKV[0:2, half * 512:(half + 1) * 512],
                         start=False, stop=True, inc=("s_pe", 1))
                    v_done = K.ctr["s_pe"]
                    if t == 0 and s == 0 and half == 0:
                        K.w(TT, self.wk_sem[l], self.wld["wk"][l])
                        skk_done[0] = skck(WK, SKK)
                    K.w(TT, "s_v", kfree[half])
                    K.w(TT, "s_a", skk_done[0])
                    for kc in range(8):
                        K.mm(PS[kb][:],
                             XBS[:, sp * 1024 + kc * 128: sp * 1024 + (kc + 1) * 128],
                             WK[:, kc * 1024 + half * 512: kc * 1024 + (half + 1) * 512],
                             start=(kc == 0), stop=False)
                    K.mm(PS[kb][:], MRIT[0:2, s * 128:(s + 1) * 128],
                         SKK[0:2, half * 512:(half + 1) * 512],
                         start=False, stop=True, inc=("s_pe", 1))
                    k_done = K.ctr["s_pe"]
                    K.w(A, "s_pe", v_done)
                    K.w(A, "s_pe", kkfree[half])
                    K.op(A, "activation", VV[half][:], PS[vb][:], AF.Identity,
                         scale=RSTDC[:, par * 4 + s:par * 4 + s + 1], inc=("s_a", 1))
                    vfree[half] = K.ctr["s_a"]
                    K.w(A, "s_pe", k_done)
                    K.op(A, "activation", KK[half][:], PS[kb][:], AF.Exp,
                         scale=RSTDC[:, par * 4 + s:par * 4 + s + 1], inc=("s_a", 1))
                    ev_done = K.ctr["s_a"]
                    if pend is not None:
                        emit_u(*pend)
                    pend = (kb, half, ev_done, t == 0 and s == 0 and half == 0,
                            t == NT - 1 and s == 3 and half == 1)
                xbs_use[sp] = K.ctr["s_pe"]
            fix_last = K.ctr["s_pe"]
            if t + 2 < NT:
                stats1(t + 2)
        emit_u(*pend)
        K.w(A, "s_pe")
        K.op(A, "activation", DRW[0:2, :], PS[6][0:2, 0:512], AF.Identity)
        K.di(A, "s_a")
        K.w(SY, "s_v")
        K.w(SY, "s_a")
        K.dma(SY, self.cc_in.ap()[0:1024, :].rearrange("(h d) e -> d h e", d=128),
              USB[:].rearrange("p (h e) -> p h e", h=8))
        n = K.dma(SY, self.cc_in.ap()[1024:1032, :].rearrange("(a h) d -> a h d", a=2),
                  DRW[0:2, :].rearrange("a (h d) -> a h d", h=4))
        K.w(SY, "s_x")
        K.w(G, "s_x", n)
        K.op(G, "collective_compute", "AllReduce", ALU.add,
             replica_groups=RG,
             ins=[self.cc_in.ap()], outs=[self.cc_out.ap()], inc=("s_cc", 1))
        K.w(G, "s_pe")
        self.load_wb(WK[:], "wq", l)
        self.wld["wq"][l] = self.wld["wo"][l] = self.load_wb(WV[:], "wo", l)
        K.w(G, "s_w")
        if l + 1 < DEPTH:
            self.cast_w12(l + 1)
            K.w(G, "s_c")
            self.cast_qkvo(l + 1)
            K.w(G, "s_c")

    def pass2a_prep(self, l):
        K = self.K
        PS = self.PS
        USB2 = K.sb("p2U", [128, 1024], F32, local=True)
        DTMP = K.sb("p2DT", [8, 128], F32, local=True)
        DCOL = K.sb("p2DC", [128, 8], F32, local=True)
        K.w(G, "s_cc")
        K.dma(G, DTMP[:], self.cc_out.ap()[1024:1032, :], "s_w")
        n = K.dma(G, USB2[:].rearrange("p (h e) -> p h e", h=8),
                  self.cc_out.ap()[0:1024, :].rearrange("(h d) e -> d h e", d=128), "s_w")
        K.w(G, "s_w")
        K.w(TT, "s_w", n)
        K.w(TT, "s_v")
        K.op(TT, "transpose", PS[0][:, 0:8], DTMP[:], self.IDF[0:8, 0:8],
             inc=("s_pe", 1))
        K.w(A, "s_pe")
        K.op(A, "activation", DCOL[:], PS[0][:, 0:8], AF.Ln)
        K.op(A, "drain")
        K.op(A, "activation", DCOL[:], DCOL[:], AF.Exp, scale=-1.0)
        K.di(A, "s_a")
        self._prep_a = K.ctr["s_a"]
        K.w(V, "s_a")
        K.w(V, "s_pe")
        for h in range(8):
            K.op(V, "tensor_scalar", self.CTX[:, h * 128:(h + 1) * 128],
                 USB2[:, h * 128:(h + 1) * 128], DCOL[:, h:h + 1], None, ALU.mult)
        K.di(V, "s_v")
        self.ctx_v = K.ctr["s_v"]

    def pass2a(self, l):
        K = self.K
        K.phase()
        self.pass2a_prep(l)
        X, PS = self.X, self.PS
        XB = K.sb("p2XB", [128, 8 * TILE], BF16, local=True)
        QE = K.sb("p2QE", [128, 8 * TILE], BF16, local=True)
        ATT = QE
        SEL8B = K.sb("p2SEL", [8, 8 * 128], BF16, local=True)
        QROWB = K.sb("p2QR", [8, TILE], BF16, local=True)
        MRIT = K.sb("p2MRI", [2, TILE], BF16, local=True)
        SKQ = K.sb("p2SKQ", [2, 1024], BF16, local=True)
        GB1 = K.sb("p2GB", [128, 16], BF16, local=True)
        RSTDB = self.ABR
        WQ = self.WS[l % 2]
        WO = self.WS[1 - l % 2]
        SCRX = [self.SCR, self.SCR2]

        K.op(G, "memset", SEL8B[:], 1.0)
        K.op(G, "drain")
        K.op(G, "affine_select", out=SEL8B[:], in_=SEL8B[:], compare_op=ALU.is_gt,
             fill=0.0, base=1, pattern=[[1, 1024]], channel_multiplier=-128)
        K.op(G, "drain")
        K.op(G, "affine_select", out=SEL8B[:], in_=SEL8B[:], compare_op=ALU.is_gt,
             fill=0.0, base=128, pattern=[[-1, 1024]], channel_multiplier=128)
        K.op(G, "drain", inc=("s_g", 1))
        sel8_g = K.ctr["s_g"]

        for kc in range(8):
            K.op(A, "activation", GB1[:, kc * 2:kc * 2 + 1],
                 self.PK["ln1_w"][:, l * 8 + kc:l * 8 + kc + 1], AF.Identity)
            K.op(A, "activation", GB1[:, kc * 2 + 1:kc * 2 + 2],
                 self.PK["ln1_b"][:, l * 8 + kc:l * 8 + kc + 1], AF.Identity)
        K.di(A, "s_a")
        gb1_done = K.ctr["s_a"]
        K.w(TT, "s_w", self.wld["wq"][l])
        K.w(TT, "s_a", gb1_done)
        skq_done = 0
        for half in range(2):
            K.w(TT, "s_a")
            for kc in range(8):
                K.mm(PS[4][0:2, 0:512], GB1[:, kc * 2:kc * 2 + 2],
                     WQ[:, kc * 1024 + half * 512: kc * 1024 + (half + 1) * 512],
                     start=(kc == 0), stop=(kc == 7),
                     inc=("s_pe", 1) if kc == 7 else None)
            pe = K.ctr["s_pe"]
            K.w(A, "s_pe", pe)
            K.op(A, "activation", SKQ[0:2, half * 512:(half + 1) * 512],
                 PS[4][0:2, 0:512], AF.Identity, inc=("s_a", 1))
            skq_done = K.ctr["s_a"]

        xcast_done = [0] * NT

        def emit_xcast(tt):
            K.w(A, "s_pe", self._p2a_xbuse)
            for c in range(8):
                K.op(A, "activation", XB[:, c * TILE:(c + 1) * TILE],
                     X[c][:, tt * TILE:(tt + 1) * TILE], AF.Identity,
                     scale=self.PK["ln1_w"][:, l * 8 + c:l * 8 + c + 1],
                     inc=("s_a", 1) if c == 7 else None)
            xcast_done[tt] = K.ctr["s_a"]

        self._p2a_xbuse = 0
        rowld = 0
        mritld = 0
        bc_prev = 0
        rstdb_evac = 0
        rstdb_war = 0
        xb_use = 0
        fix_last = 0
        qe_war = 0
        scr_war = [skq_done, skq_done]
        qev = [self.ctx_v, self.ctx_v]
        aev_v, aev_a = 0, self._prep_a
        wev = [0, 0]
        qr_prev = 0
        for t in range(NT):
            t0 = t * TILE
            K.w(SY, "s_pe", bc_prev)
            K.w(SY, "s_st", self.rows_x[t])
            rowld = K.dma(SY, self.ROW[0:1, 0:TILE],
                          self.rowsd.ap()[t][0:1, 0:TILE], "s_w2b")
            K.w(TT, "s_w2b", rowld)
            K.w(TT, "s_a", rstdb_evac if t > 0 else skq_done)
            K.w(TT, "s_a", qr_prev)
            K.mm(PS[4][:, 0:TILE], self.ONEF[:], self.ROW[0:1, 0:TILE],
                 start=True, stop=True, inc=("s_pe", 1))
            bc_prev = K.ctr["s_pe"]
            K.w(A, "s_pe", bc_prev)
            K.w(A, "s_v", rstdb_war)
            K.op(A, "activation", RSTDB[:, 0:TILE], PS[4][:, 0:TILE], AF.Identity,
                 inc=("s_a", 1))
            rstdb_evac = K.ctr["s_a"]
            K.w(SY, "s_pe", fix_last)
            K.w(SY, "s_st", self.mrid1_wr[t])
            mritld = K.dma(SY, MRIT[0:2, :],
                           self.mrid1.ap()[t].rearrange("(a b) -> a b", a=2), "s_w2a")
            if t == 0:
                emit_xcast(0)
            cast_a = xcast_done[t]
            for m in range(8):
                qb = 2 + m % 2
                K.w(TT, "s_a", cast_a)
                K.w(TT, "s_w2a", mritld)
                K.w(TT, "s_v", qev[m % 2])
                for kc in range(8):
                    K.mm(PS[qb][:], WQ[:, kc * 1024 + m * 128: kc * 1024 + (m + 1) * 128],
                         XB[:, kc * TILE:(kc + 1) * TILE], start=(kc == 0), stop=False)
                K.mm(PS[qb][:], SKQ[0:2, m * 128:(m + 1) * 128], MRIT[0:2, :],
                     start=False, stop=True, inc=("s_pe", 1))
                qdone = K.ctr["s_pe"]
                K.w(V, "s_pe", qdone)
                K.w(V, "s_a", rstdb_evac)
                K.w(V, "s_a", scr_war[m % 2])
                K.op(V, "tensor_mul", SCRX[m % 2][:], PS[qb][:], RSTDB[:, 0:TILE],
                     inc=("s_v", 1))
                qev[m % 2] = K.ctr["s_v"]
                K.w(A, "s_v", qev[m % 2])
                K.w(A, "s_pe", qe_war)
                K.op(A, "activation", QE[:, m * TILE:(m + 1) * TILE], SCRX[m % 2][:],
                     AF.Exp, inc=("s_a", 1))
                scr_war[m % 2] = K.ctr["s_a"]
                K.w(TT, "s_a", scr_war[m % 2])
                if m == 0:
                    K.w(TT, "s_a", rstdb_evac)
                K.mm(PS[4][0:8, 0:TILE], self.QMASK[:, m * 8:(m + 1) * 8],
                     QE[:, m * TILE:(m + 1) * TILE], start=(m == 0), stop=(m == 7),
                     inc=("s_pe", 1) if m == 7 else None)
            xb_use = K.ctr["s_pe"]
            self._p2a_xbuse = xb_use
            fix_last = xb_use
            if t + 1 < NT:
                emit_xcast(t + 1)
            qd = K.ctr["s_pe"]
            K.w(A, "s_pe", qd)
            K.op(A, "activation", self.SCR[0:8, :], PS[4][0:8, 0:TILE], AF.Ln)
            K.op(A, "drain")
            K.op(A, "activation", QROWB[0:8, :], self.SCR[0:8, 0:TILE], AF.Exp,
                 scale=-1.0, inc=("s_a", 1))
            qr_done = K.ctr["s_a"]
            qr_prev = qr_done
            scr_war[0] = qr_done
            if t == 0:
                K.w(TT, "s_v", self.ctx_v)
                K.w(TT, "s_g", sel8_g)
            for h in range(8):
                ab = 5 if h % 2 == 0 else 6
                bb = 0 if h % 2 == 0 else 1
                K.w(TT, "s_a", qr_done)
                K.w(TT, "s_v", aev_v)
                K.w(TT, "s_a", aev_a)
                K.mm(PS[ab][:], self.CTX[:, h * 128:(h + 1) * 128],
                     QE[:, h * TILE:(h + 1) * TILE], start=True, stop=True)
                K.mm(PS[bb][:], SEL8B[:, h * 128:(h + 1) * 128], QROWB[0:8, :],
                     start=True, stop=True, inc=("s_pe", 1))
                hdone = K.ctr["s_pe"]
                K.w(A, "s_pe", hdone)
                K.w(A, "s_v")
                K.op(A, "activation", self.SCR2[:], PS[bb][:], AF.Identity,
                     inc=("s_a", 1))
                aev_a = K.ctr["s_a"]
                K.w(V, "s_a", aev_a)
                K.op(V, "tensor_mul", ATT[:, h * TILE:(h + 1) * TILE], PS[ab][:],
                     self.SCR2[:], inc=("s_v", 1))
                aev_v = K.ctr["s_v"]
            for m in range(8):
                ob = 5 if m % 2 == 0 else 6
                K.w(TT, "s_v", wev[m % 2])
                K.w(TT, "s_v", aev_v)
                for kc in range(8):
                    K.mm(PS[ob][:], WO[:, kc * 1024 + m * 128: kc * 1024 + (m + 1) * 128],
                         ATT[:, kc * TILE:(kc + 1) * TILE], start=(kc == 0),
                         stop=(kc == 7), inc=("s_pe", 1) if kc == 7 else None)
                odone = K.ctr["s_pe"]
                K.w(V, "s_pe", odone)
                K.op(V, "scalar_tensor_tensor", X[m][:, t0:t0 + TILE], PS[ob][:],
                     self.PK["bo"][:, l * 8 + m:l * 8 + m + 1], X[m][:, t0:t0 + TILE],
                     ALU.add, ALU.add, inc=("s_v", 1))
                wev[m % 2] = K.ctr["s_v"]
            qe_war = K.ctr["s_pe"]
        K.w(G, "s_pe")
        K.w(G, "s_c", self.w12_cast[l])
        self.load_eighth(l, 0, G)

    def load_eighth(self, l, e, eng):
        K = self.K
        b = (l + e) % 2
        sem = "s_w1a" if b == 0 else "s_w1b"
        WSb = self.WS[b]
        K.dma(eng, WSb[:, 0:4096].rearrange("p (m c) -> p m c", m=4),
              self.w1_s.ap()[l, 4 * e:4 * e + 4].rearrange("m p c -> p m c"), sem)
        self.pair_ld[b] = K.dma(
            eng, WSb[:, 4096:8192].rearrange("p (j c) -> p j c", j=8),
            self.w2_s.ap()[l, :, :, e * 512:(e + 1) * 512].rearrange("j p c -> p j c"), sem)
        return self.pair_ld[b]

    def finish2(self, t, par, FINROW, MRX, bc_war, mrd=None, rowd=None):
        K = self.K
        K.w(V, "s_a")
        K.op(V, "tensor_mul", self.SCR2[0:1, 0:TILE], self.ROW[0:1, 0:TILE],
             self.ROW[0:1, 0:TILE])
        K.op(V, "drain")
        K.op(V, "tensor_sub", self.ROW[0:1, TILE:2 * TILE],
             self.ROW[0:1, TILE:2 * TILE], self.SCR2[0:1, 0:TILE])
        K.di(V, "s_v")
        if mrd is None:
            mrd = self.mrid2.ap()[t:t + 1, :]
        if rowd is None:
            rowd = self.rowsd2.ap()[t:t + 1, :]
        K.w(A, "s_v")
        K.w(A, "s_st", self.mrx_dma)
        K.w(A, "s_pe", bc_war)
        K.op(A, "activation", MRX[0:1, 0:TILE], self.ROW[0:1, 0:TILE], AF.Identity,
             scale=-1.0, inc=("s_a", 1))
        fa1 = self.K.ctr["s_a"]
        K.w(SY, "s_a", fa1)
        n1 = K.dma(SY, mrd[0:1, 0:TILE], MRX[0:1, 0:TILE], "s_st")
        K.w(SY, "s_st", n1)
        K.w(A, "s_st", n1)
        K.op(A, "activation", self.ROW[0:1, TILE:2 * TILE],
             self.ROW[0:1, TILE:2 * TILE], AF.Ln, bias=EPS_LN)
        K.op(A, "drain")
        K.op(A, "activation", MRX[0:1, 0:TILE], self.ROW[0:1, TILE:2 * TILE],
             AF.Exp, scale=0.5)
        K.op(A, "activation", FINROW[0:1, par * TILE:(par + 1) * TILE],
             self.ROW[0:1, TILE:2 * TILE], AF.Exp, scale=-0.5)
        K.di(A, "s_a")
        fin_a = self.K.ctr["s_a"]
        K.w(SY, "s_a", fin_a)
        K.dma(SY, mrd[0:1, TILE:2 * TILE], MRX[0:1, 0:TILE], "s_st")
        self.mrx_dma = K.dma(SY, rowd,
                             FINROW[0:1, par * TILE:(par + 1) * TILE], "s_st")
        K.w(SY, "s_st", self.mrx_dma)
        self.rows2_wr[t] = self.mrx_dma
        return fin_a

    def pass2b(self, l):
        K = self.K
        K.phase()
        X, PS = self.X, self.PS
        XB2P = K.sb("p3XB", [128, 2 * 8 * TILE], BF16, local=True)
        HID = K.sb("p3HID", [128, 2 * 4 * TILE], BF16, local=True)
        RSTD2B = K.sb("p3RSB", [128, TILE], F32, local=True)
        MRIT2 = K.sb("p3MRI", [2, TILE], BF16, local=True)
        MRX = K.sb("p3MRX", [1, TILE], BF16, local=True)
        FINROW = self.ABR
        S1C1 = K.sb("p3SC", [2, 4 * 128], BF16, local=True)
        GB2 = K.sb("p3GB", [128, 16], BF16, local=True)

        for kc in range(8):
            K.op(A, "activation", GB2[:, kc * 2:kc * 2 + 1],
                 self.PK["ln2_w"][:, l * 8 + kc:l * 8 + kc + 1], AF.Identity)
            K.op(A, "activation", GB2[:, kc * 2 + 1:kc * 2 + 2],
                 self.PK["ln2_b"][:, l * 8 + kc:l * 8 + kc + 1], AF.Identity)
        K.di(A, "s_a")
        gb2_done = K.ctr["s_a"]

        hid_free = [0, 0]
        hid_done = [0] * NT
        hid_rd = [0, 0]
        pend_t = None
        y_free = [0, 0]
        xb2_use = [0, 0]
        xb2_wr = [0] * NT
        cast_done = [0] * NT
        fin_done = [0] * NT
        mrid2_wr = [0] * NT
        fix_done = 0
        mrit2_ld = 0
        rstdb_evac = 0
        rstdb_war = 0
        bc_done = [0] * NT
        row_ld = 0
        prev_eighth_pe = 0
        eighth_pe = 0

        def stats2(t):
            K.w(TT, "s_a")
            self.ln_stats_rows((t * TILE, TILE), 0, self.ROW, EPS_LN, scr=self.SCR2,
                               raw=True)
            fin_done[t] = self.finish2(t, t % 2, FINROW, MRX,
                                       bc_done[t - 2] if t >= 2 else 0)
            mrid2_wr[t] = self.mrx_dma

        stats2(0)
        stats2(1)

        for e in range(8):
            be = (l + e) % 2
            sem_e = "s_w1a" if be == 0 else "s_w1b"
            pair = self.pair_ld[be]
            e_gate = eighth_pe
            if e + 1 < 8:
                K.w(G, "s_pe", e_gate)
                K.w(G, "s_c", self.w12_cast[l])
                self.load_eighth(l, e + 1, G)
            K.w(TT, sem_e, pair)
            K.w(TT, "s_a", gb2_done)
            K.w(TT, "s_a", rstdb_evac)
            K.w(TT, "s_pe", fix_done)
            for m in range(4):
                for kc in range(8):
                    K.mm(PS[6][0:2, m * 128:(m + 1) * 128], GB2[:, kc * 2:kc * 2 + 2],
                         self.WS[be][:, m * 1024 + kc * 128:m * 1024 + (kc + 1) * 128],
                         start=(kc == 0), stop=(kc == 7),
                         inc=("s_pe", 1) if (m == 3 and kc == 7) else None)
            s1c1_pe = K.ctr["s_pe"]
            K.w(A, "s_pe", s1c1_pe)
            K.op(A, "activation", S1C1[0:2, :], PS[6][0:2, 0:512], AF.Identity,
                 inc=("s_a", 1))
            s1c1_done = K.ctr["s_a"]

            for t in range(NT):
                t0 = t * TILE
                buf = t % 2
                if e > 0:
                    K.w(SY, "s_pe", bc_done[t - 1] if t >= 1 else bc_done[NT - 1])
                    row_ld = K.dma(SY, FINROW[0:1, buf * TILE:(buf + 1) * TILE],
                                   self.rowsd2.ap()[t:t + 1, :], "s_w2b")
                K.w(TT, "s_a", rstdb_evac)
                K.w(TT, "s_a", s1c1_done)
                if e == 0:
                    K.w(TT, "s_a", fin_done[t])
                else:
                    K.w(TT, "s_w2b", row_ld)
                K.mm(PS[6][:, 0:TILE], self.ONEF[:],
                     FINROW[0:1, buf * TILE:(buf + 1) * TILE],
                     start=True, stop=True, inc=("s_pe", 1))
                bc_done[t] = K.ctr["s_pe"]
                K.w(A, "s_pe", bc_done[t])
                K.w(A, "s_v", rstdb_war)
                K.op(A, "activation", RSTD2B[:], PS[6][:, 0:TILE], AF.Identity,
                     inc=("s_a", 1))
                rstdb_evac = K.ctr["s_a"]
                xb = XB2P[:, buf * 8 * TILE:(buf + 1) * 8 * TILE]
                if e == 0:
                    K.w(A, "s_pe", xb2_use[buf])
                    if t >= 2:
                        K.w(A, "s_xia" if buf == 0 else "s_xib", xb2_wr[t - 2])
                    for c in range(8):
                        K.op(A, "activation", xb[:, c * TILE:(c + 1) * TILE],
                             X[c][:, t0:t0 + TILE], AF.Identity,
                             scale=self.PK["ln2_w"][:, l * 8 + c:l * 8 + c + 1],
                             inc=("s_a", 1) if c == 7 else None)
                    cast_done[t] = K.ctr["s_a"]
                    K.w(SY, "s_a", cast_done[t])
                    xb2_wr[t] = K.dma(SY, self.xb2d.ap()[:, :, t0:t0 + TILE],
                                      xb.rearrange("p (c w) -> p c w", c=8),
                                      "s_xia" if buf == 0 else "s_xib")
                    xb_rdy = ("s_a", cast_done[t])
                else:
                    K.w(SY, "s_pe", xb2_use[buf])
                    if e == 1:
                        K.w(SY, "s_xia" if buf == 0 else "s_xib",
                            xb2_wr[NT - 2 + buf])
                    n = K.dma(SY, xb.rearrange("p (c w) -> p c w", c=8),
                              self.xb2d.ap()[:, :, t0:t0 + TILE],
                              "s_xia" if buf == 0 else "s_xib")
                    xb_rdy = ("s_xia" if buf == 0 else "s_xib", n)
                K.w(SY, "s_pe", fix_done)
                mrit2_ld = K.dma(SY, MRIT2[0:2, :],
                                 self.mrid2.ap()[t].rearrange("(a b) -> a b", a=2),
                                 "s_w2a")
                def w2j(j, pt):
                    jb = 4 + j % 2
                    pb = pt % 2
                    K.w(TT, "s_v", hid_done[pt])
                    K.w(TT, "s_v", y_free[j % 2])
                    for mi in range(4):
                        K.mm(PS[jb][:, 0:TILE],
                             self.WS[be][:, 4096 + j * 512 + mi * 128:
                                         4096 + j * 512 + (mi + 1) * 128],
                             HID[:, pb * 4 * TILE + mi * TILE:
                                 pb * 4 * TILE + (mi + 1) * TILE],
                             start=(mi == 0), stop=(mi == 3),
                             inc=("s_pe", 1) if mi == 3 else None)
                    jdone = K.ctr["s_pe"]
                    K.w(V, "s_pe", jdone)
                    K.op(V, "tensor_add", X[j][:, pt * TILE:(pt + 1) * TILE],
                         X[j][:, pt * TILE:(pt + 1) * TILE],
                         PS[jb][:, 0:TILE], inc=("s_v", 1))
                    y_free[j % 2] = K.ctr["s_v"]

                for m in range(4):
                    hb = 2 + m % 2
                    K.w(TT, sem_e, pair)
                    K.w(TT, xb_rdy[0], xb_rdy[1])
                    K.w(TT, "s_w2a", mrit2_ld)
                    K.w(TT, "s_a", s1c1_done)
                    K.w(TT, "s_v", hid_free[m % 2])
                    for kc in range(8):
                        K.mm(PS[hb][:, 0:TILE],
                             self.WS[be][:, m * 1024 + kc * 128:m * 1024 + (kc + 1) * 128],
                             xb[:, kc * TILE:(kc + 1) * TILE],
                             start=(kc == 0), stop=False)
                    K.mm(PS[hb][:, 0:TILE], S1C1[0:2, m * 128:(m + 1) * 128],
                         MRIT2[0:2, :], start=False, stop=True, inc=("s_pe", 1))
                    hdone = K.ctr["s_pe"]
                    K.w(A, "s_pe", hdone)
                    K.w(A, "s_v")
                    K.op(A, "activation", self.SCR[:], PS[hb][:, 0:TILE], AF.Identity,
                         scale=LEAK, inc=("s_a", 1))
                    K.w(V, "s_a")
                    K.w(V, "s_a", rstdb_evac)
                    K.w(V, "s_pe", hid_rd[buf])
                    K.op(V, "tensor_tensor", self.SCR[:], PS[hb][:, 0:TILE],
                         self.SCR[:], ALU.max)
                    K.op(V, "drain")
                    K.op(V, "tensor_mul",
                         HID[:, buf * 4 * TILE + m * TILE:
                             buf * 4 * TILE + (m + 1) * TILE],
                         self.SCR[:], RSTD2B[:], inc=("s_v", 1))
                    hid_free[m % 2] = K.ctr["s_v"]
                    if pend_t is not None:
                        w2j(2 * m, pend_t)
                        w2j(2 * m + 1, pend_t)
                fix_done = K.ctr["s_pe"]
                xb2_use[buf] = fix_done
                if pend_t is not None:
                    hid_rd[pend_t % 2] = K.ctr["s_pe"]
                hid_done[t] = K.ctr["s_v"]
                rstdb_war = hid_done[t]
                pend_t = t
                eighth_pe = K.ctr["s_pe"]
                if e == 0 and t + 2 < NT:
                    stats2(t + 2)
            for j in range(8):
                w2j(j, pend_t)
            hid_rd[pend_t % 2] = K.ctr["s_pe"]
            pend_t = None
            eighth_pe = K.ctr["s_pe"]
            if e == 7 and l + 1 < DEPTH:
                K.w(G, "s_pe", e_gate)
                self.wld["wv"][l + 1] = self.load_wb(self.WS[l % 2][:], "wv", l + 1)
        if l + 1 < DEPTH:
            K.w(G, "s_pe")
            self.wld["wk"][l + 1] = self.load_wb(self.WS[(l + 1) % 2][:], "wk",
                                                 l + 1, "s_ac")
            self.wk_sem[l + 1] = "s_ac"

    def x1_form(self):
        K = self.K
        K.phase()
        TMJ = [K.sb(f"fTMJ{i}", [128, D], F32, local=True) for i in range(2)]
        X, PS, IDF = self.X, self.PS, self.IDF
        abrs = [self.ABR, self.ABR]
        self.ln_stats_rows((0, TILE), 0, self.ROW, EPS_LN)
        ab_done = self.ln_bcast(self.ROW, TILE, (0, 1), abrs[0])
        for t in range(NT):
            t0 = t * TILE
            abr = abrs[t % 2]
            for s in range(4):
                par = s % 2
                K.w(SY, "s_pe")
                ld = K.dma(SY, TMJ[par][:], self.xin[t0 + s * 128: t0 + (s + 1) * 128, :])
                K.w(TT, "s_x", ld)
                K.w(TT, "s_v")
                for c in range(8):
                    K.op(TT, "transpose", PS[2 + (c % 4)][:, (c // 4) * 128:(c // 4) * 128 + 128],
                         TMJ[par][:, c * 128:(c + 1) * 128], IDF[:],
                         inc=("s_pe", 1) if c == 7 else None)
                trd = K.ctr["s_pe"]
                K.w(V, "s_pe", trd)
                K.w(V, "s_a", ab_done)
                for c in range(8):
                    cs = slice(t0 + s * 128, t0 + (s + 1) * 128)
                    K.op(V, "tensor_mul", self.SCR[:, 0:128], X[c][:, cs],
                         abr[:, s * 128:(s + 1) * 128])
                    K.op(V, "drain")
                    K.op(V, "tensor_add", self.SCR[:, 0:128], self.SCR[:, 0:128],
                         abr[:, TILE + s * 128:TILE + (s + 1) * 128])
                    K.op(V, "drain")
                    K.op(V, "tensor_add", X[c][:, cs], self.SCR[:, 0:128],
                         PS[2 + (c % 4)][:, (c // 4) * 128:(c // 4) * 128 + 128])
                    K.op(V, "drain")
                K.di(V, "s_v")
            K.w(SY, "s_v")
            for c in range(8):
                self.spill_x = K.dma(SY, self.x1sp.ap()[c, :, t0:t0 + TILE],
                                     X[c][:, t0:t0 + TILE])
            if t + 1 < NT:
                self.ln_stats_rows(((t + 1) * TILE, TILE), 0, self.ROW, EPS_LN)
                ab_done = self.ln_bcast(self.ROW, TILE, (0, 1), abrs[(t + 1) % 2])

    def ffn_s1_s2(self):
        K = self.K
        K.phase()
        X, PS = self.X, self.PS
        HB = K.sb("s2HB", [128, 2], F32, local=True)
        for t in range(NT):
            self.ln_stats_rows((t * TILE, TILE), 0, self.ROW, EPS_LN)
            K.w(SY, "s_v")
            self.rows_x[t] = K.dma(SY, self.rowsd.ap()[t], self.ROW[0:1, :])
            K.w(SY, "s_x")
            if t == NT - 1:
                K.w(TT, "s_v")
                K.mm(PS[1][:, 0:2], self.ONEF[:], self.ROW[0:1, 510:512],
                     start=True, stop=True)
                K.mm(PS[2][:, 0:2], self.ONEF[:], self.ROW[0:1, TILE + 510:TILE + 512],
                     start=True, stop=True, inc=("s_pe", 1))
                K.w(V, "s_pe")
                nh = 0
                for c in range(8):
                    K.w(V, "s_x")
                    K.op(V, "tensor_mul", HB[:, 0:2], X[c][:, 4094:4096], PS[1][:, 0:2])
                    K.op(V, "drain")
                    K.op(V, "tensor_add", HB[:, 0:2], HB[:, 0:2], PS[2][:, 0:2])
                    K.di(V, "s_v")
                    K.w(SY, "s_v")
                    nh = K.dma(SY, self.hl_in.ap()[c * 128:(c + 1) * 128, :], HB[:, 0:2])
                K.w(G, "s_x", nh)
                K.op(G, "collective_compute", "AllGather", ALU.bypass,
                     replica_groups=RG,
                     ins=[self.hl_in.ap()], outs=[self.hl_out.ap()], inc=("s_cc", 1))
                K.w(G, "s_cc")
                nhl = K.dma(G, self.HALO[:].rearrange("p (c t) -> p c t", c=8),
                            self.hl_out.ap()[0:1024, :].rearrange("(c p) t -> p c t", p=128),
                            "s_w")
                K.w(G, "s_w")
                K.w(V, "s_w", nhl)
                K.op(V, "tensor_scalar", self.HALO[:], self.HALO[:], self.MCOL[:, 0:1],
                     None, ALU.mult)
                K.di(V, "s_v")

        XA = K.sb("s2XA", [128, 8 * (TILE + 2)], F32, local=True)
        XC = K.sb("s2XC", [128, 8 * TILE], BF16, local=True)
        PWT = self.WS[0]
        K.w(SY, "s_x")
        K.w(SY, "s_c", self.g_scr)
        K.w(SY, "s_pe")
        npw = 0
        for st in range(8):
            npw = K.dma(SY, PWT[:, st * 1024:(st + 1) * 1024],
                        self.pw_s.ap()[:, st * 128:(st + 1) * 128], transpose=True)
        K.w(SY, "s_x")
        MG = TILE + 2
        resv = 0
        for t in range(NT):
            t0 = t * TILE
            K.w(SY, "s_pe")
            K.w(SY, "s_x", self.rows_x[t])
            nr = K.dma(SY, self.ROW[0:1, :], self.rowsd.ap()[t])
            K.w(V, "s_x", nr)
            K.w(TT, "s_x", nr)
            K.w(V, "s_a")
            self.ln_bcast(self.ROW, TILE, (0, 1), self.ABR)
            ab = K.ctr["s_a"]
            K.w(V, "s_a", ab)
            for c in range(8):
                src = self.HALO[:, c * 2:c * 2 + 2] if t == 0 else self.CARRY[:, c * 2:c * 2 + 2]
                K.w(V, "s_a")
                K.op(V, "tensor_copy", XA[:, c * MG:c * MG + 2], src)
                K.op(V, "tensor_mul", self.SCR[:], X[c][:, t0:t0 + TILE], self.ABR[:, 0:TILE])
                K.op(V, "drain")
                K.op(V, "tensor_add", XA[:, c * MG + 2:c * MG + 2 + TILE], self.SCR[:],
                     self.ABR[:, TILE:2 * TILE])
                K.op(V, "drain")
                K.op(V, "tensor_copy", self.CARRY[:, c * 2:c * 2 + 2],
                     XA[:, c * MG + TILE:c * MG + 2 + TILE])
                K.op(V, "tensor_scalar", self.SCR[:], XA[:, c * MG:c * MG + TILE],
                     self.PK["dwk"][:, c * 3:c * 3 + 1], None, ALU.mult)
                K.op(V, "drain")
                K.op(V, "scalar_tensor_tensor", self.SCR[:], XA[:, c * MG + 1:c * MG + 1 + TILE],
                     self.PK["dwk"][:, c * 3 + 1:c * 3 + 2], self.SCR[:], ALU.mult, ALU.add)
                K.op(V, "drain")
                K.op(V, "scalar_tensor_tensor", self.SCR[:], XA[:, c * MG + 2:c * MG + 2 + TILE],
                     self.PK["dwk"][:, c * 3 + 2:c * 3 + 3], self.SCR[:], ALU.mult, ALU.add)
                K.di(V, "s_v")
                K.w(A, "s_v")
                K.op(A, "activation", XC[:, c * TILE:(c + 1) * TILE], self.SCR[:], AF.Identity,
                     bias=self.PK["vec5"][:, 0 * 8 + c:0 * 8 + c + 1])
                K.di(A, "s_a")
            xc_done = K.ctr["s_a"]
            if t == 0:
                K.w(TT, "s_x", npw)
            K.w(TT, "s_a", xc_done)
            pwv = [resv, resv]
            for m in range(8):
                pb = 2 + m % 2
                K.w(TT, "s_v", pwv[m % 2])
                for kc in range(8):
                    K.mm(PS[pb][:], PWT[:, kc * 1024 + m * 128: kc * 1024 + (m + 1) * 128],
                         XC[:, kc * TILE:(kc + 1) * TILE], start=(kc == 0), stop=(kc == 7),
                         inc=("s_pe", 1) if kc == 7 else None)
                pdone = K.ctr["s_pe"]
                K.w(A, "s_pe", pdone)
                K.w(A, "s_v")
                K.op(A, "activation", self.SCR2[:], PS[pb][:], AF.Silu,
                     bias=self.PK["vec5"][:, 1 * 8 + m:1 * 8 + m + 1])
                K.di(A, "s_a")
                K.w(V, "s_a")
                K.op(V, "tensor_add", X[m][:, t0:t0 + TILE],
                     XA[:, m * MG + 2:m * MG + 2 + TILE], self.SCR2[:])
                K.di(V, "s_v")
                pwv[m % 2] = K.ctr["s_v"]
            resv = K.ctr["s_v"]

    def ffn_s3(self):
        K = self.K
        K.phase()
        X, PS = self.X, self.PS
        LGR = K.sb("s3LGR", [128, 8], F32, local=True)
        EL = K.sb("s3EL", [128, 8], F32, local=True)
        TP = K.sb("s3TP", [128, 8], F32, local=True)
        M1 = K.sb("s3M1", [128, 4], F32, local=True)
        RWTC = K.sb("s3RWTC", [128, 8 * 8], F32, local=True)
        RWST = K.sb("s3RWST", [8, 1024], F32, local=True)
        LGD = K.sb("s3LGD", [128, 8], F32, local=True)
        self._LGD = LGD
        n = K.dma(SY, RWST[:], self.win["router_w"])
        K.w(SY, "s_x")
        K.w(TT, "s_x", n)
        for kc in range(8):
            K.op(TT, "transpose", PS[2][:, (kc % 2) * 8:(kc % 2) * 8 + 8],
                 RWST[:, kc * 128:(kc + 1) * 128], self.IDF[0:8, 0:8],
                 inc=("s_pe", 1))
            K.w(V, "s_pe")
            K.op(V, "tensor_copy", RWTC[:, kc * 8:(kc + 1) * 8],
                 PS[2][:, (kc % 2) * 8:(kc % 2) * 8 + 8])
            K.di(V, "s_v")
            K.w(TT, "s_v")
        rw_ready = K.ctr["s_v"]
        K.w(TT, "s_v", rw_ready)
        self.rw_done = [0] * NT
        for t in range(NT):
            t0 = t * TILE
            for s in range(4):
                K.w(TT, "s_v")
                for kc in range(8):
                    K.mm(PS[2][0:128, 0:8],
                         X[kc][:, t0 + s * 128:t0 + s * 128 + 128],
                         RWTC[:, kc * 8:(kc + 1) * 8], start=(kc == 0), stop=(kc == 7),
                         inc=("s_pe", 1) if kc == 7 else None)
                lgd = K.ctr["s_pe"]
                K.w(V, "s_pe", lgd)
                K.op(V, "tensor_add", LGR[:], PS[2][0:128, 0:8], self.RB128[:])
                K.op(V, "drain")
                if self.dbg == "rw" and t == 0 and s == 0:
                    K.op(V, "tensor_copy", LGD[:], LGR[:])
                    K.op(V, "drain")
                K.op(V, "reduce_max", M1[:, 0:1], LGR[:], axis=AX.X)
                K.op(V, "drain")
                K.op(V, "tensor_scalar", M1[:, 1:2], M1[:, 0:1], -1.0, None, ALU.mult)
                K.op(V, "tensor_scalar", TP[:], LGR[:], M1[:, 0:1], None, ALU.is_ge)
                K.op(V, "drain")
                K.op(V, "scalar_tensor_tensor", TP[:], TP[:], -1e30, LGR[:], ALU.mult, ALU.add)
                K.op(V, "drain")
                K.op(V, "reduce_max", M1[:, 2:3], TP[:], axis=AX.X)
                K.di(V, "s_v")
                K.w(A, "s_v")
                K.op(A, "activation", EL[:], LGR[:], AF.Exp, bias=M1[:, 1:2])
                K.op(A, "activation", M1[:, 3:4], M1[:, 2:3], AF.Exp, bias=M1[:, 1:2])
                K.op(A, "drain")
                K.op(A, "activation", M1[:, 3:4], M1[:, 3:4], AF.Ln, bias=1.0)
                K.op(A, "drain")
                K.op(A, "activation", M1[:, 3:4], M1[:, 3:4], AF.Exp, scale=-1.0)
                K.di(A, "s_a")
                K.w(V, "s_a")
                K.op(V, "tensor_scalar", TP[:], LGR[:], M1[:, 2:3], None, ALU.is_ge)
                K.op(V, "drain")
                K.op(V, "tensor_mul", TP[:], TP[:], EL[:])
                K.op(V, "drain")
                sg = t * 4 + s
                K.op(V, "tensor_scalar", self.RW[:, sg * 8:(sg + 1) * 8], TP[:],
                     M1[:, 3:4], None, ALU.mult)
                K.di(V, "s_v")
            self.rw_done[t] = K.ctr["s_v"]
        K.w(G, "s_v")
        self.xib_ld = 0
        for c in range(8):
            self.xib_ld = K.dma(G, self.xibd.ap()[c], X[c][:], "s_c")
        K.w(G, "s_c")

    def ffn_s4(self):
        K = self.K
        K.phase()
        PS = self.PS
        XIB = K.sb("s4XIB", [128, 2 * 8 * TILE], BF16, local=True)
        COMBP = K.sb("s4CP", [128, 2 * 1024], F32, local=True)
        EBR = K.sb("s4EB", [1, D], BF16, local=True)
        accd = [[0] * NT for _ in range(E)]
        last_comb_dma = [0]
        xld = [0, 0]

        def load_xib(t, buf):
            K.w(SY, "s_pe")
            sm = "s_xia" if buf == 0 else "s_xib"
            n = 0
            for c in range(8):
                n = K.dma(SY, XIB[:, buf * 8 * TILE + c * TILE: buf * 8 * TILE + (c + 1) * TILE],
                          self.xibd.ap()[c, :, t * TILE:(t + 1) * TILE], sm)
            xld[buf] = n

        K.w(SY, "s_c", self.xib_ld)
        load_xib(0, 0)
        load_xib(1, 1)
        ws_free_pe = 0
        EMAX = 1 if self.dbg == "comb1" else (2 if self.dbg == "comb2" else E)
        for e in range(EMAX):
            K.w(G, "s_ac")
            WT = self.WS[1 - e % 2]
            K.w(SY, "s_pe", ws_free_pe)
            nw = 0
            for st in range(8):
                nw = K.dma(SY, WT[:, st * 1024:(st + 1) * 1024],
                           self.ew_s.ap()[e][:, st * 128:(st + 1) * 128], transpose=True)
            K.w(SY, "s_x")
            ne = K.dma(G, EBR[:], self.eb_s.ap()[e:e + 1, :], "s_w")
            K.w(G, "s_w")
            K.w(TT, "s_x", nw)
            K.w(TT, "s_w", ne)
            for t in range(NT):
                buf = t % 2
                K.w(TT, "s_xia" if buf == 0 else "s_xib", xld[buf])
                for s in range(4):
                    if s % 2 == 0:
                        K.w(V, "s_ac", last_comb_dma[0])
                    for half in range(2):
                        yb = 2 + (s * 2 + half) % 2
                        K.w(TT, "s_a")
                        xb = XIB[:, buf * 8 * TILE:]
                        for kc in range(8):
                            K.mm(PS[yb][:],
                                 xb[:, kc * TILE + s * 128: kc * TILE + s * 128 + 128],
                                 WT[:, kc * 1024 + half * 512: kc * 1024 + (half + 1) * 512],
                                 start=(kc == 0), stop=False)
                        K.mm(PS[yb][:], self.ONEB[:],
                             EBR[0:1, half * 512:(half + 1) * 512],
                             start=False, stop=True, inc=("s_pe", 1))
                        ydone = K.ctr["s_pe"]
                        K.w(A, "s_pe", ydone)
                        K.w(A, "s_v")
                        scr = self.SCR if half == 0 else self.SCR2
                        K.op(A, "activation", scr[:], PS[yb][:], AF.Silu,
                             inc=("s_a", 1))
                        K.w(V, "s_a")
                        dst = COMBP[:, (s % 2) * 1024 + half * 512:
                                    (s % 2) * 1024 + (half + 1) * 512]
                        rwc = self.RW[:, (t * 4 + s) * 8 + e:(t * 4 + s) * 8 + e + 1]
                        K.op(V, "tensor_scalar", dst, scr[:], rwc, None, ALU.mult,
                             inc=("s_v", 1))
                    if s % 2 == 1:
                        hf0 = t * TILE + (s // 2) * 256
                        K.w(G, "s_v")
                        K.w(G, "s_ac")
                        accd[e][t] = K.dma(
                            G, self.combd.ap()[hf0:hf0 + 256, :]
                            .rearrange("(sl p) o -> p sl o", p=128),
                            COMBP[:].rearrange("p (sl o) -> p sl o", sl=2), "s_ac",
                            accum_op=(ALU.bypass if e == 0 else ALU.add))
                        last_comb_dma[0] = accd[e][t]
                if not (e == E - 1 and t >= NT - 2):
                    load_xib((t + 2) % NT, buf)
            ws_free_pe = K.ctr["s_pe"]
        self.acc_last = accd[E - 1]
        K.w(SY, "s_xia")
        K.w(SY, "s_xib")

    def ffn_s4b(self):
        K = self.K
        K.phase()
        X, PS, PSB = self.X, self.PS, self.PSB
        CMB = K.sb("s4bC", [128, 2 * 1024], F32, local=True)
        CB = K.sb("s4bCB", [128, 2 * 1024], BF16, local=True)
        CMT = K.sb("s4bT", [128, 8 * 256], BF16, local=True)
        OWT = self.WS[1]
        K.w(SY, "s_pe")
        nw = 0
        for st in range(8):
            nw = K.dma(SY, OWT[:, st * 1024:(st + 1) * 1024],
                       self.ow_s.ap()[:, st * 128:(st + 1) * 128], transpose=True)
        K.w(SY, "s_x")
        K.w(TT, "s_x", nw)
        for t in range(NT):
            t0 = t * TILE
            K.w(SY, "s_ac", self.acc_last[t])
            for hf in range(2):
                hf0 = t0 + hf * 256
                K.w(SY, "s_v")
                nc_ = K.dma(SY, CMB[:].rearrange("p (sl o) -> p sl o", sl=2),
                            self.combd.ap()[hf0:hf0 + 256, :]
                            .rearrange("(sl p) o -> p sl o", p=128))
                K.w(V, "s_x", nc_)
                K.op(V, "tensor_copy", CB[:], CMB[:])
                K.di(V, "s_v")
                cbd = K.ctr["s_v"]
                K.w(TT, "s_v", cbd)
                K.w(TT, "s_a")
                for sl in range(2):
                    for oc in range(8):
                        K.op(TT, "transpose", PSB[:, (oc % 4) * 256 + sl * 128:
                                                  (oc % 4) * 256 + sl * 128 + 128],
                             CB[:, sl * 1024 + oc * 128: sl * 1024 + (oc + 1) * 128],
                             self.IDB[:], inc=("s_pe", 1) if oc % 4 == 3 else None)
                        if oc % 4 == 3:
                            grp = K.ctr["s_pe"]
                            K.w(V, "s_pe", grp)
                            for o2 in range(oc - 3, oc + 1):
                                K.op(V, "tensor_copy",
                                     CMT[:, o2 * 256 + sl * 128: o2 * 256 + sl * 128 + 128],
                                     PSB[:, (o2 % 4) * 256 + sl * 128:
                                         (o2 % 4) * 256 + sl * 128 + 128])
                            K.di(V, "s_v")
                            K.w(TT, "s_v")
                cmt_v = K.ctr["s_v"]
                K.w(TT, "s_v", cmt_v)
                oev = [0, 0]
                for m in range(8):
                    ob = 2 + m % 2
                    K.w(TT, "s_v", oev[m % 2])
                    for oc in range(8):
                        K.mm(PS[ob][:, 0:256],
                             OWT[:, oc * 1024 + m * 128: oc * 1024 + (m + 1) * 128],
                             CMT[:, oc * 256:(oc + 1) * 256], start=(oc == 0), stop=(oc == 7),
                             inc=("s_pe", 1) if oc == 7 else None)
                    od = K.ctr["s_pe"]
                    K.w(A, "s_pe", od)
                    K.w(A, "s_v")
                    K.op(A, "activation", self.SCR2[:, 0:256], PS[ob][:, 0:256], AF.Identity,
                         bias=self.PK["vec5"][:, 2 * 8 + m:2 * 8 + m + 1])
                    K.di(A, "s_a")
                    K.w(V, "s_a")
                    K.op(V, "tensor_add", X[m][:, hf0:hf0 + 256], X[m][:, hf0:hf0 + 256],
                         self.SCR2[:, 0:256])
                    K.di(V, "s_v")
                    oev[m % 2] = K.ctr["s_v"]
            self.ln_stats_rows((t0, TILE), 0, self.ROW, EPS_RMS, with_mean=False, rms=True,
                               raw=True)
            K.w(SY, "s_a")
            self.rows_x[t] = K.dma(SY, self.rowsd.ap()[t], self.ROW[0:1, :])
            K.w(SY, "s_x")

    def ffn_s5(self):
        K = self.K
        K.phase()
        X, PS = self.X, self.PS
        HBUF = K.sb("s5HB", [128, 8 * TILE], BF16, local=True)
        FH = K.sb("s5FH", [128, 2 * TILE], BF16, local=True)
        FDT = K.sb("s5FDT", [128, 8 * HIDD], BF16, local=True)
        FUT = K.sb("s5FUT", [128, 2 * 1024], BF16, local=True)
        n = 0
        K.w(SY, "s_pe")
        for st in range(8):
            n = K.dma(SY, FDT[:, st * HIDD:(st + 1) * HIDD],
                      self.fd_s.ap()[:, st * 128:(st + 1) * 128], transpose=True)
        for st in range(2):
            n = K.dma(SY, FUT[:, st * 1024:(st + 1) * 1024],
                      self.fu_s.ap()[:, st * 128:(st + 1) * 128], transpose=True)
        K.w(SY, "s_x")
        K.w(TT, "s_x", n)
        for t in range(NT):
            t0 = t * TILE
            K.w(SY, "s_pe")
            K.w(SY, "s_x", self.rows_x[t])
            nr = K.dma(SY, self.ROW[0:1, :], self.rowsd.ap()[t])
            K.w(A, "s_x", nr)
            K.w(A, "s_v")
            K.op(A, "activation", self.ROW[0:1, 0:TILE], self.ROW[0:1, TILE:2 * TILE],
                 AF.Ln, bias=EPS_RMS)
            K.op(A, "drain")
            K.op(A, "activation", self.ROW[0:1, 0:TILE], self.ROW[0:1, 0:TILE],
                 AF.Exp, scale=-0.5)
            K.di(A, "s_a")
            self.ln_bcast(self.ROW, TILE, (0, 1), self.ABR, beta=False)
            abd = K.ctr["s_a"]
            K.w(V, "s_a", abd)
            for c in range(8):
                K.w(V, "s_a")
                K.op(V, "scalar_tensor_tensor", self.SCR[:], X[c][:, t0:t0 + TILE],
                     1.0, self.ABR[:, 0:TILE], ALU.mult, ALU.mult)
                K.di(V, "s_v")
                K.w(A, "s_v")
                K.op(A, "activation", X[c][:, t0:t0 + TILE], self.SCR[:], AF.Identity,
                     scale=self.PK["vec5"][:, 3 * 8 + c:3 * 8 + c + 1])
                K.op(A, "drain")
                K.op(A, "copy", HBUF[:, c * TILE:(c + 1) * TILE], X[c][:, t0:t0 + TILE])
                K.di(A, "s_a")
            hb_done = K.ctr["s_a"]
            K.w(TT, "s_a", hb_done)
            for m2 in range(2):
                fb = 2 + m2
                K.w(TT, "s_a")
                for kc in range(8):
                    K.mm(PS[fb][:], FDT[:, kc * HIDD + m2 * 128: kc * HIDD + (m2 + 1) * 128],
                         HBUF[:, kc * TILE:(kc + 1) * TILE], start=(kc == 0), stop=(kc == 7),
                         inc=("s_pe", 1) if kc == 7 else None)
                fd = K.ctr["s_pe"]
                K.w(A, "s_pe", fd)
                K.op(A, "activation", FH[:, m2 * TILE:(m2 + 1) * TILE], PS[fb][:], AF.Gelu,
                     bias=self.PK["fdb"][:, m2:m2 + 1])
                K.di(A, "s_a")
            fh_done = K.ctr["s_a"]
            K.w(TT, "s_a", fh_done)
            fev = [0, 0]
            for m in range(8):
                ub = 4 + m % 2
                K.w(TT, "s_v", fev[m % 2])
                for kc in range(2):
                    K.mm(PS[ub][:], FUT[:, kc * 1024 + m * 128: kc * 1024 + (m + 1) * 128],
                         FH[:, kc * TILE:(kc + 1) * TILE], start=(kc == 0), stop=(kc == 1),
                         inc=("s_pe", 1) if kc == 1 else None)
                ud = K.ctr["s_pe"]
                K.w(A, "s_pe", ud)
                K.w(A, "s_v")
                K.op(A, "activation", self.SCR2[:], PS[ub][:], AF.Identity,
                     bias=self.PK["vec5"][:, 4 * 8 + m:4 * 8 + m + 1])
                K.di(A, "s_a")
                K.w(V, "s_a")
                K.op(V, "tensor_add", X[m][:, t0:t0 + TILE], X[m][:, t0:t0 + TILE],
                     self.SCR2[:])
                K.di(V, "s_v")
                fev[m % 2] = K.ctr["s_v"]
            self.ln_stats_rows((t0, TILE), 0, self.ROW, EPS_LN, raw=True)
            K.w(SY, "s_a")
            self.rows_x[t] = K.dma(SY, self.rowsd.ap()[t], self.ROW[0:1, :])
            K.w(SY, "s_x")

    def ffn_s7(self):
        K = self.K
        K.phase()
        X, PS, IDF = self.X, self.PS, self.IDF
        X1B = K.sb("s7X1", [128, 8 * TILE], F32, local=True)
        OUTT = [K.sb(f"s7O{i}", [128, 1024], F32, local=True) for i in range(2)]
        st_done = 0
        K.w(SY, "s_x", self.spill_x)
        for t in range(NT):
            t0 = t * TILE
            K.w(SY, "s_pe")
            K.w(SY, "s_x", self.rows_x[t])
            nr = K.dma(SY, self.ROW[0:1, :], self.rowsd.ap()[t])
            K.w(SY, "s_x")
            K.w(SY, "s_v")
            nx = 0
            for c in range(8):
                nx = K.dma(SY, X1B[:, c * TILE:(c + 1) * TILE], self.x1sp.ap()[c, :, t0:t0 + TILE])
            K.w(A, "s_x", nr)
            K.w(V, "s_x", nr)
            self.finish_rows(self.ROW, TILE, EPS_LN, True)
            self.ln_bcast(self.ROW, TILE, (0, 1), self.ABR)
            abd = K.ctr["s_a"]
            K.w(V, "s_a", abd)
            K.w(V, "s_x", nx)
            for c in range(8):
                K.op(V, "tensor_mul", self.SCR[:], X[c][:, t0:t0 + TILE], self.ABR[:, 0:TILE])
                K.op(V, "drain")
                K.op(V, "tensor_add", self.SCR[:], self.SCR[:], self.ABR[:, TILE:2 * TILE])
                K.op(V, "drain")
                K.op(V, "tensor_add", X[c][:, t0:t0 + TILE], self.SCR[:],
                     X1B[:, c * TILE:(c + 1) * TILE])
                K.op(V, "drain")
            K.di(V, "s_v")
            od = K.ctr["s_v"]
            for s in range(4):
                par = s % 2
                K.w(TT, "s_v", od)
                K.w(TT, "s_st")
                K.w(TT, "s_a")
                for c in range(8):
                    K.op(TT, "transpose", PS[c // 4][:, (c % 4) * 128:(c % 4 + 1) * 128],
                         X[c][:, t0 + s * 128:t0 + (s + 1) * 128], IDF[:],
                         inc=("s_pe", 1) if c == 7 else None)
                trd = K.ctr["s_pe"]
                K.w(V, "s_pe", trd)
                K.w(A, "s_pe", trd)
                for c in range(8):
                    eng, meth = (V, "tensor_copy") if c < 4 else (A, "copy")
                    K.op(eng, meth, OUTT[par][:, c * 128:(c + 1) * 128],
                         PS[c // 4][:, (c % 4) * 128:(c % 4 + 1) * 128])
                K.di(V, "s_v")
                K.di(A, "s_a")
                K.w(SY, "s_v")
                K.w(SY, "s_a")
                st_done = K.dma(SY, self.out[t0 + s * 128:t0 + (s + 1) * 128, :], OUTT[par][:],
                                "s_st")
                K.w(TT, "s_v")
                K.w(TT, "s_a")
        K.w(G, "s_st", K.ctr["s_st"])
        K.w(G, "s_x", K.ctr["s_x"])

    def dump_X(self):
        K = self.K
        K.w(SY, "s_v")
        K.w(SY, "s_a")
        K.w(SY, "s_pe")
        n = 0
        for c in range(8):
            n = K.dma(SY, self.dump[c], self.X[c][:])
        K.w(G, "s_x", n)
        K.w(G, "s_w")
        K.w(G, "s_cc")
        K.stopped = True

    def dump_masks(self):
        K = self.K
        K.w(SY, "s_g", self.g_masks)
        K.w(SY, "s_v")
        K.w(SY, "s_a")
        n = K.dma(SY, self.dump2[0:64, :].rearrange("(a b) c -> a b c", a=8),
                  self.SEL8[:].rearrange("a (b c) -> a b c", b=8))
        n = K.dma(SY, self.dump2[128:256, :], self.IDF[:])
        n = K.dma(SY, self.dump2[256:384, 0:8], self.CONSTP[:, 0:8])
        K.w(G, "s_x", n)
        K.w(G, "s_w")
        K.stopped = True

    def dump_cc(self):
        K = self.K
        K.w(SY, "s_cc")
        n = K.dma(SY, self.dump2[:], self.cc_out.ap())
        K.w(G, "s_x", n)
        K.stopped = True

    def dump_rw(self):
        K = self.K
        K.w(SY, "s_v")
        n = K.dma(SY, self.dump2[0:128, :], self.RW[:, 0:128])
        n = K.dma(SY, self.dump2[128:256, 0:8], self._LGD[:])
        K.w(G, "s_x", n)
        K.stopped = True

    def generate(self):
        dbg = self.dbg
        if dbg == "prep":
            self.dump_masks()
            return self.finish()
        if dbg == "init2":
            self.initial(nsl=2)
            self.dump_X()
            return self.finish()
        if dbg == "initnoln":
            self.initial(do_ln=False)
            self.dump_X()
            return self.finish()
        self.initial()
        if dbg == "xint0":
            self.dump_X()
            return self.finish()
        for l in range(DEPTH):
            self.pass1(l)
            if dbg == f"l{l}_U":
                self.dump_cc()
                return self.finish()
            self.pass2a(l)
            if dbg == f"l{l}_xmid":
                self.dump_X()
                return self.finish()
            self.pass2b(l)
            if dbg == f"l{l}_x":
                self.dump_X()
                return self.finish()
        self.x1_form()
        if dbg == "x1":
            self.dump_X()
            return self.finish()
        self.ffn_s1_s2()
        if dbg == "x1i":
            self.dump_X()
            return self.finish()
        self.ffn_s3()
        if dbg == "rw":
            self.dump_rw()
            return self.finish()
        self.ffn_s4()
        if dbg in ("comb", "comb1", "comb2"):
            K = self.K
            K.w(SY, "s_ac")
            n = K.dma(SY, self.dump.rearrange("a b c -> (a b c)"),
                      self.combd.ap().rearrange("a b -> (a b)"))
            K.w(G, "s_x", n)
            K.stopped = True
            return self.finish()
        self.ffn_s4b()
        if dbg == "u":
            self.dump_X()
            return self.finish()
        self.ffn_s5()
        if dbg == "ffret":
            self.dump_X()
            return self.finish()
        self.ffn_s7()
        return self.finish()

    def finish(self):
        if self.K.ph is not None:
            self.K.ph.close()
            self.K.ph = None
        nc = self.K.build()
        self.K.es.close()
        return nc


_NC_CACHE = {}


def get_nc(dbg=None):
    if dbg not in _NC_CACHE:
        _NC_CACHE[dbg] = Mega(dbg).generate()
    return _NC_CACHE[dbg]


def kernel(**inputs):
    nc = get_nc(None)
    return _run(nc, inputs, None)


def _run(nc, inputs, dbg):
    xfull = np.ascontiguousarray(np.asarray(inputs["L0_attn_x"], np.float32))
    in_maps = []
    for c in range(8):
        b, half = c // 2, c % 2
        m = {"x": np.ascontiguousarray(xfull[b, half * 4096:(half + 1) * 4096, :]),
             "hmask": np.full((128, 1), 1.0 if half == 1 else 0.0, np.float32)}
        for nm in ["ln1_w", "ln1_b", "wq", "wk", "wv", "wo", "bo", "ln2_w", "ln2_b",
                   "w1", "w2"]:
            m["w_" + nm] = np.ascontiguousarray(np.asarray(inputs["L0_attn_" + nm], np.float32))
        for nm in ["dw_kernel", "dw_bias", "pw_w", "pw_b", "router_w", "router_b",
                   "expert_w", "expert_b", "out_w", "out_b", "norm_w", "ff_down_w",
                   "ff_down_b", "ff_up_w", "ff_up_b"]:
            m["w_" + nm] = np.ascontiguousarray(np.asarray(inputs["L0_ffn_" + nm], np.float32))
        in_maps.append(m)
    res = run_bass_kernel_spmd(nc, in_maps, core_ids=list(range(8)))
    if dbg:
        return res
    out = np.empty((4, 8192, 1024), np.float32)
    for c in range(8):
        b, half = c // 2, c % 2
        out[b, half * 4096:(half + 1) * 4096, :] = res.results[c]["out"]
    return out


def run_debug(dbg, inputs):
    nc = get_nc(dbg)
    return _run(nc, inputs, dbg)
